# version 07
# baseline (speedup 1.0000x reference)
import dataclasses
import numpy as np
import ml_dtypes

import concourse.bass as bass
import concourse.mybir as mybir
from concourse.bass_utils import run_bass_kernel_spmd

F32 = mybir.dt.float32
BF16 = mybir.dt.bfloat16
AF = mybir.ActivationFunctionType
ALU = mybir.AluOpType

E = 1024
H = 16
D = 64
S = 8192
B = 4
NCHUNK = 8
TOK = 4224
NTT = 33
NKG = 64
QGS = [512] * 8 + [128]
QG0 = [sum(QGS[:i]) for i in range(len(QGS))]
NQG = len(QGS)
EPS_LN = 1e-5
EPS_NRM = 1e-8
EPS_ATT = 1e-6
NCONST = -3.1 * S + EPS_ATT

SEM_NAMES = ["dx", "dk", "dq", "dw", "dst", "dsx", "qk", "rcp", "rv", "pe", "dve",
             "acs", "cc", "oput"]


def _act_raw(nc, eng, out, in_, func, bias=0.0, scale=1.0):
    bi = eng.activation(out=out, in_=in_, func=AF.Exp, bias=bias, scale=scale)
    i2 = dataclasses.replace(bi.ins, func=func)
    nc.register_instruction(i2, overwrite=True)
    insts = nc.cur_bb.bb.instructions
    assert insts[-1] is bi.ins
    insts[-1] = i2
    return bass.BassInstruction(i2)


class Walk:

    def __init__(self, nc, kind, eng, sems):
        self.nc = nc
        self.kind = kind
        self.eng = eng
        self.sems = sems
        self.c = {s: 0 for s in SEM_NAMES}

    def inc(self, ins, name, n=1):
        self.c[name] += n
        if ins is not None:
            ins.then_inc(self.sems[name], n)

    def wait(self, name, kind, val=None):
        if self.kind == kind:
            v = self.c[name] if val is None else val
            if v > 0:
                self.eng.wait_ge(self.sems[name], v)

    def val(self, name):
        return self.c[name]

    _OWNER = {"qk": 'pe', "rv": 'pe', "pe": 'pe', "rcp": 'act', "acs": 'act',
              "dve": 'dve', "dk": 'sync', "dq": 'sync', "dx": 'sync', "dsx": 'sync',
              "oput": 'sync', "dw": 'pool', "dst": 'pool', "cc": 'pool'}

    def barrier(self):
        for s in SEM_NAMES:
            own = self._OWNER[s]
            for k in ('pe', 'act', 'dve', 'sync', 'pool'):
                if k != own:
                    self.wait(s, k)

    def mm(self, out, lhsT, rhs, start=True, stop=True, tp=None):
        if self.kind != 'pe':
            return None
        return self.eng.matmul(out, lhsT=lhsT, rhs=rhs, start=start, stop=stop,
                               tile_position=tp)

    def trans(self, out, in_, ident):
        if self.kind != 'pe':
            return None
        return self.eng.transpose(out, in_, ident)

    def act(self, out, in_, func, bias=0.0, scale=1.0, accum_out=None):
        if self.kind != 'act':
            return None
        if func in (AF.Reciprocal, AF.Rsqrt):
            return _act_raw(self.nc, self.eng, out, in_, func, bias, scale)
        kw = {} if accum_out is None else {"accum_out": accum_out}
        return self.eng.activation(out=out, in_=in_, func=func, bias=bias,
                                   scale=scale, **kw)

    def _dve(self, f):
        if self.kind != 'dve':
            self.c["dve"] += 1
            return None
        self.eng.wait_ge(self.sems["dve"], self.c["dve"])
        ins = f()
        self.c["dve"] += 1
        ins.then_inc(self.sems["dve"], 1)
        return ins

    def ts(self, out, in0, s1, s2=None, op0=ALU.mult, op1=ALU.bypass):
        return self._dve(lambda: self.eng.tensor_scalar(
            out=out, in0=in0, scalar1=s1, scalar2=s2, op0=op0, op1=op1))

    def tt(self, out, in0, in1, op=ALU.add):
        return self._dve(lambda: self.eng.tensor_tensor(out=out, in0=in0, in1=in1, op=op))

    def stt(self, out, in0, scalar, in1, op0, op1):
        return self._dve(lambda: self.eng.scalar_tensor_tensor(
            out=out, in0=in0, scalar=scalar, in1=in1, op0=op0, op1=op1))

    def copy(self, out, in_):
        return self._dve(lambda: self.eng.tensor_copy(out=out, in_=in_))

    def memset(self, ap, v):
        return self._dve(lambda: self.eng.memset(ap, v))

    def red(self, out, in_, op=ALU.add):
        return self._dve(lambda: self.eng.tensor_reduce(out, in_, mybir.AxisListType.X, op))

    def vrecip(self, out, in_):
        def f():
            if out.dtype == BF16:
                with self.nc.allow_low_precision(reason="bf16 store of fp32 reciprocal"):
                    return self.eng.reciprocal(out=out, in_=in_)
            return self.eng.reciprocal(out=out, in_=in_)
        return self._dve(f)

    def dma(self, queue, out, in_):
        if self.kind != queue:
            return None
        return self.eng.dma_start(out=out, in_=in_)

    def cc(self, ins_, outs_):
        if self.kind != 'pool':
            return None
        return self.eng.collective_compute(
            "AllReduce", ALU.add, replica_groups=[[0, 1], [2, 3], [4, 5], [6, 7]],
            ins=ins_, outs=outs_)


class L:
    pass


def build_program(stop_stage=None):
    nc = bass.Bass("TRN2", target_bir_lowering=False, debug=False)
    t = L()

    t.xq = nc.dram_tensor("xq", [TOK, E], F32, kind="ExternalInput").ap()
    t.xfull = nc.dram_tensor("xfull", [S, E], F32, kind="ExternalInput").ap()
    for nm in ["w_aq", "w_ak", "w_av", "w_ao", "w_fr", "w_fk", "w_fv",
               "w_bq", "w_bk", "w_bv", "w_bo"]:
        setattr(t, nm, nc.dram_tensor(nm, [E, E], F32, kind="ExternalInput").ap())
    t.b_rows = nc.dram_tensor("b_rows", [5, E], F32, kind="ExternalInput").ap()
    t.b_chunk = nc.dram_tensor("b_chunk", [128, 4 * NCHUNK], F32, kind="ExternalInput").ap()
    t.mu_chunk = nc.dram_tensor("mu_chunk", [128, 2 * NCHUNK], F32, kind="ExternalInput").ap()
    t.eyef = nc.dram_tensor("eyef", [128, 128], F32, kind="ExternalInput").ap()
    t.eyeb = nc.dram_tensor("eyeb", [128, 128], BF16, kind="ExternalInput").ap()
    t.maskv = nc.dram_tensor("maskv", [128, NTT], F32, kind="ExternalInput").ap()

    t.knT_hbm = nc.dram_tensor("knT_hbm", [NKG, 128, NCHUNK * 128], BF16, kind="Internal").ap()
    t.vk_hbm = nc.dram_tensor("vk_hbm", [NKG, 128, H * 65], BF16, kind="Internal").ap()
    t.qn_hbm = nc.dram_tensor("qn_hbm", [NCHUNK, 128, TOK], BF16, kind="Internal").ap()
    t.qb_hbm = nc.dram_tensor("qb_hbm", [NCHUNK, 128, TOK], BF16, kind="Internal").ap()
    t.kv_in = nc.dram_tensor("kv_in", [128, 1024], F32, kind="Internal").ap()
    t.kv_out = nc.dram_tensor("kv_out", [128, 1024], F32, kind="Internal").ap()
    t.x_hbm = nc.dram_tensor("x_hbm", [NCHUNK, 128, TOK], F32, kind="Internal").ap()
    t.out_h = nc.dram_tensor("out", [TOK, E], F32, kind="ExternalOutput").ap()
    t.dbg = None
    if stop_stage is not None:
        t.dbg = nc.dram_tensor("dbg", [128, NCHUNK * TOK], F32, kind="ExternalOutput").ap()

    import contextlib
    with contextlib.ExitStack() as st:
        def sb(name, shape, dt):
            return st.enter_context(nc.sbuf_tensor(name, shape, dt))

        def ps(name, shape):
            return st.enter_context(nc.psum_tensor(name, shape, F32))

        t.xg = sb("xg", [128, NCHUNK * 513], F32)
        t.xfst = sb("xfst", [128, 1024], F32)
        t.consts = sb("consts", [128, 16], F32)
        t.onesb = sb("onesb", [128, 576], BF16)
        t.eyef_sb = sb("eyef_sb", [128, 128], F32)
        t.eyeb_sb = sb("eyeb_sb", [128, 128], BF16)
        t.maskv_sb = sb("maskv_sb", [128, NTT], F32)
        t.brow_sb = sb("brow_sb", [1, 5 * E], BF16)
        t.brow_lo = sb("brow_lo", [1, 5 * E], BF16)
        t.bchunk_sb = sb("bchunk_sb", [128, 4 * NCHUNK], F32)
        t.mu_sb = sb("mu_sb", [128, 2 * NCHUNK], F32)
        t.ksum_bf = sb("ksum_bf", [128, NCHUNK], BF16)
        t.vsum_sb = sb("vsum_sb", [64, H], F32)
        t.kvblk_a = sb("kvblk_a", [128, NCHUNK * 128], BF16)
        t.kvblk_b = sb("kvblk_b", [128, NCHUNK * 128], BF16)
        t.kvacc_b = sb("kvacc_b", [128, 1024], F32)
        t.wbig = sb("wbig", [128, 3 * 8192], BF16)
        t.wcol = sb("wcol", [128, 2 * 1024], BF16)
        t.stage = sb("stage", [128, 2048 + H * 65], BF16)
        t.xtok = sb("xtok", [128, 2 * 1024], F32)
        t.scrf = sb("scrf", [128, 3 * 1024], F32)
        t.scrb = sb("scrb", [128, 4 * 1024], BF16)
        t.sml = sb("sml", [128, 96], F32)
        t.kvacc_a = sb("kvacc_a", [128, 1024], F32)
        t.kstream = sb("kstream", [128, 2 * 1024], BF16)
        t.vstream = sb("vstream", [128, 2 * 1040], BF16)
        t.qtile = sb("qtile", [128, 2 * 512], BF16)
        t.rbuf = sb("rbuf", [128, 2 * 1024], BF16)
        t.ut = sb("ut", [64, 2 * 512], F32)
        t.invn = sb("invn", [65, 1024], BF16)
        t.ofb = sb("ofb", [128, NCHUNK * 512], BF16)
        t.xn_sb = sb("xn_sb", [128, NCHUNK * 520], BF16)
        t.ffb = sb("ffb", [128, 2 * NCHUNK * 512], BF16)
        t.kkb = sb("kkb", [128, NCHUNK * 512], BF16)

        t.p_qk_a = ps("p_qk_a", [128, 1024])
        t.p_qk_b = ps("p_qk_b", [128, 1024])
        t.p_kv = ps("p_kv", [128, 1024])
        t.p_mis = ps("p_mis", [128, 1024])

        for i, v in enumerate([1.1, -1.0, 0.0, 1.0, EPS_LN, EPS_NRM, 4.41, -3.1,
                               1.0 / E, NCONST, 0.5, 2.0]):
            nc.const_aps.register(F32, v, t.consts[:, i:i + 1])

        sems = {s: st.enter_context(nc.semaphore(s)) for s in SEM_NAMES}

        with nc.Block() as block:
            @block.sync
            def _(eng):
                _emit(Walk(nc, 'sync', eng, sems), t, stop_stage)

            @block.tensor
            def _(eng):
                _emit(Walk(nc, 'pe', eng, sems), t, stop_stage)

            @block.scalar
            def _(eng):
                _emit(Walk(nc, 'act', eng, sems), t, stop_stage)

            @block.vector
            def _(eng):
                _emit(Walk(nc, 'dve', eng, sems), t, stop_stage)

            @block.gpsimd
            def _(eng):
                _emit(Walk(nc, 'pool', eng, sems), t, stop_stage)

    return nc



def _emit(w: Walk, t, stop_stage):
    _setup(w, t)
    if stop_stage == 'p0':
        _prep(w, t, stop_p0=True)
        _fin(w, t); return
    _prep(w, t)
    w.barrier()
    if stop_stage == 'prep':
        _dump(w, t); _fin(w, t); return
    if stop_stage == 'a0':
        _attn_a(w, t, 0)
        _dump_a0(w, t); _fin(w, t); return
    for qg in range(NQG):
        _attn_a(w, t, qg)
        w.barrier()
    if stop_stage == 'x1':
        _dump(w, t); _fin(w, t); return
    _load3(w, t, [t.w_fr, t.w_fk, t.w_fv])
    for qg in range(NQG):
        _ffn(w, t, qg)
        w.barrier()
    if stop_stage == 'x2':
        _dump(w, t); _fin(w, t); return
    _load3(w, t, [t.w_bq, t.w_bk, t.w_bv])
    for qg in range(NQG):
        _attnb_pre(w, t, qg)
        w.barrier()
        if stop_stage == 'b0' and qg == 0:
            w.wait("dve", 'pool')
            w.wait("dst", 'pool')
            d = t.dbg
            w.inc(w.dma('pool', d[:, 0:1024], t.kvacc_b[:, :]), "oput", 16)
            w.inc(w.dma('pool', d[:, 1024:2048], t.kvblk_b[:, :]), "oput", 16)
            w.inc(w.dma('pool', d[:, 2048:2560], t.qb_hbm[0, :, 0:512]), "oput", 16)
            w.inc(w.dma('pool', d[:, 2560:3584], t.scrb[:, 0:1024]), "oput", 16)
            w.inc(w.dma('pool', d[:, 3584:4608], t.scrb[:, 1024:2048]), "oput", 16)
            w.inc(w.dma('pool', d[:, 4608:5128], t.xn_sb[:, 0:520]), "oput", 16)
            _fin(w, t); return
    w.barrier()
    _attnb_reduce(w, t)
    w.barrier()
    if stop_stage == 'b1':
        w.wait("dve", 'pool')
        d = t.dbg
        w.inc(w.dma('pool', d[:, 0:1024], t.kvacc_b[:, :]), "oput", 16)
        w.inc(w.dma('pool', d[:, 1024:2048], t.kvblk_b[:, :]), "oput", 16)
        _fin(w, t); return
    for qg in range(NQG):
        _attnb_out(w, t, qg)
        w.barrier()
        if stop_stage == 'b2' and qg == NQG - 1:
            w.wait("dve", 'pool')
            w.wait("dst", 'pool')
            w.wait("dsx", 'pool')
            d = t.dbg
            w.inc(w.dma('pool', d[:, 0:4096], t.ofb[:, :]), "oput", 16)
            w.inc(w.dma('pool', d[:, 4096:4608], t.x_hbm[0, :, 0:512]), "oput", 16)
            w.inc(w.dma('pool', d[:, 4608:5121], t.xg[:, 0:513]), "oput", 16)
            _fin(w, t); return
    if stop_stage == 'x3':
        _dump(w, t); _fin(w, t); return
    _load3(w, t, [t.w_fr, t.w_fk, t.w_fv])
    for qg in range(NQG):
        _ffn(w, t, qg)
        w.barrier()
    if stop_stage == 'x4':
        _dump(w, t); _fin(w, t); return
    _output(w, t)
    _fin(w, t)


def _dump(w, t):
    w.wait("dve", 'sync')
    w.wait("pe", 'sync')
    w.wait("dst", 'sync')
    w.wait("dsx", 'sync')
    for a in range(NCHUNK):
        w.inc(w.dma('sync', t.dbg[:, a * TOK:(a + 1) * TOK], t.x_hbm[a]), "oput", 16)


def _dump_a0(w, t):
    w.wait("dve", 'pool')
    w.wait("pe", 'pool')
    w.wait("acs", 'pool')
    w.wait("dst", 'pool')
    d = t.dbg
    w.inc(w.dma('pool', d[:, 0:4096], t.ofb[:, :]), "oput", 16)
    w.inc(w.dma('pool', d[0:65, 4096:5120], t.invn[:, :]), "oput", 16)
    w.inc(w.dma('pool', d[0:64, 5120:6144], t.ut[:, :]), "oput", 16)
    w.inc(w.dma('pool', d[64:65, 6144:7168], t.scrf[64:65, 0:1024]), "oput", 16)
    w.inc(w.dma('pool', d[:, 7168:7680], t.qn_hbm[0, :, 0:512]), "oput", 16)
    w.inc(w.dma('pool', d[:, 7680:7808], t.knT_hbm[0, :, 0:128]), "oput", 16)
    w.inc(w.dma('pool', d[:, 7808:7938], t.vk_hbm[0, :, 0:130]), "oput", 16)
    w.inc(w.dma('pool', d[:, 7938:7946], t.ksum_bf[:, :]), "oput", 16)
    w.inc(w.dma('pool', d[0:64, 7946:7962], t.vsum_sb[:, :]), "oput", 16)
    w.inc(w.dma('pool', d[:, 8000:9024], t.kvblk_a[:, :]), "oput", 16)
    w.inc(w.dma('pool', d[:, 9024:9056], t.bchunk_sb[:, :]), "oput", 16)
    w.copy(t.scrf[64:65, 1024:2048], t.p_kv[64:65, 0:1024])
    w.wait("dve", 'pool')
    w.inc(w.dma('pool', d[64:65, 9056:10080], t.scrf[64:65, 1024:2048]), "oput", 16)
    w.inc(w.dma('pool', d[:, 10080:12128], t.rbuf[:, :]), "oput", 16)


def _fin(w, t):
    if w.kind == 'pool':
        w.eng.wait_ge(w.sems["oput"], w.c["oput"])


def _setup(w, t):
    for i, v in enumerate([1.1, -1.0, 0.0, 1.0, EPS_LN, EPS_NRM, 4.41, -3.1,
                           1.0 / E, NCONST, 0.5, 2.0]):
        w.memset(t.consts[:, i:i + 1], v)
    w.memset(t.onesb[:, :], 1.0)
    w.memset(t.kvacc_b[:, :], 0.0)
    w.memset(t.kvacc_a[:, :], 0.0)
    w.memset(t.sml[:, 64:88], 0.0)
    w.memset(t.kvblk_a[:, :], 0.0)
    w.memset(t.kvblk_b[:, :], 0.0)
    w.inc(w.dma('sync', t.eyef_sb[:, :], t.eyef), "dx", 16)
    w.inc(w.dma('sync', t.eyeb_sb[:, :], t.eyeb), "dx", 16)
    w.inc(w.dma('sync', t.maskv_sb[:, :], t.maskv), "dx", 16)
    w.inc(w.dma('sync', t.bchunk_sb[:, :], t.b_chunk), "dx", 16)
    w.inc(w.dma('sync', t.mu_sb[:, :], t.mu_chunk), "dx", 16)
    for r_ in range(5):
        w.wait("dve", 'pool')
        w.inc(w.dma('pool', t.scrf[0:1, 0:E], t.b_rows[r_:r_ + 1, :]), "dw", 16)
        w.wait("dw", 'dve')
        w.ts(t.brow_sb[0:1, r_ * E:(r_ + 1) * E], t.scrf[0:1, 0:E], 1.0)
        w.ts(t.scrf[0:1, E:2 * E], t.brow_sb[0:1, r_ * E:(r_ + 1) * E], -1.0)
        w.tt(t.brow_lo[0:1, r_ * E:(r_ + 1) * E], t.scrf[0:1, E:2 * E],
             t.scrf[0:1, 0:E], op=ALU.add)


def _load3(w, t, srcs):
    w.wait("pe", 'pool')
    for i, src in enumerate(srcs):
        w.inc(w.dma('pool', t.wbig[:, i * 8192:(i + 1) * 8192].rearrange("p (a d) -> p a d", a=8),
                    src.rearrange("(a p) d -> p a d", p=128)), "dw", 16)


def _ln_tok(w, t, xt, xn_out):
    m = t.sml[:, 0:1]
    v = t.sml[:, 1:2]
    r = t.sml[:, 2:3]
    sq = t.scrf[:, E:2 * E]
    w.red(m, xt)
    w.ts(m, m, -1.0 / E)
    w.wait("dve", 'act')
    w.wait("acs", 'act')
    w.inc(w.act(sq, xt, AF.Square, bias=m, accum_out=v), "acs", 1)
    w.wait("acs", 'act')
    w.inc(w.act(r, v, AF.Rsqrt, bias=EPS_LN, scale=1.0 / E), "acs", 1)
    w.wait("acs", 'dve')
    w.ts(xn_out, xt, m, r, op0=ALU.add, op1=ALU.mult)


def _norm_heads(w, t, src_f32, dst_bf):
    sq = t.scrf[:, E:2 * E]
    ss = t.sml[:, 16:32]
    src3 = src_f32.rearrange("p (h d) -> p h d", h=H)
    w.wait("dve", 'act')
    w.wait("acs", 'act')
    w.inc(w.act(sq, src_f32, AF.Square), "acs", 1)
    w.wait("acs", 'dve')
    w.red(ss, sq.rearrange("p (h d) -> p h d", h=H))
    w.wait("dve", 'act')
    w.inc(w.act(t.sml[:, 32:48], ss, AF.Rsqrt), "acs", 1)
    w.wait("acs", 'dve')
    w.tt(t.sml[:, 32:48], t.sml[:, 32:48], ss, op=ALU.mult)
    w.ts(t.sml[:, 32:48], t.sml[:, 32:48], EPS_NRM, op0=ALU.add)
    w.vrecip(ss, t.sml[:, 32:48])
    w.tt(dst_bf.rearrange("p (h d) -> p h d", h=H), src3,
         ss.rearrange("p (h o) -> p h o", o=1).broadcast_to([128, H, D]),
         op=ALU.mult)


def _transpose8(w, t, src_bf, dst_bf, pt):
    w.wait("dve", 'pe')
    i = None
    for a in range(NCHUNK):
        i = w.trans(pt[:, a * 64:(a + 1) * 64].bitcast(BF16),
                    src_bf[:, a * 128:(a + 1) * 128], t.eyeb_sb[:, :])
    w.inc(i, "pe", 1)
    w.wait("pe", 'dve')
    w.copy(dst_bf[:, 0:1024], pt[:, 0:512].bitcast(BF16))


def _proj_tok(w, t, xnT, wbase, brow_idx, pdst):
    last = None
    for half in range(2):
        po = pdst[:, half * 512:(half + 1) * 512]
        for a in range(NCHUNK):
            w.mm(po, xnT[:, a * 128:(a + 1) * 128],
                 t.wbig[:, wbase + a * 1024 + half * 512: wbase + a * 1024 + (half + 1) * 512],
                 start=(a == 0), stop=False)
        w.mm(po, t.onesb[0:1, 0:128],
             t.brow_sb[0:1, brow_idx * E + half * 512: brow_idx * E + (half + 1) * 512],
             start=False, stop=False)
        last = w.mm(po, t.onesb[0:1, 0:128],
                    t.brow_lo[0:1, brow_idx * E + half * 512: brow_idx * E + (half + 1) * 512],
                    start=False, stop=True)
    w.inc(last, "pe", 1)


def _prep(w, t, stop_p0=False):
    _load3(w, t, [t.w_ak, t.w_av, t.w_aq])
    w.wait("dw", 'pe')
    w.wait("dx", 'dve')
    w.wait("dve", 'pe')
    ONES_COL = t.onesb[:, 0:1]

    xtok_tok = [0, 0]
    st_tok = {"kn": 0, "vk": 0, "qn": 0}
    for tc in range(NKG + NTT):
        full = tc < NKG
        buf = tc % 2
        xt = t.xtok[:, buf * 1024:(buf + 1) * 1024]
        src = (t.xfull[tc * 128:(tc + 1) * 128, :] if full
               else t.xq[(tc - NKG) * 128:(tc - NKG + 1) * 128, :])
        w.wait("dve", 'sync', val=xtok_tok[buf])
        w.wait("pe", 'sync', val=0 if full else w.val("pe"))
        w.inc(w.dma('sync', xt, src), "dx", 16)
        w.wait("dx", 'dve')

        xnb = t.scrb[:, 0:1024]
        w.wait("pe", 'dve')
        _ln_tok(w, t, xt, xnb)

        if not full:
            tt_ = tc - NKG
            w.wait("dve", 'pe')
            i = None
            for a in range(NCHUNK):
                i = w.trans(t.p_mis[:, a * 128:(a + 1) * 128], xt[:, a * 128:(a + 1) * 128],
                            t.eyef_sb[:, :])
            w.inc(i, "pe", 1)
            w.wait("pe", 'dve')
            w.wait("dst", 'dve', val=st_tok.get("xf", 0))
            w.copy(t.xfst[:, 0:1024], t.p_mis[:, 0:1024])
            w.wait("dve", 'pool')
            w.inc(w.dma('pool',
                        t.x_hbm[:, :, tt_ * 128:(tt_ + 1) * 128].rearrange("a p c -> p a c"),
                        t.xfst.rearrange("p (a c) -> p a c", a=8)), "dst", 16)
            st_tok["xf"] = w.val("dst")
        xtok_tok[buf] = w.val("dve")

        xnT = t.scrb[:, 1024:2048]
        _transpose8(w, t, xnb, xnT, t.p_qk_a)
        w.wait("dve", 'pe')

        if full:
            _proj_tok(w, t, xnT, 0, 1, t.p_mis)
            w.wait("pe", 'dve')
            w.wait("pe", 'act')
            kn_tok = t.scrb[:, 2048:3072]
            _norm_heads(w, t, t.p_mis[:, 0:1024], kn_tok)
            if stop_p0:
                w.wait("dve", 'pool')
                w.wait("acs", 'pool')
                d = t.dbg
                w.inc(w.dma('pool', d[:, 0:1024], t.scrb[:, 0:1024]), "oput", 16)
                w.inc(w.dma('pool', d[:, 1024:2048], t.scrb[:, 1024:2048]), "oput", 16)
                w.inc(w.dma('pool', d[:, 2048:3072], t.p_mis[:, 0:1024]), "oput", 16)
                w.inc(w.dma('pool', d[:, 3072:4096], kn_tok), "oput", 16)
                w.inc(w.dma('pool', d[:, 4096:4160], t.sml[:, :]), "oput", 16)
                w.inc(w.dma('pool', d[:, 4160:4288], t.xtok[:, 0:128]), "oput", 16)
                w.inc(w.dma('pool', d[:, 4288:6336], t.wbig[:, 0:2048]), "oput", 16)
                w.inc(w.dma('pool', d[0:1, 6336:11456], t.brow_sb[:, :]), "oput", 16)
                w.inc(w.dma('pool', d[0:1, 11456:16576], t.brow_lo[:, :]), "oput", 16)
                return
            w.wait("dve", 'pe')
            _proj_tok(w, t, xnT, 8192, 2, t.p_mis)
            w.wait("pe", 'dve')
            w.wait("pe", 'act')
            v_tok = t.scrb[:, 3072:4096]
            w.wait("acs", 'act')
            w.inc(w.act(v_tok, t.p_mis[:, 0:1024], AF.Copy), "acs", 1)
            w.wait("acs", 'dve')
            v_lo = t.scrb[:, 0:1024]
            w.ts(t.scrf[:, 0:1024], v_tok, -1.0)
            w.tt(v_lo, t.scrf[:, 0:1024], t.p_mis[:, 0:1024], op=ALU.add)
            w.wait("dst", 'dve', val=st_tok["vk"])
            vkst = t.stage[:, 2048:2048 + H * 65]
            vk3 = vkst.rearrange("p (h e) -> p h e", h=H)
            if tc == 0:
                w.memset(vk3[:, :, 64:65], 1.0)
            w.copy(vk3[:, :, 0:64], v_tok.rearrange("p (h d) -> p h d", h=H))
            w.wait("dst", 'dve', val=st_tok["kn"])
            knst = t.stage[:, 1024:2048]
            _transpose8(w, t, kn_tok, knst, t.p_qk_a)
            w.wait("dve", 'pe')
            w.wait("acs", 'pe')
            for a in range(NCHUNK):
                w.mm(t.p_kv[:, a * 128:(a + 1) * 128], kn_tok[:, a * 128:(a + 1) * 128],
                     v_tok[:, a * 128:(a + 1) * 128], start=True, stop=True)
            for a in range(NCHUNK):
                w.mm(t.p_qk_b[:, a:a + 1], kn_tok[:, a * 128:(a + 1) * 128], ONES_COL,
                     start=True, stop=True)
            i = None
            for hh in range(H):
                w.mm(t.p_qk_b[0:64, 16 + hh:17 + hh], v_tok[:, hh * 64:(hh + 1) * 64],
                     ONES_COL, start=True, stop=False)
                i = w.mm(t.p_qk_b[0:64, 16 + hh:17 + hh], v_lo[:, hh * 64:(hh + 1) * 64],
                         ONES_COL, start=False, stop=True)
            w.inc(i, "pe", 1)
            w.wait("pe", 'dve')
            w.tt(t.kvacc_a[:, :], t.kvacc_a[:, :], t.p_kv[:, :], op=ALU.add)
            w.tt(t.sml[:, 64:72], t.sml[:, 64:72], t.p_qk_b[:, 0:8], op=ALU.add)
            w.tt(t.sml[0:64, 72:88], t.sml[0:64, 72:88], t.p_qk_b[0:64, 16:32], op=ALU.add)
            w.wait("dve", 'pool')
            w.wait("pe", 'pool', val=0)
            w.inc(w.dma('pool', t.knT_hbm[tc], knst), "dst", 16)
            st_tok["kn"] = w.val("dst")
            w.inc(w.dma('pool', t.vk_hbm[tc], vkst), "dst", 16)
            st_tok["vk"] = w.val("dst")
        else:
            _proj_tok(w, t, xnT, 16384, 0, t.p_mis)
            w.wait("pe", 'dve')
            w.wait("pe", 'act')
            qn_tok = t.scrb[:, 2048:3072]
            _norm_heads(w, t, t.p_mis[:, 0:1024], qn_tok)
            w.wait("dst", 'dve', val=st_tok["qn"])
            qnst = t.stage[:, 0:1024]
            _transpose8(w, t, qn_tok, qnst, t.p_qk_a)
            tt_ = tc - NKG
            w.wait("dve", 'pool')
            w.inc(w.dma('pool',
                        t.qn_hbm[:, :, tt_ * 128:(tt_ + 1) * 128].rearrange("a p c -> p a c"),
                        qnst.rearrange("p (a c) -> p a c", a=8)), "dst", 16)
            st_tok["qn"] = w.val("dst")

    w.wait("pe", 'dve')
    kb3 = t.kvblk_a.rearrange("p (a c) -> p a c", a=NCHUNK)
    ka3 = t.kvacc_a.rearrange("p (a c) -> p a c", a=NCHUNK)
    w.copy(kb3[0:64, :, 0:64], ka3[0:64, :, 0:64])
    w.copy(kb3[64:128, :, 64:128], ka3[64:128, :, 64:128])
    w.copy(t.ksum_bf[:, :], t.sml[:, 64:64 + NCHUNK])
    w.copy(t.vsum_sb[:, :], t.sml[0:64, 72:72 + H])
    w.ts(t.vsum_sb[:, :], t.vsum_sb[:, :], 3.1)


def _emit_rv(w, t, qg, kg, nq):
    if kg < 0:
        return
    oo = kg // 8
    kk = kg % 8
    ob = oo % 2
    pb = kg % 2
    vk_e = t.vstream[:, ob * 1040 + kk * 130: ob * 1040 + kk * 130 + 65]
    vk_o = t.vstream[:, ob * 1040 + kk * 130 + 65: ob * 1040 + kk * 130 + 130]
    rb_ = t.rbuf[:, pb * 1024:(pb + 1) * 1024]
    rs_, rv_ = t.rec[kg]
    w.wait(rs_, 'pe', val=rv_)
    w.mm(t.p_kv[0:65, 0:nq], vk_e, rb_[:, 0:nq], start=(kg == 0), stop=(kg == NKG - 1))
    i = w.mm(t.p_kv[0:65, 512:512 + nq], vk_o, rb_[:, 512:512 + nq],
             start=(kg == 0), stop=(kg == NKG - 1))
    w.inc(i, "rv", 1)


def _attn_a(w, t, qg):
    nq = QGS[qg]
    q0 = QG0[qg]
    t.rec = {}
    t.rec_hist = []
    w.wait("dst", 'sync')
    for p in range(NCHUNK):
        qb = p % 2
        qt = t.qtile[:, qb * 512:qb * 512 + nq]
        w.wait("pe", 'sync', val=w.val("pe"))
        w.wait("qk", 'sync', val=w.val("qk"))
        w.inc(w.dma('sync', qt, t.qn_hbm[p, :, q0:q0 + nq]), "dq", 16)
        w.wait("dve", 'pe')
        dk_tok = {}
        for oct_ in range(10):
            if oct_ >= 2:
                oo = oct_ - 2
                ob = oo % 2
                for kk in range(8):
                    kg = oo * 8 + kk
                    pb = kg % 2
                    pqk = (t.p_qk_a, t.p_qk_b, t.p_mis)[kg % 3]
                    kn_e = t.kstream[0:64, ob * 1024 + kk * 128: ob * 1024 + (kk + 1) * 128]
                    kn_o = t.kstream[64:128, ob * 1024 + kk * 128: ob * 1024 + (kk + 1) * 128]
                    if kk == 0:
                        w.wait("dk", 'pe', val=dk_tok[oo])
                        if oo == 0:
                            w.wait("dq", 'pe')
                    if len(t.rec_hist) >= 3:
                        rs, ds = t.rec_hist[-3]
                        w.wait("rcp", 'pe', val=rs)
                        w.wait("dve", 'pe', val=ds)
                    w.mm(pqk[:, 0:nq], kn_e, qt[0:64, :], tp=(0, 0))
                    i = w.mm(pqk[:, 512:512 + nq], kn_o, qt[64:128, :], tp=(64, 0))
                    w.inc(i, "qk", 1)
                    _emit_rv(w, t, qg, kg - 1, nq)
                    rb_ = t.rbuf[:, pb * 1024:(pb + 1) * 1024]
                    if nq == 512 and kg % 2 == 1:
                        w.wait("qk", 'dve')
                        w.wait("rv", 'dve', val=w.val("rv") - 1)
                        tmp = t.scrf[:, 2048:3072]
                        w.ts(tmp, pqk[:, 0:1024], 1.1, -1.0,
                             op0=ALU.subtract, op1=ALU.mult)
                        w.vrecip(rb_[:, 0:1024], tmp)
                        t.rec[kg] = ("dve", w.val("dve"))
                    else:
                        w.wait("qk", 'act')
                        w.wait("rv", 'act', val=w.val("rv") - 1)
                        if nq == 512:
                            i = w.act(rb_[:, 0:1024], pqk[:, 0:1024], AF.Reciprocal,
                                      bias=1.1, scale=-1.0)
                        else:
                            w.act(rb_[:, 0:nq], pqk[:, 0:nq], AF.Reciprocal,
                                  bias=1.1, scale=-1.0)
                            i = w.act(rb_[:, 512:512 + nq], pqk[:, 512:512 + nq],
                                      AF.Reciprocal, bias=1.1, scale=-1.0)
                        w.inc(i, "rcp", 1)
                        t.rec[kg] = ("rcp", w.val("rcp"))
                    t.rec_hist.append((w.val("rcp"), w.val("dve")))
            if oct_ < 8:
                ob2 = oct_ % 2
                w.wait("qk", 'sync', val=w.val("qk"))
                w.wait("rv", 'sync', val=w.val("rv") + (1 if oct_ >= 2 else 0))
                w.inc(w.dma('sync',
                            t.kstream[:, ob2 * 1024:(ob2 + 1) * 1024]
                            .rearrange("p (k c) -> p k c", k=8),
                            t.knT_hbm[oct_ * 8:(oct_ + 1) * 8, :, p * 128:(p + 1) * 128]
                            .rearrange("k p c -> p k c")),
                      "dk", 16)
                w.inc(w.dma('sync',
                            t.vstream[:, ob2 * 1040:(ob2 + 1) * 1040]
                            .rearrange("p (k c) -> p k c", k=8),
                            t.vk_hbm[oct_ * 8:(oct_ + 1) * 8, :, 2 * p * 65:(2 * p + 2) * 65]
                            .rearrange("k p c -> p k c")),
                      "dk", 16)
                dk_tok[oct_] = w.val("dk")
        _emit_rv(w, t, qg, NKG - 1, nq)
        w.wait("rv", 'dve')
        ue = t.ut[:, 0:nq]
        uo = t.ut[:, 512:512 + nq]
        w.copy(ue, t.p_kv[0:64, 0:nq])
        w.copy(uo, t.p_kv[0:64, 512:512 + nq])
        w.wait("rv", 'pe')
        w.mm(t.p_mis[64:65, 0:nq], t.ksum_bf[0:64, p:p + 1], qt[0:64, :], tp=(0, 64))
        i = w.mm(t.p_mis[64:65, 512:512 + nq], t.ksum_bf[64:128, p:p + 1],
                 qt[64:128, :], tp=(64, 64))
        w.inc(i, "pe", 1)
        w.wait("pe", 'dve')
        nr_e = t.scrf[64:65, 0:nq]
        nr_o = t.scrf[64:65, 512:512 + nq]
        w.copy(nr_e, t.p_kv[64:65, 0:nq])
        w.copy(nr_o, t.p_kv[64:65, 512:512 + nq])
        w.stt(nr_e, nr_e, 4.41, t.p_mis[64:65, 0:nq], ALU.mult, ALU.subtract)
        w.stt(nr_o, nr_o, 4.41, t.p_mis[64:65, 512:512 + nq], ALU.mult, ALU.subtract)
        w.ts(nr_e, nr_e, NCONST, op0=ALU.add)
        w.ts(nr_o, nr_o, NCONST, op0=ALU.add)
        w.wait("dve", 'act')
        w.wait("pe", 'act')
        w.act(t.invn[64:65, 0:nq], nr_e, AF.Reciprocal)
        w.inc(w.act(t.invn[64:65, 512:512 + nq], nr_o, AF.Reciprocal), "acs", 1)
        w.wait("acs", 'pe')
        for half in range(2):
            co = half * 512
            usrc = ue if half == 0 else uo
            w.mm(t.p_mis[0:64, 0:nq], t.onesb[64:65, 0:64], t.invn[64:65, co:co + nq])
            if half == 0:
                i = w.mm(t.p_mis[0:64, 512:512 + nq], t.kvblk_a[0:64, p * 128:p * 128 + 64],
                         qt[0:64, :], tp=(0, 0))
            else:
                i = w.mm(t.p_mis[0:64, 512:512 + nq],
                         t.kvblk_a[64:128, p * 128 + 64:p * 128 + 128],
                         qt[64:128, :], tp=(64, 0))
            w.inc(i, "pe", 1)
            w.wait("pe", 'dve')
            e1 = t.scrf[0:64, 1024:1024 + nq]
            w.stt(e1, usrc, 4.41, t.p_mis[0:64, 512:512 + nq], ALU.mult, ALU.subtract)
            if half == 1:
                w.wait("dst", 'dve')
            dst = t.ofb[0:64, p * 512:p * 512 + nq] if half == 0 else t.scrb[0:64, 0:nq]
            w.stt(dst, e1, t.vsum_sb[:, 2 * p + half:2 * p + half + 1],
                        t.p_mis[0:64, 0:nq], ALU.subtract, ALU.mult)
            if half == 1:
                w.wait("dve", 'pool')
                w.inc(w.dma('pool', t.ofb[64:128, p * 512:p * 512 + nq], t.scrb[0:64, 0:nq]),
                      "dst", 16)
            else:
                w.wait("dve", 'pe')
    w.wait("dst", 'pe')
    w.wait("dve", 'sync')
    xgv = t.xg.rearrange("p (a c) -> p a c", c=513)[:, :, 1:1 + nq]
    w.inc(w.dma('sync', xgv, t.x_hbm[:, :, q0:q0 + nq].rearrange("a p c -> p a c")), "dx", 16)
    w.wait("dx", 'dve')
    wc_tok = [None, None]
    for fc in range(NCHUNK):
        wb = fc % 2
        w.wait("pe", 'pool', val=(wc_tok[wb] if wc_tok[wb] is not None else w.val("pe")))
        w.inc(w.dma('pool', t.wcol[:, wb * 1024:(wb + 1) * 1024].rearrange("p (a c) -> p a c", a=8),
                    t.w_ao[:, fc * 128:(fc + 1) * 128].rearrange("(a p) c -> p a c", p=128)),
              "dw", 16)
        w.wait("dw", 'pe')
        w.wait("dve", 'pe')
        i = None
        for a in range(NCHUNK):
            i = w.mm(t.p_mis[:, 0:nq], t.wcol[:, wb * 1024 + a * 128:wb * 1024 + (a + 1) * 128],
                     t.ofb[:, a * 512:a * 512 + nq], start=(a == 0), stop=(a == NCHUNK - 1))
        w.inc(i, "pe", 1)
        wc_tok[wb] = w.val("pe")
        w.wait("pe", 'dve')
        xc = t.xg[:, fc * 513 + 1: fc * 513 + 1 + nq]
        w.ts(t.scrf[:, 0:nq], t.p_mis[:, 0:nq], t.bchunk_sb[:, fc:fc + 1], op0=ALU.add)
        w.tt(xc, xc, t.scrf[:, 0:nq], op=ALU.add)
    w.wait("dve", 'sync')
    w.inc(w.dma('sync', t.x_hbm[:, :, q0:q0 + nq].rearrange("a p c -> p a c"), xgv), "dsx", 16)
    w.wait("dve", 'pe')


def _ln_fm(w, t, qg):
    nq = QGS[qg]
    q0 = QG0[qg]
    nb = nq + 1
    bnd = qg > 0
    ONES_COL = t.onesb[:, 0:1]
    w.wait("dve", 'sync')
    w.wait("pe", 'sync')
    if not bnd:
        xgv = t.xg.rearrange("p (a c) -> p a c", c=513)[:, :, 1:1 + nq]
        w.inc(w.dma('sync', xgv, t.x_hbm[:, :, 0:nq].rearrange("a p c -> p a c")), "dx", 16)
    else:
        xgv = t.xg.rearrange("p (a c) -> p a c", c=513)[:, :, 0:nb]
        w.inc(w.dma('sync', xgv, t.x_hbm[:, :, q0 - 1:q0 + nq].rearrange("a p c -> p a c")),
              "dx", 16)
    w.wait("dx", 'dve')
    cast_tok = [None, None]
    for a in range(NCHUNK):
        sl = a % 2
        xb = t.scrb[:, sl * 1024:sl * 1024 + nb]
        x2 = t.scrb[:, 2048 + sl * 1024:2048 + sl * 1024 + nb]
        w.wait("pe", 'dve', val=(cast_tok[sl] if cast_tok[sl] is not None else w.val("pe")))
        if bnd:
            xsrc = t.xg[:, a * 513:a * 513 + nb]
            w.ts(xb, xsrc, 1.0)
            w.tt(x2, xsrc, xsrc, op=ALU.mult)
        else:
            xsrc = t.xg[:, a * 513 + 1:a * 513 + 1 + nq]
            w.ts(xb[:, 1:nb], xsrc, 1.0)
            w.tt(x2[:, 1:nb], xsrc, xsrc, op=ALU.mult)
        w.wait("dve", 'pe')
        w.mm(t.p_qk_a[0:1, 0:nq], ONES_COL, xb[:, 1:nb], start=(a == 0), stop=(a == NCHUNK - 1))
        i = w.mm(t.p_qk_b[0:1, 0:nq], ONES_COL, x2[:, 1:nb], start=(a == 0), stop=(a == NCHUNK - 1))
        if bnd:
            w.mm(t.p_qk_a[0:1, 512:513], ONES_COL, xb[:, 0:1],
                 start=(a == 0), stop=(a == NCHUNK - 1))
            i = w.mm(t.p_qk_b[0:1, 512:513], ONES_COL, x2[:, 0:1],
                     start=(a == 0), stop=(a == NCHUNK - 1))
        w.inc(i, "pe", 1)
        cast_tok[sl] = w.val("pe")
    w.wait("pe", 'dve')
    ncols = 513 if bnd else nq
    mrow = t.scrf[0:1, 0:ncols]
    vrow = t.scrf[0:1, 1024:1024 + ncols]
    w.ts(mrow, t.p_qk_a[0:1, 0:ncols], 1.0 / E)
    w.ts(vrow, t.p_qk_b[0:1, 0:ncols], 1.0 / E)
    w.tt(t.scrf[0:1, 2048:2048 + ncols], mrow, mrow, op=ALU.mult)
    w.tt(vrow, vrow, t.scrf[0:1, 2048:2048 + ncols], op=ALU.subtract)
    w.wait("dve", 'act')
    w.inc(w.act(t.scrf[0:1, 2048:2048 + ncols], vrow, AF.Rsqrt, bias=EPS_LN), "acs", 1)
    w.wait("acs", 'dve')
    w.ts(t.scrb[0:1, 0:ncols], mrow, 1.0)
    w.ts(t.scrb[0:1, 1024:1024 + ncols], t.scrf[0:1, 2048:2048 + ncols], 1.0)
    w.wait("dve", 'pe')
    w.mm(t.p_qk_b[:, 0:nq], t.onesb[0:1, 0:128], t.scrb[0:1, 0:nq])
    i = w.mm(t.p_qk_a[:, 0:nq], t.onesb[0:1, 0:128], t.scrb[0:1, 1024:1024 + nq])
    if bnd:
        w.mm(t.p_qk_b[:, 512:513], t.onesb[0:1, 0:128], t.scrb[0:1, 512:513])
        i = w.mm(t.p_qk_a[:, 512:513], t.onesb[0:1, 0:128], t.scrb[0:1, 1024 + 512:1024 + 513])
    w.inc(i, "pe", 1)
    w.wait("pe", 'dve')
    for a in range(NCHUNK):
        w.tt(t.scrf[:, 0:nq], t.xg[:, a * 513 + 1:a * 513 + 1 + nq],
             t.p_qk_b[:, 0:nq], op=ALU.subtract)
        w.tt(t.xn_sb[:, a * 520 + 1:a * 520 + 1 + nq], t.scrf[:, 0:nq],
             t.p_qk_a[:, 0:nq], op=ALU.mult)
        if bnd:
            w.tt(t.scrf[:, 512:513], t.xg[:, a * 513:a * 513 + 1],
                 t.p_qk_b[:, 512:513], op=ALU.subtract)
            w.tt(t.xn_sb[:, a * 520:a * 520 + 1], t.scrf[:, 512:513],
                 t.p_qk_a[:, 512:513], op=ALU.mult)
        else:
            w.memset(t.xn_sb[:, a * 520:a * 520 + 1], 0.0)


def _ffn(w, t, qg):
    nq = QGS[qg]
    q0 = QG0[qg]
    _ln_fm(w, t, qg)
    i = None
    for a in range(NCHUNK):
        xn_c = t.xn_sb[:, a * 520 + 1:a * 520 + 1 + nq]
        xn_p = t.xn_sb[:, a * 520:a * 520 + nq]
        d = t.scrb[:, 0:nq]
        w.tt(d, xn_c, xn_p, op=ALU.subtract)
        w.stt(t.ffb[:, a * 512:a * 512 + nq], d, t.mu_sb[:, a:a + 1], xn_p, ALU.mult, ALU.add)
        i = w.stt(t.ffb[:, 4096 + a * 512:4096 + a * 512 + nq], d,
                  t.mu_sb[:, 8 + a:9 + a], xn_p, ALU.mult, ALU.add)
    w.wait("dve", 'pe')
    for dc in range(NCHUNK):
        pd = t.p_mis[:, (dc % 2) * 512:(dc % 2) * 512 + nq]
        if dc >= 2:
            w.wait("acs", 'pe', val=w.val("acs") - 1)
        i = None
        for a in range(NCHUNK):
            i = w.mm(pd, t.wbig[:, 8192 + a * 1024 + dc * 128:8192 + a * 1024 + (dc + 1) * 128],
                     t.ffb[:, 4096 + a * 512:4096 + a * 512 + nq],
                     start=(a == 0), stop=(a == NCHUNK - 1))
        w.inc(i, "qk", 1)
        w.wait("qk", 'act')
        w.inc(w.act(t.scrf[:, (dc % 2) * 512:(dc % 2) * 512 + nq], pd, AF.Relu), "acs", 1)
        w.wait("acs", 'dve')
        w.tt(t.kkb[:, dc * 512:dc * 512 + nq],
                   t.scrf[:, (dc % 2) * 512:(dc % 2) * 512 + nq],
                   t.scrf[:, (dc % 2) * 512:(dc % 2) * 512 + nq], op=ALU.mult)
    w.wait("dve", 'pe')
    for dc in range(NCHUNK):
        pr = t.p_mis[:, 0:nq]
        pv = t.p_mis[:, 512:512 + nq]
        if dc >= 1:
            w.wait("acs", 'pe')
        i = None
        for a in range(NCHUNK):
            i = w.mm(pr, t.wbig[:, a * 1024 + dc * 128:a * 1024 + (dc + 1) * 128],
                     t.ffb[:, a * 512:a * 512 + nq], start=(a == 0), stop=(a == NCHUNK - 1))
        w.inc(i, "qk", 1)
        w.wait("qk", 'act')
        if dc >= 1:
            w.wait("dve", 'act')
        w.inc(w.act(t.scrf[:, 0:nq], pr, AF.Sigmoid), "acs", 1)
        if dc >= 1:
            w.wait("dve", 'pe')
        i = None
        for a in range(NCHUNK):
            i = w.mm(pv, t.wbig[:, 16384 + a * 1024 + dc * 128:16384 + a * 1024 + (dc + 1) * 128],
                     t.kkb[:, a * 512:a * 512 + nq], start=(a == 0), stop=(a == NCHUNK - 1))
        w.inc(i, "rv", 1)
        w.wait("acs", 'dve')
        w.wait("rv", 'dve')
        xc = t.xg[:, dc * 513 + 1: dc * 513 + 1 + nq]
        w.tt(t.scrf[:, 512:512 + nq], pv, t.scrf[:, 0:nq], op=ALU.mult)
        w.tt(xc, xc, t.scrf[:, 512:512 + nq], op=ALU.add)
    w.wait("dve", 'sync')
    xgv = t.xg.rearrange("p (a c) -> p a c", c=513)[:, :, 1:1 + nq]
    w.inc(w.dma('sync', t.x_hbm[:, :, q0:q0 + nq].rearrange("a p c -> p a c"), xgv), "dsx", 16)
    w.wait("dve", 'pe')


def _attnb_pre(w, t, qg):
    nq = QGS[qg]
    q0 = QG0[qg]
    _ln_fm(w, t, qg)
    w.wait("dve", 'pe')
    for dc in range(NCHUNK):
        pd = t.p_mis[:, (dc % 2) * 512:(dc % 2) * 512 + nq]
        if dc >= 2:
            w.wait("dve", 'pe', val=w.val("dve") - 3)
        i = None
        for a in range(NCHUNK):
            i = w.mm(pd, t.wbig[:, a * 1024 + dc * 128:a * 1024 + (dc + 1) * 128],
                     t.xn_sb[:, a * 520 + 1:a * 520 + 1 + nq],
                     start=(a == 0), stop=(a == NCHUNK - 1))
        w.inc(i, "qk", 1)
        w.wait("qk", 'dve')
        bq = t.bchunk_sb[:, 8 + dc:9 + dc]
        mn = t.scrf[:, 0:nq]
        mx = t.scrf[:, 512:512 + nq]
        w.ts(mn, pd, bq, 0.0, op0=ALU.add, op1=ALU.min)
        w.ts(mx, pd, bq, 0.0, op0=ALU.add, op1=ALU.max)
        w.wait("dve", 'act')
        w.inc(w.act(t.scrf[:, 1024:1024 + nq], mn, AF.Exp), "acs", 1)
        w.wait("acs", 'dve')
        w.wait("dst", 'dve', val=w.val("dst") - 16)
        w.tt(t.stage[:, (dc % 2) * 1024:(dc % 2) * 1024 + nq],
                   t.scrf[:, 1024:1024 + nq], mx, op=ALU.add)
        w.wait("dve", 'pool')
        w.inc(w.dma('pool', t.qb_hbm[dc, :, q0:q0 + nq],
                    t.stage[:, (dc % 2) * 1024:(dc % 2) * 1024 + nq]), "dst", 16)
    ntt_q = nq // 128
    for tl in range(ntt_q):
        ttg = q0 // 128 + tl
        mcol = t.maskv_sb[:, ttg:ttg + 1]
        for half in range(2):
            pk = t.p_mis[:, 0:512]
            if tl > 0 or half > 0:
                w.wait("dve", 'pe')
            i = None
            for a in range(NCHUNK):
                i = w.mm(pk, t.xn_sb[:, a * 520 + 1 + tl * 128:a * 520 + 1 + (tl + 1) * 128],
                         t.wbig[:, 8192 + a * 1024 + half * 512:8192 + a * 1024 + (half + 1) * 512],
                         start=(a == 0), stop=False)
            w.mm(pk, t.onesb[0:1, 0:128],
                 t.brow_sb[0:1, 3 * E + half * 512:3 * E + (half + 1) * 512],
                 start=False, stop=False)
            i = w.mm(pk, t.onesb[0:1, 0:128],
                     t.brow_lo[0:1, 3 * E + half * 512:3 * E + (half + 1) * 512],
                     start=False, stop=True)
            w.inc(i, "qk", 1)
            w.wait("qk", 'dve')
            mn = t.scrf[:, 0:512]
            mx = t.scrf[:, 512:1024]
            w.ts(mn, pk, 0.0, None, op0=ALU.min)
            w.ts(mx, pk, 0.0, None, op0=ALU.max)
            w.wait("dve", 'act')
            w.inc(w.act(t.scrf[:, 1024:1536], mn, AF.Exp), "acs", 1)
            w.wait("acs", 'dve')
            w.wait("rv", 'dve')
            w.tt(t.scrf[:, 1536:2048], t.scrf[:, 1024:1536], mx, op=ALU.add)
            w.ts(t.scrb[:, half * 512:(half + 1) * 512], t.scrf[:, 1536:2048], mcol)
            w.wait("dve", 'pe')
            pv = t.p_mis[:, 512:1024]
            i = None
            for a in range(NCHUNK):
                i = w.mm(pv, t.xn_sb[:, a * 520 + 1 + tl * 128:a * 520 + 1 + (tl + 1) * 128],
                         t.wbig[:, 16384 + a * 1024 + half * 512:16384 + a * 1024 + (half + 1) * 512],
                         start=(a == 0), stop=False)
            w.mm(pv, t.onesb[0:1, 0:128],
                 t.brow_sb[0:1, 4 * E + half * 512:4 * E + (half + 1) * 512],
                 start=False, stop=False)
            i = w.mm(pv, t.onesb[0:1, 0:128],
                     t.brow_lo[0:1, 4 * E + half * 512:4 * E + (half + 1) * 512],
                     start=False, stop=True)
            w.inc(i, "qk", 1)
            w.wait("qk", 'dve')
            w.copy(t.scrb[:, 1024 + half * 512:1024 + (half + 1) * 512], pv)
        w.wait("dve", 'pe')
        i = None
        for pp in range(NCHUNK):
            i = w.mm(t.p_qk_a[:, pp * 128:(pp + 1) * 128], t.scrb[:, pp * 128:(pp + 1) * 128],
                     t.scrb[:, 1024 + pp * 128:1024 + (pp + 1) * 128],
                     start=True, stop=True)
        w.inc(i, "rv", 1)
        w.wait("rv", 'dve')
        w.tt(t.kvacc_b[:, :], t.kvacc_b[:, :], t.p_qk_a[:, :], op=ALU.add)
    w.wait("dve", 'pe')


def _attnb_reduce(w, t):
    w.wait("dve", 'pool')
    w.inc(w.dma('pool', t.kv_in, t.kvacc_b[:, :]), "dst", 16)
    w.wait("dst", 'pool')
    w.inc(w.cc([t.kv_in], [t.kv_out]), "cc", 1)
    w.wait("cc", 'pool')
    w.inc(w.dma('pool', t.scrf[:, 0:1024], t.kv_out), "dst", 16)
    w.wait("dst", 'dve')
    i = None
    for a in range(NCHUNK):
        w.copy(t.kvblk_b[0:64, a * 128:a * 128 + 64], t.scrf[0:64, a * 128:a * 128 + 64])
        i = w.copy(t.kvblk_b[64:128, a * 128 + 64:a * 128 + 128],
                   t.scrf[64:128, a * 128 + 64:a * 128 + 128])
    w.wait("dve", 'pe')


def _attnb_out(w, t, qg):
    nq = QGS[qg]
    q0 = QG0[qg]
    w.wait("dst", 'sync')
    qk_tok = [None, None]
    for p in range(NCHUNK):
        sl = p % 2
        qp = t.qtile[:, sl * 512:sl * 512 + nq]
        w.wait("qk", 'sync', val=(qk_tok[sl] if qk_tok[sl] is not None else w.val("qk")))
        w.inc(w.dma('sync', qp, t.qb_hbm[p, :, q0:q0 + nq]), "dq", 16)
        w.wait("dq", 'pe')
        w.wait("dve", 'pe', val=w.val("dve") - 1 if p >= 2 else w.val("dve"))
        w.inc(w.mm(t.p_mis[:, sl * 512:sl * 512 + nq],
                   t.kvblk_b[:, p * 128:(p + 1) * 128], qp), "qk", 1)
        qk_tok[sl] = w.val("qk")
        w.wait("qk", 'dve')
        if p == 0:
            w.wait("pe", 'dve')
        w.copy(t.ofb[:, p * 512:p * 512 + nq], t.p_mis[:, sl * 512:sl * 512 + nq])
    w.barrier()
    w.wait("dve", 'pe')
    w.wait("dve", 'sync')
    w.wait("pe", 'sync')
    xgv = t.xg.rearrange("p (a c) -> p a c", c=513)[:, :, 1:1 + nq]
    w.inc(w.dma('sync', xgv, t.x_hbm[:, :, q0:q0 + nq].rearrange("a p c -> p a c")), "dx", 16)
    w.wait("dx", 'dve')
    wc_tok = [None, None]
    for fc in range(NCHUNK):
        wb = fc % 2
        w.wait("pe", 'pool', val=(wc_tok[wb] if wc_tok[wb] is not None else w.val("pe")))
        w.inc(w.dma('pool', t.wcol[:, wb * 1024:(wb + 1) * 1024].rearrange("p (a c) -> p a c", a=8),
                    t.w_bo[:, fc * 128:(fc + 1) * 128].rearrange("(a p) c -> p a c", p=128)),
              "dw", 16)
        w.wait("dw", 'pe')
        w.wait("dve", 'pe')
        i = None
        for a in range(NCHUNK):
            i = w.mm(t.p_kv[:, 0:nq], t.wcol[:, wb * 1024 + a * 128:wb * 1024 + (a + 1) * 128],
                     t.ofb[:, a * 512:a * 512 + nq], start=(a == 0), stop=(a == NCHUNK - 1))
        w.inc(i, "pe", 1)
        wc_tok[wb] = w.val("pe")
        w.wait("pe", 'dve')
        xc = t.xg[:, fc * 513 + 1: fc * 513 + 1 + nq]
        w.ts(t.scrf[:, 0:nq], t.p_kv[:, 0:nq], t.bchunk_sb[:, 16 + fc:17 + fc], op0=ALU.add)
        w.tt(xc, xc, t.scrf[:, 0:nq], op=ALU.add)
        w.barrier()
    w.wait("dve", 'sync')
    w.inc(w.dma('sync', t.x_hbm[:, :, q0:q0 + nq].rearrange("a p c -> p a c"), xgv), "dsx", 16)
    w.wait("dve", 'pe')


def _output(w, t):
    ot_tok = [0, 0]
    ld_tok = [0, 0]
    for tt_ in range(NTT):
        lb = tt_ % 2
        xf = t.xfst[:, 0:1024] if lb == 0 else t.scrf[:, 0:1024]
        w.wait("pe", 'sync', val=ld_tok[lb])
        w.inc(w.dma('sync', xf.rearrange("p (a c) -> p a c", a=8),
                    t.x_hbm[:, :, tt_ * 128:(tt_ + 1) * 128].rearrange("a p c -> p a c")),
              "dx", 16)
        w.wait("dx", 'pe')
        w.wait("dve", 'pe')
        i = None
        for a in range(NCHUNK):
            i = w.trans(t.p_mis[:, a * 128:(a + 1) * 128], xf[:, a * 128:(a + 1) * 128],
                        t.eyef_sb[:, :])
        w.inc(i, "pe", 1)
        ld_tok[lb] = w.val("pe")
        w.wait("pe", 'dve')
        ob = tt_ % 2
        w.wait("oput", 'dve', val=ot_tok[ob])
        ot = t.xtok[:, ob * 1024:(ob + 1) * 1024]
        i = None
        for a in range(NCHUNK):
            i = w.copy(ot[:, a * 128:(a + 1) * 128], t.p_mis[:, a * 128:(a + 1) * 128])
        w.wait("dve", 'sync')
        w.inc(w.dma('sync', t.out_h[tt_ * 128:(tt_ + 1) * 128, :], ot), "oput", 16)
        ot_tok[ob] = w.val("oput")



def _host_inputs(inputs):
    x = np.ascontiguousarray(np.asarray(inputs["L0_attn_a_x"], dtype=np.float32))
    g = lambda k: np.ascontiguousarray(np.asarray(inputs[k], dtype=np.float32))
    shared = {
        "w_aq": g("L0_attn_a_Wq"), "w_ak": g("L0_attn_a_Wk"),
        "w_av": g("L0_attn_a_Wv"), "w_ao": g("L0_attn_a_Wo"),
        "w_fr": np.ascontiguousarray(g("L0_ffn_W_r").T),
        "w_fk": np.ascontiguousarray(g("L0_ffn_W_k").T),
        "w_fv": np.ascontiguousarray(g("L0_ffn_W_v").T),
        "w_bq": np.ascontiguousarray(g("L0_attn_b_Wq").T),
        "w_bk": np.ascontiguousarray(g("L0_attn_b_Wk").T),
        "w_bv": np.ascontiguousarray(g("L0_attn_b_Wv").T),
        "w_bo": np.ascontiguousarray(g("L0_attn_b_Wo").T),
    }
    b_rows = np.stack([g("L0_attn_a_bq"), g("L0_attn_a_bk"), g("L0_attn_a_bv"),
                       g("L0_attn_b_bk"), g("L0_attn_b_bv")]).astype(np.float32)
    b_chunk = np.zeros((128, 4 * NCHUNK), np.float32)
    b_chunk[:, 0:8] = g("L0_attn_a_bo").reshape(NCHUNK, 128).T
    b_chunk[:, 8:16] = g("L0_attn_b_bq").reshape(NCHUNK, 128).T
    b_chunk[:, 16:24] = g("L0_attn_b_bo").reshape(NCHUNK, 128).T
    mu_chunk = np.zeros((128, 2 * NCHUNK), np.float32)
    mu_chunk[:, 0:8] = g("L0_ffn_mu_r").reshape(NCHUNK, 128).T
    mu_chunk[:, 8:16] = g("L0_ffn_mu_k").reshape(NCHUNK, 128).T
    shared["b_rows"] = np.ascontiguousarray(b_rows)
    shared["b_chunk"] = np.ascontiguousarray(b_chunk)
    shared["mu_chunk"] = np.ascontiguousarray(mu_chunk)
    shared["eyef"] = np.eye(128, dtype=np.float32)
    shared["eyeb"] = np.eye(128, dtype=np.float32).astype(ml_dtypes.bfloat16)
    in_maps = []
    for c in range(8):
        b, h = c // 2, c % 2
        m = dict(shared)
        m["xq"] = np.ascontiguousarray(x[b, 3968 * h: 3968 * h + TOK, :])
        m["xfull"] = np.ascontiguousarray(x[b])
        mk = np.zeros(TOK, np.float32)
        if h == 0:
            mk[0:4096] = 1.0
        else:
            mk[128:TOK] = 1.0
        m["maskv"] = np.ascontiguousarray(mk.reshape(NTT, 128).T)
        in_maps.append(m)
    return in_maps


def run_staged(inputs, stop_stage, cores=None):
    nc = build_program(stop_stage=stop_stage)
    in_maps = _host_inputs(inputs)
    cores = cores if cores is not None else list(range(8))
    res = run_bass_kernel_spmd(nc, [in_maps[c] for c in cores], core_ids=cores)
    return [r["dbg"] for r in res.results]


def kernel(**inputs):
    nc = build_program()
    in_maps = _host_inputs(inputs)
    res = run_bass_kernel_spmd(nc, in_maps, core_ids=list(range(8)))
    y = np.empty((B, S, E), np.float32)
    for c in range(8):
        b, h = c // 2, c % 2
        o = res.results[c]["out"]
        if h == 0:
            y[b, 0:4096] = o[0:4096]
        else:
            y[b, 4096:8192] = o[128:TOK]
    return y


# version 10
# speedup vs baseline: 2.0720x; 2.0720x over previous
import dataclasses
import numpy as np
import ml_dtypes

import concourse.bass as bass
import concourse.mybir as mybir
from concourse.bass_utils import run_bass_kernel_spmd

F32 = mybir.dt.float32
BF16 = mybir.dt.bfloat16
AF = mybir.ActivationFunctionType
ALU = mybir.AluOpType

E = 1024
H = 16
D = 64
S = 8192
B = 4
NCHUNK = 8
TOK = 4224
NTT = 33
NKG = 64
QGS = [512] * 8 + [128]
QG0 = [sum(QGS[:i]) for i in range(len(QGS))]
NQG = len(QGS)
EPS_LN = 1e-5
EPS_NRM = 1e-8
EPS_ATT = 1e-6
NCONST = -3.1 * S + EPS_ATT

SEM_NAMES = ["dx", "dk", "dq", "dw", "dst", "dsx", "qk", "rcp", "rv", "pe", "dve",
             "acs", "cc", "oput"]


def _act_raw(nc, eng, out, in_, func, bias=0.0, scale=1.0):
    bi = eng.activation(out=out, in_=in_, func=AF.Exp, bias=bias, scale=scale)
    i2 = dataclasses.replace(bi.ins, func=func)
    nc.register_instruction(i2, overwrite=True)
    insts = nc.cur_bb.bb.instructions
    assert insts[-1] is bi.ins
    insts[-1] = i2
    return bass.BassInstruction(i2)


class Walk:

    def __init__(self, nc, kind, eng, sems):
        self.nc = nc
        self.kind = kind
        self.eng = eng
        self.sems = sems
        self.c = {s: 0 for s in SEM_NAMES}

    def inc(self, ins, name, n=1):
        self.c[name] += n
        if ins is not None:
            ins.then_inc(self.sems[name], n)

    def wait(self, name, kind, val=None):
        if self.kind == kind:
            v = self.c[name] if val is None else val
            if v > 0:
                self.eng.wait_ge(self.sems[name], v)

    def val(self, name):
        return self.c[name]

    _OWNER = {"qk": 'pe', "rv": 'pe', "pe": 'pe', "rcp": 'act', "acs": 'act',
              "dve": 'dve', "dk": 'sync', "dq": 'sync', "dx": 'sync', "dsx": 'sync',
              "oput": 'sync', "dw": 'pool', "dst": 'pool', "cc": 'pool'}

    def barrier(self):
        for s in SEM_NAMES:
            own = self._OWNER[s]
            for k in ('pe', 'act', 'dve', 'sync', 'pool'):
                if k != own:
                    self.wait(s, k)

    def mm(self, out, lhsT, rhs, start=True, stop=True, tp=None):
        if self.kind != 'pe':
            return None
        return self.eng.matmul(out, lhsT=lhsT, rhs=rhs, start=start, stop=stop,
                               tile_position=tp)

    def trans(self, out, in_, ident):
        if self.kind != 'pe':
            return None
        return self.eng.transpose(out, in_, ident)

    def act(self, out, in_, func, bias=0.0, scale=1.0, accum_out=None):
        if self.kind != 'act':
            return None
        if func in (AF.Reciprocal, AF.Rsqrt):
            return _act_raw(self.nc, self.eng, out, in_, func, bias, scale)
        kw = {} if accum_out is None else {"accum_out": accum_out}
        return self.eng.activation(out=out, in_=in_, func=func, bias=bias,
                                   scale=scale, **kw)

    def _dve(self, f):
        if self.kind != 'dve':
            self.c["dve"] += 1
            return None
        self.eng.wait_ge(self.sems["dve"], self.c["dve"])
        ins = f()
        self.c["dve"] += 1
        ins.then_inc(self.sems["dve"], 1)
        return ins

    def ts(self, out, in0, s1, s2=None, op0=ALU.mult, op1=ALU.bypass):
        return self._dve(lambda: self.eng.tensor_scalar(
            out=out, in0=in0, scalar1=s1, scalar2=s2, op0=op0, op1=op1))

    def tt(self, out, in0, in1, op=ALU.add):
        return self._dve(lambda: self.eng.tensor_tensor(out=out, in0=in0, in1=in1, op=op))

    def stt(self, out, in0, scalar, in1, op0, op1):
        return self._dve(lambda: self.eng.scalar_tensor_tensor(
            out=out, in0=in0, scalar=scalar, in1=in1, op0=op0, op1=op1))

    def copy(self, out, in_):
        return self._dve(lambda: self.eng.tensor_copy(out=out, in_=in_))

    def memset(self, ap, v):
        return self._dve(lambda: self.eng.memset(ap, v))

    def red(self, out, in_, op=ALU.add):
        return self._dve(lambda: self.eng.tensor_reduce(out, in_, mybir.AxisListType.X, op))

    def vrecip(self, out, in_):
        def f():
            if out.dtype == BF16:
                with self.nc.allow_low_precision(reason="bf16 store of fp32 reciprocal"):
                    return self.eng.reciprocal(out=out, in_=in_)
            return self.eng.reciprocal(out=out, in_=in_)
        return self._dve(f)

    def dma(self, queue, out, in_):
        if self.kind != queue:
            return None
        return self.eng.dma_start(out=out, in_=in_)

    def cc(self, ins_, outs_):
        if self.kind != 'pool':
            return None
        return self.eng.collective_compute(
            "AllReduce", ALU.add, replica_groups=[[0, 1], [2, 3], [4, 5], [6, 7]],
            ins=ins_, outs=outs_)


class L:
    pass


def build_program(stop_stage=None):
    nc = bass.Bass("TRN2", target_bir_lowering=False, debug=False)
    t = L()

    t.xq = nc.dram_tensor("xq", [TOK, E], F32, kind="ExternalInput").ap()
    t.xfull = nc.dram_tensor("xfull", [S, E], F32, kind="ExternalInput").ap()
    for nm in ["w_aq", "w_ak", "w_av", "w_ao", "w_fr", "w_fk", "w_fv",
               "w_bq", "w_bk", "w_bv", "w_bo"]:
        setattr(t, nm, nc.dram_tensor(nm, [E, E], F32, kind="ExternalInput").ap())
    t.b_rows = nc.dram_tensor("b_rows", [5, E], F32, kind="ExternalInput").ap()
    t.b_chunk = nc.dram_tensor("b_chunk", [128, 4 * NCHUNK], F32, kind="ExternalInput").ap()
    t.mu_chunk = nc.dram_tensor("mu_chunk", [128, 2 * NCHUNK], F32, kind="ExternalInput").ap()
    t.eyef = nc.dram_tensor("eyef", [128, 128], F32, kind="ExternalInput").ap()
    t.eyeb = nc.dram_tensor("eyeb", [128, 128], BF16, kind="ExternalInput").ap()
    t.maskv = nc.dram_tensor("maskv", [128, NTT], F32, kind="ExternalInput").ap()

    t.knT_hbm = nc.dram_tensor("knT_hbm", [NKG, 128, NCHUNK * 128], BF16, kind="Internal").ap()
    t.vk_hbm = nc.dram_tensor("vk_hbm", [NKG, 128, H * 65], BF16, kind="Internal").ap()
    t.qn_hbm = nc.dram_tensor("qn_hbm", [NCHUNK, 128, TOK], BF16, kind="Internal").ap()
    t.qb_hbm = nc.dram_tensor("qb_hbm", [NCHUNK, 128, TOK], BF16, kind="Internal").ap()
    t.kv_in = nc.dram_tensor("kv_in", [128, 1024], F32, kind="Internal").ap()
    t.kv_out = nc.dram_tensor("kv_out", [128, 1024], F32, kind="Internal").ap()
    t.x_hbm = nc.dram_tensor("x_hbm", [NCHUNK, 128, TOK], F32, kind="Internal").ap()
    t.out_h = nc.dram_tensor("out", [TOK, E], F32, kind="ExternalOutput").ap()
    t.dbg = None
    if stop_stage is not None:
        t.dbg = nc.dram_tensor("dbg", [128, NCHUNK * TOK], F32, kind="ExternalOutput").ap()

    import contextlib
    with contextlib.ExitStack() as st:
        def sb(name, shape, dt):
            return st.enter_context(nc.sbuf_tensor(name, shape, dt))

        def ps(name, shape):
            return st.enter_context(nc.psum_tensor(name, shape, F32))

        t.xg = sb("xg", [128, NCHUNK * 513], F32)
        t.xfst = sb("xfst", [128, 1024], F32)
        t.consts = sb("consts", [128, 16], F32)
        t.onesb = sb("onesb", [128, 576], BF16)
        t.eyef_sb = sb("eyef_sb", [128, 128], F32)
        t.eyeb_sb = sb("eyeb_sb", [128, 128], BF16)
        t.maskv_sb = sb("maskv_sb", [128, NTT], F32)
        t.brow_sb = sb("brow_sb", [1, 5 * E], BF16)
        t.brow_lo = sb("brow_lo", [1, 5 * E], BF16)
        t.bchunk_sb = sb("bchunk_sb", [128, 4 * NCHUNK], F32)
        t.mu_sb = sb("mu_sb", [128, 2 * NCHUNK], F32)
        t.ksum_bf = sb("ksum_bf", [128, NCHUNK], BF16)
        t.vsum_sb = sb("vsum_sb", [64, H], F32)
        t.kvblk_a = sb("kvblk_a", [128, NCHUNK * 128], BF16)
        t.kvblk_b = sb("kvblk_b", [128, NCHUNK * 128], BF16)
        t.kvacc_b = sb("kvacc_b", [128, 1024], F32)
        t.wbig = sb("wbig", [128, 3 * 8192], BF16)
        t.wcol = sb("wcol", [128, 2 * 1024], BF16)
        t.stage = sb("stage", [128, 2048 + H * 65], BF16)
        t.xtok = sb("xtok", [128, 2 * 1024], F32)
        t.scrf = sb("scrf", [128, 3 * 1024], F32)
        t.scrb = sb("scrb", [128, 4 * 1024], BF16)
        t.sml = sb("sml", [128, 96], F32)
        t.kvacc_a = sb("kvacc_a", [128, 1024], F32)
        t.kstream = sb("kstream", [128, 2 * 1024], BF16)
        t.vstream = sb("vstream", [128, 2 * 1040], BF16)
        t.qtile = sb("qtile", [128, 2 * 512], BF16)
        t.rbuf = sb("rbuf", [128, 2 * 1024], BF16)
        t.ut = sb("ut", [64, 2 * 512], F32)
        t.invn = sb("invn", [65, 1024], BF16)
        t.ofb = sb("ofb", [128, NCHUNK * 512], BF16)
        t.xn_sb = sb("xn_sb", [128, NCHUNK * 520], BF16)
        t.ffb = sb("ffb", [128, 2 * NCHUNK * 512], BF16)
        t.kkb = sb("kkb", [128, NCHUNK * 512], BF16)

        t.p_qk_a = ps("p_qk_a", [128, 1024])
        t.p_qk_b = ps("p_qk_b", [128, 1024])
        t.p_kv = ps("p_kv", [128, 1024])
        t.p_mis = ps("p_mis", [128, 1024])

        for i, v in enumerate([1.1, -1.0, 0.0, 1.0, EPS_LN, EPS_NRM, 4.41, -3.1,
                               1.0 / E, NCONST, 0.5, 2.0]):
            nc.const_aps.register(F32, v, t.consts[:, i:i + 1])

        sems = {s: st.enter_context(nc.semaphore(s)) for s in SEM_NAMES}

        with nc.Block() as block:
            @block.sync
            def _(eng):
                _emit(Walk(nc, 'sync', eng, sems), t, stop_stage)

            @block.tensor
            def _(eng):
                _emit(Walk(nc, 'pe', eng, sems), t, stop_stage)

            @block.scalar
            def _(eng):
                _emit(Walk(nc, 'act', eng, sems), t, stop_stage)

            @block.vector
            def _(eng):
                _emit(Walk(nc, 'dve', eng, sems), t, stop_stage)

            @block.gpsimd
            def _(eng):
                _emit(Walk(nc, 'pool', eng, sems), t, stop_stage)

    return nc



def _emit(w: Walk, t, stop_stage):
    _setup(w, t)
    if stop_stage == 'p0':
        _prep(w, t, stop_p0=True)
        _fin(w, t); return
    _prep(w, t)
    w.barrier()
    if stop_stage == 'prep':
        _dump(w, t); _fin(w, t); return
    if stop_stage == 'a0':
        _attn_a(w, t, 0)
        _dump_a0(w, t); _fin(w, t); return
    for qg in range(NQG):
        _attn_a(w, t, qg)
        w.barrier()
    if stop_stage == 'x1':
        _dump(w, t); _fin(w, t); return
    _load3(w, t, [t.w_fr, t.w_fk, t.w_fv])
    for qg in range(NQG):
        _ffn(w, t, qg)
        w.barrier()
    if stop_stage == 'x2':
        _dump(w, t); _fin(w, t); return
    _load3(w, t, [t.w_bq, t.w_bk, t.w_bv])
    for qg in range(NQG):
        _attnb_pre(w, t, qg)
        w.barrier()
        if stop_stage == 'b0' and qg == 0:
            w.wait("dve", 'pool')
            w.wait("dst", 'pool')
            d = t.dbg
            w.inc(w.dma('pool', d[:, 0:1024], t.kvacc_b[:, :]), "oput", 16)
            w.inc(w.dma('pool', d[:, 1024:2048], t.kvblk_b[:, :]), "oput", 16)
            w.inc(w.dma('pool', d[:, 2048:2560], t.qb_hbm[0, :, 0:512]), "oput", 16)
            w.inc(w.dma('pool', d[:, 2560:3584], t.scrb[:, 0:1024]), "oput", 16)
            w.inc(w.dma('pool', d[:, 3584:4608], t.scrb[:, 1024:2048]), "oput", 16)
            w.inc(w.dma('pool', d[:, 4608:5128], t.xn_sb[:, 0:520]), "oput", 16)
            _fin(w, t); return
    w.barrier()
    _attnb_reduce(w, t)
    w.barrier()
    if stop_stage == 'b1':
        w.wait("dve", 'pool')
        d = t.dbg
        w.inc(w.dma('pool', d[:, 0:1024], t.kvacc_b[:, :]), "oput", 16)
        w.inc(w.dma('pool', d[:, 1024:2048], t.kvblk_b[:, :]), "oput", 16)
        _fin(w, t); return
    for qg in range(NQG):
        _attnb_out(w, t, qg)
        w.barrier()
        if stop_stage == 'b2' and qg == NQG - 1:
            w.wait("dve", 'pool')
            w.wait("dst", 'pool')
            w.wait("dsx", 'pool')
            d = t.dbg
            w.inc(w.dma('pool', d[:, 0:4096], t.ofb[:, :]), "oput", 16)
            w.inc(w.dma('pool', d[:, 4096:4608], t.x_hbm[0, :, 0:512]), "oput", 16)
            w.inc(w.dma('pool', d[:, 4608:5121], t.xg[:, 0:513]), "oput", 16)
            _fin(w, t); return
    if stop_stage == 'x3':
        _dump(w, t); _fin(w, t); return
    _load3(w, t, [t.w_fr, t.w_fk, t.w_fv])
    for qg in range(NQG):
        _ffn(w, t, qg)
        w.barrier()
    if stop_stage == 'x4':
        _dump(w, t); _fin(w, t); return
    _output(w, t)
    _fin(w, t)


def _dump(w, t):
    w.wait("dve", 'sync')
    w.wait("pe", 'sync')
    w.wait("dst", 'sync')
    w.wait("dsx", 'sync')
    for a in range(NCHUNK):
        w.inc(w.dma('sync', t.dbg[:, a * TOK:(a + 1) * TOK], t.x_hbm[a]), "oput", 16)


def _dump_a0(w, t):
    w.wait("dve", 'pool')
    w.wait("pe", 'pool')
    w.wait("acs", 'pool')
    w.wait("dst", 'pool')
    d = t.dbg
    w.inc(w.dma('pool', d[:, 0:4096], t.ofb[:, :]), "oput", 16)
    w.inc(w.dma('pool', d[0:65, 4096:5120], t.invn[:, :]), "oput", 16)
    w.inc(w.dma('pool', d[0:64, 5120:6144], t.ut[:, :]), "oput", 16)
    w.inc(w.dma('pool', d[64:65, 6144:7168], t.scrf[64:65, 0:1024]), "oput", 16)
    w.inc(w.dma('pool', d[:, 7168:7680], t.qn_hbm[0, :, 0:512]), "oput", 16)
    w.inc(w.dma('pool', d[:, 7680:7808], t.knT_hbm[0, :, 0:128]), "oput", 16)
    w.inc(w.dma('pool', d[:, 7808:7938], t.vk_hbm[0, :, 0:130]), "oput", 16)
    w.inc(w.dma('pool', d[:, 7938:7946], t.ksum_bf[:, :]), "oput", 16)
    w.inc(w.dma('pool', d[0:64, 7946:7962], t.vsum_sb[:, :]), "oput", 16)
    w.inc(w.dma('pool', d[:, 8000:9024], t.kvblk_a[:, :]), "oput", 16)
    w.inc(w.dma('pool', d[:, 9024:9056], t.bchunk_sb[:, :]), "oput", 16)
    w.copy(t.scrf[64:65, 1024:2048], t.p_kv[64:65, 0:1024])
    w.wait("dve", 'pool')
    w.inc(w.dma('pool', d[64:65, 9056:10080], t.scrf[64:65, 1024:2048]), "oput", 16)
    w.inc(w.dma('pool', d[:, 10080:12128], t.rbuf[:, :]), "oput", 16)


def _fin(w, t):
    if w.kind == 'pool':
        w.eng.wait_ge(w.sems["oput"], w.c["oput"])


def _setup(w, t):
    for i, v in enumerate([1.1, -1.0, 0.0, 1.0, EPS_LN, EPS_NRM, 4.41, -3.1,
                           1.0 / E, NCONST, 0.5, 2.0]):
        w.memset(t.consts[:, i:i + 1], v)
    w.memset(t.onesb[:, :], 1.0)
    w.memset(t.kvacc_b[:, :], 0.0)
    w.memset(t.kvacc_a[:, :], 0.0)
    w.memset(t.sml[:, 64:88], 0.0)
    w.memset(t.kvblk_a[:, :], 0.0)
    w.memset(t.kvblk_b[:, :], 0.0)
    w.inc(w.dma('sync', t.eyef_sb[:, :], t.eyef), "dx", 16)
    w.inc(w.dma('sync', t.eyeb_sb[:, :], t.eyeb), "dx", 16)
    w.inc(w.dma('sync', t.maskv_sb[:, :], t.maskv), "dx", 16)
    w.inc(w.dma('sync', t.bchunk_sb[:, :], t.b_chunk), "dx", 16)
    w.inc(w.dma('sync', t.mu_sb[:, :], t.mu_chunk), "dx", 16)
    for r_ in range(5):
        w.wait("dve", 'pool')
        w.inc(w.dma('pool', t.scrf[0:1, 0:E], t.b_rows[r_:r_ + 1, :]), "dw", 16)
        w.wait("dw", 'dve')
        w.ts(t.brow_sb[0:1, r_ * E:(r_ + 1) * E], t.scrf[0:1, 0:E], 1.0)
        w.ts(t.scrf[0:1, E:2 * E], t.brow_sb[0:1, r_ * E:(r_ + 1) * E], -1.0)
        w.tt(t.brow_lo[0:1, r_ * E:(r_ + 1) * E], t.scrf[0:1, E:2 * E],
             t.scrf[0:1, 0:E], op=ALU.add)


def _load3(w, t, srcs):
    w.wait("pe", 'pool')
    for i, src in enumerate(srcs):
        w.inc(w.dma('pool', t.wbig[:, i * 8192:(i + 1) * 8192].rearrange("p (a d) -> p a d", a=8),
                    src.rearrange("(a p) d -> p a d", p=128)), "dw", 16)


def _ln_tok(w, t, xt, xn_out):
    m = t.sml[:, 0:1]
    v = t.sml[:, 1:2]
    r = t.sml[:, 2:3]
    sq = t.scrf[:, E:2 * E]
    w.red(m, xt)
    w.ts(m, m, -1.0 / E)
    w.wait("dve", 'act')
    w.wait("acs", 'act')
    w.inc(w.act(sq, xt, AF.Square, bias=m, accum_out=v), "acs", 1)
    w.wait("acs", 'act')
    w.inc(w.act(r, v, AF.Rsqrt, bias=EPS_LN, scale=1.0 / E), "acs", 1)
    w.wait("acs", 'dve')
    w.ts(xn_out, xt, m, r, op0=ALU.add, op1=ALU.mult)


def _norm_heads(w, t, src_f32, dst_bf):
    sq = t.scrf[:, E:2 * E]
    ss = t.sml[:, 16:32]
    src3 = src_f32.rearrange("p (h d) -> p h d", h=H)
    w.wait("dve", 'act')
    w.wait("acs", 'act')
    w.inc(w.act(sq, src_f32, AF.Square), "acs", 1)
    w.wait("acs", 'dve')
    w.red(ss, sq.rearrange("p (h d) -> p h d", h=H))
    w.wait("dve", 'act')
    w.inc(w.act(t.sml[:, 32:48], ss, AF.Rsqrt), "acs", 1)
    w.wait("acs", 'dve')
    w.tt(t.sml[:, 32:48], t.sml[:, 32:48], ss, op=ALU.mult)
    w.ts(t.sml[:, 32:48], t.sml[:, 32:48], EPS_NRM, op0=ALU.add)
    w.vrecip(ss, t.sml[:, 32:48])
    w.tt(dst_bf.rearrange("p (h d) -> p h d", h=H), src3,
         ss.rearrange("p (h o) -> p h o", o=1).broadcast_to([128, H, D]),
         op=ALU.mult)


def _transpose8(w, t, src_bf, dst_bf, pt, dve_val=None):
    w.wait("dve", 'pe', val=(dve_val if dve_val is not None else w.val("dve")))
    i = None
    for a in range(NCHUNK):
        i = w.trans(pt[:, a * 64:(a + 1) * 64].bitcast(BF16),
                    src_bf[:, a * 128:(a + 1) * 128], t.eyeb_sb[:, :])
    w.inc(i, "pe", 1)
    w.wait("pe", 'dve')
    w.copy(dst_bf[:, 0:1024], pt[:, 0:512].bitcast(BF16))


def _proj_tok(w, t, xnT, wbase, brow_idx, pdst):
    last = None
    for half in range(2):
        po = pdst[:, half * 512:(half + 1) * 512]
        for a in range(NCHUNK):
            w.mm(po, xnT[:, a * 128:(a + 1) * 128],
                 t.wbig[:, wbase + a * 1024 + half * 512: wbase + a * 1024 + (half + 1) * 512],
                 start=(a == 0), stop=False)
        w.mm(po, t.onesb[0:1, 0:128],
             t.brow_sb[0:1, brow_idx * E + half * 512: brow_idx * E + (half + 1) * 512],
             start=False, stop=False)
        last = w.mm(po, t.onesb[0:1, 0:128],
                    t.brow_lo[0:1, brow_idx * E + half * 512: brow_idx * E + (half + 1) * 512],
                    start=False, stop=True)
    w.inc(last, "pe", 1)


def _prep(w, t, stop_p0=False):
    _load3(w, t, [t.w_ak, t.w_av, t.w_aq])
    w.wait("dw", 'pe')
    w.wait("dx", 'dve')
    w.wait("dve", 'pe')
    ONES_COL = t.onesb[:, 0:1]

    xtok_tok = [0, 0]
    xtok_pe = [0, 0]
    tile_pe = [0, 0]
    flush_pe = None
    st_tok = {"kn": 0, "vk": 0, "qn": 0}
    for tc in range(NKG + NTT):
        full = tc < NKG
        buf = tc % 2
        xt = t.xtok[:, buf * 1024:(buf + 1) * 1024]
        src = (t.xfull[tc * 128:(tc + 1) * 128, :] if full
               else t.xq[(tc - NKG) * 128:(tc - NKG + 1) * 128, :])
        w.wait("dve", 'sync', val=xtok_tok[buf])
        w.wait("pe", 'sync', val=xtok_pe[buf])
        w.inc(w.dma('sync', xt, src), "dx", 16)
        w.wait("dx", 'dve')

        nbase = t.scrb if tc % 2 == 0 else t.ffb
        xnb = nbase[:, 0:1024]
        w.wait("pe", 'dve', val=tile_pe[tc % 2])
        _ln_tok(w, t, xt, xnb)
        ln_dve = w.val("dve")
        if flush_pe is not None:
            w.wait("pe", 'dve', val=flush_pe)
            w.tt(t.kvacc_a[:, :], t.kvacc_a[:, :], t.p_kv[:, :], op=ALU.add)
            w.tt(t.sml[:, 64:72], t.sml[:, 64:72], t.p_qk_b[:, 0:8], op=ALU.add)
            w.tt(t.sml[0:64, 72:88], t.sml[0:64, 72:88], t.p_qk_b[0:64, 16:32], op=ALU.add)
            flush_pe = None

        if not full:
            tt_ = tc - NKG
            w.wait("dve", 'pe')
            i = None
            for a in range(NCHUNK):
                i = w.trans(t.p_mis[:, a * 128:(a + 1) * 128], xt[:, a * 128:(a + 1) * 128],
                            t.eyef_sb[:, :])
            w.inc(i, "pe", 1)
            xtok_pe[buf] = w.val("pe")
            w.wait("pe", 'dve')
            w.wait("dst", 'dve', val=st_tok.get("xf", 0))
            w.copy(t.xfst[:, 0:1024], t.p_mis[:, 0:1024])
            w.wait("dve", 'pool')
            w.inc(w.dma('pool',
                        t.x_hbm[:, :, tt_ * 128:(tt_ + 1) * 128].rearrange("a p c -> p a c"),
                        t.xfst.rearrange("p (a c) -> p a c", a=8)), "dst", 16)
            st_tok["xf"] = w.val("dst")
            ln_dve = w.val("dve")
        xtok_tok[buf] = w.val("dve")

        xnT = nbase[:, 1024:2048]
        _transpose8(w, t, xnb, xnT, t.p_qk_a, dve_val=ln_dve)
        w.wait("dve", 'pe')

        if full:
            _proj_tok(w, t, xnT, 0, 1, t.p_mis)
            w.wait("pe", 'dve')
            w.wait("pe", 'act')
            kn_tok = t.scrb[:, 2048:3072]
            _norm_heads(w, t, t.p_mis[:, 0:1024], kn_tok)
            if stop_p0:
                w.wait("dve", 'pool')
                w.wait("acs", 'pool')
                d = t.dbg
                w.inc(w.dma('pool', d[:, 0:1024], t.scrb[:, 0:1024]), "oput", 16)
                w.inc(w.dma('pool', d[:, 1024:2048], t.scrb[:, 1024:2048]), "oput", 16)
                w.inc(w.dma('pool', d[:, 2048:3072], t.p_mis[:, 0:1024]), "oput", 16)
                w.inc(w.dma('pool', d[:, 3072:4096], kn_tok), "oput", 16)
                w.inc(w.dma('pool', d[:, 4096:4160], t.sml[:, :]), "oput", 16)
                w.inc(w.dma('pool', d[:, 4160:4288], t.xtok[:, 0:128]), "oput", 16)
                w.inc(w.dma('pool', d[:, 4288:6336], t.wbig[:, 0:2048]), "oput", 16)
                w.inc(w.dma('pool', d[0:1, 6336:11456], t.brow_sb[:, :]), "oput", 16)
                w.inc(w.dma('pool', d[0:1, 11456:16576], t.brow_lo[:, :]), "oput", 16)
                return
            w.wait("dve", 'pe')
            _proj_tok(w, t, xnT, 8192, 2, t.p_mis)
            w.wait("pe", 'dve')
            w.wait("pe", 'act')
            v_tok = t.scrb[:, 3072:4096]
            w.wait("acs", 'act')
            w.inc(w.act(v_tok, t.p_mis[:, 0:1024], AF.Copy), "acs", 1)
            w.wait("acs", 'dve')
            v_lo = nbase[:, 0:1024]
            w.ts(t.scrf[:, 0:1024], v_tok, -1.0)
            w.tt(v_lo, t.scrf[:, 0:1024], t.p_mis[:, 0:1024], op=ALU.add)
            w.wait("dst", 'dve', val=st_tok["vk"])
            vkst = t.stage[:, 2048:2048 + H * 65]
            vk3 = vkst.rearrange("p (h e) -> p h e", h=H)
            if tc == 0:
                w.memset(vk3[:, :, 64:65], 1.0)
            w.copy(vk3[:, :, 0:64], v_tok.rearrange("p (h d) -> p h d", h=H))
            w.wait("dst", 'dve', val=st_tok["kn"])
            knst = t.stage[:, 1024:2048]
            _transpose8(w, t, kn_tok, knst, t.p_qk_a)
            w.wait("dve", 'pe')
            w.wait("acs", 'pe')
            for a in range(NCHUNK):
                w.mm(t.p_kv[:, a * 128:(a + 1) * 128], kn_tok[:, a * 128:(a + 1) * 128],
                     v_tok[:, a * 128:(a + 1) * 128], start=True, stop=True)
            for a in range(NCHUNK):
                w.mm(t.p_qk_b[:, a:a + 1], kn_tok[:, a * 128:(a + 1) * 128], ONES_COL,
                     start=True, stop=True)
            i = None
            for hh in range(H):
                w.mm(t.p_qk_b[0:64, 16 + hh:17 + hh], v_tok[:, hh * 64:(hh + 1) * 64],
                     ONES_COL, start=True, stop=False)
                i = w.mm(t.p_qk_b[0:64, 16 + hh:17 + hh], v_lo[:, hh * 64:(hh + 1) * 64],
                         ONES_COL, start=False, stop=True)
            w.inc(i, "pe", 1)
            flush_pe = w.val("pe")
            w.wait("dve", 'pool')
            w.wait("pe", 'pool', val=0)
            w.inc(w.dma('pool', t.knT_hbm[tc], knst), "dst", 16)
            st_tok["kn"] = w.val("dst")
            w.inc(w.dma('pool', t.vk_hbm[tc], vkst), "dst", 16)
            st_tok["vk"] = w.val("dst")
        else:
            _proj_tok(w, t, xnT, 16384, 0, t.p_mis)
            w.wait("pe", 'dve')
            w.wait("pe", 'act')
            qn_tok = t.scrb[:, 2048:3072]
            _norm_heads(w, t, t.p_mis[:, 0:1024], qn_tok)
            w.wait("dst", 'dve', val=st_tok["qn"])
            qnst = t.stage[:, 0:1024]
            _transpose8(w, t, qn_tok, qnst, t.p_qk_a)
            tt_ = tc - NKG
            w.wait("dve", 'pool')
            w.inc(w.dma('pool',
                        t.qn_hbm[:, :, tt_ * 128:(tt_ + 1) * 128].rearrange("a p c -> p a c"),
                        qnst.rearrange("p (a c) -> p a c", a=8)), "dst", 16)
            st_tok["qn"] = w.val("dst")
        tile_pe[tc % 2] = w.val("pe")

    if flush_pe is not None:
        w.wait("pe", 'dve', val=flush_pe)
        w.tt(t.kvacc_a[:, :], t.kvacc_a[:, :], t.p_kv[:, :], op=ALU.add)
        w.tt(t.sml[:, 64:72], t.sml[:, 64:72], t.p_qk_b[:, 0:8], op=ALU.add)
        w.tt(t.sml[0:64, 72:88], t.sml[0:64, 72:88], t.p_qk_b[0:64, 16:32], op=ALU.add)
        flush_pe = None

    w.wait("pe", 'dve')
    kb3 = t.kvblk_a.rearrange("p (a c) -> p a c", a=NCHUNK)
    ka3 = t.kvacc_a.rearrange("p (a c) -> p a c", a=NCHUNK)
    w.copy(kb3[0:64, :, 0:64], ka3[0:64, :, 0:64])
    w.copy(kb3[64:128, :, 64:128], ka3[64:128, :, 64:128])
    w.copy(t.ksum_bf[:, :], t.sml[:, 64:64 + NCHUNK])
    w.copy(t.vsum_sb[:, :], t.sml[0:64, 72:72 + H])
    w.ts(t.vsum_sb[:, :], t.vsum_sb[:, :], 3.1)


def _rb_slot(t, kg):
    s = kg % 4
    if s < 2:
        return t.rbuf[:, s * 1024:(s + 1) * 1024]
    return t.scrb[:, s * 1024:(s + 1) * 1024]


def _emit_rv(w, t, qg, kg, nq):
    if kg < 0:
        return
    oo = kg // 8
    kk = kg % 8
    ob = oo % 2
    vk_e = t.vstream[:, ob * 1040 + kk * 130: ob * 1040 + kk * 130 + 65]
    vk_o = t.vstream[:, ob * 1040 + kk * 130 + 65: ob * 1040 + kk * 130 + 130]
    rb_ = _rb_slot(t, kg)
    w.wait("rcp", 'pe', val=t.rcp_of[kg])
    w.mm(t.p_kv[0:65, 0:nq], vk_e, rb_[:, 0:nq], start=(kg == 0), stop=(kg == NKG - 1))
    i = w.mm(t.p_kv[0:65, 512:512 + nq], vk_o, rb_[:, 512:512 + nq],
             start=(kg == 0), stop=(kg == NKG - 1))
    w.inc(i, "rv", 1)


def _attn_a(w, t, qg):
    nq = QGS[qg]
    q0 = QG0[qg]
    t.rcp_of = {}
    w.wait("dst", 'sync')
    for p in range(NCHUNK):
        qb = p % 2
        qt = t.qtile[:, qb * 512:qb * 512 + nq]
        w.wait("pe", 'sync', val=w.val("pe"))
        w.wait("qk", 'sync', val=w.val("qk"))
        w.inc(w.dma('sync', qt, t.qn_hbm[p, :, q0:q0 + nq]), "dq", 16)
        w.wait("dve", 'pe')
        dk_tok = {}
        for oct_ in range(10):
            if oct_ >= 2:
                oo = oct_ - 2
                ob = oo % 2
                for kk in range(8):
                    kg = oo * 8 + kk
                    pb = kg % 2
                    pqk = (t.p_qk_a, t.p_qk_b, t.p_mis)[kg % 3]
                    kn_e = t.kstream[0:64, ob * 1024 + kk * 128: ob * 1024 + (kk + 1) * 128]
                    kn_o = t.kstream[64:128, ob * 1024 + kk * 128: ob * 1024 + (kk + 1) * 128]
                    if kk == 0:
                        w.wait("dk", 'pe', val=dk_tok[oo])
                        if oo == 0:
                            w.wait("dq", 'pe')
                    w.wait("rcp", 'pe', val=w.val("rcp") - 2)
                    w.mm(pqk[:, 0:nq], kn_e, qt[0:64, :], tp=(0, 0))
                    i = w.mm(pqk[:, 512:512 + nq], kn_o, qt[64:128, :], tp=(64, 0))
                    w.inc(i, "qk", 1)
                    _emit_rv(w, t, qg, kg - 2, nq)
                    w.wait("qk", 'act')
                    w.wait("rv", 'act', val=w.val("rv") - 2)
                    rb_ = _rb_slot(t, kg)
                    if nq == 512:
                        i = w.act(rb_[:, 0:1024], pqk[:, 0:1024], AF.Reciprocal,
                                  bias=1.1, scale=-1.0)
                    else:
                        w.act(rb_[:, 0:nq], pqk[:, 0:nq], AF.Reciprocal, bias=1.1, scale=-1.0)
                        i = w.act(rb_[:, 512:512 + nq], pqk[:, 512:512 + nq], AF.Reciprocal,
                                  bias=1.1, scale=-1.0)
                    w.inc(i, "rcp", 1)
                    t.rcp_of[kg] = w.val("rcp")
            if oct_ < 8:
                ob2 = oct_ % 2
                w.wait("qk", 'sync', val=w.val("qk"))
                w.wait("rv", 'sync', val=w.val("rv") + (2 if oct_ >= 2 else 0))
                w.inc(w.dma('sync',
                            t.kstream[:, ob2 * 1024:(ob2 + 1) * 1024]
                            .rearrange("p (k c) -> p k c", k=8),
                            t.knT_hbm[oct_ * 8:(oct_ + 1) * 8, :, p * 128:(p + 1) * 128]
                            .rearrange("k p c -> p k c")),
                      "dk", 16)
                w.inc(w.dma('sync',
                            t.vstream[:, ob2 * 1040:(ob2 + 1) * 1040]
                            .rearrange("p (k c) -> p k c", k=8),
                            t.vk_hbm[oct_ * 8:(oct_ + 1) * 8, :, 2 * p * 65:(2 * p + 2) * 65]
                            .rearrange("k p c -> p k c")),
                      "dk", 16)
                dk_tok[oct_] = w.val("dk")
        _emit_rv(w, t, qg, NKG - 2, nq)
        _emit_rv(w, t, qg, NKG - 1, nq)
        w.wait("rv", 'dve')
        ue = t.ut[:, 0:nq]
        uo = t.ut[:, 512:512 + nq]
        w.copy(ue, t.p_kv[0:64, 0:nq])
        w.copy(uo, t.p_kv[0:64, 512:512 + nq])
        w.wait("rv", 'pe')
        w.mm(t.p_mis[64:65, 0:nq], t.ksum_bf[0:64, p:p + 1], qt[0:64, :], tp=(0, 64))
        i = w.mm(t.p_mis[64:65, 512:512 + nq], t.ksum_bf[64:128, p:p + 1],
                 qt[64:128, :], tp=(64, 64))
        w.inc(i, "pe", 1)
        w.wait("pe", 'dve')
        nr_e = t.scrf[64:65, 0:nq]
        nr_o = t.scrf[64:65, 512:512 + nq]
        w.copy(nr_e, t.p_kv[64:65, 0:nq])
        w.copy(nr_o, t.p_kv[64:65, 512:512 + nq])
        w.stt(nr_e, nr_e, 4.41, t.p_mis[64:65, 0:nq], ALU.mult, ALU.subtract)
        w.stt(nr_o, nr_o, 4.41, t.p_mis[64:65, 512:512 + nq], ALU.mult, ALU.subtract)
        w.ts(nr_e, nr_e, NCONST, op0=ALU.add)
        w.ts(nr_o, nr_o, NCONST, op0=ALU.add)
        w.wait("dve", 'act')
        w.wait("pe", 'act')
        w.act(t.invn[64:65, 0:nq], nr_e, AF.Reciprocal)
        w.inc(w.act(t.invn[64:65, 512:512 + nq], nr_o, AF.Reciprocal), "acs", 1)
        w.wait("acs", 'pe')
        for half in range(2):
            co = half * 512
            usrc = ue if half == 0 else uo
            w.mm(t.p_mis[0:64, 0:nq], t.onesb[64:65, 0:64], t.invn[64:65, co:co + nq])
            if half == 0:
                i = w.mm(t.p_mis[0:64, 512:512 + nq], t.kvblk_a[0:64, p * 128:p * 128 + 64],
                         qt[0:64, :], tp=(0, 0))
            else:
                i = w.mm(t.p_mis[0:64, 512:512 + nq],
                         t.kvblk_a[64:128, p * 128 + 64:p * 128 + 128],
                         qt[64:128, :], tp=(64, 0))
            w.inc(i, "pe", 1)
            w.wait("pe", 'dve')
            e1 = t.scrf[0:64, 1024:1024 + nq]
            w.stt(e1, usrc, 4.41, t.p_mis[0:64, 512:512 + nq], ALU.mult, ALU.subtract)
            if half == 1:
                w.wait("dst", 'dve')
            dst = t.ofb[0:64, p * 512:p * 512 + nq] if half == 0 else t.scrb[0:64, 0:nq]
            w.stt(dst, e1, t.vsum_sb[:, 2 * p + half:2 * p + half + 1],
                        t.p_mis[0:64, 0:nq], ALU.subtract, ALU.mult)
            if half == 1:
                w.wait("dve", 'pool')
                w.inc(w.dma('pool', t.ofb[64:128, p * 512:p * 512 + nq], t.scrb[0:64, 0:nq]),
                      "dst", 16)
            else:
                w.wait("dve", 'pe')
    w.wait("dst", 'pe')
    w.wait("dve", 'sync')
    xgv = t.xg.rearrange("p (a c) -> p a c", c=513)[:, :, 1:1 + nq]
    w.inc(w.dma('sync', xgv, t.x_hbm[:, :, q0:q0 + nq].rearrange("a p c -> p a c")), "dx", 16)
    w.wait("dx", 'dve')
    wc_tok = [None, None]
    for fc in range(NCHUNK):
        wb = fc % 2
        w.wait("pe", 'pool', val=(wc_tok[wb] if wc_tok[wb] is not None else w.val("pe")))
        w.inc(w.dma('pool', t.wcol[:, wb * 1024:(wb + 1) * 1024].rearrange("p (a c) -> p a c", a=8),
                    t.w_ao[:, fc * 128:(fc + 1) * 128].rearrange("(a p) c -> p a c", p=128)),
              "dw", 16)
        w.wait("dw", 'pe')
        w.wait("dve", 'pe')
        i = None
        for a in range(NCHUNK):
            i = w.mm(t.p_mis[:, 0:nq], t.wcol[:, wb * 1024 + a * 128:wb * 1024 + (a + 1) * 128],
                     t.ofb[:, a * 512:a * 512 + nq], start=(a == 0), stop=(a == NCHUNK - 1))
        w.inc(i, "pe", 1)
        wc_tok[wb] = w.val("pe")
        w.wait("pe", 'dve')
        xc = t.xg[:, fc * 513 + 1: fc * 513 + 1 + nq]
        w.ts(t.scrf[:, 0:nq], t.p_mis[:, 0:nq], t.bchunk_sb[:, fc:fc + 1], op0=ALU.add)
        w.tt(xc, xc, t.scrf[:, 0:nq], op=ALU.add)
    w.wait("dve", 'sync')
    w.inc(w.dma('sync', t.x_hbm[:, :, q0:q0 + nq].rearrange("a p c -> p a c"), xgv), "dsx", 16)
    w.wait("dve", 'pe')


def _ln_fm(w, t, qg):
    nq = QGS[qg]
    q0 = QG0[qg]
    nb = nq + 1
    bnd = qg > 0
    ONES_COL = t.onesb[:, 0:1]
    w.wait("dve", 'sync')
    w.wait("pe", 'sync')
    if not bnd:
        xgv = t.xg.rearrange("p (a c) -> p a c", c=513)[:, :, 1:1 + nq]
        w.inc(w.dma('sync', xgv, t.x_hbm[:, :, 0:nq].rearrange("a p c -> p a c")), "dx", 16)
    else:
        xgv = t.xg.rearrange("p (a c) -> p a c", c=513)[:, :, 0:nb]
        w.inc(w.dma('sync', xgv, t.x_hbm[:, :, q0 - 1:q0 + nq].rearrange("a p c -> p a c")),
              "dx", 16)
    w.wait("dx", 'dve')
    cast_tok = [None, None]
    for a in range(NCHUNK):
        sl = a % 2
        xb = t.scrb[:, sl * 1024:sl * 1024 + nb]
        x2 = t.scrb[:, 2048 + sl * 1024:2048 + sl * 1024 + nb]
        w.wait("pe", 'dve', val=(cast_tok[sl] if cast_tok[sl] is not None else w.val("pe")))
        if bnd:
            xsrc = t.xg[:, a * 513:a * 513 + nb]
            w.ts(xb, xsrc, 1.0)
            w.tt(x2, xsrc, xsrc, op=ALU.mult)
        else:
            xsrc = t.xg[:, a * 513 + 1:a * 513 + 1 + nq]
            w.ts(xb[:, 1:nb], xsrc, 1.0)
            w.tt(x2[:, 1:nb], xsrc, xsrc, op=ALU.mult)
        w.wait("dve", 'pe')
        w.mm(t.p_qk_a[0:1, 0:nq], ONES_COL, xb[:, 1:nb], start=(a == 0), stop=(a == NCHUNK - 1))
        i = w.mm(t.p_qk_b[0:1, 0:nq], ONES_COL, x2[:, 1:nb], start=(a == 0), stop=(a == NCHUNK - 1))
        if bnd:
            w.mm(t.p_qk_a[0:1, 512:513], ONES_COL, xb[:, 0:1],
                 start=(a == 0), stop=(a == NCHUNK - 1))
            i = w.mm(t.p_qk_b[0:1, 512:513], ONES_COL, x2[:, 0:1],
                     start=(a == 0), stop=(a == NCHUNK - 1))
        w.inc(i, "pe", 1)
        cast_tok[sl] = w.val("pe")
    w.wait("pe", 'dve')
    ncols = 513 if bnd else nq
    mrow = t.scrf[0:1, 0:ncols]
    vrow = t.scrf[0:1, 1024:1024 + ncols]
    w.ts(mrow, t.p_qk_a[0:1, 0:ncols], 1.0 / E)
    w.ts(vrow, t.p_qk_b[0:1, 0:ncols], 1.0 / E)
    w.tt(t.scrf[0:1, 2048:2048 + ncols], mrow, mrow, op=ALU.mult)
    w.tt(vrow, vrow, t.scrf[0:1, 2048:2048 + ncols], op=ALU.subtract)
    w.wait("dve", 'act')
    w.inc(w.act(t.scrf[0:1, 2048:2048 + ncols], vrow, AF.Rsqrt, bias=EPS_LN), "acs", 1)
    w.wait("acs", 'dve')
    w.ts(t.scrb[0:1, 0:ncols], mrow, 1.0)
    w.ts(t.scrb[0:1, 1024:1024 + ncols], t.scrf[0:1, 2048:2048 + ncols], 1.0)
    w.wait("dve", 'pe')
    w.mm(t.p_qk_b[:, 0:nq], t.onesb[0:1, 0:128], t.scrb[0:1, 0:nq])
    i = w.mm(t.p_qk_a[:, 0:nq], t.onesb[0:1, 0:128], t.scrb[0:1, 1024:1024 + nq])
    if bnd:
        w.mm(t.p_qk_b[:, 512:513], t.onesb[0:1, 0:128], t.scrb[0:1, 512:513])
        i = w.mm(t.p_qk_a[:, 512:513], t.onesb[0:1, 0:128], t.scrb[0:1, 1024 + 512:1024 + 513])
    w.inc(i, "pe", 1)
    w.wait("pe", 'dve')
    for a in range(NCHUNK):
        w.tt(t.scrf[:, 0:nq], t.xg[:, a * 513 + 1:a * 513 + 1 + nq],
             t.p_qk_b[:, 0:nq], op=ALU.subtract)
        w.tt(t.xn_sb[:, a * 520 + 1:a * 520 + 1 + nq], t.scrf[:, 0:nq],
             t.p_qk_a[:, 0:nq], op=ALU.mult)
        if bnd:
            w.tt(t.scrf[:, 512:513], t.xg[:, a * 513:a * 513 + 1],
                 t.p_qk_b[:, 512:513], op=ALU.subtract)
            w.tt(t.xn_sb[:, a * 520:a * 520 + 1], t.scrf[:, 512:513],
                 t.p_qk_a[:, 512:513], op=ALU.mult)
        else:
            w.memset(t.xn_sb[:, a * 520:a * 520 + 1], 0.0)


def _ffn(w, t, qg):
    nq = QGS[qg]
    q0 = QG0[qg]
    _ln_fm(w, t, qg)
    i = None
    for a in range(NCHUNK):
        xn_c = t.xn_sb[:, a * 520 + 1:a * 520 + 1 + nq]
        xn_p = t.xn_sb[:, a * 520:a * 520 + nq]
        d = t.scrb[:, 0:nq]
        w.tt(d, xn_c, xn_p, op=ALU.subtract)
        w.stt(t.ffb[:, a * 512:a * 512 + nq], d, t.mu_sb[:, a:a + 1], xn_p, ALU.mult, ALU.add)
        i = w.stt(t.ffb[:, 4096 + a * 512:4096 + a * 512 + nq], d,
                  t.mu_sb[:, 8 + a:9 + a], xn_p, ALU.mult, ALU.add)
    w.wait("dve", 'pe')
    for dc in range(NCHUNK):
        pd = t.p_mis[:, (dc % 2) * 512:(dc % 2) * 512 + nq]
        if dc >= 2:
            w.wait("acs", 'pe', val=w.val("acs") - 1)
        i = None
        for a in range(NCHUNK):
            i = w.mm(pd, t.wbig[:, 8192 + a * 1024 + dc * 128:8192 + a * 1024 + (dc + 1) * 128],
                     t.ffb[:, 4096 + a * 512:4096 + a * 512 + nq],
                     start=(a == 0), stop=(a == NCHUNK - 1))
        w.inc(i, "qk", 1)
        w.wait("qk", 'act')
        w.inc(w.act(t.scrf[:, (dc % 2) * 512:(dc % 2) * 512 + nq], pd, AF.Relu), "acs", 1)
        w.wait("acs", 'dve')
        w.tt(t.kkb[:, dc * 512:dc * 512 + nq],
                   t.scrf[:, (dc % 2) * 512:(dc % 2) * 512 + nq],
                   t.scrf[:, (dc % 2) * 512:(dc % 2) * 512 + nq], op=ALU.mult)
    w.wait("dve", 'pe')
    for dc in range(NCHUNK):
        pr = t.p_mis[:, 0:nq]
        pv = t.p_mis[:, 512:512 + nq]
        if dc >= 1:
            w.wait("acs", 'pe')
        i = None
        for a in range(NCHUNK):
            i = w.mm(pr, t.wbig[:, a * 1024 + dc * 128:a * 1024 + (dc + 1) * 128],
                     t.ffb[:, a * 512:a * 512 + nq], start=(a == 0), stop=(a == NCHUNK - 1))
        w.inc(i, "qk", 1)
        w.wait("qk", 'act')
        if dc >= 1:
            w.wait("dve", 'act')
        w.inc(w.act(t.scrf[:, 0:nq], pr, AF.Sigmoid), "acs", 1)
        if dc >= 1:
            w.wait("dve", 'pe')
        i = None
        for a in range(NCHUNK):
            i = w.mm(pv, t.wbig[:, 16384 + a * 1024 + dc * 128:16384 + a * 1024 + (dc + 1) * 128],
                     t.kkb[:, a * 512:a * 512 + nq], start=(a == 0), stop=(a == NCHUNK - 1))
        w.inc(i, "rv", 1)
        w.wait("acs", 'dve')
        w.wait("rv", 'dve')
        xc = t.xg[:, dc * 513 + 1: dc * 513 + 1 + nq]
        w.tt(t.scrf[:, 512:512 + nq], pv, t.scrf[:, 0:nq], op=ALU.mult)
        w.tt(xc, xc, t.scrf[:, 512:512 + nq], op=ALU.add)
    w.wait("dve", 'sync')
    xgv = t.xg.rearrange("p (a c) -> p a c", c=513)[:, :, 1:1 + nq]
    w.inc(w.dma('sync', t.x_hbm[:, :, q0:q0 + nq].rearrange("a p c -> p a c"), xgv), "dsx", 16)
    w.wait("dve", 'pe')


def _attnb_pre(w, t, qg):
    nq = QGS[qg]
    q0 = QG0[qg]
    _ln_fm(w, t, qg)
    w.wait("dve", 'pe')
    for dc in range(NCHUNK):
        pd = t.p_mis[:, (dc % 2) * 512:(dc % 2) * 512 + nq]
        if dc >= 2:
            w.wait("dve", 'pe', val=w.val("dve") - 3)
        i = None
        for a in range(NCHUNK):
            i = w.mm(pd, t.wbig[:, a * 1024 + dc * 128:a * 1024 + (dc + 1) * 128],
                     t.xn_sb[:, a * 520 + 1:a * 520 + 1 + nq],
                     start=(a == 0), stop=(a == NCHUNK - 1))
        w.inc(i, "qk", 1)
        w.wait("qk", 'dve')
        bq = t.bchunk_sb[:, 8 + dc:9 + dc]
        mn = t.scrf[:, 0:nq]
        mx = t.scrf[:, 512:512 + nq]
        w.ts(mn, pd, bq, 0.0, op0=ALU.add, op1=ALU.min)
        w.ts(mx, pd, bq, 0.0, op0=ALU.add, op1=ALU.max)
        w.wait("dve", 'act')
        w.inc(w.act(t.scrf[:, 1024:1024 + nq], mn, AF.Exp), "acs", 1)
        w.wait("acs", 'dve')
        w.wait("dst", 'dve', val=w.val("dst") - 16)
        w.tt(t.stage[:, (dc % 2) * 1024:(dc % 2) * 1024 + nq],
                   t.scrf[:, 1024:1024 + nq], mx, op=ALU.add)
        w.wait("dve", 'pool')
        w.inc(w.dma('pool', t.qb_hbm[dc, :, q0:q0 + nq],
                    t.stage[:, (dc % 2) * 1024:(dc % 2) * 1024 + nq]), "dst", 16)
    ntt_q = nq // 128
    for tl in range(ntt_q):
        ttg = q0 // 128 + tl
        mcol = t.maskv_sb[:, ttg:ttg + 1]
        for half in range(2):
            pk = t.p_mis[:, 0:512]
            if tl > 0 or half > 0:
                w.wait("dve", 'pe')
            i = None
            for a in range(NCHUNK):
                i = w.mm(pk, t.xn_sb[:, a * 520 + 1 + tl * 128:a * 520 + 1 + (tl + 1) * 128],
                         t.wbig[:, 8192 + a * 1024 + half * 512:8192 + a * 1024 + (half + 1) * 512],
                         start=(a == 0), stop=False)
            w.mm(pk, t.onesb[0:1, 0:128],
                 t.brow_sb[0:1, 3 * E + half * 512:3 * E + (half + 1) * 512],
                 start=False, stop=False)
            i = w.mm(pk, t.onesb[0:1, 0:128],
                     t.brow_lo[0:1, 3 * E + half * 512:3 * E + (half + 1) * 512],
                     start=False, stop=True)
            w.inc(i, "qk", 1)
            w.wait("qk", 'dve')
            mn = t.scrf[:, 0:512]
            mx = t.scrf[:, 512:1024]
            w.ts(mn, pk, 0.0, None, op0=ALU.min)
            w.ts(mx, pk, 0.0, None, op0=ALU.max)
            w.wait("dve", 'act')
            w.inc(w.act(t.scrf[:, 1024:1536], mn, AF.Exp), "acs", 1)
            w.wait("acs", 'dve')
            w.wait("rv", 'dve')
            w.tt(t.scrf[:, 1536:2048], t.scrf[:, 1024:1536], mx, op=ALU.add)
            w.ts(t.scrb[:, half * 512:(half + 1) * 512], t.scrf[:, 1536:2048], mcol)
            w.wait("dve", 'pe')
            pv = t.p_mis[:, 512:1024]
            i = None
            for a in range(NCHUNK):
                i = w.mm(pv, t.xn_sb[:, a * 520 + 1 + tl * 128:a * 520 + 1 + (tl + 1) * 128],
                         t.wbig[:, 16384 + a * 1024 + half * 512:16384 + a * 1024 + (half + 1) * 512],
                         start=(a == 0), stop=False)
            w.mm(pv, t.onesb[0:1, 0:128],
                 t.brow_sb[0:1, 4 * E + half * 512:4 * E + (half + 1) * 512],
                 start=False, stop=False)
            i = w.mm(pv, t.onesb[0:1, 0:128],
                     t.brow_lo[0:1, 4 * E + half * 512:4 * E + (half + 1) * 512],
                     start=False, stop=True)
            w.inc(i, "qk", 1)
            w.wait("qk", 'dve')
            w.copy(t.scrb[:, 1024 + half * 512:1024 + (half + 1) * 512], pv)
        w.wait("dve", 'pe')
        i = None
        for pp in range(NCHUNK):
            i = w.mm(t.p_qk_a[:, pp * 128:(pp + 1) * 128], t.scrb[:, pp * 128:(pp + 1) * 128],
                     t.scrb[:, 1024 + pp * 128:1024 + (pp + 1) * 128],
                     start=True, stop=True)
        w.inc(i, "rv", 1)
        w.wait("rv", 'dve')
        w.tt(t.kvacc_b[:, :], t.kvacc_b[:, :], t.p_qk_a[:, :], op=ALU.add)
    w.wait("dve", 'pe')


def _attnb_reduce(w, t):
    w.wait("dve", 'pool')
    w.inc(w.dma('pool', t.kv_in, t.kvacc_b[:, :]), "dst", 16)
    w.wait("dst", 'pool')
    w.inc(w.cc([t.kv_in], [t.kv_out]), "cc", 1)
    w.wait("cc", 'pool')
    w.inc(w.dma('pool', t.scrf[:, 0:1024], t.kv_out), "dst", 16)
    w.wait("dst", 'dve')
    i = None
    for a in range(NCHUNK):
        w.copy(t.kvblk_b[0:64, a * 128:a * 128 + 64], t.scrf[0:64, a * 128:a * 128 + 64])
        i = w.copy(t.kvblk_b[64:128, a * 128 + 64:a * 128 + 128],
                   t.scrf[64:128, a * 128 + 64:a * 128 + 128])
    w.wait("dve", 'pe')


def _attnb_out(w, t, qg):
    nq = QGS[qg]
    q0 = QG0[qg]
    w.wait("dst", 'sync')
    qk_tok = [None, None]
    for p in range(NCHUNK):
        sl = p % 2
        qp = t.qtile[:, sl * 512:sl * 512 + nq]
        w.wait("qk", 'sync', val=(qk_tok[sl] if qk_tok[sl] is not None else w.val("qk")))
        w.inc(w.dma('sync', qp, t.qb_hbm[p, :, q0:q0 + nq]), "dq", 16)
        w.wait("dq", 'pe')
        w.wait("dve", 'pe', val=w.val("dve") - 1 if p >= 2 else w.val("dve"))
        w.inc(w.mm(t.p_mis[:, sl * 512:sl * 512 + nq],
                   t.kvblk_b[:, p * 128:(p + 1) * 128], qp), "qk", 1)
        qk_tok[sl] = w.val("qk")
        w.wait("qk", 'dve')
        if p == 0:
            w.wait("pe", 'dve')
        w.copy(t.ofb[:, p * 512:p * 512 + nq], t.p_mis[:, sl * 512:sl * 512 + nq])
    w.barrier()
    w.wait("dve", 'pe')
    w.wait("dve", 'sync')
    w.wait("pe", 'sync')
    xgv = t.xg.rearrange("p (a c) -> p a c", c=513)[:, :, 1:1 + nq]
    w.inc(w.dma('sync', xgv, t.x_hbm[:, :, q0:q0 + nq].rearrange("a p c -> p a c")), "dx", 16)
    w.wait("dx", 'dve')
    wc_tok = [None, None]
    for fc in range(NCHUNK):
        wb = fc % 2
        w.wait("pe", 'pool', val=(wc_tok[wb] if wc_tok[wb] is not None else w.val("pe")))
        w.inc(w.dma('pool', t.wcol[:, wb * 1024:(wb + 1) * 1024].rearrange("p (a c) -> p a c", a=8),
                    t.w_bo[:, fc * 128:(fc + 1) * 128].rearrange("(a p) c -> p a c", p=128)),
              "dw", 16)
        w.wait("dw", 'pe')
        w.wait("dve", 'pe')
        i = None
        for a in range(NCHUNK):
            i = w.mm(t.p_kv[:, 0:nq], t.wcol[:, wb * 1024 + a * 128:wb * 1024 + (a + 1) * 128],
                     t.ofb[:, a * 512:a * 512 + nq], start=(a == 0), stop=(a == NCHUNK - 1))
        w.inc(i, "pe", 1)
        wc_tok[wb] = w.val("pe")
        w.wait("pe", 'dve')
        xc = t.xg[:, fc * 513 + 1: fc * 513 + 1 + nq]
        w.ts(t.scrf[:, 0:nq], t.p_kv[:, 0:nq], t.bchunk_sb[:, 16 + fc:17 + fc], op0=ALU.add)
        w.tt(xc, xc, t.scrf[:, 0:nq], op=ALU.add)
        w.barrier()
    w.wait("dve", 'sync')
    w.inc(w.dma('sync', t.x_hbm[:, :, q0:q0 + nq].rearrange("a p c -> p a c"), xgv), "dsx", 16)
    w.wait("dve", 'pe')


def _output(w, t):
    ot_tok = [0, 0]
    ld_tok = [0, 0]
    for tt_ in range(NTT):
        lb = tt_ % 2
        xf = t.xfst[:, 0:1024] if lb == 0 else t.scrf[:, 0:1024]
        w.wait("pe", 'sync', val=ld_tok[lb])
        w.inc(w.dma('sync', xf.rearrange("p (a c) -> p a c", a=8),
                    t.x_hbm[:, :, tt_ * 128:(tt_ + 1) * 128].rearrange("a p c -> p a c")),
              "dx", 16)
        w.wait("dx", 'pe')
        w.wait("dve", 'pe')
        i = None
        for a in range(NCHUNK):
            i = w.trans(t.p_mis[:, a * 128:(a + 1) * 128], xf[:, a * 128:(a + 1) * 128],
                        t.eyef_sb[:, :])
        w.inc(i, "pe", 1)
        ld_tok[lb] = w.val("pe")
        w.wait("pe", 'dve')
        ob = tt_ % 2
        w.wait("oput", 'dve', val=ot_tok[ob])
        ot = t.xtok[:, ob * 1024:(ob + 1) * 1024]
        i = None
        for a in range(NCHUNK):
            i = w.copy(ot[:, a * 128:(a + 1) * 128], t.p_mis[:, a * 128:(a + 1) * 128])
        w.wait("dve", 'sync')
        w.inc(w.dma('sync', t.out_h[tt_ * 128:(tt_ + 1) * 128, :], ot), "oput", 16)
        ot_tok[ob] = w.val("oput")



def _host_inputs(inputs):
    x = np.ascontiguousarray(np.asarray(inputs["L0_attn_a_x"], dtype=np.float32))
    g = lambda k: np.ascontiguousarray(np.asarray(inputs[k], dtype=np.float32))
    shared = {
        "w_aq": g("L0_attn_a_Wq"), "w_ak": g("L0_attn_a_Wk"),
        "w_av": g("L0_attn_a_Wv"), "w_ao": g("L0_attn_a_Wo"),
        "w_fr": np.ascontiguousarray(g("L0_ffn_W_r").T),
        "w_fk": np.ascontiguousarray(g("L0_ffn_W_k").T),
        "w_fv": np.ascontiguousarray(g("L0_ffn_W_v").T),
        "w_bq": np.ascontiguousarray(g("L0_attn_b_Wq").T),
        "w_bk": np.ascontiguousarray(g("L0_attn_b_Wk").T),
        "w_bv": np.ascontiguousarray(g("L0_attn_b_Wv").T),
        "w_bo": np.ascontiguousarray(g("L0_attn_b_Wo").T),
    }
    b_rows = np.stack([g("L0_attn_a_bq"), g("L0_attn_a_bk"), g("L0_attn_a_bv"),
                       g("L0_attn_b_bk"), g("L0_attn_b_bv")]).astype(np.float32)
    b_chunk = np.zeros((128, 4 * NCHUNK), np.float32)
    b_chunk[:, 0:8] = g("L0_attn_a_bo").reshape(NCHUNK, 128).T
    b_chunk[:, 8:16] = g("L0_attn_b_bq").reshape(NCHUNK, 128).T
    b_chunk[:, 16:24] = g("L0_attn_b_bo").reshape(NCHUNK, 128).T
    mu_chunk = np.zeros((128, 2 * NCHUNK), np.float32)
    mu_chunk[:, 0:8] = g("L0_ffn_mu_r").reshape(NCHUNK, 128).T
    mu_chunk[:, 8:16] = g("L0_ffn_mu_k").reshape(NCHUNK, 128).T
    shared["b_rows"] = np.ascontiguousarray(b_rows)
    shared["b_chunk"] = np.ascontiguousarray(b_chunk)
    shared["mu_chunk"] = np.ascontiguousarray(mu_chunk)
    shared["eyef"] = np.eye(128, dtype=np.float32)
    shared["eyeb"] = np.eye(128, dtype=np.float32).astype(ml_dtypes.bfloat16)
    in_maps = []
    for c in range(8):
        b, h = c // 2, c % 2
        m = dict(shared)
        m["xq"] = np.ascontiguousarray(x[b, 3968 * h: 3968 * h + TOK, :])
        m["xfull"] = np.ascontiguousarray(x[b])
        mk = np.zeros(TOK, np.float32)
        if h == 0:
            mk[0:4096] = 1.0
        else:
            mk[128:TOK] = 1.0
        m["maskv"] = np.ascontiguousarray(mk.reshape(NTT, 128).T)
        in_maps.append(m)
    return in_maps


def run_staged(inputs, stop_stage, cores=None):
    nc = build_program(stop_stage=stop_stage)
    in_maps = _host_inputs(inputs)
    cores = cores if cores is not None else list(range(8))
    res = run_bass_kernel_spmd(nc, [in_maps[c] for c in cores], core_ids=cores)
    return [r["dbg"] for r in res.results]


def kernel(**inputs):
    nc = build_program()
    in_maps = _host_inputs(inputs)
    res = run_bass_kernel_spmd(nc, in_maps, core_ids=list(range(8)))
    y = np.empty((B, S, E), np.float32)
    for c in range(8):
        b, h = c // 2, c % 2
        o = res.results[c]["out"]
        if h == 0:
            y[b, 0:4096] = o[0:4096]
        else:
            y[b, 4096:8192] = o[128:TOK]
    return y


# version 12
# speedup vs baseline: 2.1176x; 1.0220x over previous
import dataclasses
import numpy as np
import ml_dtypes

import concourse.bass as bass
import concourse.mybir as mybir
from concourse.bass_utils import run_bass_kernel_spmd

F32 = mybir.dt.float32
BF16 = mybir.dt.bfloat16
AF = mybir.ActivationFunctionType
ALU = mybir.AluOpType

E = 1024
H = 16
D = 64
S = 8192
B = 4
NCHUNK = 8
TOK = 4224
NTT = 33
NKG = 64
QGS = [512] * 8 + [128]
QG0 = [sum(QGS[:i]) for i in range(len(QGS))]
NQG = len(QGS)
EPS_LN = 1e-5
EPS_NRM = 1e-8
EPS_ATT = 1e-6
NCONST = -3.1 * S + EPS_ATT

SEM_NAMES = ["dx", "dk", "dq", "dw", "dst", "dsx", "qk", "rcp", "rv", "pe", "dve",
             "acs", "cc", "oput"]


def _act_raw(nc, eng, out, in_, func, bias=0.0, scale=1.0):
    bi = eng.activation(out=out, in_=in_, func=AF.Exp, bias=bias, scale=scale)
    i2 = dataclasses.replace(bi.ins, func=func)
    nc.register_instruction(i2, overwrite=True)
    insts = nc.cur_bb.bb.instructions
    assert insts[-1] is bi.ins
    insts[-1] = i2
    return bass.BassInstruction(i2)


class Walk:

    def __init__(self, nc, kind, eng, sems):
        self.nc = nc
        self.kind = kind
        self.eng = eng
        self.sems = sems
        self.c = {s: 0 for s in SEM_NAMES}

    def inc(self, ins, name, n=1):
        self.c[name] += n
        if ins is not None:
            ins.then_inc(self.sems[name], n)

    def wait(self, name, kind, val=None):
        if self.kind == kind:
            v = self.c[name] if val is None else val
            if v > 0:
                self.eng.wait_ge(self.sems[name], v)

    def val(self, name):
        return self.c[name]

    _OWNER = {"qk": 'pe', "rv": 'pe', "pe": 'pe', "rcp": 'act', "acs": 'act',
              "dve": 'dve', "dk": 'sync', "dq": 'sync', "dx": 'sync', "dsx": 'sync',
              "oput": 'sync', "dw": 'pool', "dst": 'pool', "cc": 'pool'}

    def barrier(self):
        for s in SEM_NAMES:
            own = self._OWNER[s]
            for k in ('pe', 'act', 'dve', 'sync', 'pool'):
                if k != own:
                    self.wait(s, k)

    def mm(self, out, lhsT, rhs, start=True, stop=True, tp=None):
        if self.kind != 'pe':
            return None
        return self.eng.matmul(out, lhsT=lhsT, rhs=rhs, start=start, stop=stop,
                               tile_position=tp)

    def trans(self, out, in_, ident):
        if self.kind != 'pe':
            return None
        return self.eng.transpose(out, in_, ident)

    def act(self, out, in_, func, bias=0.0, scale=1.0, accum_out=None):
        if self.kind != 'act':
            return None
        if func in (AF.Reciprocal, AF.Rsqrt):
            return _act_raw(self.nc, self.eng, out, in_, func, bias, scale)
        kw = {} if accum_out is None else {"accum_out": accum_out}
        return self.eng.activation(out=out, in_=in_, func=func, bias=bias,
                                   scale=scale, **kw)

    def _dve(self, f):
        if self.kind != 'dve':
            self.c["dve"] += 1
            return None
        self.eng.wait_ge(self.sems["dve"], self.c["dve"])
        ins = f()
        self.c["dve"] += 1
        ins.then_inc(self.sems["dve"], 1)
        return ins

    def ts(self, out, in0, s1, s2=None, op0=ALU.mult, op1=ALU.bypass):
        return self._dve(lambda: self.eng.tensor_scalar(
            out=out, in0=in0, scalar1=s1, scalar2=s2, op0=op0, op1=op1))

    def tt(self, out, in0, in1, op=ALU.add):
        return self._dve(lambda: self.eng.tensor_tensor(out=out, in0=in0, in1=in1, op=op))

    def stt(self, out, in0, scalar, in1, op0, op1):
        return self._dve(lambda: self.eng.scalar_tensor_tensor(
            out=out, in0=in0, scalar=scalar, in1=in1, op0=op0, op1=op1))

    def copy(self, out, in_):
        return self._dve(lambda: self.eng.tensor_copy(out=out, in_=in_))

    def memset(self, ap, v):
        return self._dve(lambda: self.eng.memset(ap, v))

    def red(self, out, in_, op=ALU.add):
        return self._dve(lambda: self.eng.tensor_reduce(out, in_, mybir.AxisListType.X, op))

    def vrecip_fast(self, out, in_):
        return self._dve(lambda: self.eng.reciprocal_approx_fast(out=out, in_=in_))

    def vrecip(self, out, in_):
        def f():
            if out.dtype == BF16:
                with self.nc.allow_low_precision(reason="bf16 store of fp32 reciprocal"):
                    return self.eng.reciprocal(out=out, in_=in_)
            return self.eng.reciprocal(out=out, in_=in_)
        return self._dve(f)

    def dma(self, queue, out, in_):
        if self.kind != queue:
            return None
        return self.eng.dma_start(out=out, in_=in_)

    def cc(self, ins_, outs_):
        if self.kind != 'pool':
            return None
        return self.eng.collective_compute(
            "AllReduce", ALU.add, replica_groups=[[0, 1], [2, 3], [4, 5], [6, 7]],
            ins=ins_, outs=outs_)


class L:
    pass


def build_program(stop_stage=None):
    nc = bass.Bass("TRN2", target_bir_lowering=False, debug=False)
    t = L()

    t.xq = nc.dram_tensor("xq", [TOK, E], F32, kind="ExternalInput").ap()
    t.xfull = nc.dram_tensor("xfull", [S, E], F32, kind="ExternalInput").ap()
    for nm in ["w_aq", "w_ak", "w_av", "w_ao", "w_fr", "w_fk", "w_fv",
               "w_bq", "w_bk", "w_bv", "w_bo"]:
        setattr(t, nm, nc.dram_tensor(nm, [E, E], F32, kind="ExternalInput").ap())
    t.b_rows = nc.dram_tensor("b_rows", [5, E], F32, kind="ExternalInput").ap()
    t.b_chunk = nc.dram_tensor("b_chunk", [128, 4 * NCHUNK], F32, kind="ExternalInput").ap()
    t.mu_chunk = nc.dram_tensor("mu_chunk", [128, 2 * NCHUNK], F32, kind="ExternalInput").ap()
    t.eyef = nc.dram_tensor("eyef", [128, 128], F32, kind="ExternalInput").ap()
    t.eyeb = nc.dram_tensor("eyeb", [128, 128], BF16, kind="ExternalInput").ap()
    t.maskv = nc.dram_tensor("maskv", [128, NTT], F32, kind="ExternalInput").ap()

    t.knT_hbm = nc.dram_tensor("knT_hbm", [NKG, 128, NCHUNK * 128], BF16, kind="Internal").ap()
    t.vk_hbm = nc.dram_tensor("vk_hbm", [NKG, 128, H * 65], BF16, kind="Internal").ap()
    t.qn_hbm = nc.dram_tensor("qn_hbm", [NCHUNK, 128, TOK], BF16, kind="Internal").ap()
    t.qb_hbm = nc.dram_tensor("qb_hbm", [NCHUNK, 128, TOK], BF16, kind="Internal").ap()
    t.kv_in = nc.dram_tensor("kv_in", [128, 1024], F32, kind="Internal").ap()
    t.kv_out = nc.dram_tensor("kv_out", [128, 1024], F32, kind="Internal").ap()
    t.x_hbm = nc.dram_tensor("x_hbm", [NCHUNK, 128, TOK], F32, kind="Internal").ap()
    t.out_h = nc.dram_tensor("out", [TOK, E], F32, kind="ExternalOutput").ap()
    t.dbg = None
    if stop_stage is not None:
        t.dbg = nc.dram_tensor("dbg", [128, NCHUNK * TOK], F32, kind="ExternalOutput").ap()

    import contextlib
    with contextlib.ExitStack() as st:
        def sb(name, shape, dt):
            return st.enter_context(nc.sbuf_tensor(name, shape, dt))

        def ps(name, shape):
            return st.enter_context(nc.psum_tensor(name, shape, F32))

        t.xg = sb("xg", [128, NCHUNK * 513], F32)
        t.xfst = sb("xfst", [128, 1024], F32)
        t.consts = sb("consts", [128, 16], F32)
        t.onesb = sb("onesb", [128, 576], BF16)
        t.eyef_sb = sb("eyef_sb", [128, 128], F32)
        t.eyeb_sb = sb("eyeb_sb", [128, 128], BF16)
        t.maskv_sb = sb("maskv_sb", [128, NTT], F32)
        t.brow_sb = sb("brow_sb", [1, 5 * E], BF16)
        t.brow_lo = sb("brow_lo", [1, 5 * E], BF16)
        t.bchunk_sb = sb("bchunk_sb", [128, 4 * NCHUNK], F32)
        t.mu_sb = sb("mu_sb", [128, 2 * NCHUNK], F32)
        t.ksum_bf = sb("ksum_bf", [128, NCHUNK], BF16)
        t.vsum_sb = sb("vsum_sb", [64, H], F32)
        t.kvblk_a = sb("kvblk_a", [128, NCHUNK * 128], BF16)
        t.kvblk_b = sb("kvblk_b", [128, NCHUNK * 128], BF16)
        t.kvacc_b = sb("kvacc_b", [128, 1024], F32)
        t.wbig = sb("wbig", [128, 3 * 8192], BF16)
        t.wcol = sb("wcol", [128, 2 * 1024], BF16)
        t.stage = sb("stage", [128, 2048 + H * 65], BF16)
        t.xtok = sb("xtok", [128, 2 * 1024], F32)
        t.scrf = sb("scrf", [128, 3 * 1024], F32)
        t.scrb = sb("scrb", [128, 4 * 1024], BF16)
        t.sml = sb("sml", [128, 96], F32)
        t.stash = sb("stash", [128, 16], F32)
        t.kvacc_a = sb("kvacc_a", [128, 1024], F32)
        t.kstream = sb("kstream", [128, 2 * 1024], BF16)
        t.vstream = sb("vstream", [128, 2 * 1040], BF16)
        t.qtile = sb("qtile", [128, 2 * 512], BF16)
        t.rbuf = sb("rbuf", [128, 2 * 1024], BF16)
        t.ut = sb("ut", [64, 2 * 512], F32)
        t.invn = sb("invn", [65, 1024], BF16)
        t.ofb = sb("ofb", [128, NCHUNK * 512], BF16)
        t.xn_sb = sb("xn_sb", [128, NCHUNK * 520], BF16)
        t.ffb = sb("ffb", [128, 2 * NCHUNK * 512], BF16)
        t.kkb = sb("kkb", [128, NCHUNK * 512], BF16)

        t.p_qk_a = ps("p_qk_a", [128, 1024])
        t.p_qk_b = ps("p_qk_b", [128, 1024])
        t.p_kv = ps("p_kv", [128, 1024])
        t.p_mis = ps("p_mis", [128, 1024])

        for i, v in enumerate([1.1, -1.0, 0.0, 1.0, EPS_LN, EPS_NRM, 4.41, -3.1,
                               1.0 / E, NCONST, 0.5, 2.0]):
            nc.const_aps.register(F32, v, t.consts[:, i:i + 1])

        sems = {s: st.enter_context(nc.semaphore(s)) for s in SEM_NAMES}

        with nc.Block() as block:
            @block.sync
            def _(eng):
                _emit(Walk(nc, 'sync', eng, sems), t, stop_stage)

            @block.tensor
            def _(eng):
                _emit(Walk(nc, 'pe', eng, sems), t, stop_stage)

            @block.scalar
            def _(eng):
                _emit(Walk(nc, 'act', eng, sems), t, stop_stage)

            @block.vector
            def _(eng):
                _emit(Walk(nc, 'dve', eng, sems), t, stop_stage)

            @block.gpsimd
            def _(eng):
                _emit(Walk(nc, 'pool', eng, sems), t, stop_stage)

    return nc



def _emit(w: Walk, t, stop_stage):
    _setup(w, t)
    if stop_stage == 'p0':
        _prep(w, t, stop_p0=True)
        _fin(w, t); return
    _prep(w, t)
    w.barrier()
    if stop_stage == 'prep':
        _dump(w, t); _fin(w, t); return
    if stop_stage == 'a0':
        _attn_a(w, t, 0)
        _dump_a0(w, t); _fin(w, t); return
    t.ot_tok = [0, 0]
    _load3(w, t, [t.w_fr, t.w_fk, t.w_fv])
    for qg in range(NQG):
        _attn_a(w, t, qg)
        w.barrier()
        _ffn(w, t, qg, resident=True)
        w.barrier()
    if stop_stage == 'x2':
        _dump(w, t); _fin(w, t); return
    _load3(w, t, [t.w_bq, t.w_bk, t.w_bv])
    for qg in range(NQG):
        _attnb_pre(w, t, qg)
        w.barrier()
        if stop_stage == 'b0' and qg == 0:
            w.wait("dve", 'pool')
            w.wait("dst", 'pool')
            d = t.dbg
            w.inc(w.dma('pool', d[:, 0:1024], t.kvacc_b[:, :]), "oput", 16)
            w.inc(w.dma('pool', d[:, 1024:2048], t.kvblk_b[:, :]), "oput", 16)
            w.inc(w.dma('pool', d[:, 2048:2560], t.qb_hbm[0, :, 0:512]), "oput", 16)
            w.inc(w.dma('pool', d[:, 2560:3584], t.scrb[:, 0:1024]), "oput", 16)
            w.inc(w.dma('pool', d[:, 3584:4608], t.scrb[:, 1024:2048]), "oput", 16)
            w.inc(w.dma('pool', d[:, 4608:5128], t.xn_sb[:, 0:520]), "oput", 16)
            _fin(w, t); return
    w.barrier()
    _attnb_reduce(w, t)
    w.barrier()
    if stop_stage == 'b1':
        w.wait("dve", 'pool')
        d = t.dbg
        w.inc(w.dma('pool', d[:, 0:1024], t.kvacc_b[:, :]), "oput", 16)
        w.inc(w.dma('pool', d[:, 1024:2048], t.kvblk_b[:, :]), "oput", 16)
        _fin(w, t); return
    _load3(w, t, [t.w_fr, t.w_fk, t.w_fv])
    for qg in range(NQG):
        _attnb_out(w, t, qg)
        w.barrier()
        _ffn(w, t, qg, resident=True, out_fuse=True)
        w.barrier()
    _fin(w, t)


def _dump(w, t):
    w.wait("dve", 'sync')
    w.wait("pe", 'sync')
    w.wait("dst", 'sync')
    w.wait("dsx", 'sync')
    for a in range(NCHUNK):
        w.inc(w.dma('sync', t.dbg[:, a * TOK:(a + 1) * TOK], t.x_hbm[a]), "oput", 16)


def _dump_a0(w, t):
    w.wait("dve", 'pool')
    w.wait("pe", 'pool')
    w.wait("acs", 'pool')
    w.wait("dst", 'pool')
    d = t.dbg
    w.inc(w.dma('pool', d[:, 0:4096], t.ofb[:, :]), "oput", 16)
    w.inc(w.dma('pool', d[0:65, 4096:5120], t.invn[:, :]), "oput", 16)
    w.inc(w.dma('pool', d[0:64, 5120:6144], t.ut[:, :]), "oput", 16)
    w.inc(w.dma('pool', d[64:65, 6144:7168], t.scrf[64:65, 0:1024]), "oput", 16)
    w.inc(w.dma('pool', d[:, 7168:7680], t.qn_hbm[0, :, 0:512]), "oput", 16)
    w.inc(w.dma('pool', d[:, 7680:7808], t.knT_hbm[0, :, 0:128]), "oput", 16)
    w.inc(w.dma('pool', d[:, 7808:7938], t.vk_hbm[0, :, 0:130]), "oput", 16)
    w.inc(w.dma('pool', d[:, 7938:7946], t.ksum_bf[:, :]), "oput", 16)
    w.inc(w.dma('pool', d[0:64, 7946:7962], t.vsum_sb[:, :]), "oput", 16)
    w.inc(w.dma('pool', d[:, 8000:9024], t.kvblk_a[:, :]), "oput", 16)
    w.inc(w.dma('pool', d[:, 9024:9056], t.bchunk_sb[:, :]), "oput", 16)
    w.copy(t.scrf[64:65, 1024:2048], t.p_kv[64:65, 0:1024])
    w.wait("dve", 'pool')
    w.inc(w.dma('pool', d[64:65, 9056:10080], t.scrf[64:65, 1024:2048]), "oput", 16)
    w.inc(w.dma('pool', d[:, 10080:12128], t.rbuf[:, :]), "oput", 16)


def _fin(w, t):
    if w.kind == 'pool':
        w.eng.wait_ge(w.sems["oput"], w.c["oput"])


def _setup(w, t):
    for i, v in enumerate([1.1, -1.0, 0.0, 1.0, EPS_LN, EPS_NRM, 4.41, -3.1,
                           1.0 / E, NCONST, 0.5, 2.0]):
        w.memset(t.consts[:, i:i + 1], v)
    w.memset(t.onesb[:, :], 1.0)
    w.memset(t.kvacc_b[:, :], 0.0)
    w.memset(t.kvacc_a[:, :], 0.0)
    w.memset(t.sml[:, 64:88], 0.0)
    w.memset(t.kvblk_a[:, :], 0.0)
    w.memset(t.kvblk_b[:, :], 0.0)
    w.inc(w.dma('sync', t.eyef_sb[:, :], t.eyef), "dx", 16)
    w.inc(w.dma('sync', t.eyeb_sb[:, :], t.eyeb), "dx", 16)
    w.inc(w.dma('sync', t.maskv_sb[:, :], t.maskv), "dx", 16)
    w.inc(w.dma('sync', t.bchunk_sb[:, :], t.b_chunk), "dx", 16)
    w.inc(w.dma('sync', t.mu_sb[:, :], t.mu_chunk), "dx", 16)
    for r_ in range(5):
        w.wait("dve", 'pool')
        w.inc(w.dma('pool', t.scrf[0:1, 0:E], t.b_rows[r_:r_ + 1, :]), "dw", 16)
        w.wait("dw", 'dve')
        w.ts(t.brow_sb[0:1, r_ * E:(r_ + 1) * E], t.scrf[0:1, 0:E], 1.0)
        w.ts(t.scrf[0:1, E:2 * E], t.brow_sb[0:1, r_ * E:(r_ + 1) * E], -1.0)
        w.tt(t.brow_lo[0:1, r_ * E:(r_ + 1) * E], t.scrf[0:1, E:2 * E],
             t.scrf[0:1, 0:E], op=ALU.add)


def _load3(w, t, srcs):
    w.wait("pe", 'pool')
    for i, src in enumerate(srcs):
        w.inc(w.dma('pool', t.wbig[:, i * 8192:(i + 1) * 8192].rearrange("p (a d) -> p a d", a=8),
                    src.rearrange("(a p) d -> p a d", p=128)), "dw", 16)


def _ln_tok(w, t, xt, xn_out):
    m = t.sml[:, 0:1]
    v = t.sml[:, 1:2]
    r = t.sml[:, 2:3]
    sq = t.scrf[:, E:2 * E]
    w.red(m, xt)
    w.ts(m, m, -1.0 / E)
    w.wait("dve", 'act')
    w.wait("acs", 'act')
    w.inc(w.act(sq, xt, AF.Square, bias=m, accum_out=v), "acs", 1)
    w.wait("acs", 'act')
    w.inc(w.act(r, v, AF.Rsqrt, bias=EPS_LN, scale=1.0 / E), "acs", 1)
    w.wait("acs", 'dve')
    w.ts(xn_out, xt, m, r, op0=ALU.add, op1=ALU.mult)


def _norm_heads(w, t, src_f32, dst_bf):
    sq = t.scrf[:, E:2 * E]
    ss = t.sml[:, 16:32]
    src3 = src_f32.rearrange("p (h d) -> p h d", h=H)
    w.wait("dve", 'act')
    w.wait("acs", 'act')
    w.inc(w.act(sq, src_f32, AF.Square), "acs", 1)
    w.wait("acs", 'dve')
    w.red(ss, sq.rearrange("p (h d) -> p h d", h=H))
    w.wait("dve", 'act')
    w.inc(w.act(t.sml[:, 32:48], ss, AF.Rsqrt), "acs", 1)
    w.wait("acs", 'dve')
    w.tt(t.sml[:, 32:48], t.sml[:, 32:48], ss, op=ALU.mult)
    w.ts(t.sml[:, 32:48], t.sml[:, 32:48], EPS_NRM, op0=ALU.add)
    w.vrecip(ss, t.sml[:, 32:48])
    w.tt(dst_bf.rearrange("p (h d) -> p h d", h=H), src3,
         ss.rearrange("p (h o) -> p h o", o=1).broadcast_to([128, H, D]),
         op=ALU.mult)


def _transpose8(w, t, src_bf, dst_bf, pt, dve_val=None):
    w.wait("dve", 'pe', val=(dve_val if dve_val is not None else w.val("dve")))
    i = None
    for a in range(NCHUNK):
        i = w.trans(pt[:, a * 64:(a + 1) * 64].bitcast(BF16),
                    src_bf[:, a * 128:(a + 1) * 128], t.eyeb_sb[:, :])
    w.inc(i, "pe", 1)
    w.wait("pe", 'dve')
    w.copy(dst_bf[:, 0:1024], pt[:, 0:512].bitcast(BF16))


def _proj_tok(w, t, xnT, wbase, brow_idx, pdst):
    last = None
    for half in range(2):
        po = pdst[:, half * 512:(half + 1) * 512]
        for a in range(NCHUNK):
            w.mm(po, xnT[:, a * 128:(a + 1) * 128],
                 t.wbig[:, wbase + a * 1024 + half * 512: wbase + a * 1024 + (half + 1) * 512],
                 start=(a == 0), stop=False)
        w.mm(po, t.onesb[0:1, 0:128],
             t.brow_sb[0:1, brow_idx * E + half * 512: brow_idx * E + (half + 1) * 512],
             start=False, stop=False)
        last = w.mm(po, t.onesb[0:1, 0:128],
                    t.brow_lo[0:1, brow_idx * E + half * 512: brow_idx * E + (half + 1) * 512],
                    start=False, stop=True)
    w.inc(last, "pe", 1)


def _prep(w, t, stop_p0=False):
    _load3(w, t, [t.w_ak, t.w_av, t.w_aq])
    w.wait("dw", 'pe')
    w.wait("dx", 'dve')
    w.wait("dve", 'pe')
    ONES_COL = t.onesb[:, 0:1]

    xtok_tok = [0, 0]
    xtok_pe = [0, 0]
    tile_pe = [0, 0]
    flush_pe = None
    st_tok = {"kn": 0, "vk": 0, "qn": 0}
    for tc in range(NKG + NTT):
        full = tc < NKG
        buf = tc % 2
        xt = t.xtok[:, buf * 1024:(buf + 1) * 1024]
        src = (t.xfull[tc * 128:(tc + 1) * 128, :] if full
               else t.xq[(tc - NKG) * 128:(tc - NKG + 1) * 128, :])
        w.wait("dve", 'sync', val=xtok_tok[buf])
        w.wait("pe", 'sync', val=xtok_pe[buf])
        w.inc(w.dma('sync', xt, src), "dx", 16)
        w.wait("dx", 'dve')

        nbase = t.scrb if tc % 2 == 0 else t.ffb
        xnb = nbase[:, 0:1024]
        w.wait("pe", 'dve', val=tile_pe[tc % 2])
        _ln_tok(w, t, xt, xnb)
        ln_dve = w.val("dve")
        if flush_pe is not None:
            w.wait("pe", 'dve', val=flush_pe)
            w.tt(t.kvacc_a[:, :], t.kvacc_a[:, :], t.p_kv[:, :], op=ALU.add)
            w.tt(t.sml[:, 64:72], t.sml[:, 64:72], t.p_qk_b[:, 0:8], op=ALU.add)
            w.tt(t.sml[0:64, 72:88], t.sml[0:64, 72:88], t.p_qk_b[0:64, 16:32], op=ALU.add)
            flush_pe = None

        if not full:
            tt_ = tc - NKG
            w.wait("dve", 'pe')
            i = None
            for a in range(NCHUNK):
                i = w.trans(t.p_mis[:, a * 128:(a + 1) * 128], xt[:, a * 128:(a + 1) * 128],
                            t.eyef_sb[:, :])
            w.inc(i, "pe", 1)
            xtok_pe[buf] = w.val("pe")
            w.wait("pe", 'dve')
            w.wait("dst", 'dve', val=st_tok.get("xf", 0))
            w.copy(t.xfst[:, 0:1024], t.p_mis[:, 0:1024])
            w.wait("dve", 'pool')
            w.inc(w.dma('pool',
                        t.x_hbm[:, :, tt_ * 128:(tt_ + 1) * 128].rearrange("a p c -> p a c"),
                        t.xfst.rearrange("p (a c) -> p a c", a=8)), "dst", 16)
            st_tok["xf"] = w.val("dst")
            ln_dve = w.val("dve")
        xtok_tok[buf] = w.val("dve")

        xnT = nbase[:, 1024:2048]
        _transpose8(w, t, xnb, xnT, t.p_qk_a, dve_val=ln_dve)
        w.wait("dve", 'pe')

        if full:
            _proj_tok(w, t, xnT, 0, 1, t.p_mis)
            w.wait("pe", 'dve')
            w.wait("pe", 'act')
            kn_tok = t.scrb[:, 2048:3072]
            _norm_heads(w, t, t.p_mis[:, 0:1024], kn_tok)
            if stop_p0:
                w.wait("dve", 'pool')
                w.wait("acs", 'pool')
                d = t.dbg
                w.inc(w.dma('pool', d[:, 0:1024], t.scrb[:, 0:1024]), "oput", 16)
                w.inc(w.dma('pool', d[:, 1024:2048], t.scrb[:, 1024:2048]), "oput", 16)
                w.inc(w.dma('pool', d[:, 2048:3072], t.p_mis[:, 0:1024]), "oput", 16)
                w.inc(w.dma('pool', d[:, 3072:4096], kn_tok), "oput", 16)
                w.inc(w.dma('pool', d[:, 4096:4160], t.sml[:, :]), "oput", 16)
                w.inc(w.dma('pool', d[:, 4160:4288], t.xtok[:, 0:128]), "oput", 16)
                w.inc(w.dma('pool', d[:, 4288:6336], t.wbig[:, 0:2048]), "oput", 16)
                w.inc(w.dma('pool', d[0:1, 6336:11456], t.brow_sb[:, :]), "oput", 16)
                w.inc(w.dma('pool', d[0:1, 11456:16576], t.brow_lo[:, :]), "oput", 16)
                return
            w.wait("dve", 'pe')
            _proj_tok(w, t, xnT, 8192, 2, t.p_mis)
            w.wait("pe", 'dve')
            w.wait("pe", 'act')
            v_tok = t.scrb[:, 3072:4096]
            w.wait("acs", 'act')
            w.inc(w.act(v_tok, t.p_mis[:, 0:1024], AF.Copy), "acs", 1)
            w.wait("acs", 'dve')
            v_lo = nbase[:, 0:1024]
            w.ts(t.scrf[:, 0:1024], v_tok, -1.0)
            w.tt(v_lo, t.scrf[:, 0:1024], t.p_mis[:, 0:1024], op=ALU.add)
            w.wait("dst", 'dve', val=st_tok["vk"])
            vkst = t.stage[:, 2048:2048 + H * 65]
            vk3 = vkst.rearrange("p (h e) -> p h e", h=H)
            if tc == 0:
                w.memset(vk3[:, :, 64:65], 1.0)
            w.copy(vk3[:, :, 0:64], v_tok.rearrange("p (h d) -> p h d", h=H))
            w.wait("dst", 'dve', val=st_tok["kn"])
            knst = t.stage[:, 1024:2048]
            _transpose8(w, t, kn_tok, knst, t.p_qk_a)
            w.wait("dve", 'pe')
            w.wait("acs", 'pe')
            for a in range(NCHUNK):
                w.mm(t.p_kv[:, a * 128:(a + 1) * 128], kn_tok[:, a * 128:(a + 1) * 128],
                     v_tok[:, a * 128:(a + 1) * 128], start=True, stop=True)
            for a in range(NCHUNK):
                w.mm(t.p_qk_b[:, a:a + 1], kn_tok[:, a * 128:(a + 1) * 128], ONES_COL,
                     start=True, stop=True)
            i = None
            for hh in range(H):
                w.mm(t.p_qk_b[0:64, 16 + hh:17 + hh], v_tok[:, hh * 64:(hh + 1) * 64],
                     ONES_COL, start=True, stop=False)
                i = w.mm(t.p_qk_b[0:64, 16 + hh:17 + hh], v_lo[:, hh * 64:(hh + 1) * 64],
                         ONES_COL, start=False, stop=True)
            w.inc(i, "pe", 1)
            flush_pe = w.val("pe")
            w.wait("dve", 'pool')
            w.wait("pe", 'pool', val=0)
            w.inc(w.dma('pool', t.knT_hbm[tc], knst), "dst", 16)
            st_tok["kn"] = w.val("dst")
            w.inc(w.dma('pool', t.vk_hbm[tc], vkst), "dst", 16)
            st_tok["vk"] = w.val("dst")
        else:
            _proj_tok(w, t, xnT, 16384, 0, t.p_mis)
            w.wait("pe", 'dve')
            w.wait("pe", 'act')
            qn_tok = t.scrb[:, 2048:3072]
            _norm_heads(w, t, t.p_mis[:, 0:1024], qn_tok)
            w.wait("dst", 'dve', val=st_tok["qn"])
            qnst = t.stage[:, 0:1024]
            _transpose8(w, t, qn_tok, qnst, t.p_qk_a)
            tt_ = tc - NKG
            w.wait("dve", 'pool')
            w.inc(w.dma('pool',
                        t.qn_hbm[:, :, tt_ * 128:(tt_ + 1) * 128].rearrange("a p c -> p a c"),
                        qnst.rearrange("p (a c) -> p a c", a=8)), "dst", 16)
            st_tok["qn"] = w.val("dst")
        tile_pe[tc % 2] = w.val("pe")

    if flush_pe is not None:
        w.wait("pe", 'dve', val=flush_pe)
        w.tt(t.kvacc_a[:, :], t.kvacc_a[:, :], t.p_kv[:, :], op=ALU.add)
        w.tt(t.sml[:, 64:72], t.sml[:, 64:72], t.p_qk_b[:, 0:8], op=ALU.add)
        w.tt(t.sml[0:64, 72:88], t.sml[0:64, 72:88], t.p_qk_b[0:64, 16:32], op=ALU.add)
        flush_pe = None

    w.wait("pe", 'dve')
    kb3 = t.kvblk_a.rearrange("p (a c) -> p a c", a=NCHUNK)
    ka3 = t.kvacc_a.rearrange("p (a c) -> p a c", a=NCHUNK)
    w.copy(kb3[0:64, :, 0:64], ka3[0:64, :, 0:64])
    w.copy(kb3[64:128, :, 64:128], ka3[64:128, :, 64:128])
    w.copy(t.ksum_bf[:, :], t.sml[:, 64:64 + NCHUNK])
    w.copy(t.vsum_sb[:, :], t.sml[0:64, 72:72 + H])
    w.ts(t.vsum_sb[:, :], t.vsum_sb[:, :], 3.1)


def _rb_slot(t, kg):
    s = kg % 4
    if s < 2:
        return t.rbuf[:, s * 1024:(s + 1) * 1024]
    return t.scrb[:, s * 1024:(s + 1) * 1024]


def _emit_rv(w, t, qg, kg, nq):
    if kg < 0:
        return
    oo = kg // 8
    kk = kg % 8
    ob = oo % 2
    vk_e = t.vstream[:, ob * 1040 + kk * 130: ob * 1040 + kk * 130 + 65]
    vk_o = t.vstream[:, ob * 1040 + kk * 130 + 65: ob * 1040 + kk * 130 + 130]
    rb_ = _rb_slot(t, kg)
    rs_, rv_ = t.rcp_of[kg]
    w.wait(rs_, 'pe', val=rv_)
    w.mm(t.p_kv[0:65, 0:nq], vk_e, rb_[:, 0:nq], start=(kg == 0), stop=(kg == NKG - 1))
    i = w.mm(t.p_kv[0:65, 512:512 + nq], vk_o, rb_[:, 512:512 + nq],
             start=(kg == 0), stop=(kg == NKG - 1))
    w.inc(i, "rv", 1)


def _attn_a(w, t, qg):
    nq = QGS[qg]
    q0 = QG0[qg]
    t.rcp_of = {}
    t.rhist = []
    w.wait("dst", 'sync')
    for p in range(NCHUNK):
        qb = p % 2
        qt = t.qtile[:, qb * 512:qb * 512 + nq]
        w.wait("pe", 'sync', val=w.val("pe"))
        w.wait("qk", 'sync', val=w.val("qk"))
        w.inc(w.dma('sync', qt, t.qn_hbm[p, :, q0:q0 + nq]), "dq", 16)
        w.wait("dve", 'pe')
        dk_tok = {}
        for oct_ in range(10):
            if oct_ >= 2:
                oo = oct_ - 2
                ob = oo % 2
                for kk in range(8):
                    kg = oo * 8 + kk
                    pb = kg % 2
                    pqk = (t.p_qk_a, t.p_qk_b, t.p_mis)[kg % 3]
                    kn_e = t.kstream[0:64, ob * 1024 + kk * 128: ob * 1024 + (kk + 1) * 128]
                    kn_o = t.kstream[64:128, ob * 1024 + kk * 128: ob * 1024 + (kk + 1) * 128]
                    if kk == 0:
                        w.wait("dk", 'pe', val=dk_tok[oo])
                        if oo == 0:
                            w.wait("dq", 'pe')
                    w.wait("rcp", 'pe', val=w.val("rcp") - 2)
                    w.mm(pqk[:, 0:nq], kn_e, qt[0:64, :], tp=(0, 0))
                    i = w.mm(pqk[:, 512:512 + nq], kn_o, qt[64:128, :], tp=(64, 0))
                    w.inc(i, "qk", 1)
                    _emit_rv(w, t, qg, kg - 2, nq)
                    rb_ = _rb_slot(t, kg)
                    w.wait("qk", 'act')
                    w.wait("rv", 'act', val=w.val("rv") - 2)
                    if nq == 512:
                        i = w.act(rb_[:, 0:1024], pqk[:, 0:1024], AF.Reciprocal,
                                  bias=1.1, scale=-1.0)
                    else:
                        w.act(rb_[:, 0:nq], pqk[:, 0:nq], AF.Reciprocal, bias=1.1, scale=-1.0)
                        i = w.act(rb_[:, 512:512 + nq], pqk[:, 512:512 + nq], AF.Reciprocal,
                                  bias=1.1, scale=-1.0)
                    w.inc(i, "rcp", 1)
                    t.rcp_of[kg] = ("rcp", w.val("rcp"))
            if oct_ < 8:
                ob2 = oct_ % 2
                w.wait("qk", 'sync', val=w.val("qk"))
                w.wait("rv", 'sync', val=w.val("rv") + (2 if oct_ >= 2 else 0))
                w.inc(w.dma('sync',
                            t.kstream[:, ob2 * 1024:(ob2 + 1) * 1024]
                            .rearrange("p (k c) -> p k c", k=8),
                            t.knT_hbm[oct_ * 8:(oct_ + 1) * 8, :, p * 128:(p + 1) * 128]
                            .rearrange("k p c -> p k c")),
                      "dk", 16)
                w.inc(w.dma('sync',
                            t.vstream[:, ob2 * 1040:(ob2 + 1) * 1040]
                            .rearrange("p (k c) -> p k c", k=8),
                            t.vk_hbm[oct_ * 8:(oct_ + 1) * 8, :, 2 * p * 65:(2 * p + 2) * 65]
                            .rearrange("k p c -> p k c")),
                      "dk", 16)
                dk_tok[oct_] = w.val("dk")
        _emit_rv(w, t, qg, NKG - 2, nq)
        _emit_rv(w, t, qg, NKG - 1, nq)
        w.wait("rv", 'dve')
        ue = t.ut[:, 0:nq]
        uo = t.ut[:, 512:512 + nq]
        w.copy(ue, t.p_kv[0:64, 0:nq])
        w.copy(uo, t.p_kv[0:64, 512:512 + nq])
        w.wait("rv", 'pe')
        w.mm(t.p_mis[64:65, 0:nq], t.ksum_bf[0:64, p:p + 1], qt[0:64, :], tp=(0, 64))
        i = w.mm(t.p_mis[64:65, 512:512 + nq], t.ksum_bf[64:128, p:p + 1],
                 qt[64:128, :], tp=(64, 64))
        w.inc(i, "pe", 1)
        w.wait("pe", 'dve')
        nr_e = t.scrf[64:65, 0:nq]
        nr_o = t.scrf[64:65, 512:512 + nq]
        w.copy(nr_e, t.p_kv[64:65, 0:nq])
        w.copy(nr_o, t.p_kv[64:65, 512:512 + nq])
        w.stt(nr_e, nr_e, 4.41, t.p_mis[64:65, 0:nq], ALU.mult, ALU.subtract)
        w.stt(nr_o, nr_o, 4.41, t.p_mis[64:65, 512:512 + nq], ALU.mult, ALU.subtract)
        w.ts(nr_e, nr_e, NCONST, op0=ALU.add)
        w.ts(nr_o, nr_o, NCONST, op0=ALU.add)
        w.wait("dve", 'act')
        w.wait("pe", 'act')
        w.act(t.invn[64:65, 0:nq], nr_e, AF.Reciprocal)
        w.inc(w.act(t.invn[64:65, 512:512 + nq], nr_o, AF.Reciprocal), "acs", 1)
        w.wait("acs", 'pe')
        for half in range(2):
            co = half * 512
            usrc = ue if half == 0 else uo
            w.mm(t.p_mis[0:64, 0:nq], t.onesb[64:65, 0:64], t.invn[64:65, co:co + nq])
            if half == 0:
                i = w.mm(t.p_mis[0:64, 512:512 + nq], t.kvblk_a[0:64, p * 128:p * 128 + 64],
                         qt[0:64, :], tp=(0, 0))
            else:
                i = w.mm(t.p_mis[0:64, 512:512 + nq],
                         t.kvblk_a[64:128, p * 128 + 64:p * 128 + 128],
                         qt[64:128, :], tp=(64, 0))
            w.inc(i, "pe", 1)
            w.wait("pe", 'dve')
            e1 = t.scrf[0:64, 1024:1024 + nq]
            w.stt(e1, usrc, 4.41, t.p_mis[0:64, 512:512 + nq], ALU.mult, ALU.subtract)
            if half == 1:
                w.wait("dst", 'dve')
            dst = t.ofb[0:64, p * 512:p * 512 + nq] if half == 0 else t.scrb[0:64, 0:nq]
            w.stt(dst, e1, t.vsum_sb[:, 2 * p + half:2 * p + half + 1],
                        t.p_mis[0:64, 0:nq], ALU.subtract, ALU.mult)
            if half == 1:
                w.wait("dve", 'pool')
                w.inc(w.dma('pool', t.ofb[64:128, p * 512:p * 512 + nq], t.scrb[0:64, 0:nq]),
                      "dst", 16)
            else:
                w.wait("dve", 'pe')
    w.wait("dst", 'pe')
    w.wait("dve", 'sync')
    xgv = t.xg.rearrange("p (a c) -> p a c", c=513)[:, :, 1:1 + nq]
    w.inc(w.dma('sync', xgv, t.x_hbm[:, :, q0:q0 + nq].rearrange("a p c -> p a c")), "dx", 16)
    w.wait("dx", 'dve')
    wc_tok = [None, None]
    for fc in range(NCHUNK):
        wb = fc % 2
        w.wait("pe", 'pool', val=(wc_tok[wb] if wc_tok[wb] is not None else w.val("pe")))
        w.inc(w.dma('pool', t.wcol[:, wb * 1024:(wb + 1) * 1024].rearrange("p (a c) -> p a c", a=8),
                    t.w_ao[:, fc * 128:(fc + 1) * 128].rearrange("(a p) c -> p a c", p=128)),
              "dw", 16)
        w.wait("dw", 'pe')
        w.wait("dve", 'pe')
        i = None
        for a in range(NCHUNK):
            i = w.mm(t.p_mis[:, 0:nq], t.wcol[:, wb * 1024 + a * 128:wb * 1024 + (a + 1) * 128],
                     t.ofb[:, a * 512:a * 512 + nq], start=(a == 0), stop=(a == NCHUNK - 1))
        w.inc(i, "pe", 1)
        wc_tok[wb] = w.val("pe")
        w.wait("pe", 'dve')
        xc = t.xg[:, fc * 513 + 1: fc * 513 + 1 + nq]
        w.ts(t.scrf[:, 0:nq], t.p_mis[:, 0:nq], t.bchunk_sb[:, fc:fc + 1], op0=ALU.add)
        w.tt(xc, xc, t.scrf[:, 0:nq], op=ALU.add)
    sl_ = (qg % 2) * 8
    w.copy(t.stash[:, sl_:sl_ + 8].rearrange("p (a one) -> p a one", one=1),
           t.xg.rearrange("p (a c) -> p a c", c=513)[:, :, nq:nq + 1])
    w.wait("dve", 'pe')


def _ln_fm(w, t, qg, resident=False):
    nq = QGS[qg]
    q0 = QG0[qg]
    nb = nq + 1
    bnd = qg > 0
    ONES_COL = t.onesb[:, 0:1]
    if resident:
        if bnd:
            sl_ = ((qg + 1) % 2) * 8
            w.copy(t.xg.rearrange("p (a c) -> p a c", c=513)[:, :, 0:1],
                   t.stash[:, sl_:sl_ + 8].rearrange("p (a one) -> p a one", one=1))
    else:
        w.wait("dve", 'sync')
        w.wait("pe", 'sync')
        if not bnd:
            xgv = t.xg.rearrange("p (a c) -> p a c", c=513)[:, :, 1:1 + nq]
            w.inc(w.dma('sync', xgv, t.x_hbm[:, :, 0:nq].rearrange("a p c -> p a c")), "dx", 16)
        else:
            xgv = t.xg.rearrange("p (a c) -> p a c", c=513)[:, :, 0:nb]
            w.inc(w.dma('sync', xgv, t.x_hbm[:, :, q0 - 1:q0 + nq].rearrange("a p c -> p a c")),
                  "dx", 16)
        w.wait("dx", 'dve')
    cast_tok = [None, None]
    for a in range(NCHUNK):
        sl = a % 2
        xb = t.scrb[:, sl * 1024:sl * 1024 + nb]
        x2 = t.scrb[:, 2048 + sl * 1024:2048 + sl * 1024 + nb]
        w.wait("pe", 'dve', val=(cast_tok[sl] if cast_tok[sl] is not None else w.val("pe")))
        if bnd:
            xsrc = t.xg[:, a * 513:a * 513 + nb]
            w.ts(xb, xsrc, 1.0)
            w.tt(x2, xsrc, xsrc, op=ALU.mult)
        else:
            xsrc = t.xg[:, a * 513 + 1:a * 513 + 1 + nq]
            w.ts(xb[:, 1:nb], xsrc, 1.0)
            w.tt(x2[:, 1:nb], xsrc, xsrc, op=ALU.mult)
        w.wait("dve", 'pe')
        w.mm(t.p_qk_a[0:1, 0:nq], ONES_COL, xb[:, 1:nb], start=(a == 0), stop=(a == NCHUNK - 1))
        i = w.mm(t.p_qk_b[0:1, 0:nq], ONES_COL, x2[:, 1:nb], start=(a == 0), stop=(a == NCHUNK - 1))
        if bnd:
            w.mm(t.p_qk_a[0:1, 512:513], ONES_COL, xb[:, 0:1],
                 start=(a == 0), stop=(a == NCHUNK - 1))
            i = w.mm(t.p_qk_b[0:1, 512:513], ONES_COL, x2[:, 0:1],
                     start=(a == 0), stop=(a == NCHUNK - 1))
        w.inc(i, "pe", 1)
        cast_tok[sl] = w.val("pe")
    w.wait("pe", 'dve')
    ncols = 513 if bnd else nq
    mrow = t.scrf[0:1, 0:ncols]
    vrow = t.scrf[0:1, 1024:1024 + ncols]
    w.ts(mrow, t.p_qk_a[0:1, 0:ncols], 1.0 / E)
    w.ts(vrow, t.p_qk_b[0:1, 0:ncols], 1.0 / E)
    w.tt(t.scrf[0:1, 2048:2048 + ncols], mrow, mrow, op=ALU.mult)
    w.tt(vrow, vrow, t.scrf[0:1, 2048:2048 + ncols], op=ALU.subtract)
    w.wait("dve", 'act')
    w.inc(w.act(t.scrf[0:1, 2048:2048 + ncols], vrow, AF.Rsqrt, bias=EPS_LN), "acs", 1)
    w.wait("acs", 'dve')
    w.ts(t.scrb[0:1, 0:ncols], mrow, 1.0)
    w.ts(t.scrb[0:1, 1024:1024 + ncols], t.scrf[0:1, 2048:2048 + ncols], 1.0)
    w.wait("dve", 'pe')
    w.mm(t.p_qk_b[:, 0:nq], t.onesb[0:1, 0:128], t.scrb[0:1, 0:nq])
    i = w.mm(t.p_qk_a[:, 0:nq], t.onesb[0:1, 0:128], t.scrb[0:1, 1024:1024 + nq])
    if bnd:
        w.mm(t.p_qk_b[:, 512:513], t.onesb[0:1, 0:128], t.scrb[0:1, 512:513])
        i = w.mm(t.p_qk_a[:, 512:513], t.onesb[0:1, 0:128], t.scrb[0:1, 1024 + 512:1024 + 513])
    w.inc(i, "pe", 1)
    w.wait("pe", 'dve')
    for a in range(NCHUNK):
        w.tt(t.scrf[:, 0:nq], t.xg[:, a * 513 + 1:a * 513 + 1 + nq],
             t.p_qk_b[:, 0:nq], op=ALU.subtract)
        w.tt(t.xn_sb[:, a * 520 + 1:a * 520 + 1 + nq], t.scrf[:, 0:nq],
             t.p_qk_a[:, 0:nq], op=ALU.mult)
        if bnd:
            w.tt(t.scrf[:, 512:513], t.xg[:, a * 513:a * 513 + 1],
                 t.p_qk_b[:, 512:513], op=ALU.subtract)
            w.tt(t.xn_sb[:, a * 520:a * 520 + 1], t.scrf[:, 512:513],
                 t.p_qk_a[:, 512:513], op=ALU.mult)
        else:
            w.memset(t.xn_sb[:, a * 520:a * 520 + 1], 0.0)


def _ffn(w, t, qg, resident=False, out_fuse=False):
    nq = QGS[qg]
    q0 = QG0[qg]
    _ln_fm(w, t, qg, resident=resident)
    i = None
    for a in range(NCHUNK):
        xn_c = t.xn_sb[:, a * 520 + 1:a * 520 + 1 + nq]
        xn_p = t.xn_sb[:, a * 520:a * 520 + nq]
        d = t.scrb[:, 0:nq]
        w.tt(d, xn_c, xn_p, op=ALU.subtract)
        w.stt(t.ffb[:, a * 512:a * 512 + nq], d, t.mu_sb[:, a:a + 1], xn_p, ALU.mult, ALU.add)
        i = w.stt(t.ffb[:, 4096 + a * 512:4096 + a * 512 + nq], d,
                  t.mu_sb[:, 8 + a:9 + a], xn_p, ALU.mult, ALU.add)
    w.wait("dve", 'pe')
    for dc in range(NCHUNK):
        pd = t.p_mis[:, (dc % 2) * 512:(dc % 2) * 512 + nq]
        if dc >= 2:
            w.wait("acs", 'pe', val=w.val("acs") - 1)
        i = None
        for a in range(NCHUNK):
            i = w.mm(pd, t.wbig[:, 8192 + a * 1024 + dc * 128:8192 + a * 1024 + (dc + 1) * 128],
                     t.ffb[:, 4096 + a * 512:4096 + a * 512 + nq],
                     start=(a == 0), stop=(a == NCHUNK - 1))
        w.inc(i, "qk", 1)
        w.wait("qk", 'act')
        w.inc(w.act(t.scrf[:, (dc % 2) * 512:(dc % 2) * 512 + nq], pd, AF.Relu), "acs", 1)
        w.wait("acs", 'dve')
        w.tt(t.kkb[:, dc * 512:dc * 512 + nq],
                   t.scrf[:, (dc % 2) * 512:(dc % 2) * 512 + nq],
                   t.scrf[:, (dc % 2) * 512:(dc % 2) * 512 + nq], op=ALU.mult)
    w.wait("dve", 'pe')
    for dc in range(NCHUNK):
        pr = t.p_mis[:, 0:nq]
        pv = t.p_mis[:, 512:512 + nq]
        if dc >= 1:
            w.wait("acs", 'pe')
        i = None
        for a in range(NCHUNK):
            i = w.mm(pr, t.wbig[:, a * 1024 + dc * 128:a * 1024 + (dc + 1) * 128],
                     t.ffb[:, a * 512:a * 512 + nq], start=(a == 0), stop=(a == NCHUNK - 1))
        w.inc(i, "qk", 1)
        w.wait("qk", 'act')
        if dc >= 1:
            w.wait("dve", 'act')
        w.inc(w.act(t.scrf[:, 0:nq], pr, AF.Sigmoid), "acs", 1)
        if dc >= 1:
            w.wait("dve", 'pe')
        i = None
        for a in range(NCHUNK):
            i = w.mm(pv, t.wbig[:, 16384 + a * 1024 + dc * 128:16384 + a * 1024 + (dc + 1) * 128],
                     t.kkb[:, a * 512:a * 512 + nq], start=(a == 0), stop=(a == NCHUNK - 1))
        w.inc(i, "rv", 1)
        w.wait("acs", 'dve')
        w.wait("rv", 'dve')
        xc = t.xg[:, dc * 513 + 1: dc * 513 + 1 + nq]
        w.tt(t.scrf[:, 512:512 + nq], pv, t.scrf[:, 0:nq], op=ALU.mult)
        w.tt(xc, xc, t.scrf[:, 512:512 + nq], op=ALU.add)
    if out_fuse:
        for tl in range(nq // 128):
            w.wait("dve", 'pe')
            i = None
            for a in range(NCHUNK):
                i = w.trans(t.p_mis[:, a * 128:(a + 1) * 128],
                            t.xg[:, a * 513 + 1 + tl * 128:a * 513 + 1 + (tl + 1) * 128],
                            t.eyef_sb[:, :])
            w.inc(i, "pe", 1)
            w.wait("pe", 'dve')
            ob = tl % 2
            w.wait("oput", 'dve', val=t.ot_tok[ob])
            ot = t.xtok[:, ob * 1024:(ob + 1) * 1024]
            for a in range(NCHUNK):
                w.copy(ot[:, a * 128:(a + 1) * 128], t.p_mis[:, a * 128:(a + 1) * 128])
            w.wait("dve", 'sync')
            w.inc(w.dma('sync', t.out_h[q0 + tl * 128:q0 + (tl + 1) * 128, :], ot), "oput", 16)
            t.ot_tok[ob] = w.val("oput")
    else:
        w.wait("dve", 'sync')
        xgv = t.xg.rearrange("p (a c) -> p a c", c=513)[:, :, 1:1 + nq]
        w.inc(w.dma('sync', t.x_hbm[:, :, q0:q0 + nq].rearrange("a p c -> p a c"), xgv), "dsx", 16)
    w.wait("dve", 'pe')


def _attnb_pre(w, t, qg):
    nq = QGS[qg]
    q0 = QG0[qg]
    _ln_fm(w, t, qg)
    w.wait("dve", 'pe')
    for dc in range(NCHUNK):
        pd = t.p_mis[:, (dc % 2) * 512:(dc % 2) * 512 + nq]
        if dc >= 2:
            w.wait("dve", 'pe', val=w.val("dve") - 3)
        i = None
        for a in range(NCHUNK):
            i = w.mm(pd, t.wbig[:, a * 1024 + dc * 128:a * 1024 + (dc + 1) * 128],
                     t.xn_sb[:, a * 520 + 1:a * 520 + 1 + nq],
                     start=(a == 0), stop=(a == NCHUNK - 1))
        w.inc(i, "qk", 1)
        w.wait("qk", 'dve')
        bq = t.bchunk_sb[:, 8 + dc:9 + dc]
        mn = t.scrf[:, 0:nq]
        mx = t.scrf[:, 512:512 + nq]
        w.ts(mn, pd, bq, 0.0, op0=ALU.add, op1=ALU.min)
        w.ts(mx, pd, bq, 0.0, op0=ALU.add, op1=ALU.max)
        w.wait("dve", 'act')
        w.inc(w.act(t.scrf[:, 1024:1024 + nq], mn, AF.Exp), "acs", 1)
        w.wait("acs", 'dve')
        w.wait("dst", 'dve', val=w.val("dst") - 16)
        w.tt(t.stage[:, (dc % 2) * 1024:(dc % 2) * 1024 + nq],
                   t.scrf[:, 1024:1024 + nq], mx, op=ALU.add)
        w.wait("dve", 'pool')
        w.inc(w.dma('pool', t.qb_hbm[dc, :, q0:q0 + nq],
                    t.stage[:, (dc % 2) * 1024:(dc % 2) * 1024 + nq]), "dst", 16)
    ntt_q = nq // 128
    for tl in range(ntt_q):
        ttg = q0 // 128 + tl
        mcol = t.maskv_sb[:, ttg:ttg + 1]
        for half in range(2):
            pk = t.p_mis[:, 0:512]
            if tl > 0 or half > 0:
                w.wait("dve", 'pe')
            i = None
            for a in range(NCHUNK):
                i = w.mm(pk, t.xn_sb[:, a * 520 + 1 + tl * 128:a * 520 + 1 + (tl + 1) * 128],
                         t.wbig[:, 8192 + a * 1024 + half * 512:8192 + a * 1024 + (half + 1) * 512],
                         start=(a == 0), stop=False)
            w.mm(pk, t.onesb[0:1, 0:128],
                 t.brow_sb[0:1, 3 * E + half * 512:3 * E + (half + 1) * 512],
                 start=False, stop=False)
            i = w.mm(pk, t.onesb[0:1, 0:128],
                     t.brow_lo[0:1, 3 * E + half * 512:3 * E + (half + 1) * 512],
                     start=False, stop=True)
            w.inc(i, "qk", 1)
            w.wait("qk", 'dve')
            mn = t.scrf[:, 0:512]
            mx = t.scrf[:, 512:1024]
            w.ts(mn, pk, 0.0, None, op0=ALU.min)
            w.ts(mx, pk, 0.0, None, op0=ALU.max)
            w.wait("dve", 'act')
            w.inc(w.act(t.scrf[:, 1024:1536], mn, AF.Exp), "acs", 1)
            w.wait("acs", 'dve')
            w.wait("rv", 'dve')
            w.tt(t.scrf[:, 1536:2048], t.scrf[:, 1024:1536], mx, op=ALU.add)
            w.ts(t.scrb[:, half * 512:(half + 1) * 512], t.scrf[:, 1536:2048], mcol)
            w.wait("dve", 'pe')
            pv = t.p_mis[:, 512:1024]
            i = None
            for a in range(NCHUNK):
                i = w.mm(pv, t.xn_sb[:, a * 520 + 1 + tl * 128:a * 520 + 1 + (tl + 1) * 128],
                         t.wbig[:, 16384 + a * 1024 + half * 512:16384 + a * 1024 + (half + 1) * 512],
                         start=(a == 0), stop=False)
            w.mm(pv, t.onesb[0:1, 0:128],
                 t.brow_sb[0:1, 4 * E + half * 512:4 * E + (half + 1) * 512],
                 start=False, stop=False)
            i = w.mm(pv, t.onesb[0:1, 0:128],
                     t.brow_lo[0:1, 4 * E + half * 512:4 * E + (half + 1) * 512],
                     start=False, stop=True)
            w.inc(i, "qk", 1)
            w.wait("qk", 'dve')
            w.copy(t.scrb[:, 1024 + half * 512:1024 + (half + 1) * 512], pv)
        w.wait("dve", 'pe')
        i = None
        for pp in range(NCHUNK):
            i = w.mm(t.p_qk_a[:, pp * 128:(pp + 1) * 128], t.scrb[:, pp * 128:(pp + 1) * 128],
                     t.scrb[:, 1024 + pp * 128:1024 + (pp + 1) * 128],
                     start=True, stop=True)
        w.inc(i, "rv", 1)
        w.wait("rv", 'dve')
        w.tt(t.kvacc_b[:, :], t.kvacc_b[:, :], t.p_qk_a[:, :], op=ALU.add)
    w.wait("dve", 'pe')


def _attnb_reduce(w, t):
    w.wait("dve", 'pool')
    w.inc(w.dma('pool', t.kv_in, t.kvacc_b[:, :]), "dst", 16)
    w.wait("dst", 'pool')
    w.inc(w.cc([t.kv_in], [t.kv_out]), "cc", 1)
    w.wait("cc", 'pool')
    w.inc(w.dma('pool', t.scrf[:, 0:1024], t.kv_out), "dst", 16)
    w.wait("dst", 'dve')
    i = None
    for a in range(NCHUNK):
        w.copy(t.kvblk_b[0:64, a * 128:a * 128 + 64], t.scrf[0:64, a * 128:a * 128 + 64])
        i = w.copy(t.kvblk_b[64:128, a * 128 + 64:a * 128 + 128],
                   t.scrf[64:128, a * 128 + 64:a * 128 + 128])
    w.wait("dve", 'pe')


def _attnb_out(w, t, qg):
    nq = QGS[qg]
    q0 = QG0[qg]
    w.wait("dst", 'sync')
    qk_tok = [None, None]
    for p in range(NCHUNK):
        sl = p % 2
        qp = t.qtile[:, sl * 512:sl * 512 + nq]
        w.wait("qk", 'sync', val=(qk_tok[sl] if qk_tok[sl] is not None else w.val("qk")))
        w.inc(w.dma('sync', qp, t.qb_hbm[p, :, q0:q0 + nq]), "dq", 16)
        w.wait("dq", 'pe')
        w.wait("dve", 'pe', val=w.val("dve") - 1 if p >= 2 else w.val("dve"))
        w.inc(w.mm(t.p_mis[:, sl * 512:sl * 512 + nq],
                   t.kvblk_b[:, p * 128:(p + 1) * 128], qp), "qk", 1)
        qk_tok[sl] = w.val("qk")
        w.wait("qk", 'dve')
        if p == 0:
            w.wait("pe", 'dve')
        w.copy(t.ofb[:, p * 512:p * 512 + nq], t.p_mis[:, sl * 512:sl * 512 + nq])
    w.barrier()
    w.wait("dve", 'pe')
    w.wait("dve", 'sync')
    w.wait("pe", 'sync')
    xgv = t.xg.rearrange("p (a c) -> p a c", c=513)[:, :, 1:1 + nq]
    w.inc(w.dma('sync', xgv, t.x_hbm[:, :, q0:q0 + nq].rearrange("a p c -> p a c")), "dx", 16)
    w.wait("dx", 'dve')
    wc_tok = [None, None]
    for fc in range(NCHUNK):
        wb = fc % 2
        w.wait("pe", 'pool', val=(wc_tok[wb] if wc_tok[wb] is not None else w.val("pe")))
        w.inc(w.dma('pool', t.wcol[:, wb * 1024:(wb + 1) * 1024].rearrange("p (a c) -> p a c", a=8),
                    t.w_bo[:, fc * 128:(fc + 1) * 128].rearrange("(a p) c -> p a c", p=128)),
              "dw", 16)
        w.wait("dw", 'pe')
        w.wait("dve", 'pe')
        i = None
        for a in range(NCHUNK):
            i = w.mm(t.p_kv[:, 0:nq], t.wcol[:, wb * 1024 + a * 128:wb * 1024 + (a + 1) * 128],
                     t.ofb[:, a * 512:a * 512 + nq], start=(a == 0), stop=(a == NCHUNK - 1))
        w.inc(i, "pe", 1)
        wc_tok[wb] = w.val("pe")
        w.wait("pe", 'dve')
        xc = t.xg[:, fc * 513 + 1: fc * 513 + 1 + nq]
        w.ts(t.scrf[:, 0:nq], t.p_kv[:, 0:nq], t.bchunk_sb[:, 16 + fc:17 + fc], op0=ALU.add)
        w.tt(xc, xc, t.scrf[:, 0:nq], op=ALU.add)
        w.barrier()
    sl_ = (qg % 2) * 8
    w.copy(t.stash[:, sl_:sl_ + 8].rearrange("p (a one) -> p a one", one=1),
           t.xg.rearrange("p (a c) -> p a c", c=513)[:, :, nq:nq + 1])
    w.wait("dve", 'pe')


def _output(w, t):
    ot_tok = [0, 0]
    ld_tok = [0, 0]
    for tt_ in range(NTT):
        lb = tt_ % 2
        xf = t.xfst[:, 0:1024] if lb == 0 else t.scrf[:, 0:1024]
        w.wait("pe", 'sync', val=ld_tok[lb])
        w.inc(w.dma('sync', xf.rearrange("p (a c) -> p a c", a=8),
                    t.x_hbm[:, :, tt_ * 128:(tt_ + 1) * 128].rearrange("a p c -> p a c")),
              "dx", 16)
        w.wait("dx", 'pe')
        w.wait("dve", 'pe')
        i = None
        for a in range(NCHUNK):
            i = w.trans(t.p_mis[:, a * 128:(a + 1) * 128], xf[:, a * 128:(a + 1) * 128],
                        t.eyef_sb[:, :])
        w.inc(i, "pe", 1)
        ld_tok[lb] = w.val("pe")
        w.wait("pe", 'dve')
        ob = tt_ % 2
        w.wait("oput", 'dve', val=ot_tok[ob])
        ot = t.xtok[:, ob * 1024:(ob + 1) * 1024]
        i = None
        for a in range(NCHUNK):
            i = w.copy(ot[:, a * 128:(a + 1) * 128], t.p_mis[:, a * 128:(a + 1) * 128])
        w.wait("dve", 'sync')
        w.inc(w.dma('sync', t.out_h[tt_ * 128:(tt_ + 1) * 128, :], ot), "oput", 16)
        ot_tok[ob] = w.val("oput")



def _host_inputs(inputs):
    x = np.ascontiguousarray(np.asarray(inputs["L0_attn_a_x"], dtype=np.float32))
    g = lambda k: np.ascontiguousarray(np.asarray(inputs[k], dtype=np.float32))
    shared = {
        "w_aq": g("L0_attn_a_Wq"), "w_ak": g("L0_attn_a_Wk"),
        "w_av": g("L0_attn_a_Wv"), "w_ao": g("L0_attn_a_Wo"),
        "w_fr": np.ascontiguousarray(g("L0_ffn_W_r").T),
        "w_fk": np.ascontiguousarray(g("L0_ffn_W_k").T),
        "w_fv": np.ascontiguousarray(g("L0_ffn_W_v").T),
        "w_bq": np.ascontiguousarray(g("L0_attn_b_Wq").T),
        "w_bk": np.ascontiguousarray(g("L0_attn_b_Wk").T),
        "w_bv": np.ascontiguousarray(g("L0_attn_b_Wv").T),
        "w_bo": np.ascontiguousarray(g("L0_attn_b_Wo").T),
    }
    b_rows = np.stack([g("L0_attn_a_bq"), g("L0_attn_a_bk"), g("L0_attn_a_bv"),
                       g("L0_attn_b_bk"), g("L0_attn_b_bv")]).astype(np.float32)
    b_chunk = np.zeros((128, 4 * NCHUNK), np.float32)
    b_chunk[:, 0:8] = g("L0_attn_a_bo").reshape(NCHUNK, 128).T
    b_chunk[:, 8:16] = g("L0_attn_b_bq").reshape(NCHUNK, 128).T
    b_chunk[:, 16:24] = g("L0_attn_b_bo").reshape(NCHUNK, 128).T
    mu_chunk = np.zeros((128, 2 * NCHUNK), np.float32)
    mu_chunk[:, 0:8] = g("L0_ffn_mu_r").reshape(NCHUNK, 128).T
    mu_chunk[:, 8:16] = g("L0_ffn_mu_k").reshape(NCHUNK, 128).T
    shared["b_rows"] = np.ascontiguousarray(b_rows)
    shared["b_chunk"] = np.ascontiguousarray(b_chunk)
    shared["mu_chunk"] = np.ascontiguousarray(mu_chunk)
    shared["eyef"] = np.eye(128, dtype=np.float32)
    shared["eyeb"] = np.eye(128, dtype=np.float32).astype(ml_dtypes.bfloat16)
    in_maps = []
    for c in range(8):
        b, h = c // 2, c % 2
        m = dict(shared)
        m["xq"] = np.ascontiguousarray(x[b, 3968 * h: 3968 * h + TOK, :])
        m["xfull"] = np.ascontiguousarray(x[b])
        mk = np.zeros(TOK, np.float32)
        if h == 0:
            mk[0:4096] = 1.0
        else:
            mk[128:TOK] = 1.0
        m["maskv"] = np.ascontiguousarray(mk.reshape(NTT, 128).T)
        in_maps.append(m)
    return in_maps


def run_staged(inputs, stop_stage, cores=None):
    nc = build_program(stop_stage=stop_stage)
    in_maps = _host_inputs(inputs)
    cores = cores if cores is not None else list(range(8))
    res = run_bass_kernel_spmd(nc, [in_maps[c] for c in cores], core_ids=cores)
    return [r["dbg"] for r in res.results]


def kernel(**inputs):
    nc = build_program()
    in_maps = _host_inputs(inputs)
    res = run_bass_kernel_spmd(nc, in_maps, core_ids=list(range(8)))
    y = np.empty((B, S, E), np.float32)
    for c in range(8):
        b, h = c // 2, c % 2
        o = res.results[c]["out"]
        if h == 0:
            y[b, 0:4096] = o[0:4096]
        else:
            y[b, 4096:8192] = o[128:TOK]
    return y


# version 13
# speedup vs baseline: 2.1629x; 1.0214x over previous
import dataclasses
import numpy as np
import ml_dtypes

import concourse.bass as bass
import concourse.mybir as mybir
from concourse.bass_utils import run_bass_kernel_spmd

F32 = mybir.dt.float32
BF16 = mybir.dt.bfloat16
AF = mybir.ActivationFunctionType
ALU = mybir.AluOpType

E = 1024
H = 16
D = 64
S = 8192
B = 4
NCHUNK = 8
TOK = 4224
NTT = 33
NKG = 64
QGS = [512] * 8 + [128]
QG0 = [sum(QGS[:i]) for i in range(len(QGS))]
NQG = len(QGS)
EPS_LN = 1e-5
EPS_NRM = 1e-8
EPS_ATT = 1e-6
NCONST = -3.1 * S + EPS_ATT

SEM_NAMES = ["dx", "dk", "dq", "dw", "dst", "dsx", "qk", "rcp", "rv", "pe", "dve",
             "acs", "cc", "oput"]


def _act_raw(nc, eng, out, in_, func, bias=0.0, scale=1.0):
    bi = eng.activation(out=out, in_=in_, func=AF.Exp, bias=bias, scale=scale)
    i2 = dataclasses.replace(bi.ins, func=func)
    nc.register_instruction(i2, overwrite=True)
    insts = nc.cur_bb.bb.instructions
    assert insts[-1] is bi.ins
    insts[-1] = i2
    return bass.BassInstruction(i2)


class Walk:

    def __init__(self, nc, kind, eng, sems):
        self.nc = nc
        self.kind = kind
        self.eng = eng
        self.sems = sems
        self.c = {s: 0 for s in SEM_NAMES}

    def inc(self, ins, name, n=1):
        self.c[name] += n
        if ins is not None:
            ins.then_inc(self.sems[name], n)

    def wait(self, name, kind, val=None):
        if self.kind == kind:
            v = self.c[name] if val is None else val
            if v > 0:
                self.eng.wait_ge(self.sems[name], v)

    def val(self, name):
        return self.c[name]

    _OWNER = {"qk": 'pe', "rv": 'pe', "pe": 'pe', "rcp": 'act', "acs": 'act',
              "dve": 'dve', "dk": 'sync', "dq": 'sync', "dx": 'sync', "dsx": 'sync',
              "oput": 'sync', "dw": 'pool', "dst": 'pool', "cc": 'pool'}

    def barrier(self):
        for s in SEM_NAMES:
            own = self._OWNER[s]
            for k in ('pe', 'act', 'dve', 'sync', 'pool'):
                if k != own:
                    self.wait(s, k)

    def mm(self, out, lhsT, rhs, start=True, stop=True, tp=None):
        if self.kind != 'pe':
            return None
        return self.eng.matmul(out, lhsT=lhsT, rhs=rhs, start=start, stop=stop,
                               tile_position=tp)

    def trans(self, out, in_, ident):
        if self.kind != 'pe':
            return None
        return self.eng.transpose(out, in_, ident)

    def act(self, out, in_, func, bias=0.0, scale=1.0, accum_out=None):
        if self.kind != 'act':
            return None
        if func in (AF.Reciprocal, AF.Rsqrt):
            return _act_raw(self.nc, self.eng, out, in_, func, bias, scale)
        kw = {} if accum_out is None else {"accum_out": accum_out}
        return self.eng.activation(out=out, in_=in_, func=func, bias=bias,
                                   scale=scale, **kw)

    def _dve(self, f):
        if self.kind != 'dve':
            self.c["dve"] += 1
            return None
        self.eng.wait_ge(self.sems["dve"], self.c["dve"])
        ins = f()
        self.c["dve"] += 1
        ins.then_inc(self.sems["dve"], 1)
        return ins

    def ts(self, out, in0, s1, s2=None, op0=ALU.mult, op1=ALU.bypass):
        return self._dve(lambda: self.eng.tensor_scalar(
            out=out, in0=in0, scalar1=s1, scalar2=s2, op0=op0, op1=op1))

    def tt(self, out, in0, in1, op=ALU.add):
        return self._dve(lambda: self.eng.tensor_tensor(out=out, in0=in0, in1=in1, op=op))

    def stt(self, out, in0, scalar, in1, op0, op1):
        return self._dve(lambda: self.eng.scalar_tensor_tensor(
            out=out, in0=in0, scalar=scalar, in1=in1, op0=op0, op1=op1))

    def copy(self, out, in_):
        return self._dve(lambda: self.eng.tensor_copy(out=out, in_=in_))

    def memset(self, ap, v):
        return self._dve(lambda: self.eng.memset(ap, v))

    def red(self, out, in_, op=ALU.add):
        return self._dve(lambda: self.eng.tensor_reduce(out, in_, mybir.AxisListType.X, op))

    def vrecip_fast(self, out, in_):
        return self._dve(lambda: self.eng.reciprocal_approx_fast(out=out, in_=in_))

    def vrecip(self, out, in_):
        def f():
            if out.dtype == BF16:
                with self.nc.allow_low_precision(reason="bf16 store of fp32 reciprocal"):
                    return self.eng.reciprocal(out=out, in_=in_)
            return self.eng.reciprocal(out=out, in_=in_)
        return self._dve(f)

    def dma(self, queue, out, in_):
        if self.kind != queue:
            return None
        return self.eng.dma_start(out=out, in_=in_)

    def cc(self, ins_, outs_):
        if self.kind != 'pool':
            return None
        return self.eng.collective_compute(
            "AllReduce", ALU.add, replica_groups=[[0, 1], [2, 3], [4, 5], [6, 7]],
            ins=ins_, outs=outs_)


class L:
    pass


def build_program(stop_stage=None):
    nc = bass.Bass("TRN2", target_bir_lowering=False, debug=False)
    t = L()

    t.xq = nc.dram_tensor("xq", [TOK, E], F32, kind="ExternalInput").ap()
    t.xfull = nc.dram_tensor("xfull", [S, E], F32, kind="ExternalInput").ap()
    for nm in ["w_aq", "w_ak", "w_av", "w_ao", "w_fr", "w_fk", "w_fv",
               "w_bq", "w_bk", "w_bv", "w_bo"]:
        setattr(t, nm, nc.dram_tensor(nm, [E, E], F32, kind="ExternalInput").ap())
    t.b_rows = nc.dram_tensor("b_rows", [5, E], F32, kind="ExternalInput").ap()
    t.b_chunk = nc.dram_tensor("b_chunk", [128, 4 * NCHUNK], F32, kind="ExternalInput").ap()
    t.mu_chunk = nc.dram_tensor("mu_chunk", [128, 2 * NCHUNK], F32, kind="ExternalInput").ap()
    t.eyef = nc.dram_tensor("eyef", [128, 128], F32, kind="ExternalInput").ap()
    t.eyeb = nc.dram_tensor("eyeb", [128, 128], BF16, kind="ExternalInput").ap()
    t.maskv = nc.dram_tensor("maskv", [128, NTT], F32, kind="ExternalInput").ap()

    t.knT_hbm = nc.dram_tensor("knT_hbm", [NKG, 128, NCHUNK * 128], BF16, kind="Internal").ap()
    t.vk_hbm = nc.dram_tensor("vk_hbm", [NKG, 128, H * 65], BF16, kind="Internal").ap()
    t.qn_hbm = nc.dram_tensor("qn_hbm", [NCHUNK, 128, TOK], BF16, kind="Internal").ap()
    t.qb_hbm = nc.dram_tensor("qb_hbm", [NCHUNK, 128, TOK], BF16, kind="Internal").ap()
    t.kv_in = nc.dram_tensor("kv_in", [128, 1024], F32, kind="Internal").ap()
    t.kv_out = nc.dram_tensor("kv_out", [128, 1024], F32, kind="Internal").ap()
    t.x_hbm = nc.dram_tensor("x_hbm", [NCHUNK, 128, TOK], F32, kind="Internal").ap()
    t.out_h = nc.dram_tensor("out", [TOK, E], F32, kind="ExternalOutput").ap()
    t.dbg = None
    if stop_stage is not None:
        t.dbg = nc.dram_tensor("dbg", [128, NCHUNK * TOK], F32, kind="ExternalOutput").ap()

    import contextlib
    with contextlib.ExitStack() as st:
        def sb(name, shape, dt):
            return st.enter_context(nc.sbuf_tensor(name, shape, dt))

        def ps(name, shape):
            return st.enter_context(nc.psum_tensor(name, shape, F32))

        t.xg = sb("xg", [128, NCHUNK * 513], F32)
        t.xfst = sb("xfst", [128, 1024], F32)
        t.consts = sb("consts", [128, 16], F32)
        t.onesb = sb("onesb", [128, 576], BF16)
        t.eyef_sb = sb("eyef_sb", [128, 128], F32)
        t.eyeb_sb = sb("eyeb_sb", [128, 128], BF16)
        t.maskv_sb = sb("maskv_sb", [128, NTT], F32)
        t.brow_sb = sb("brow_sb", [1, 5 * E], BF16)
        t.brow_lo = sb("brow_lo", [1, 5 * E], BF16)
        t.bchunk_sb = sb("bchunk_sb", [128, 4 * NCHUNK], F32)
        t.mu_sb = sb("mu_sb", [128, 2 * NCHUNK], F32)
        t.ksum_bf = sb("ksum_bf", [128, NCHUNK], BF16)
        t.vsum_sb = sb("vsum_sb", [64, H], F32)
        t.kvblk_a = sb("kvblk_a", [128, NCHUNK * 128], BF16)
        t.kvblk_b = sb("kvblk_b", [128, NCHUNK * 128], BF16)
        t.kvacc_b = sb("kvacc_b", [128, 1024], F32)
        t.wbig = sb("wbig", [128, 3 * 8192], BF16)
        t.wcol = sb("wcol", [128, 2 * 1024], BF16)
        t.stage = sb("stage", [128, 2048 + H * 65], BF16)
        t.xtok = sb("xtok", [128, 2 * 1024], F32)
        t.scrf = sb("scrf", [128, 3 * 1024], F32)
        t.scrb = sb("scrb", [128, 4 * 1024], BF16)
        t.sml = sb("sml", [128, 96], F32)
        t.stash = sb("stash", [128, 16], F32)
        t.kvacc_a = sb("kvacc_a", [128, 1024], F32)
        t.kstream = sb("kstream", [128, 2 * 1024], BF16)
        t.vstream = sb("vstream", [128, 2 * 1040], BF16)
        t.qtile = sb("qtile", [128, 2 * 512], BF16)
        t.rbuf = sb("rbuf", [128, 2 * 1024], BF16)
        t.ut = sb("ut", [64, 2 * 512], F32)
        t.invn = sb("invn", [65, 1024], BF16)
        t.ofb = sb("ofb", [128, NCHUNK * 512], BF16)
        t.xn_sb = sb("xn_sb", [128, NCHUNK * 520], BF16)
        t.ffb = sb("ffb", [128, 2 * NCHUNK * 512], BF16)
        t.kkb = sb("kkb", [128, NCHUNK * 512], BF16)

        t.p_qk_a = ps("p_qk_a", [128, 1024])
        t.p_qk_b = ps("p_qk_b", [128, 1024])
        t.p_kv = ps("p_kv", [128, 1024])
        t.p_mis = ps("p_mis", [128, 1024])

        for i, v in enumerate([1.1, -1.0, 0.0, 1.0, EPS_LN, EPS_NRM, 4.41, -3.1,
                               1.0 / E, NCONST, 0.5, 2.0]):
            nc.const_aps.register(F32, v, t.consts[:, i:i + 1])

        sems = {s: st.enter_context(nc.semaphore(s)) for s in SEM_NAMES}

        with nc.Block() as block:
            @block.sync
            def _(eng):
                _emit(Walk(nc, 'sync', eng, sems), t, stop_stage)

            @block.tensor
            def _(eng):
                _emit(Walk(nc, 'pe', eng, sems), t, stop_stage)

            @block.scalar
            def _(eng):
                _emit(Walk(nc, 'act', eng, sems), t, stop_stage)

            @block.vector
            def _(eng):
                _emit(Walk(nc, 'dve', eng, sems), t, stop_stage)

            @block.gpsimd
            def _(eng):
                _emit(Walk(nc, 'pool', eng, sems), t, stop_stage)

    return nc



def _emit(w: Walk, t, stop_stage):
    _setup(w, t)
    if stop_stage == 'p0':
        _prep(w, t, stop_p0=True)
        _fin(w, t); return
    _prep(w, t)
    w.barrier()
    if stop_stage == 'prep':
        _dump(w, t); _fin(w, t); return
    if stop_stage == 'a0':
        _attn_a(w, t, 0)
        _dump_a0(w, t); _fin(w, t); return
    t.ot_tok = [0, 0]
    _load3(w, t, [t.w_fr, t.w_fk, t.w_fv])
    for qg in range(NQG):
        _attn_a(w, t, qg)
        w.barrier()
        _ffn(w, t, qg, resident=True)
        w.barrier()
    if stop_stage == 'x2':
        _dump(w, t); _fin(w, t); return
    _load3(w, t, [t.w_bq, t.w_bk, t.w_bv])
    for qg in range(NQG):
        _attnb_pre(w, t, qg)
        w.barrier()
        if stop_stage == 'b0' and qg == 0:
            w.wait("dve", 'pool')
            w.wait("dst", 'pool')
            d = t.dbg
            w.inc(w.dma('pool', d[:, 0:1024], t.kvacc_b[:, :]), "oput", 16)
            w.inc(w.dma('pool', d[:, 1024:2048], t.kvblk_b[:, :]), "oput", 16)
            w.inc(w.dma('pool', d[:, 2048:2560], t.qb_hbm[0, :, 0:512]), "oput", 16)
            w.inc(w.dma('pool', d[:, 2560:3584], t.scrb[:, 0:1024]), "oput", 16)
            w.inc(w.dma('pool', d[:, 3584:4608], t.scrb[:, 1024:2048]), "oput", 16)
            w.inc(w.dma('pool', d[:, 4608:5128], t.xn_sb[:, 0:520]), "oput", 16)
            _fin(w, t); return
    w.barrier()
    _attnb_reduce(w, t)
    w.barrier()
    if stop_stage == 'b1':
        w.wait("dve", 'pool')
        d = t.dbg
        w.inc(w.dma('pool', d[:, 0:1024], t.kvacc_b[:, :]), "oput", 16)
        w.inc(w.dma('pool', d[:, 1024:2048], t.kvblk_b[:, :]), "oput", 16)
        _fin(w, t); return
    _load3(w, t, [t.w_fr, t.w_fk, t.w_fv])
    for qg in range(NQG):
        _attnb_out(w, t, qg)
        w.barrier()
        _ffn(w, t, qg, resident=True, out_fuse=True)
        w.barrier()
    _fin(w, t)


def _dump(w, t):
    w.wait("dve", 'sync')
    w.wait("pe", 'sync')
    w.wait("dst", 'sync')
    w.wait("dsx", 'sync')
    for a in range(NCHUNK):
        w.inc(w.dma('sync', t.dbg[:, a * TOK:(a + 1) * TOK], t.x_hbm[a]), "oput", 16)


def _dump_a0(w, t):
    w.wait("dve", 'pool')
    w.wait("pe", 'pool')
    w.wait("acs", 'pool')
    w.wait("dst", 'pool')
    d = t.dbg
    w.inc(w.dma('pool', d[:, 0:4096], t.ofb[:, :]), "oput", 16)
    w.inc(w.dma('pool', d[0:65, 4096:5120], t.invn[:, :]), "oput", 16)
    w.inc(w.dma('pool', d[0:64, 5120:6144], t.ut[:, :]), "oput", 16)
    w.inc(w.dma('pool', d[64:65, 6144:7168], t.scrf[64:65, 0:1024]), "oput", 16)
    w.inc(w.dma('pool', d[:, 7168:7680], t.qn_hbm[0, :, 0:512]), "oput", 16)
    w.inc(w.dma('pool', d[:, 7680:7808], t.knT_hbm[0, :, 0:128]), "oput", 16)
    w.inc(w.dma('pool', d[:, 7808:7938], t.vk_hbm[0, :, 0:130]), "oput", 16)
    w.inc(w.dma('pool', d[:, 7938:7946], t.ksum_bf[:, :]), "oput", 16)
    w.inc(w.dma('pool', d[0:64, 7946:7962], t.vsum_sb[:, :]), "oput", 16)
    w.inc(w.dma('pool', d[:, 8000:9024], t.kvblk_a[:, :]), "oput", 16)
    w.inc(w.dma('pool', d[:, 9024:9056], t.bchunk_sb[:, :]), "oput", 16)
    w.copy(t.scrf[64:65, 1024:2048], t.p_kv[64:65, 0:1024])
    w.wait("dve", 'pool')
    w.inc(w.dma('pool', d[64:65, 9056:10080], t.scrf[64:65, 1024:2048]), "oput", 16)
    w.inc(w.dma('pool', d[:, 10080:12128], t.rbuf[:, :]), "oput", 16)


def _fin(w, t):
    if w.kind == 'pool':
        w.eng.wait_ge(w.sems["oput"], w.c["oput"])


def _setup(w, t):
    for i, v in enumerate([1.1, -1.0, 0.0, 1.0, EPS_LN, EPS_NRM, 4.41, -3.1,
                           1.0 / E, NCONST, 0.5, 2.0]):
        w.memset(t.consts[:, i:i + 1], v)
    w.memset(t.onesb[:, :], 1.0)
    w.memset(t.kvacc_b[:, :], 0.0)
    w.memset(t.kvacc_a[:, :], 0.0)
    w.memset(t.sml[:, 64:88], 0.0)
    w.memset(t.kvblk_a[:, :], 0.0)
    w.memset(t.kvblk_b[:, :], 0.0)
    w.inc(w.dma('sync', t.eyef_sb[:, :], t.eyef), "dx", 16)
    w.inc(w.dma('sync', t.eyeb_sb[:, :], t.eyeb), "dx", 16)
    w.inc(w.dma('sync', t.maskv_sb[:, :], t.maskv), "dx", 16)
    w.inc(w.dma('sync', t.bchunk_sb[:, :], t.b_chunk), "dx", 16)
    w.inc(w.dma('sync', t.mu_sb[:, :], t.mu_chunk), "dx", 16)
    for r_ in range(5):
        w.wait("dve", 'pool')
        w.inc(w.dma('pool', t.scrf[0:1, 0:E], t.b_rows[r_:r_ + 1, :]), "dw", 16)
        w.wait("dw", 'dve')
        w.ts(t.brow_sb[0:1, r_ * E:(r_ + 1) * E], t.scrf[0:1, 0:E], 1.0)
        w.ts(t.scrf[0:1, E:2 * E], t.brow_sb[0:1, r_ * E:(r_ + 1) * E], -1.0)
        w.tt(t.brow_lo[0:1, r_ * E:(r_ + 1) * E], t.scrf[0:1, E:2 * E],
             t.scrf[0:1, 0:E], op=ALU.add)


def _load3(w, t, srcs):
    w.wait("pe", 'pool')
    for i, src in enumerate(srcs):
        w.inc(w.dma('pool', t.wbig[:, i * 8192:(i + 1) * 8192].rearrange("p (a d) -> p a d", a=8),
                    src.rearrange("(a p) d -> p a d", p=128)), "dw", 16)


def _ln_tok(w, t, xt, xn_out):
    m = t.sml[:, 0:1]
    v = t.sml[:, 1:2]
    r = t.sml[:, 2:3]
    sq = t.scrf[:, E:2 * E]
    w.red(m, xt)
    w.ts(m, m, -1.0 / E)
    w.wait("dve", 'act')
    w.wait("acs", 'act')
    w.inc(w.act(sq, xt, AF.Square, bias=m, accum_out=v), "acs", 1)
    w.wait("acs", 'act')
    w.inc(w.act(r, v, AF.Rsqrt, bias=EPS_LN, scale=1.0 / E), "acs", 1)
    w.wait("acs", 'dve')
    w.ts(xn_out, xt, m, r, op0=ALU.add, op1=ALU.mult)


def _norm_heads(w, t, src_f32, dst_bf):
    sq = t.scrf[:, E:2 * E]
    ss = t.sml[:, 16:32]
    src3 = src_f32.rearrange("p (h d) -> p h d", h=H)
    w.wait("dve", 'act')
    w.wait("acs", 'act')
    w.inc(w.act(sq, src_f32, AF.Square), "acs", 1)
    w.wait("acs", 'dve')
    w.red(ss, sq.rearrange("p (h d) -> p h d", h=H))
    w.wait("dve", 'act')
    w.inc(w.act(t.sml[:, 32:48], ss, AF.Rsqrt), "acs", 1)
    w.wait("acs", 'dve')
    w.tt(t.sml[:, 32:48], t.sml[:, 32:48], ss, op=ALU.mult)
    w.ts(t.sml[:, 32:48], t.sml[:, 32:48], EPS_NRM, op0=ALU.add)
    w.vrecip(ss, t.sml[:, 32:48])
    w.tt(dst_bf.rearrange("p (h d) -> p h d", h=H), src3,
         ss.rearrange("p (h o) -> p h o", o=1).broadcast_to([128, H, D]),
         op=ALU.mult)


def _transpose8(w, t, src_bf, dst_bf, pt, dve_val=None):
    w.wait("dve", 'pe', val=(dve_val if dve_val is not None else w.val("dve")))
    i = None
    for a in range(NCHUNK):
        i = w.trans(pt[:, a * 64:(a + 1) * 64].bitcast(BF16),
                    src_bf[:, a * 128:(a + 1) * 128], t.eyeb_sb[:, :])
    w.inc(i, "pe", 1)
    w.wait("pe", 'dve')
    w.copy(dst_bf[:, 0:1024], pt[:, 0:512].bitcast(BF16))


def _proj_tok(w, t, xnT, wbase, brow_idx, pdst):
    last = None
    for half in range(2):
        po = pdst[:, half * 512:(half + 1) * 512]
        for a in range(NCHUNK):
            w.mm(po, xnT[:, a * 128:(a + 1) * 128],
                 t.wbig[:, wbase + a * 1024 + half * 512: wbase + a * 1024 + (half + 1) * 512],
                 start=(a == 0), stop=False)
        w.mm(po, t.onesb[0:1, 0:128],
             t.brow_sb[0:1, brow_idx * E + half * 512: brow_idx * E + (half + 1) * 512],
             start=False, stop=False)
        last = w.mm(po, t.onesb[0:1, 0:128],
                    t.brow_lo[0:1, brow_idx * E + half * 512: brow_idx * E + (half + 1) * 512],
                    start=False, stop=True)
    w.inc(last, "pe", 1)


def _prep(w, t, stop_p0=False):
    _load3(w, t, [t.w_ak, t.w_av, t.w_aq])
    w.wait("dw", 'pe')
    w.wait("dx", 'dve')
    w.wait("dve", 'pe')
    ONES_COL = t.onesb[:, 0:1]

    xtok_tok = [0, 0]
    xtok_pe = [0, 0]
    tile_pe = [0, 0]
    flush_pe = [None]
    st_tok = {"kn": 0, "vk": 0, "qn": 0}
    NTOT = NKG + NTT
    ln_dve = {}

    def _load_ln(tc2):
        if tc2 >= NTOT:
            return
        full2 = tc2 < NKG
        buf2 = tc2 % 2
        xt2 = t.xtok[:, buf2 * 1024:(buf2 + 1) * 1024]
        src2 = (t.xfull[tc2 * 128:(tc2 + 1) * 128, :] if full2
                else t.xq[(tc2 - NKG) * 128:(tc2 - NKG + 1) * 128, :])
        w.wait("dve", 'sync', val=xtok_tok[buf2])
        w.wait("pe", 'sync', val=xtok_pe[buf2])
        w.inc(w.dma('sync', xt2, src2), "dx", 16)
        w.wait("dx", 'dve')
        nb2 = t.scrb if tc2 % 2 == 0 else t.ffb
        w.wait("pe", 'dve', val=tile_pe[tc2 % 2])
        _ln_tok(w, t, xt2, nb2[:, 0:1024])
        ln_dve[tc2] = w.val("dve")
        xtok_tok[buf2] = w.val("dve")

    def _flush():
        if flush_pe[0] is not None:
            w.wait("pe", 'dve', val=flush_pe[0])
            w.tt(t.kvacc_a[:, :], t.kvacc_a[:, :], t.p_kv[:, :], op=ALU.add)
            w.tt(t.sml[:, 64:72], t.sml[:, 64:72], t.p_qk_b[:, 0:8], op=ALU.add)
            w.tt(t.sml[0:64, 72:88], t.sml[0:64, 72:88], t.p_qk_b[0:64, 16:32], op=ALU.add)
            flush_pe[0] = None

    _load_ln(0)
    for tc in range(NTOT):
        full = tc < NKG
        buf = tc % 2
        xt = t.xtok[:, buf * 1024:(buf + 1) * 1024]
        nbase = t.scrb if tc % 2 == 0 else t.ffb
        xnb = nbase[:, 0:1024]

        if not full:
            tt_ = tc - NKG
            w.wait("dve", 'pe')
            i = None
            for a in range(NCHUNK):
                i = w.trans(t.p_mis[:, a * 128:(a + 1) * 128], xt[:, a * 128:(a + 1) * 128],
                            t.eyef_sb[:, :])
            w.inc(i, "pe", 1)
            xtok_pe[buf] = w.val("pe")
            w.wait("pe", 'dve')
            w.wait("dst", 'dve', val=st_tok.get("xf", 0))
            w.copy(t.xfst[:, 0:1024], t.p_mis[:, 0:1024])
            w.wait("dve", 'pool')
            w.inc(w.dma('pool',
                        t.x_hbm[:, :, tt_ * 128:(tt_ + 1) * 128].rearrange("a p c -> p a c"),
                        t.xfst.rearrange("p (a c) -> p a c", a=8)), "dst", 16)
            st_tok["xf"] = w.val("dst")
            dve_tp = w.val("dve")
        else:
            dve_tp = ln_dve[tc]

        xnT = nbase[:, 1024:2048]
        _transpose8(w, t, xnb, xnT, t.p_qk_a, dve_val=dve_tp)
        w.wait("dve", 'pe')

        if full:
            _proj_tok(w, t, xnT, 0, 1, t.p_mis)
            _load_ln(tc + 1)
            _flush()
            w.wait("pe", 'dve')
            w.wait("pe", 'act')
            kn_tok = t.scrb[:, 2048:3072]
            _norm_heads(w, t, t.p_mis[:, 0:1024], kn_tok)
            if stop_p0:
                w.wait("dve", 'pool')
                w.wait("acs", 'pool')
                d = t.dbg
                w.inc(w.dma('pool', d[:, 0:1024], t.scrb[:, 0:1024]), "oput", 16)
                w.inc(w.dma('pool', d[:, 1024:2048], t.scrb[:, 1024:2048]), "oput", 16)
                w.inc(w.dma('pool', d[:, 2048:3072], t.p_mis[:, 0:1024]), "oput", 16)
                w.inc(w.dma('pool', d[:, 3072:4096], kn_tok), "oput", 16)
                w.inc(w.dma('pool', d[:, 4096:4160], t.sml[:, :]), "oput", 16)
                w.inc(w.dma('pool', d[:, 4160:4288], t.xtok[:, 0:128]), "oput", 16)
                w.inc(w.dma('pool', d[:, 4288:6336], t.wbig[:, 0:2048]), "oput", 16)
                w.inc(w.dma('pool', d[0:1, 6336:11456], t.brow_sb[:, :]), "oput", 16)
                w.inc(w.dma('pool', d[0:1, 11456:16576], t.brow_lo[:, :]), "oput", 16)
                return
            w.wait("dve", 'pe')
            _proj_tok(w, t, xnT, 8192, 2, t.p_mis)
            w.wait("pe", 'dve')
            w.wait("pe", 'act')
            v_tok = t.scrb[:, 3072:4096]
            w.wait("acs", 'act')
            w.inc(w.act(v_tok, t.p_mis[:, 0:1024], AF.Copy), "acs", 1)
            w.wait("acs", 'dve')
            v_lo = nbase[:, 0:1024]
            w.ts(t.scrf[:, 0:1024], v_tok, -1.0)
            w.tt(v_lo, t.scrf[:, 0:1024], t.p_mis[:, 0:1024], op=ALU.add)
            w.wait("dst", 'dve', val=st_tok["vk"])
            vkst = t.stage[:, 2048:2048 + H * 65]
            vk3 = vkst.rearrange("p (h e) -> p h e", h=H)
            if tc == 0:
                w.memset(vk3[:, :, 64:65], 1.0)
            w.copy(vk3[:, :, 0:64], v_tok.rearrange("p (h d) -> p h d", h=H))
            w.wait("dst", 'dve', val=st_tok["kn"])
            knst = t.stage[:, 1024:2048]
            _transpose8(w, t, kn_tok, knst, t.p_qk_a)
            w.wait("dve", 'pe')
            w.wait("acs", 'pe')
            for a in range(NCHUNK):
                w.mm(t.p_kv[:, a * 128:(a + 1) * 128], kn_tok[:, a * 128:(a + 1) * 128],
                     v_tok[:, a * 128:(a + 1) * 128], start=True, stop=True)
            for a in range(NCHUNK):
                w.mm(t.p_qk_b[:, a:a + 1], kn_tok[:, a * 128:(a + 1) * 128], ONES_COL,
                     start=True, stop=True)
            i = None
            for hh in range(H):
                w.mm(t.p_qk_b[0:64, 16 + hh:17 + hh], v_tok[:, hh * 64:(hh + 1) * 64],
                     ONES_COL, start=True, stop=False)
                i = w.mm(t.p_qk_b[0:64, 16 + hh:17 + hh], v_lo[:, hh * 64:(hh + 1) * 64],
                         ONES_COL, start=False, stop=True)
            w.inc(i, "pe", 1)
            flush_pe[0] = w.val("pe")
            w.wait("dve", 'pool')
            w.wait("pe", 'pool', val=0)
            w.inc(w.dma('pool', t.knT_hbm[tc], knst), "dst", 16)
            st_tok["kn"] = w.val("dst")
            w.inc(w.dma('pool', t.vk_hbm[tc], vkst), "dst", 16)
            st_tok["vk"] = w.val("dst")
        else:
            _proj_tok(w, t, xnT, 16384, 0, t.p_mis)
            _load_ln(tc + 1)
            _flush()
            w.wait("pe", 'dve')
            w.wait("pe", 'act')
            qn_tok = t.scrb[:, 2048:3072]
            _norm_heads(w, t, t.p_mis[:, 0:1024], qn_tok)
            w.wait("dst", 'dve', val=st_tok["qn"])
            qnst = t.stage[:, 0:1024]
            _transpose8(w, t, qn_tok, qnst, t.p_qk_a)
            tt_ = tc - NKG
            w.wait("dve", 'pool')
            w.inc(w.dma('pool',
                        t.qn_hbm[:, :, tt_ * 128:(tt_ + 1) * 128].rearrange("a p c -> p a c"),
                        qnst.rearrange("p (a c) -> p a c", a=8)), "dst", 16)
            st_tok["qn"] = w.val("dst")
        tile_pe[tc % 2] = w.val("pe")

    _flush()

    w.wait("pe", 'dve')
    kb3 = t.kvblk_a.rearrange("p (a c) -> p a c", a=NCHUNK)
    ka3 = t.kvacc_a.rearrange("p (a c) -> p a c", a=NCHUNK)
    w.copy(kb3[0:64, :, 0:64], ka3[0:64, :, 0:64])
    w.copy(kb3[64:128, :, 64:128], ka3[64:128, :, 64:128])
    w.copy(t.ksum_bf[:, :], t.sml[:, 64:64 + NCHUNK])
    w.copy(t.vsum_sb[:, :], t.sml[0:64, 72:72 + H])
    w.ts(t.vsum_sb[:, :], t.vsum_sb[:, :], 3.1)


def _rb_slot(t, kg):
    s = kg % 4
    if s < 2:
        return t.rbuf[:, s * 1024:(s + 1) * 1024]
    return t.scrb[:, s * 1024:(s + 1) * 1024]


def _emit_rv(w, t, qg, kg, nq):
    if kg < 0:
        return
    oo = kg // 8
    kk = kg % 8
    ob = oo % 2
    vk_e = t.vstream[:, ob * 1040 + kk * 130: ob * 1040 + kk * 130 + 65]
    vk_o = t.vstream[:, ob * 1040 + kk * 130 + 65: ob * 1040 + kk * 130 + 130]
    rb_ = _rb_slot(t, kg)
    rs_, rv_ = t.rcp_of[kg]
    w.wait(rs_, 'pe', val=rv_)
    w.mm(t.p_kv[0:65, 0:nq], vk_e, rb_[:, 0:nq], start=(kg == 0), stop=(kg == NKG - 1))
    i = w.mm(t.p_kv[0:65, 512:512 + nq], vk_o, rb_[:, 512:512 + nq],
             start=(kg == 0), stop=(kg == NKG - 1))
    w.inc(i, "rv", 1)


def _attn_a(w, t, qg):
    nq = QGS[qg]
    q0 = QG0[qg]
    t.rcp_of = {}
    t.rhist = []
    w.wait("dst", 'sync')
    for p in range(NCHUNK):
        qb = p % 2
        qt = t.qtile[:, qb * 512:qb * 512 + nq]
        w.wait("pe", 'sync', val=w.val("pe"))
        w.wait("qk", 'sync', val=w.val("qk"))
        w.inc(w.dma('sync', qt, t.qn_hbm[p, :, q0:q0 + nq]), "dq", 16)
        w.wait("dve", 'pe')
        dk_tok = {}
        for oct_ in range(10):
            if oct_ >= 2:
                oo = oct_ - 2
                ob = oo % 2
                for kk in range(8):
                    kg = oo * 8 + kk
                    pb = kg % 2
                    pqk = (t.p_qk_a, t.p_qk_b, t.p_mis)[kg % 3]
                    kn_e = t.kstream[0:64, ob * 1024 + kk * 128: ob * 1024 + (kk + 1) * 128]
                    kn_o = t.kstream[64:128, ob * 1024 + kk * 128: ob * 1024 + (kk + 1) * 128]
                    if kk == 0:
                        w.wait("dk", 'pe', val=dk_tok[oo])
                        if oo == 0:
                            w.wait("dq", 'pe')
                    w.wait("rcp", 'pe', val=w.val("rcp") - 2)
                    w.mm(pqk[:, 0:nq], kn_e, qt[0:64, :], tp=(0, 0))
                    i = w.mm(pqk[:, 512:512 + nq], kn_o, qt[64:128, :], tp=(64, 0))
                    w.inc(i, "qk", 1)
                    _emit_rv(w, t, qg, kg - 2, nq)
                    rb_ = _rb_slot(t, kg)
                    w.wait("qk", 'act')
                    w.wait("rv", 'act', val=w.val("rv") - 2)
                    if nq == 512:
                        i = w.act(rb_[:, 0:1024], pqk[:, 0:1024], AF.Reciprocal,
                                  bias=1.1, scale=-1.0)
                    else:
                        w.act(rb_[:, 0:nq], pqk[:, 0:nq], AF.Reciprocal, bias=1.1, scale=-1.0)
                        i = w.act(rb_[:, 512:512 + nq], pqk[:, 512:512 + nq], AF.Reciprocal,
                                  bias=1.1, scale=-1.0)
                    w.inc(i, "rcp", 1)
                    t.rcp_of[kg] = ("rcp", w.val("rcp"))
            if oct_ < 8:
                ob2 = oct_ % 2
                w.wait("qk", 'sync', val=w.val("qk"))
                w.wait("rv", 'sync', val=w.val("rv") + (2 if oct_ >= 2 else 0))
                w.inc(w.dma('sync',
                            t.kstream[:, ob2 * 1024:(ob2 + 1) * 1024]
                            .rearrange("p (k c) -> p k c", k=8),
                            t.knT_hbm[oct_ * 8:(oct_ + 1) * 8, :, p * 128:(p + 1) * 128]
                            .rearrange("k p c -> p k c")),
                      "dk", 16)
                w.inc(w.dma('sync',
                            t.vstream[:, ob2 * 1040:(ob2 + 1) * 1040]
                            .rearrange("p (k c) -> p k c", k=8),
                            t.vk_hbm[oct_ * 8:(oct_ + 1) * 8, :, 2 * p * 65:(2 * p + 2) * 65]
                            .rearrange("k p c -> p k c")),
                      "dk", 16)
                dk_tok[oct_] = w.val("dk")
        _emit_rv(w, t, qg, NKG - 2, nq)
        _emit_rv(w, t, qg, NKG - 1, nq)
        w.wait("rv", 'dve')
        ue = t.ut[:, 0:nq]
        uo = t.ut[:, 512:512 + nq]
        w.copy(ue, t.p_kv[0:64, 0:nq])
        w.copy(uo, t.p_kv[0:64, 512:512 + nq])
        w.wait("rv", 'pe')
        w.mm(t.p_mis[64:65, 0:nq], t.ksum_bf[0:64, p:p + 1], qt[0:64, :], tp=(0, 64))
        i = w.mm(t.p_mis[64:65, 512:512 + nq], t.ksum_bf[64:128, p:p + 1],
                 qt[64:128, :], tp=(64, 64))
        w.inc(i, "pe", 1)
        w.wait("pe", 'dve')
        nr_e = t.scrf[64:65, 0:nq]
        nr_o = t.scrf[64:65, 512:512 + nq]
        w.copy(nr_e, t.p_kv[64:65, 0:nq])
        w.copy(nr_o, t.p_kv[64:65, 512:512 + nq])
        w.stt(nr_e, nr_e, 4.41, t.p_mis[64:65, 0:nq], ALU.mult, ALU.subtract)
        w.stt(nr_o, nr_o, 4.41, t.p_mis[64:65, 512:512 + nq], ALU.mult, ALU.subtract)
        w.ts(nr_e, nr_e, NCONST, op0=ALU.add)
        w.ts(nr_o, nr_o, NCONST, op0=ALU.add)
        w.wait("dve", 'act')
        w.wait("pe", 'act')
        w.act(t.invn[64:65, 0:nq], nr_e, AF.Reciprocal)
        w.inc(w.act(t.invn[64:65, 512:512 + nq], nr_o, AF.Reciprocal), "acs", 1)
        w.wait("acs", 'pe')
        for half in range(2):
            co = half * 512
            usrc = ue if half == 0 else uo
            w.mm(t.p_mis[0:64, 0:nq], t.onesb[64:65, 0:64], t.invn[64:65, co:co + nq])
            if half == 0:
                i = w.mm(t.p_mis[0:64, 512:512 + nq], t.kvblk_a[0:64, p * 128:p * 128 + 64],
                         qt[0:64, :], tp=(0, 0))
            else:
                i = w.mm(t.p_mis[0:64, 512:512 + nq],
                         t.kvblk_a[64:128, p * 128 + 64:p * 128 + 128],
                         qt[64:128, :], tp=(64, 0))
            w.inc(i, "pe", 1)
            w.wait("pe", 'dve')
            e1 = t.scrf[0:64, 1024:1024 + nq]
            w.stt(e1, usrc, 4.41, t.p_mis[0:64, 512:512 + nq], ALU.mult, ALU.subtract)
            if half == 1:
                w.wait("dst", 'dve')
            dst = t.ofb[0:64, p * 512:p * 512 + nq] if half == 0 else t.scrb[0:64, 0:nq]
            w.stt(dst, e1, t.vsum_sb[:, 2 * p + half:2 * p + half + 1],
                        t.p_mis[0:64, 0:nq], ALU.subtract, ALU.mult)
            if half == 1:
                w.wait("dve", 'pool')
                w.inc(w.dma('pool', t.ofb[64:128, p * 512:p * 512 + nq], t.scrb[0:64, 0:nq]),
                      "dst", 16)
            else:
                w.wait("dve", 'pe')
    w.wait("dst", 'pe')
    w.wait("dve", 'sync')
    xgv = t.xg.rearrange("p (a c) -> p a c", c=513)[:, :, 1:1 + nq]
    w.inc(w.dma('sync', xgv, t.x_hbm[:, :, q0:q0 + nq].rearrange("a p c -> p a c")), "dx", 16)
    w.wait("dx", 'dve')
    wc_tok = [None, None]
    for fc in range(NCHUNK):
        wb = fc % 2
        w.wait("pe", 'pool', val=(wc_tok[wb] if wc_tok[wb] is not None else w.val("pe")))
        w.inc(w.dma('pool', t.wcol[:, wb * 1024:(wb + 1) * 1024].rearrange("p (a c) -> p a c", a=8),
                    t.w_ao[:, fc * 128:(fc + 1) * 128].rearrange("(a p) c -> p a c", p=128)),
              "dw", 16)
        w.wait("dw", 'pe')
        w.wait("dve", 'pe')
        i = None
        for a in range(NCHUNK):
            i = w.mm(t.p_mis[:, 0:nq], t.wcol[:, wb * 1024 + a * 128:wb * 1024 + (a + 1) * 128],
                     t.ofb[:, a * 512:a * 512 + nq], start=(a == 0), stop=(a == NCHUNK - 1))
        w.inc(i, "pe", 1)
        wc_tok[wb] = w.val("pe")
        w.wait("pe", 'dve')
        xc = t.xg[:, fc * 513 + 1: fc * 513 + 1 + nq]
        w.ts(t.scrf[:, 0:nq], t.p_mis[:, 0:nq], t.bchunk_sb[:, fc:fc + 1], op0=ALU.add)
        w.tt(xc, xc, t.scrf[:, 0:nq], op=ALU.add)
    sl_ = (qg % 2) * 8
    w.copy(t.stash[:, sl_:sl_ + 8].rearrange("p (a one) -> p a one", one=1),
           t.xg.rearrange("p (a c) -> p a c", c=513)[:, :, nq:nq + 1])
    w.wait("dve", 'pe')


def _ln_fm(w, t, qg, resident=False):
    nq = QGS[qg]
    q0 = QG0[qg]
    nb = nq + 1
    bnd = qg > 0
    ONES_COL = t.onesb[:, 0:1]
    if resident:
        if bnd:
            sl_ = ((qg + 1) % 2) * 8
            w.copy(t.xg.rearrange("p (a c) -> p a c", c=513)[:, :, 0:1],
                   t.stash[:, sl_:sl_ + 8].rearrange("p (a one) -> p a one", one=1))
    else:
        w.wait("dve", 'sync')
        w.wait("pe", 'sync')
        if not bnd:
            xgv = t.xg.rearrange("p (a c) -> p a c", c=513)[:, :, 1:1 + nq]
            w.inc(w.dma('sync', xgv, t.x_hbm[:, :, 0:nq].rearrange("a p c -> p a c")), "dx", 16)
        else:
            xgv = t.xg.rearrange("p (a c) -> p a c", c=513)[:, :, 0:nb]
            w.inc(w.dma('sync', xgv, t.x_hbm[:, :, q0 - 1:q0 + nq].rearrange("a p c -> p a c")),
                  "dx", 16)
        w.wait("dx", 'dve')
    cast_tok = [None, None]
    for a in range(NCHUNK):
        sl = a % 2
        xb = t.scrb[:, sl * 1024:sl * 1024 + nb]
        x2 = t.scrb[:, 2048 + sl * 1024:2048 + sl * 1024 + nb]
        w.wait("pe", 'dve', val=(cast_tok[sl] if cast_tok[sl] is not None else w.val("pe")))
        if bnd:
            xsrc = t.xg[:, a * 513:a * 513 + nb]
            w.ts(xb, xsrc, 1.0)
            w.tt(x2, xsrc, xsrc, op=ALU.mult)
        else:
            xsrc = t.xg[:, a * 513 + 1:a * 513 + 1 + nq]
            w.ts(xb[:, 1:nb], xsrc, 1.0)
            w.tt(x2[:, 1:nb], xsrc, xsrc, op=ALU.mult)
        w.wait("dve", 'pe')
        w.mm(t.p_qk_a[0:1, 0:nq], ONES_COL, xb[:, 1:nb], start=(a == 0), stop=(a == NCHUNK - 1))
        i = w.mm(t.p_qk_b[0:1, 0:nq], ONES_COL, x2[:, 1:nb], start=(a == 0), stop=(a == NCHUNK - 1))
        if bnd:
            w.mm(t.p_qk_a[0:1, 512:513], ONES_COL, xb[:, 0:1],
                 start=(a == 0), stop=(a == NCHUNK - 1))
            i = w.mm(t.p_qk_b[0:1, 512:513], ONES_COL, x2[:, 0:1],
                     start=(a == 0), stop=(a == NCHUNK - 1))
        w.inc(i, "pe", 1)
        cast_tok[sl] = w.val("pe")
    w.wait("pe", 'dve')
    ncols = 513 if bnd else nq
    mrow = t.scrf[0:1, 0:ncols]
    vrow = t.scrf[0:1, 1024:1024 + ncols]
    w.ts(mrow, t.p_qk_a[0:1, 0:ncols], 1.0 / E)
    w.ts(vrow, t.p_qk_b[0:1, 0:ncols], 1.0 / E)
    w.tt(t.scrf[0:1, 2048:2048 + ncols], mrow, mrow, op=ALU.mult)
    w.tt(vrow, vrow, t.scrf[0:1, 2048:2048 + ncols], op=ALU.subtract)
    w.wait("dve", 'act')
    w.inc(w.act(t.scrf[0:1, 2048:2048 + ncols], vrow, AF.Rsqrt, bias=EPS_LN), "acs", 1)
    w.wait("acs", 'dve')
    w.ts(t.scrb[0:1, 0:ncols], mrow, 1.0)
    w.ts(t.scrb[0:1, 1024:1024 + ncols], t.scrf[0:1, 2048:2048 + ncols], 1.0)
    w.wait("dve", 'pe')
    w.mm(t.p_qk_b[:, 0:nq], t.onesb[0:1, 0:128], t.scrb[0:1, 0:nq])
    i = w.mm(t.p_qk_a[:, 0:nq], t.onesb[0:1, 0:128], t.scrb[0:1, 1024:1024 + nq])
    if bnd:
        w.mm(t.p_qk_b[:, 512:513], t.onesb[0:1, 0:128], t.scrb[0:1, 512:513])
        i = w.mm(t.p_qk_a[:, 512:513], t.onesb[0:1, 0:128], t.scrb[0:1, 1024 + 512:1024 + 513])
    w.inc(i, "pe", 1)
    w.wait("pe", 'dve')
    for a in range(NCHUNK):
        w.tt(t.scrf[:, 0:nq], t.xg[:, a * 513 + 1:a * 513 + 1 + nq],
             t.p_qk_b[:, 0:nq], op=ALU.subtract)
        w.tt(t.xn_sb[:, a * 520 + 1:a * 520 + 1 + nq], t.scrf[:, 0:nq],
             t.p_qk_a[:, 0:nq], op=ALU.mult)
        if bnd:
            w.tt(t.scrf[:, 512:513], t.xg[:, a * 513:a * 513 + 1],
                 t.p_qk_b[:, 512:513], op=ALU.subtract)
            w.tt(t.xn_sb[:, a * 520:a * 520 + 1], t.scrf[:, 512:513],
                 t.p_qk_a[:, 512:513], op=ALU.mult)
        else:
            w.memset(t.xn_sb[:, a * 520:a * 520 + 1], 0.0)


def _ffn(w, t, qg, resident=False, out_fuse=False):
    nq = QGS[qg]
    q0 = QG0[qg]
    _ln_fm(w, t, qg, resident=resident)
    i = None
    for a in range(NCHUNK):
        xn_c = t.xn_sb[:, a * 520 + 1:a * 520 + 1 + nq]
        xn_p = t.xn_sb[:, a * 520:a * 520 + nq]
        d = t.scrb[:, 0:nq]
        w.tt(d, xn_c, xn_p, op=ALU.subtract)
        w.stt(t.ffb[:, a * 512:a * 512 + nq], d, t.mu_sb[:, a:a + 1], xn_p, ALU.mult, ALU.add)
        i = w.stt(t.ffb[:, 4096 + a * 512:4096 + a * 512 + nq], d,
                  t.mu_sb[:, 8 + a:9 + a], xn_p, ALU.mult, ALU.add)
    w.wait("dve", 'pe')
    for dc in range(NCHUNK):
        pd = t.p_mis[:, (dc % 2) * 512:(dc % 2) * 512 + nq]
        if dc >= 2:
            w.wait("acs", 'pe', val=w.val("acs") - 1)
        i = None
        for a in range(NCHUNK):
            i = w.mm(pd, t.wbig[:, 8192 + a * 1024 + dc * 128:8192 + a * 1024 + (dc + 1) * 128],
                     t.ffb[:, 4096 + a * 512:4096 + a * 512 + nq],
                     start=(a == 0), stop=(a == NCHUNK - 1))
        w.inc(i, "qk", 1)
        w.wait("qk", 'act')
        w.inc(w.act(t.scrf[:, (dc % 2) * 512:(dc % 2) * 512 + nq], pd, AF.Relu), "acs", 1)
        w.wait("acs", 'dve')
        w.tt(t.kkb[:, dc * 512:dc * 512 + nq],
                   t.scrf[:, (dc % 2) * 512:(dc % 2) * 512 + nq],
                   t.scrf[:, (dc % 2) * 512:(dc % 2) * 512 + nq], op=ALU.mult)
    w.wait("dve", 'pe')
    for dc in range(NCHUNK):
        pr = t.p_mis[:, 0:nq]
        pv = t.p_mis[:, 512:512 + nq]
        if dc >= 1:
            w.wait("acs", 'pe')
        i = None
        for a in range(NCHUNK):
            i = w.mm(pr, t.wbig[:, a * 1024 + dc * 128:a * 1024 + (dc + 1) * 128],
                     t.ffb[:, a * 512:a * 512 + nq], start=(a == 0), stop=(a == NCHUNK - 1))
        w.inc(i, "qk", 1)
        w.wait("qk", 'act')
        if dc >= 1:
            w.wait("dve", 'act')
        w.inc(w.act(t.scrf[:, 0:nq], pr, AF.Sigmoid), "acs", 1)
        if dc >= 1:
            w.wait("dve", 'pe')
        i = None
        for a in range(NCHUNK):
            i = w.mm(pv, t.wbig[:, 16384 + a * 1024 + dc * 128:16384 + a * 1024 + (dc + 1) * 128],
                     t.kkb[:, a * 512:a * 512 + nq], start=(a == 0), stop=(a == NCHUNK - 1))
        w.inc(i, "rv", 1)
        w.wait("acs", 'dve')
        w.wait("rv", 'dve')
        xc = t.xg[:, dc * 513 + 1: dc * 513 + 1 + nq]
        w.tt(t.scrf[:, 512:512 + nq], pv, t.scrf[:, 0:nq], op=ALU.mult)
        w.tt(xc, xc, t.scrf[:, 512:512 + nq], op=ALU.add)
    if out_fuse:
        for tl in range(nq // 128):
            w.wait("dve", 'pe')
            i = None
            for a in range(NCHUNK):
                i = w.trans(t.p_mis[:, a * 128:(a + 1) * 128],
                            t.xg[:, a * 513 + 1 + tl * 128:a * 513 + 1 + (tl + 1) * 128],
                            t.eyef_sb[:, :])
            w.inc(i, "pe", 1)
            w.wait("pe", 'dve')
            ob = tl % 2
            w.wait("oput", 'dve', val=t.ot_tok[ob])
            ot = t.xtok[:, ob * 1024:(ob + 1) * 1024]
            for a in range(NCHUNK):
                w.copy(ot[:, a * 128:(a + 1) * 128], t.p_mis[:, a * 128:(a + 1) * 128])
            w.wait("dve", 'sync')
            w.inc(w.dma('sync', t.out_h[q0 + tl * 128:q0 + (tl + 1) * 128, :], ot), "oput", 16)
            t.ot_tok[ob] = w.val("oput")
    else:
        w.wait("dve", 'sync')
        xgv = t.xg.rearrange("p (a c) -> p a c", c=513)[:, :, 1:1 + nq]
        w.inc(w.dma('sync', t.x_hbm[:, :, q0:q0 + nq].rearrange("a p c -> p a c"), xgv), "dsx", 16)
    w.wait("dve", 'pe')


def _attnb_pre(w, t, qg):
    nq = QGS[qg]
    q0 = QG0[qg]
    _ln_fm(w, t, qg)
    w.wait("dve", 'pe')
    for dc in range(NCHUNK):
        pd = t.p_mis[:, (dc % 2) * 512:(dc % 2) * 512 + nq]
        if dc >= 2:
            w.wait("dve", 'pe', val=w.val("dve") - 3)
        i = None
        for a in range(NCHUNK):
            i = w.mm(pd, t.wbig[:, a * 1024 + dc * 128:a * 1024 + (dc + 1) * 128],
                     t.xn_sb[:, a * 520 + 1:a * 520 + 1 + nq],
                     start=(a == 0), stop=(a == NCHUNK - 1))
        w.inc(i, "qk", 1)
        w.wait("qk", 'dve')
        bq = t.bchunk_sb[:, 8 + dc:9 + dc]
        mn = t.scrf[:, 0:nq]
        mx = t.scrf[:, 512:512 + nq]
        w.ts(mn, pd, bq, 0.0, op0=ALU.add, op1=ALU.min)
        w.ts(mx, pd, bq, 0.0, op0=ALU.add, op1=ALU.max)
        w.wait("dve", 'act')
        w.inc(w.act(t.scrf[:, 1024:1024 + nq], mn, AF.Exp), "acs", 1)
        w.wait("acs", 'dve')
        w.wait("dst", 'dve', val=w.val("dst") - 16)
        w.tt(t.stage[:, (dc % 2) * 1024:(dc % 2) * 1024 + nq],
                   t.scrf[:, 1024:1024 + nq], mx, op=ALU.add)
        w.wait("dve", 'pool')
        w.inc(w.dma('pool', t.qb_hbm[dc, :, q0:q0 + nq],
                    t.stage[:, (dc % 2) * 1024:(dc % 2) * 1024 + nq]), "dst", 16)
    ntt_q = nq // 128
    for tl in range(ntt_q):
        ttg = q0 // 128 + tl
        mcol = t.maskv_sb[:, ttg:ttg + 1]
        for half in range(2):
            pk = t.p_mis[:, 0:512]
            if tl > 0 or half > 0:
                w.wait("dve", 'pe')
            i = None
            for a in range(NCHUNK):
                i = w.mm(pk, t.xn_sb[:, a * 520 + 1 + tl * 128:a * 520 + 1 + (tl + 1) * 128],
                         t.wbig[:, 8192 + a * 1024 + half * 512:8192 + a * 1024 + (half + 1) * 512],
                         start=(a == 0), stop=False)
            w.mm(pk, t.onesb[0:1, 0:128],
                 t.brow_sb[0:1, 3 * E + half * 512:3 * E + (half + 1) * 512],
                 start=False, stop=False)
            i = w.mm(pk, t.onesb[0:1, 0:128],
                     t.brow_lo[0:1, 3 * E + half * 512:3 * E + (half + 1) * 512],
                     start=False, stop=True)
            w.inc(i, "qk", 1)
            w.wait("qk", 'dve')
            mn = t.scrf[:, 0:512]
            mx = t.scrf[:, 512:1024]
            w.ts(mn, pk, 0.0, None, op0=ALU.min)
            w.ts(mx, pk, 0.0, None, op0=ALU.max)
            w.wait("dve", 'act')
            w.inc(w.act(t.scrf[:, 1024:1536], mn, AF.Exp), "acs", 1)
            w.wait("acs", 'dve')
            w.wait("rv", 'dve')
            w.tt(t.scrf[:, 1536:2048], t.scrf[:, 1024:1536], mx, op=ALU.add)
            w.ts(t.scrb[:, half * 512:(half + 1) * 512], t.scrf[:, 1536:2048], mcol)
            w.wait("dve", 'pe')
            pv = t.p_mis[:, 512:1024]
            i = None
            for a in range(NCHUNK):
                i = w.mm(pv, t.xn_sb[:, a * 520 + 1 + tl * 128:a * 520 + 1 + (tl + 1) * 128],
                         t.wbig[:, 16384 + a * 1024 + half * 512:16384 + a * 1024 + (half + 1) * 512],
                         start=(a == 0), stop=False)
            w.mm(pv, t.onesb[0:1, 0:128],
                 t.brow_sb[0:1, 4 * E + half * 512:4 * E + (half + 1) * 512],
                 start=False, stop=False)
            i = w.mm(pv, t.onesb[0:1, 0:128],
                     t.brow_lo[0:1, 4 * E + half * 512:4 * E + (half + 1) * 512],
                     start=False, stop=True)
            w.inc(i, "qk", 1)
            w.wait("qk", 'dve')
            w.copy(t.scrb[:, 1024 + half * 512:1024 + (half + 1) * 512], pv)
        w.wait("dve", 'pe')
        i = None
        for pp in range(NCHUNK):
            i = w.mm(t.p_qk_a[:, pp * 128:(pp + 1) * 128], t.scrb[:, pp * 128:(pp + 1) * 128],
                     t.scrb[:, 1024 + pp * 128:1024 + (pp + 1) * 128],
                     start=True, stop=True)
        w.inc(i, "rv", 1)
        w.wait("rv", 'dve')
        w.tt(t.kvacc_b[:, :], t.kvacc_b[:, :], t.p_qk_a[:, :], op=ALU.add)
    w.wait("dve", 'pe')


def _attnb_reduce(w, t):
    w.wait("dve", 'pool')
    w.inc(w.dma('pool', t.kv_in, t.kvacc_b[:, :]), "dst", 16)
    w.wait("dst", 'pool')
    w.inc(w.cc([t.kv_in], [t.kv_out]), "cc", 1)
    w.wait("cc", 'pool')
    w.inc(w.dma('pool', t.scrf[:, 0:1024], t.kv_out), "dst", 16)
    w.wait("dst", 'dve')
    i = None
    for a in range(NCHUNK):
        w.copy(t.kvblk_b[0:64, a * 128:a * 128 + 64], t.scrf[0:64, a * 128:a * 128 + 64])
        i = w.copy(t.kvblk_b[64:128, a * 128 + 64:a * 128 + 128],
                   t.scrf[64:128, a * 128 + 64:a * 128 + 128])
    w.wait("dve", 'pe')


def _attnb_out(w, t, qg):
    nq = QGS[qg]
    q0 = QG0[qg]
    w.wait("dst", 'sync')
    qk_tok = [None, None]
    for p in range(NCHUNK):
        sl = p % 2
        qp = t.qtile[:, sl * 512:sl * 512 + nq]
        w.wait("qk", 'sync', val=(qk_tok[sl] if qk_tok[sl] is not None else w.val("qk")))
        w.inc(w.dma('sync', qp, t.qb_hbm[p, :, q0:q0 + nq]), "dq", 16)
        w.wait("dq", 'pe')
        w.wait("dve", 'pe', val=w.val("dve") - 1 if p >= 2 else w.val("dve"))
        w.inc(w.mm(t.p_mis[:, sl * 512:sl * 512 + nq],
                   t.kvblk_b[:, p * 128:(p + 1) * 128], qp), "qk", 1)
        qk_tok[sl] = w.val("qk")
        w.wait("qk", 'dve')
        if p == 0:
            w.wait("pe", 'dve')
        w.copy(t.ofb[:, p * 512:p * 512 + nq], t.p_mis[:, sl * 512:sl * 512 + nq])
    w.barrier()
    w.wait("dve", 'pe')
    w.wait("dve", 'sync')
    w.wait("pe", 'sync')
    xgv = t.xg.rearrange("p (a c) -> p a c", c=513)[:, :, 1:1 + nq]
    w.inc(w.dma('sync', xgv, t.x_hbm[:, :, q0:q0 + nq].rearrange("a p c -> p a c")), "dx", 16)
    w.wait("dx", 'dve')
    wc_tok = [None, None]
    for fc in range(NCHUNK):
        wb = fc % 2
        w.wait("pe", 'pool', val=(wc_tok[wb] if wc_tok[wb] is not None else w.val("pe")))
        w.inc(w.dma('pool', t.wcol[:, wb * 1024:(wb + 1) * 1024].rearrange("p (a c) -> p a c", a=8),
                    t.w_bo[:, fc * 128:(fc + 1) * 128].rearrange("(a p) c -> p a c", p=128)),
              "dw", 16)
        w.wait("dw", 'pe')
        w.wait("dve", 'pe')
        i = None
        for a in range(NCHUNK):
            i = w.mm(t.p_kv[:, 0:nq], t.wcol[:, wb * 1024 + a * 128:wb * 1024 + (a + 1) * 128],
                     t.ofb[:, a * 512:a * 512 + nq], start=(a == 0), stop=(a == NCHUNK - 1))
        w.inc(i, "pe", 1)
        wc_tok[wb] = w.val("pe")
        w.wait("pe", 'dve')
        xc = t.xg[:, fc * 513 + 1: fc * 513 + 1 + nq]
        w.ts(t.scrf[:, 0:nq], t.p_kv[:, 0:nq], t.bchunk_sb[:, 16 + fc:17 + fc], op0=ALU.add)
        w.tt(xc, xc, t.scrf[:, 0:nq], op=ALU.add)
        w.barrier()
    sl_ = (qg % 2) * 8
    w.copy(t.stash[:, sl_:sl_ + 8].rearrange("p (a one) -> p a one", one=1),
           t.xg.rearrange("p (a c) -> p a c", c=513)[:, :, nq:nq + 1])
    w.wait("dve", 'pe')


def _output(w, t):
    ot_tok = [0, 0]
    ld_tok = [0, 0]
    for tt_ in range(NTT):
        lb = tt_ % 2
        xf = t.xfst[:, 0:1024] if lb == 0 else t.scrf[:, 0:1024]
        w.wait("pe", 'sync', val=ld_tok[lb])
        w.inc(w.dma('sync', xf.rearrange("p (a c) -> p a c", a=8),
                    t.x_hbm[:, :, tt_ * 128:(tt_ + 1) * 128].rearrange("a p c -> p a c")),
              "dx", 16)
        w.wait("dx", 'pe')
        w.wait("dve", 'pe')
        i = None
        for a in range(NCHUNK):
            i = w.trans(t.p_mis[:, a * 128:(a + 1) * 128], xf[:, a * 128:(a + 1) * 128],
                        t.eyef_sb[:, :])
        w.inc(i, "pe", 1)
        ld_tok[lb] = w.val("pe")
        w.wait("pe", 'dve')
        ob = tt_ % 2
        w.wait("oput", 'dve', val=ot_tok[ob])
        ot = t.xtok[:, ob * 1024:(ob + 1) * 1024]
        i = None
        for a in range(NCHUNK):
            i = w.copy(ot[:, a * 128:(a + 1) * 128], t.p_mis[:, a * 128:(a + 1) * 128])
        w.wait("dve", 'sync')
        w.inc(w.dma('sync', t.out_h[tt_ * 128:(tt_ + 1) * 128, :], ot), "oput", 16)
        ot_tok[ob] = w.val("oput")



def _host_inputs(inputs):
    x = np.ascontiguousarray(np.asarray(inputs["L0_attn_a_x"], dtype=np.float32))
    g = lambda k: np.ascontiguousarray(np.asarray(inputs[k], dtype=np.float32))
    shared = {
        "w_aq": g("L0_attn_a_Wq"), "w_ak": g("L0_attn_a_Wk"),
        "w_av": g("L0_attn_a_Wv"), "w_ao": g("L0_attn_a_Wo"),
        "w_fr": np.ascontiguousarray(g("L0_ffn_W_r").T),
        "w_fk": np.ascontiguousarray(g("L0_ffn_W_k").T),
        "w_fv": np.ascontiguousarray(g("L0_ffn_W_v").T),
        "w_bq": np.ascontiguousarray(g("L0_attn_b_Wq").T),
        "w_bk": np.ascontiguousarray(g("L0_attn_b_Wk").T),
        "w_bv": np.ascontiguousarray(g("L0_attn_b_Wv").T),
        "w_bo": np.ascontiguousarray(g("L0_attn_b_Wo").T),
    }
    b_rows = np.stack([g("L0_attn_a_bq"), g("L0_attn_a_bk"), g("L0_attn_a_bv"),
                       g("L0_attn_b_bk"), g("L0_attn_b_bv")]).astype(np.float32)
    b_chunk = np.zeros((128, 4 * NCHUNK), np.float32)
    b_chunk[:, 0:8] = g("L0_attn_a_bo").reshape(NCHUNK, 128).T
    b_chunk[:, 8:16] = g("L0_attn_b_bq").reshape(NCHUNK, 128).T
    b_chunk[:, 16:24] = g("L0_attn_b_bo").reshape(NCHUNK, 128).T
    mu_chunk = np.zeros((128, 2 * NCHUNK), np.float32)
    mu_chunk[:, 0:8] = g("L0_ffn_mu_r").reshape(NCHUNK, 128).T
    mu_chunk[:, 8:16] = g("L0_ffn_mu_k").reshape(NCHUNK, 128).T
    shared["b_rows"] = np.ascontiguousarray(b_rows)
    shared["b_chunk"] = np.ascontiguousarray(b_chunk)
    shared["mu_chunk"] = np.ascontiguousarray(mu_chunk)
    shared["eyef"] = np.eye(128, dtype=np.float32)
    shared["eyeb"] = np.eye(128, dtype=np.float32).astype(ml_dtypes.bfloat16)
    in_maps = []
    for c in range(8):
        b, h = c // 2, c % 2
        m = dict(shared)
        m["xq"] = np.ascontiguousarray(x[b, 3968 * h: 3968 * h + TOK, :])
        m["xfull"] = np.ascontiguousarray(x[b])
        mk = np.zeros(TOK, np.float32)
        if h == 0:
            mk[0:4096] = 1.0
        else:
            mk[128:TOK] = 1.0
        m["maskv"] = np.ascontiguousarray(mk.reshape(NTT, 128).T)
        in_maps.append(m)
    return in_maps


def run_staged(inputs, stop_stage, cores=None):
    nc = build_program(stop_stage=stop_stage)
    in_maps = _host_inputs(inputs)
    cores = cores if cores is not None else list(range(8))
    res = run_bass_kernel_spmd(nc, [in_maps[c] for c in cores], core_ids=cores)
    return [r["dbg"] for r in res.results]


def kernel(**inputs):
    nc = build_program()
    in_maps = _host_inputs(inputs)
    res = run_bass_kernel_spmd(nc, in_maps, core_ids=list(range(8)))
    y = np.empty((B, S, E), np.float32)
    for c in range(8):
        b, h = c // 2, c % 2
        o = res.results[c]["out"]
        if h == 0:
            y[b, 0:4096] = o[0:4096]
        else:
            y[b, 4096:8192] = o[128:TOK]
    return y


# version 14
# speedup vs baseline: 2.1775x; 1.0067x over previous
import dataclasses
import numpy as np
import ml_dtypes

import concourse.bass as bass
import concourse.mybir as mybir
from concourse.bass_utils import run_bass_kernel_spmd

F32 = mybir.dt.float32
BF16 = mybir.dt.bfloat16
AF = mybir.ActivationFunctionType
ALU = mybir.AluOpType

E = 1024
H = 16
D = 64
S = 8192
B = 4
NCHUNK = 8
TOK = 4224
NTT = 33
NKG = 64
QGS = [512] * 8 + [128]
QG0 = [sum(QGS[:i]) for i in range(len(QGS))]
NQG = len(QGS)
EPS_LN = 1e-5
EPS_NRM = 1e-8
EPS_ATT = 1e-6
NCONST = -3.1 * S + EPS_ATT

SEM_NAMES = ["dx", "dk", "dq", "dw", "dst", "dsx", "qk", "rcp", "rv", "pe", "dve",
             "acs", "cc", "oput"]


def _act_raw(nc, eng, out, in_, func, bias=0.0, scale=1.0):
    bi = eng.activation(out=out, in_=in_, func=AF.Exp, bias=bias, scale=scale)
    i2 = dataclasses.replace(bi.ins, func=func)
    nc.register_instruction(i2, overwrite=True)
    insts = nc.cur_bb.bb.instructions
    assert insts[-1] is bi.ins
    insts[-1] = i2
    return bass.BassInstruction(i2)


class Walk:

    def __init__(self, nc, kind, eng, sems):
        self.nc = nc
        self.kind = kind
        self.eng = eng
        self.sems = sems
        self.c = {s: 0 for s in SEM_NAMES}

    def inc(self, ins, name, n=1):
        self.c[name] += n
        if ins is not None:
            ins.then_inc(self.sems[name], n)

    def wait(self, name, kind, val=None):
        if self.kind == kind:
            v = self.c[name] if val is None else val
            if v > 0:
                self.eng.wait_ge(self.sems[name], v)

    def val(self, name):
        return self.c[name]

    _OWNER = {"qk": 'pe', "rv": 'pe', "pe": 'pe', "rcp": 'act', "acs": 'act',
              "dve": 'dve', "dk": 'sync', "dq": 'sync', "dx": 'sync', "dsx": 'sync',
              "oput": 'sync', "dw": 'pool', "dst": 'pool', "cc": 'pool'}

    def barrier(self):
        for s in SEM_NAMES:
            own = self._OWNER[s]
            for k in ('pe', 'act', 'dve', 'sync', 'pool'):
                if k != own:
                    self.wait(s, k)

    def mm(self, out, lhsT, rhs, start=True, stop=True, tp=None):
        if self.kind != 'pe':
            return None
        return self.eng.matmul(out, lhsT=lhsT, rhs=rhs, start=start, stop=stop,
                               tile_position=tp)

    def trans(self, out, in_, ident):
        if self.kind != 'pe':
            return None
        return self.eng.transpose(out, in_, ident)

    def act(self, out, in_, func, bias=0.0, scale=1.0, accum_out=None):
        if self.kind != 'act':
            return None
        if func in (AF.Reciprocal, AF.Rsqrt):
            return _act_raw(self.nc, self.eng, out, in_, func, bias, scale)
        kw = {} if accum_out is None else {"accum_out": accum_out}
        return self.eng.activation(out=out, in_=in_, func=func, bias=bias,
                                   scale=scale, **kw)

    def _dve(self, f):
        if self.kind != 'dve':
            self.c["dve"] += 1
            return None
        self.eng.wait_ge(self.sems["dve"], self.c["dve"])
        ins = f()
        self.c["dve"] += 1
        ins.then_inc(self.sems["dve"], 1)
        return ins

    def ts(self, out, in0, s1, s2=None, op0=ALU.mult, op1=ALU.bypass):
        return self._dve(lambda: self.eng.tensor_scalar(
            out=out, in0=in0, scalar1=s1, scalar2=s2, op0=op0, op1=op1))

    def tt(self, out, in0, in1, op=ALU.add):
        return self._dve(lambda: self.eng.tensor_tensor(out=out, in0=in0, in1=in1, op=op))

    def stt(self, out, in0, scalar, in1, op0, op1):
        return self._dve(lambda: self.eng.scalar_tensor_tensor(
            out=out, in0=in0, scalar=scalar, in1=in1, op0=op0, op1=op1))

    def copy(self, out, in_):
        return self._dve(lambda: self.eng.tensor_copy(out=out, in_=in_))

    def memset(self, ap, v):
        return self._dve(lambda: self.eng.memset(ap, v))

    def red(self, out, in_, op=ALU.add):
        return self._dve(lambda: self.eng.tensor_reduce(out, in_, mybir.AxisListType.X, op))

    def vrecip_fast(self, out, in_):
        return self._dve(lambda: self.eng.reciprocal_approx_fast(out=out, in_=in_))

    def vrecip(self, out, in_):
        def f():
            if out.dtype == BF16:
                with self.nc.allow_low_precision(reason="bf16 store of fp32 reciprocal"):
                    return self.eng.reciprocal(out=out, in_=in_)
            return self.eng.reciprocal(out=out, in_=in_)
        return self._dve(f)

    def dma(self, queue, out, in_):
        if self.kind != queue:
            return None
        return self.eng.dma_start(out=out, in_=in_)

    def cc(self, ins_, outs_):
        if self.kind != 'pool':
            return None
        return self.eng.collective_compute(
            "AllReduce", ALU.add, replica_groups=[[0, 1], [2, 3], [4, 5], [6, 7]],
            ins=ins_, outs=outs_)


class L:
    pass


def build_program(stop_stage=None):
    nc = bass.Bass("TRN2", target_bir_lowering=False, debug=False)
    t = L()

    t.xq = nc.dram_tensor("xq", [TOK, E], F32, kind="ExternalInput").ap()
    t.xfull = nc.dram_tensor("xfull", [S, E], F32, kind="ExternalInput").ap()
    for nm in ["w_aq", "w_ak", "w_av", "w_ao", "w_fr", "w_fk", "w_fv",
               "w_bq", "w_bk", "w_bv", "w_bo"]:
        setattr(t, nm, nc.dram_tensor(nm, [E, E], F32, kind="ExternalInput").ap())
    t.b_rows = nc.dram_tensor("b_rows", [5, E], F32, kind="ExternalInput").ap()
    t.b_chunk = nc.dram_tensor("b_chunk", [128, 4 * NCHUNK], F32, kind="ExternalInput").ap()
    t.mu_chunk = nc.dram_tensor("mu_chunk", [128, 2 * NCHUNK], F32, kind="ExternalInput").ap()
    t.eyef = nc.dram_tensor("eyef", [128, 128], F32, kind="ExternalInput").ap()
    t.eyeb = nc.dram_tensor("eyeb", [128, 128], BF16, kind="ExternalInput").ap()
    t.maskv = nc.dram_tensor("maskv", [128, NTT], F32, kind="ExternalInput").ap()

    t.knT_hbm = nc.dram_tensor("knT_hbm", [NKG, 128, NCHUNK * 128], BF16, kind="Internal").ap()
    t.vk_hbm = nc.dram_tensor("vk_hbm", [NKG, 128, H * 65], BF16, kind="Internal").ap()
    t.qn_hbm = nc.dram_tensor("qn_hbm", [NCHUNK, 128, TOK], BF16, kind="Internal").ap()
    t.qb_hbm = nc.dram_tensor("qb_hbm", [NCHUNK, 128, TOK], BF16, kind="Internal").ap()
    t.kv_in = nc.dram_tensor("kv_in", [128, 1024], F32, kind="Internal").ap()
    t.kv_out = nc.dram_tensor("kv_out", [128, 1024], F32, kind="Internal").ap()
    t.x_hbm = nc.dram_tensor("x_hbm", [NCHUNK, 128, TOK], F32, kind="Internal").ap()
    t.out_h = nc.dram_tensor("out", [TOK, E], F32, kind="ExternalOutput").ap()
    t.dbg = None
    if stop_stage is not None:
        t.dbg = nc.dram_tensor("dbg", [128, NCHUNK * TOK], F32, kind="ExternalOutput").ap()

    import contextlib
    with contextlib.ExitStack() as st:
        def sb(name, shape, dt):
            return st.enter_context(nc.sbuf_tensor(name, shape, dt))

        def ps(name, shape):
            return st.enter_context(nc.psum_tensor(name, shape, F32))

        t.xg = sb("xg", [128, NCHUNK * 513], F32)
        t.xfst = sb("xfst", [128, 1024], F32)
        t.consts = sb("consts", [128, 16], F32)
        t.onesb = sb("onesb", [128, 576], BF16)
        t.eyef_sb = sb("eyef_sb", [128, 128], F32)
        t.eyeb_sb = sb("eyeb_sb", [128, 128], BF16)
        t.maskv_sb = sb("maskv_sb", [128, NTT], F32)
        t.brow_sb = sb("brow_sb", [1, 5 * E], BF16)
        t.brow_lo = sb("brow_lo", [1, 5 * E], BF16)
        t.bchunk_sb = sb("bchunk_sb", [128, 4 * NCHUNK], F32)
        t.mu_sb = sb("mu_sb", [128, 2 * NCHUNK], F32)
        t.ksum_bf = sb("ksum_bf", [128, NCHUNK], BF16)
        t.vsum_sb = sb("vsum_sb", [64, H], F32)
        t.kvblk_a = sb("kvblk_a", [128, NCHUNK * 128], BF16)
        t.kvblk_b = sb("kvblk_b", [128, NCHUNK * 128], BF16)
        t.kvacc_b = sb("kvacc_b", [128, 1024], F32)
        t.wbig = sb("wbig", [128, 3 * 8192], BF16)
        t.wcol = sb("wcol", [128, 2 * 1024], BF16)
        t.stage = sb("stage", [128, 2048 + H * 65], BF16)
        t.xtok = sb("xtok", [128, 2 * 1024], F32)
        t.scrf = sb("scrf", [128, 3 * 1024], F32)
        t.scrb = sb("scrb", [128, 4 * 1024], BF16)
        t.sml = sb("sml", [128, 96], F32)
        t.stash = sb("stash", [128, 16], F32)
        t.kvacc_a = sb("kvacc_a", [128, 1024], F32)
        t.kstream = sb("kstream", [128, 2 * 1024], BF16)
        t.vstream = sb("vstream", [128, 2 * 1040], BF16)
        t.qtile = sb("qtile", [128, 2 * 512], BF16)
        t.rbuf = sb("rbuf", [128, 2 * 1024], BF16)
        t.ut = sb("ut", [64, 2 * 512], F32)
        t.invn = sb("invn", [65, 1024], BF16)
        t.ofb = sb("ofb", [128, NCHUNK * 512], BF16)
        t.xn_sb = sb("xn_sb", [128, NCHUNK * 520], BF16)
        t.ffb = sb("ffb", [128, 2 * NCHUNK * 512], BF16)
        t.kkb = sb("kkb", [128, NCHUNK * 512], BF16)

        t.p_qk_a = ps("p_qk_a", [128, 1024])
        t.p_qk_b = ps("p_qk_b", [128, 1024])
        t.p_kv = ps("p_kv", [128, 1024])
        t.p_mis = ps("p_mis", [128, 1024])

        for i, v in enumerate([1.1, -1.0, 0.0, 1.0, EPS_LN, EPS_NRM, 4.41, -3.1,
                               1.0 / E, NCONST, 0.5, 2.0]):
            nc.const_aps.register(F32, v, t.consts[:, i:i + 1])

        sems = {s: st.enter_context(nc.semaphore(s)) for s in SEM_NAMES}

        with nc.Block() as block:
            @block.sync
            def _(eng):
                _emit(Walk(nc, 'sync', eng, sems), t, stop_stage)

            @block.tensor
            def _(eng):
                _emit(Walk(nc, 'pe', eng, sems), t, stop_stage)

            @block.scalar
            def _(eng):
                _emit(Walk(nc, 'act', eng, sems), t, stop_stage)

            @block.vector
            def _(eng):
                _emit(Walk(nc, 'dve', eng, sems), t, stop_stage)

            @block.gpsimd
            def _(eng):
                _emit(Walk(nc, 'pool', eng, sems), t, stop_stage)

    return nc



def _emit(w: Walk, t, stop_stage):
    _setup(w, t)
    if stop_stage == 'p0':
        _prep(w, t, stop_p0=True)
        _fin(w, t); return
    _prep(w, t)
    w.barrier()
    if stop_stage == 'prep':
        _dump(w, t); _fin(w, t); return
    if stop_stage == 'a0':
        _attn_a(w, t, 0)
        _dump_a0(w, t); _fin(w, t); return
    t.ot_tok = [0, 0]
    _load3(w, t, [t.w_fr, t.w_fk, t.w_fv])
    for qg in range(NQG):
        _attn_a(w, t, qg)
        w.barrier()
        _ffn(w, t, qg, resident=True)
        w.barrier()
    if stop_stage == 'x2':
        _dump(w, t); _fin(w, t); return
    _load3(w, t, [t.w_bq, t.w_bk, t.w_bv])
    for qg in range(NQG):
        _attnb_pre(w, t, qg)
        w.barrier()
        if stop_stage == 'b0' and qg == 0:
            w.wait("dve", 'pool')
            w.wait("dst", 'pool')
            d = t.dbg
            w.inc(w.dma('pool', d[:, 0:1024], t.kvacc_b[:, :]), "oput", 16)
            w.inc(w.dma('pool', d[:, 1024:2048], t.kvblk_b[:, :]), "oput", 16)
            w.inc(w.dma('pool', d[:, 2048:2560], t.qb_hbm[0, :, 0:512]), "oput", 16)
            w.inc(w.dma('pool', d[:, 2560:3584], t.scrb[:, 0:1024]), "oput", 16)
            w.inc(w.dma('pool', d[:, 3584:4608], t.scrb[:, 1024:2048]), "oput", 16)
            w.inc(w.dma('pool', d[:, 4608:5128], t.xn_sb[:, 0:520]), "oput", 16)
            _fin(w, t); return
    w.barrier()
    _attnb_reduce(w, t)
    w.barrier()
    if stop_stage == 'b1':
        w.wait("dve", 'pool')
        d = t.dbg
        w.inc(w.dma('pool', d[:, 0:1024], t.kvacc_b[:, :]), "oput", 16)
        w.inc(w.dma('pool', d[:, 1024:2048], t.kvblk_b[:, :]), "oput", 16)
        _fin(w, t); return
    _load3(w, t, [t.w_fr, t.w_fk, t.w_fv])
    for qg in range(NQG):
        _attnb_out(w, t, qg)
        w.barrier()
        _ffn(w, t, qg, resident=True, out_fuse=True)
        w.barrier()
    _fin(w, t)


def _dump(w, t):
    w.wait("dve", 'sync')
    w.wait("pe", 'sync')
    w.wait("dst", 'sync')
    w.wait("dsx", 'sync')
    for a in range(NCHUNK):
        w.inc(w.dma('sync', t.dbg[:, a * TOK:(a + 1) * TOK], t.x_hbm[a]), "oput", 16)


def _dump_a0(w, t):
    w.wait("dve", 'pool')
    w.wait("pe", 'pool')
    w.wait("acs", 'pool')
    w.wait("dst", 'pool')
    d = t.dbg
    w.inc(w.dma('pool', d[:, 0:4096], t.ofb[:, :]), "oput", 16)
    w.inc(w.dma('pool', d[0:65, 4096:5120], t.invn[:, :]), "oput", 16)
    w.inc(w.dma('pool', d[0:64, 5120:6144], t.ut[:, :]), "oput", 16)
    w.inc(w.dma('pool', d[64:65, 6144:7168], t.scrf[64:65, 0:1024]), "oput", 16)
    w.inc(w.dma('pool', d[:, 7168:7680], t.qn_hbm[0, :, 0:512]), "oput", 16)
    w.inc(w.dma('pool', d[:, 7680:7808], t.knT_hbm[0, :, 0:128]), "oput", 16)
    w.inc(w.dma('pool', d[:, 7808:7938], t.vk_hbm[0, :, 0:130]), "oput", 16)
    w.inc(w.dma('pool', d[:, 7938:7946], t.ksum_bf[:, :]), "oput", 16)
    w.inc(w.dma('pool', d[0:64, 7946:7962], t.vsum_sb[:, :]), "oput", 16)
    w.inc(w.dma('pool', d[:, 8000:9024], t.kvblk_a[:, :]), "oput", 16)
    w.inc(w.dma('pool', d[:, 9024:9056], t.bchunk_sb[:, :]), "oput", 16)
    w.copy(t.scrf[64:65, 1024:2048], t.p_kv[64:65, 0:1024])
    w.wait("dve", 'pool')
    w.inc(w.dma('pool', d[64:65, 9056:10080], t.scrf[64:65, 1024:2048]), "oput", 16)
    w.inc(w.dma('pool', d[:, 10080:12128], t.rbuf[:, :]), "oput", 16)


def _fin(w, t):
    if w.kind == 'pool':
        w.eng.wait_ge(w.sems["oput"], w.c["oput"])


def _setup(w, t):
    for i, v in enumerate([1.1, -1.0, 0.0, 1.0, EPS_LN, EPS_NRM, 4.41, -3.1,
                           1.0 / E, NCONST, 0.5, 2.0]):
        w.memset(t.consts[:, i:i + 1], v)
    w.memset(t.onesb[:, :], 1.0)
    w.memset(t.kvacc_b[:, :], 0.0)
    w.memset(t.kvacc_a[:, :], 0.0)
    w.memset(t.sml[:, 64:88], 0.0)
    w.memset(t.kvblk_a[:, :], 0.0)
    w.memset(t.kvblk_b[:, :], 0.0)
    w.inc(w.dma('sync', t.eyef_sb[:, :], t.eyef), "dx", 16)
    w.inc(w.dma('sync', t.eyeb_sb[:, :], t.eyeb), "dx", 16)
    w.inc(w.dma('sync', t.maskv_sb[:, :], t.maskv), "dx", 16)
    w.inc(w.dma('sync', t.bchunk_sb[:, :], t.b_chunk), "dx", 16)
    w.inc(w.dma('sync', t.mu_sb[:, :], t.mu_chunk), "dx", 16)
    for r_ in range(5):
        w.wait("dve", 'pool')
        w.inc(w.dma('pool', t.scrf[0:1, 0:E], t.b_rows[r_:r_ + 1, :]), "dw", 16)
        w.wait("dw", 'dve')
        w.ts(t.brow_sb[0:1, r_ * E:(r_ + 1) * E], t.scrf[0:1, 0:E], 1.0)
        w.ts(t.scrf[0:1, E:2 * E], t.brow_sb[0:1, r_ * E:(r_ + 1) * E], -1.0)
        w.tt(t.brow_lo[0:1, r_ * E:(r_ + 1) * E], t.scrf[0:1, E:2 * E],
             t.scrf[0:1, 0:E], op=ALU.add)


def _load3(w, t, srcs):
    w.wait("pe", 'pool')
    for i, src in enumerate(srcs):
        w.inc(w.dma('pool', t.wbig[:, i * 8192:(i + 1) * 8192].rearrange("p (a d) -> p a d", a=8),
                    src.rearrange("(a p) d -> p a d", p=128)), "dw", 16)


def _ln_tok(w, t, xt, xn_out):
    m = t.sml[:, 0:1]
    v = t.sml[:, 1:2]
    r = t.sml[:, 2:3]
    sq = t.scrf[:, E:2 * E]
    w.red(m, xt)
    w.ts(m, m, -1.0 / E)
    w.wait("dve", 'act')
    w.wait("acs", 'act')
    w.inc(w.act(sq, xt, AF.Square, bias=m, accum_out=v), "acs", 1)
    w.wait("acs", 'act')
    w.inc(w.act(r, v, AF.Rsqrt, bias=EPS_LN, scale=1.0 / E), "acs", 1)
    w.wait("acs", 'dve')
    w.ts(xn_out, xt, m, r, op0=ALU.add, op1=ALU.mult)


def _norm_heads(w, t, src_f32, dst_bf):
    sq = t.scrf[:, E:2 * E]
    ss = t.sml[:, 16:32]
    src3 = src_f32.rearrange("p (h d) -> p h d", h=H)
    w.wait("dve", 'act')
    w.wait("acs", 'act')
    w.inc(w.act(sq, src_f32, AF.Square), "acs", 1)
    w.wait("acs", 'dve')
    w.red(ss, sq.rearrange("p (h d) -> p h d", h=H))
    w.wait("dve", 'act')
    w.inc(w.act(t.sml[:, 32:48], ss, AF.Rsqrt), "acs", 1)
    w.wait("acs", 'dve')
    w.tt(t.sml[:, 32:48], t.sml[:, 32:48], ss, op=ALU.mult)
    w.ts(t.sml[:, 32:48], t.sml[:, 32:48], EPS_NRM, op0=ALU.add)
    w.vrecip(ss, t.sml[:, 32:48])
    w.tt(dst_bf.rearrange("p (h d) -> p h d", h=H), src3,
         ss.rearrange("p (h o) -> p h o", o=1).broadcast_to([128, H, D]),
         op=ALU.mult)


def _transpose8(w, t, src_bf, dst_bf, pt, dve_val=None):
    w.wait("dve", 'pe', val=(dve_val if dve_val is not None else w.val("dve")))
    i = None
    for a in range(NCHUNK):
        i = w.trans(pt[:, a * 64:(a + 1) * 64].bitcast(BF16),
                    src_bf[:, a * 128:(a + 1) * 128], t.eyeb_sb[:, :])
    w.inc(i, "pe", 1)
    w.wait("pe", 'dve')
    w.copy(dst_bf[:, 0:1024], pt[:, 0:512].bitcast(BF16))


def _proj_tok(w, t, xnT, wbase, brow_idx, pdst):
    last = None
    for half in range(2):
        po = pdst[:, half * 512:(half + 1) * 512]
        for a in range(NCHUNK):
            w.mm(po, xnT[:, a * 128:(a + 1) * 128],
                 t.wbig[:, wbase + a * 1024 + half * 512: wbase + a * 1024 + (half + 1) * 512],
                 start=(a == 0), stop=False)
        w.mm(po, t.onesb[0:1, 0:128],
             t.brow_sb[0:1, brow_idx * E + half * 512: brow_idx * E + (half + 1) * 512],
             start=False, stop=False)
        last = w.mm(po, t.onesb[0:1, 0:128],
                    t.brow_lo[0:1, brow_idx * E + half * 512: brow_idx * E + (half + 1) * 512],
                    start=False, stop=True)
    w.inc(last, "pe", 1)


def _prep(w, t, stop_p0=False):
    _load3(w, t, [t.w_ak, t.w_av, t.w_aq])
    w.wait("dw", 'pe')
    w.wait("dx", 'dve')
    w.wait("dve", 'pe')
    ONES_COL = t.onesb[:, 0:1]

    xtok_tok = [0, 0]
    xtok_pe = [0, 0]
    tile_pe = [0, 0]
    flush_pe = [None]
    st_tok = {"kn": 0, "vk": 0, "qn": 0}
    NTOT = NKG + NTT
    ln_dve = {}

    def _load_ln(tc2):
        if tc2 >= NTOT:
            return
        full2 = tc2 < NKG
        buf2 = tc2 % 2
        xt2 = t.xtok[:, buf2 * 1024:(buf2 + 1) * 1024]
        src2 = (t.xfull[tc2 * 128:(tc2 + 1) * 128, :] if full2
                else t.xq[(tc2 - NKG) * 128:(tc2 - NKG + 1) * 128, :])
        w.wait("dve", 'sync', val=xtok_tok[buf2])
        w.wait("pe", 'sync', val=xtok_pe[buf2])
        w.inc(w.dma('sync', xt2, src2), "dx", 16)
        w.wait("dx", 'dve')
        nb2 = t.scrb if tc2 % 2 == 0 else t.ffb
        w.wait("pe", 'dve', val=tile_pe[tc2 % 2])
        _ln_tok(w, t, xt2, nb2[:, 0:1024])
        ln_dve[tc2] = w.val("dve")
        xtok_tok[buf2] = w.val("dve")

    def _flush():
        if flush_pe[0] is not None:
            w.wait("pe", 'dve', val=flush_pe[0])
            w.tt(t.kvacc_a[:, :], t.kvacc_a[:, :], t.p_kv[:, :], op=ALU.add)
            w.tt(t.sml[:, 64:72], t.sml[:, 64:72], t.p_qk_b[:, 0:8], op=ALU.add)
            w.tt(t.sml[0:64, 72:88], t.sml[0:64, 72:88], t.p_qk_b[0:64, 16:32], op=ALU.add)
            flush_pe[0] = None

    _load_ln(0)
    for tc in range(NTOT):
        full = tc < NKG
        buf = tc % 2
        xt = t.xtok[:, buf * 1024:(buf + 1) * 1024]
        nbase = t.scrb if tc % 2 == 0 else t.ffb
        xnb = nbase[:, 0:1024]

        if not full:
            tt_ = tc - NKG
            w.wait("dve", 'pe')
            i = None
            for a in range(NCHUNK):
                i = w.trans(t.p_mis[:, a * 128:(a + 1) * 128], xt[:, a * 128:(a + 1) * 128],
                            t.eyef_sb[:, :])
            w.inc(i, "pe", 1)
            xtok_pe[buf] = w.val("pe")
            w.wait("pe", 'dve')
            w.wait("dst", 'dve', val=st_tok.get("xf", 0))
            w.copy(t.xfst[:, 0:1024], t.p_mis[:, 0:1024])
            w.wait("dve", 'pool')
            w.inc(w.dma('pool',
                        t.x_hbm[:, :, tt_ * 128:(tt_ + 1) * 128].rearrange("a p c -> p a c"),
                        t.xfst.rearrange("p (a c) -> p a c", a=8)), "dst", 16)
            st_tok["xf"] = w.val("dst")
            dve_tp = w.val("dve")
        else:
            dve_tp = ln_dve[tc]

        xnT = nbase[:, 1024:2048]
        _transpose8(w, t, xnb, xnT, t.p_qk_a, dve_val=dve_tp)
        w.wait("dve", 'pe')

        if full:
            _proj_tok(w, t, xnT, 0, 1, t.p_mis)
            _load_ln(tc + 1)
            _flush()
            w.wait("pe", 'dve')
            w.wait("pe", 'act')
            kn_tok = t.scrb[:, 2048:3072]
            _norm_heads(w, t, t.p_mis[:, 0:1024], kn_tok)
            if stop_p0:
                w.wait("dve", 'pool')
                w.wait("acs", 'pool')
                d = t.dbg
                w.inc(w.dma('pool', d[:, 0:1024], t.scrb[:, 0:1024]), "oput", 16)
                w.inc(w.dma('pool', d[:, 1024:2048], t.scrb[:, 1024:2048]), "oput", 16)
                w.inc(w.dma('pool', d[:, 2048:3072], t.p_mis[:, 0:1024]), "oput", 16)
                w.inc(w.dma('pool', d[:, 3072:4096], kn_tok), "oput", 16)
                w.inc(w.dma('pool', d[:, 4096:4160], t.sml[:, :]), "oput", 16)
                w.inc(w.dma('pool', d[:, 4160:4288], t.xtok[:, 0:128]), "oput", 16)
                w.inc(w.dma('pool', d[:, 4288:6336], t.wbig[:, 0:2048]), "oput", 16)
                w.inc(w.dma('pool', d[0:1, 6336:11456], t.brow_sb[:, :]), "oput", 16)
                w.inc(w.dma('pool', d[0:1, 11456:16576], t.brow_lo[:, :]), "oput", 16)
                return
            w.wait("dve", 'pe')
            _proj_tok(w, t, xnT, 8192, 2, t.p_mis)
            w.wait("pe", 'dve')
            w.wait("pe", 'act')
            v_tok = t.scrb[:, 3072:4096]
            w.wait("acs", 'act')
            w.inc(w.act(v_tok, t.p_mis[:, 0:1024], AF.Copy), "acs", 1)
            w.wait("acs", 'dve')
            v_lo = nbase[:, 0:1024]
            w.ts(t.scrf[:, 0:1024], v_tok, -1.0)
            w.tt(v_lo, t.scrf[:, 0:1024], t.p_mis[:, 0:1024], op=ALU.add)
            w.wait("dst", 'dve', val=st_tok["vk"])
            vkst = t.stage[:, 2048:2048 + H * 65]
            vk3 = vkst.rearrange("p (h e) -> p h e", h=H)
            if tc == 0:
                w.memset(vk3[:, :, 64:65], 1.0)
            w.copy(vk3[:, :, 0:64], v_tok.rearrange("p (h d) -> p h d", h=H))
            w.wait("dst", 'dve', val=st_tok["kn"])
            knst = t.stage[:, 1024:2048]
            _transpose8(w, t, kn_tok, knst, t.p_qk_a)
            w.wait("dve", 'pe')
            w.wait("acs", 'pe')
            for a in range(NCHUNK):
                w.mm(t.p_kv[:, a * 128:(a + 1) * 128], kn_tok[:, a * 128:(a + 1) * 128],
                     v_tok[:, a * 128:(a + 1) * 128], start=True, stop=True)
            for a in range(NCHUNK):
                w.mm(t.p_qk_b[:, a:a + 1], kn_tok[:, a * 128:(a + 1) * 128], ONES_COL,
                     start=True, stop=True)
            i = None
            for hh in range(H):
                w.mm(t.p_qk_b[0:64, 16 + hh:17 + hh], v_tok[:, hh * 64:(hh + 1) * 64],
                     ONES_COL, start=True, stop=False)
                i = w.mm(t.p_qk_b[0:64, 16 + hh:17 + hh], v_lo[:, hh * 64:(hh + 1) * 64],
                         ONES_COL, start=False, stop=True)
            w.inc(i, "pe", 1)
            flush_pe[0] = w.val("pe")
            w.wait("dve", 'pool')
            w.wait("pe", 'pool', val=0)
            w.inc(w.dma('pool', t.knT_hbm[tc], knst), "dst", 16)
            st_tok["kn"] = w.val("dst")
            w.inc(w.dma('pool', t.vk_hbm[tc], vkst), "dst", 16)
            st_tok["vk"] = w.val("dst")
        else:
            _proj_tok(w, t, xnT, 16384, 0, t.p_mis)
            _load_ln(tc + 1)
            _flush()
            w.wait("pe", 'dve')
            w.wait("pe", 'act')
            qn_tok = t.scrb[:, 2048:3072]
            _norm_heads(w, t, t.p_mis[:, 0:1024], qn_tok)
            w.wait("dst", 'dve', val=st_tok["qn"])
            qnst = t.stage[:, 0:1024]
            _transpose8(w, t, qn_tok, qnst, t.p_qk_a)
            tt_ = tc - NKG
            w.wait("dve", 'pool')
            w.inc(w.dma('pool',
                        t.qn_hbm[:, :, tt_ * 128:(tt_ + 1) * 128].rearrange("a p c -> p a c"),
                        qnst.rearrange("p (a c) -> p a c", a=8)), "dst", 16)
            st_tok["qn"] = w.val("dst")
        tile_pe[tc % 2] = w.val("pe")

    _flush()

    w.wait("pe", 'dve')
    kb3 = t.kvblk_a.rearrange("p (a c) -> p a c", a=NCHUNK)
    ka3 = t.kvacc_a.rearrange("p (a c) -> p a c", a=NCHUNK)
    w.copy(kb3[0:64, :, 0:64], ka3[0:64, :, 0:64])
    w.copy(kb3[64:128, :, 64:128], ka3[64:128, :, 64:128])
    w.copy(t.ksum_bf[:, :], t.sml[:, 64:64 + NCHUNK])
    w.copy(t.vsum_sb[:, :], t.sml[0:64, 72:72 + H])
    w.ts(t.vsum_sb[:, :], t.vsum_sb[:, :], 3.1)


def _rb_slot(t, kg):
    s = kg % 4
    if s < 2:
        return t.rbuf[:, s * 1024:(s + 1) * 1024]
    return t.scrb[:, s * 1024:(s + 1) * 1024]


def _emit_rv(w, t, qg, kg, nq):
    if kg < 0:
        return
    oo = kg // 8
    kk = kg % 8
    ob = oo % 2
    vk_e = t.vstream[:, ob * 1040 + kk * 130: ob * 1040 + kk * 130 + 65]
    vk_o = t.vstream[:, ob * 1040 + kk * 130 + 65: ob * 1040 + kk * 130 + 130]
    rb_ = _rb_slot(t, kg)
    if kg == 0 and t.fin_dve is not None:
        w.wait("dve", 'pe', val=t.fin_dve)
        t.fin_dve = None
    rs_, rv_ = t.rcp_of[kg]
    w.wait(rs_, 'pe', val=rv_)
    w.mm(t.p_kv[0:65, 0:nq], vk_e, rb_[:, 0:nq], start=(kg == 0), stop=(kg == NKG - 1))
    i = w.mm(t.p_kv[0:65, 512:512 + nq], vk_o, rb_[:, 512:512 + nq],
             start=(kg == 0), stop=(kg == NKG - 1))
    w.inc(i, "rv", 1)


def _fin_pair(w, t, qg, p, nq):
    qt = t.qtile[:, (p % 2) * 512:(p % 2) * 512 + nq]
    w.wait("rv", 'dve')
    ue = t.ut[:, 0:nq]
    uo = t.ut[:, 512:512 + nq]
    w.copy(ue, t.p_kv[0:64, 0:nq])
    w.copy(uo, t.p_kv[0:64, 512:512 + nq])
    w.wait("rv", 'pe')
    w.mm(t.p_mis[64:65, 0:nq], t.ksum_bf[0:64, p:p + 1], qt[0:64, :], tp=(0, 64))
    i = w.mm(t.p_mis[64:65, 512:512 + nq], t.ksum_bf[64:128, p:p + 1],
             qt[64:128, :], tp=(64, 64))
    w.inc(i, "pe", 1)
    w.wait("pe", 'dve')
    nr_e = t.scrf[64:65, 0:nq]
    nr_o = t.scrf[64:65, 512:512 + nq]
    w.copy(nr_e, t.p_kv[64:65, 0:nq])
    w.copy(nr_o, t.p_kv[64:65, 512:512 + nq])
    t.fin_dve = w.val("dve")
    w.stt(nr_e, nr_e, 4.41, t.p_mis[64:65, 0:nq], ALU.mult, ALU.subtract)
    w.stt(nr_o, nr_o, 4.41, t.p_mis[64:65, 512:512 + nq], ALU.mult, ALU.subtract)
    w.ts(nr_e, nr_e, NCONST, op0=ALU.add)
    w.ts(nr_o, nr_o, NCONST, op0=ALU.add)
    w.wait("dve", 'act')
    w.wait("pe", 'act')
    w.act(t.invn[64:65, 0:nq], nr_e, AF.Reciprocal)
    w.inc(w.act(t.invn[64:65, 512:512 + nq], nr_o, AF.Reciprocal), "acs", 1)
    w.wait("acs", 'pe')
    for half in range(2):
        co = half * 512
        usrc = ue if half == 0 else uo
        w.mm(t.p_mis[0:64, 0:nq], t.onesb[64:65, 0:64], t.invn[64:65, co:co + nq])
        if half == 0:
            i = w.mm(t.p_mis[0:64, 512:512 + nq], t.kvblk_a[0:64, p * 128:p * 128 + 64],
                     qt[0:64, :], tp=(0, 0))
        else:
            i = w.mm(t.p_mis[0:64, 512:512 + nq],
                     t.kvblk_a[64:128, p * 128 + 64:p * 128 + 128],
                     qt[64:128, :], tp=(64, 0))
        w.inc(i, "pe", 1)
        w.wait("pe", 'dve')
        e1 = t.scrf[0:64, 1024:1024 + nq]
        w.stt(e1, usrc, 4.41, t.p_mis[0:64, 512:512 + nq], ALU.mult, ALU.subtract)
        if half == 1:
            w.wait("dst", 'dve')
        dst = t.ofb[0:64, p * 512:p * 512 + nq] if half == 0 else t.scrb[0:64, 0:nq]
        w.stt(dst, e1, t.vsum_sb[:, 2 * p + half:2 * p + half + 1],
              t.p_mis[0:64, 0:nq], ALU.subtract, ALU.mult)
        if half == 1:
            w.wait("dve", 'pool')
            w.inc(w.dma('pool', t.ofb[64:128, p * 512:p * 512 + nq], t.scrb[0:64, 0:nq]),
                  "dst", 16)
        else:
            w.wait("dve", 'pe')
    t.fin_dve2 = w.val("dve")


def _attn_a(w, t, qg):
    nq = QGS[qg]
    q0 = QG0[qg]
    t.rcp_of = {}
    t.fin_dve = None
    t.fin_dve2 = None
    pend = None
    w.wait("dst", 'sync')
    for p in range(NCHUNK):
        qb = p % 2
        qt = t.qtile[:, qb * 512:qb * 512 + nq]
        w.wait("pe", 'sync', val=w.val("pe"))
        w.wait("qk", 'sync', val=w.val("qk"))
        w.inc(w.dma('sync', qt, t.qn_hbm[p, :, q0:q0 + nq]), "dq", 16)
        w.wait("dve", 'pe')
        dk_tok = {}
        for oct_ in range(10):
            if oct_ >= 2:
                oo = oct_ - 2
                ob = oo % 2
                for kk in range(8):
                    kg = oo * 8 + kk
                    pb = kg % 2
                    pqk = (t.p_qk_a, t.p_qk_b, t.p_mis)[kg % 3]
                    kn_e = t.kstream[0:64, ob * 1024 + kk * 128: ob * 1024 + (kk + 1) * 128]
                    kn_o = t.kstream[64:128, ob * 1024 + kk * 128: ob * 1024 + (kk + 1) * 128]
                    if kk == 0:
                        w.wait("dk", 'pe', val=dk_tok[oo])
                        if oo == 0:
                            w.wait("dq", 'pe')
                    w.wait("rcp", 'pe', val=w.val("rcp") - (0 if kg == 0 else 2))
                    if kg == 2 and t.fin_dve2 is not None:
                        w.wait("dve", 'pe', val=t.fin_dve2)
                        t.fin_dve2 = None
                    w.mm(pqk[:, 0:nq], kn_e, qt[0:64, :], tp=(0, 0))
                    i = w.mm(pqk[:, 512:512 + nq], kn_o, qt[64:128, :], tp=(64, 0))
                    w.inc(i, "qk", 1)
                    _emit_rv(w, t, qg, kg - 2, nq)
                    if kg == 1 and pend is not None:
                        _fin_pair(w, t, qg, pend, nq)
                        pend = None
                    rb_ = _rb_slot(t, kg)
                    w.wait("qk", 'act')
                    w.wait("rv", 'act', val=w.val("rv") - 2)
                    if nq == 512:
                        i = w.act(rb_[:, 0:1024], pqk[:, 0:1024], AF.Reciprocal,
                                  bias=1.1, scale=-1.0)
                    else:
                        w.act(rb_[:, 0:nq], pqk[:, 0:nq], AF.Reciprocal, bias=1.1, scale=-1.0)
                        i = w.act(rb_[:, 512:512 + nq], pqk[:, 512:512 + nq], AF.Reciprocal,
                                  bias=1.1, scale=-1.0)
                    w.inc(i, "rcp", 1)
                    t.rcp_of[kg] = ("rcp", w.val("rcp"))
            if oct_ < 8:
                ob2 = oct_ % 2
                w.wait("qk", 'sync', val=w.val("qk"))
                w.wait("rv", 'sync', val=w.val("rv") + (2 if oct_ >= 2 else 0))
                w.inc(w.dma('sync',
                            t.kstream[:, ob2 * 1024:(ob2 + 1) * 1024]
                            .rearrange("p (k c) -> p k c", k=8),
                            t.knT_hbm[oct_ * 8:(oct_ + 1) * 8, :, p * 128:(p + 1) * 128]
                            .rearrange("k p c -> p k c")),
                      "dk", 16)
                w.inc(w.dma('sync',
                            t.vstream[:, ob2 * 1040:(ob2 + 1) * 1040]
                            .rearrange("p (k c) -> p k c", k=8),
                            t.vk_hbm[oct_ * 8:(oct_ + 1) * 8, :, 2 * p * 65:(2 * p + 2) * 65]
                            .rearrange("k p c -> p k c")),
                      "dk", 16)
                dk_tok[oct_] = w.val("dk")
        _emit_rv(w, t, qg, NKG - 2, nq)
        _emit_rv(w, t, qg, NKG - 1, nq)
        pend = p
    _fin_pair(w, t, qg, pend, nq)
    w.wait("dst", 'pe')
    w.wait("dve", 'sync')
    xgv = t.xg.rearrange("p (a c) -> p a c", c=513)[:, :, 1:1 + nq]
    w.inc(w.dma('sync', xgv, t.x_hbm[:, :, q0:q0 + nq].rearrange("a p c -> p a c")), "dx", 16)
    w.wait("dx", 'dve')
    wc_tok = [None, None]
    for fc in range(NCHUNK):
        wb = fc % 2
        w.wait("pe", 'pool', val=(wc_tok[wb] if wc_tok[wb] is not None else w.val("pe")))
        w.inc(w.dma('pool', t.wcol[:, wb * 1024:(wb + 1) * 1024].rearrange("p (a c) -> p a c", a=8),
                    t.w_ao[:, fc * 128:(fc + 1) * 128].rearrange("(a p) c -> p a c", p=128)),
              "dw", 16)
        w.wait("dw", 'pe')
        w.wait("dve", 'pe')
        i = None
        for a in range(NCHUNK):
            i = w.mm(t.p_mis[:, 0:nq], t.wcol[:, wb * 1024 + a * 128:wb * 1024 + (a + 1) * 128],
                     t.ofb[:, a * 512:a * 512 + nq], start=(a == 0), stop=(a == NCHUNK - 1))
        w.inc(i, "pe", 1)
        wc_tok[wb] = w.val("pe")
        w.wait("pe", 'dve')
        xc = t.xg[:, fc * 513 + 1: fc * 513 + 1 + nq]
        w.ts(t.scrf[:, 0:nq], t.p_mis[:, 0:nq], t.bchunk_sb[:, fc:fc + 1], op0=ALU.add)
        w.tt(xc, xc, t.scrf[:, 0:nq], op=ALU.add)
    sl_ = (qg % 2) * 8
    w.copy(t.stash[:, sl_:sl_ + 8].rearrange("p (a one) -> p a one", one=1),
           t.xg.rearrange("p (a c) -> p a c", c=513)[:, :, nq:nq + 1])
    w.wait("dve", 'pe')


def _ln_fm(w, t, qg, resident=False):
    nq = QGS[qg]
    q0 = QG0[qg]
    nb = nq + 1
    bnd = qg > 0
    ONES_COL = t.onesb[:, 0:1]
    if resident:
        if bnd:
            sl_ = ((qg + 1) % 2) * 8
            w.copy(t.xg.rearrange("p (a c) -> p a c", c=513)[:, :, 0:1],
                   t.stash[:, sl_:sl_ + 8].rearrange("p (a one) -> p a one", one=1))
    else:
        w.wait("dve", 'sync')
        w.wait("pe", 'sync')
        if not bnd:
            xgv = t.xg.rearrange("p (a c) -> p a c", c=513)[:, :, 1:1 + nq]
            w.inc(w.dma('sync', xgv, t.x_hbm[:, :, 0:nq].rearrange("a p c -> p a c")), "dx", 16)
        else:
            xgv = t.xg.rearrange("p (a c) -> p a c", c=513)[:, :, 0:nb]
            w.inc(w.dma('sync', xgv, t.x_hbm[:, :, q0 - 1:q0 + nq].rearrange("a p c -> p a c")),
                  "dx", 16)
        w.wait("dx", 'dve')
    cast_tok = [None, None]
    for a in range(NCHUNK):
        sl = a % 2
        xb = t.scrb[:, sl * 1024:sl * 1024 + nb]
        x2 = t.scrb[:, 2048 + sl * 1024:2048 + sl * 1024 + nb]
        w.wait("pe", 'dve', val=(cast_tok[sl] if cast_tok[sl] is not None else w.val("pe")))
        if bnd:
            xsrc = t.xg[:, a * 513:a * 513 + nb]
            w.ts(xb, xsrc, 1.0)
            w.tt(x2, xsrc, xsrc, op=ALU.mult)
        else:
            xsrc = t.xg[:, a * 513 + 1:a * 513 + 1 + nq]
            w.ts(xb[:, 1:nb], xsrc, 1.0)
            w.tt(x2[:, 1:nb], xsrc, xsrc, op=ALU.mult)
        w.wait("dve", 'pe')
        w.mm(t.p_qk_a[0:1, 0:nq], ONES_COL, xb[:, 1:nb], start=(a == 0), stop=(a == NCHUNK - 1))
        i = w.mm(t.p_qk_b[0:1, 0:nq], ONES_COL, x2[:, 1:nb], start=(a == 0), stop=(a == NCHUNK - 1))
        if bnd:
            w.mm(t.p_qk_a[0:1, 512:513], ONES_COL, xb[:, 0:1],
                 start=(a == 0), stop=(a == NCHUNK - 1))
            i = w.mm(t.p_qk_b[0:1, 512:513], ONES_COL, x2[:, 0:1],
                     start=(a == 0), stop=(a == NCHUNK - 1))
        w.inc(i, "pe", 1)
        cast_tok[sl] = w.val("pe")
    w.wait("pe", 'dve')
    ncols = 513 if bnd else nq
    mrow = t.scrf[0:1, 0:ncols]
    vrow = t.scrf[0:1, 1024:1024 + ncols]
    w.ts(mrow, t.p_qk_a[0:1, 0:ncols], 1.0 / E)
    w.ts(vrow, t.p_qk_b[0:1, 0:ncols], 1.0 / E)
    w.tt(t.scrf[0:1, 2048:2048 + ncols], mrow, mrow, op=ALU.mult)
    w.tt(vrow, vrow, t.scrf[0:1, 2048:2048 + ncols], op=ALU.subtract)
    w.wait("dve", 'act')
    w.inc(w.act(t.scrf[0:1, 2048:2048 + ncols], vrow, AF.Rsqrt, bias=EPS_LN), "acs", 1)
    w.wait("acs", 'dve')
    w.ts(t.scrb[0:1, 0:ncols], mrow, 1.0)
    w.ts(t.scrb[0:1, 1024:1024 + ncols], t.scrf[0:1, 2048:2048 + ncols], 1.0)
    w.wait("dve", 'pe')
    w.mm(t.p_qk_b[:, 0:nq], t.onesb[0:1, 0:128], t.scrb[0:1, 0:nq])
    i = w.mm(t.p_qk_a[:, 0:nq], t.onesb[0:1, 0:128], t.scrb[0:1, 1024:1024 + nq])
    if bnd:
        w.mm(t.p_qk_b[:, 512:513], t.onesb[0:1, 0:128], t.scrb[0:1, 512:513])
        i = w.mm(t.p_qk_a[:, 512:513], t.onesb[0:1, 0:128], t.scrb[0:1, 1024 + 512:1024 + 513])
    w.inc(i, "pe", 1)
    w.wait("pe", 'dve')
    for a in range(NCHUNK):
        w.tt(t.scrf[:, 0:nq], t.xg[:, a * 513 + 1:a * 513 + 1 + nq],
             t.p_qk_b[:, 0:nq], op=ALU.subtract)
        w.tt(t.xn_sb[:, a * 520 + 1:a * 520 + 1 + nq], t.scrf[:, 0:nq],
             t.p_qk_a[:, 0:nq], op=ALU.mult)
        if bnd:
            w.tt(t.scrf[:, 512:513], t.xg[:, a * 513:a * 513 + 1],
                 t.p_qk_b[:, 512:513], op=ALU.subtract)
            w.tt(t.xn_sb[:, a * 520:a * 520 + 1], t.scrf[:, 512:513],
                 t.p_qk_a[:, 512:513], op=ALU.mult)
        else:
            w.memset(t.xn_sb[:, a * 520:a * 520 + 1], 0.0)


def _ffn(w, t, qg, resident=False, out_fuse=False):
    nq = QGS[qg]
    q0 = QG0[qg]
    _ln_fm(w, t, qg, resident=resident)
    i = None
    for a in range(NCHUNK):
        xn_c = t.xn_sb[:, a * 520 + 1:a * 520 + 1 + nq]
        xn_p = t.xn_sb[:, a * 520:a * 520 + nq]
        d = t.scrb[:, 0:nq]
        w.tt(d, xn_c, xn_p, op=ALU.subtract)
        w.stt(t.ffb[:, a * 512:a * 512 + nq], d, t.mu_sb[:, a:a + 1], xn_p, ALU.mult, ALU.add)
        i = w.stt(t.ffb[:, 4096 + a * 512:4096 + a * 512 + nq], d,
                  t.mu_sb[:, 8 + a:9 + a], xn_p, ALU.mult, ALU.add)
    w.wait("dve", 'pe')
    for dc in range(NCHUNK):
        pd = t.p_mis[:, (dc % 2) * 512:(dc % 2) * 512 + nq]
        if dc >= 2:
            w.wait("acs", 'pe', val=w.val("acs") - 1)
        i = None
        for a in range(NCHUNK):
            i = w.mm(pd, t.wbig[:, 8192 + a * 1024 + dc * 128:8192 + a * 1024 + (dc + 1) * 128],
                     t.ffb[:, 4096 + a * 512:4096 + a * 512 + nq],
                     start=(a == 0), stop=(a == NCHUNK - 1))
        w.inc(i, "qk", 1)
        w.wait("qk", 'act')
        w.inc(w.act(t.scrf[:, (dc % 2) * 512:(dc % 2) * 512 + nq], pd, AF.Relu), "acs", 1)
        w.wait("acs", 'dve')
        w.tt(t.kkb[:, dc * 512:dc * 512 + nq],
                   t.scrf[:, (dc % 2) * 512:(dc % 2) * 512 + nq],
                   t.scrf[:, (dc % 2) * 512:(dc % 2) * 512 + nq], op=ALU.mult)
    w.wait("dve", 'pe')
    for dc in range(NCHUNK):
        pr = t.p_mis[:, 0:nq]
        pv = t.p_mis[:, 512:512 + nq]
        if dc >= 1:
            w.wait("acs", 'pe')
        i = None
        for a in range(NCHUNK):
            i = w.mm(pr, t.wbig[:, a * 1024 + dc * 128:a * 1024 + (dc + 1) * 128],
                     t.ffb[:, a * 512:a * 512 + nq], start=(a == 0), stop=(a == NCHUNK - 1))
        w.inc(i, "qk", 1)
        w.wait("qk", 'act')
        if dc >= 1:
            w.wait("dve", 'act')
        w.inc(w.act(t.scrf[:, 0:nq], pr, AF.Sigmoid), "acs", 1)
        if dc >= 1:
            w.wait("dve", 'pe')
        i = None
        for a in range(NCHUNK):
            i = w.mm(pv, t.wbig[:, 16384 + a * 1024 + dc * 128:16384 + a * 1024 + (dc + 1) * 128],
                     t.kkb[:, a * 512:a * 512 + nq], start=(a == 0), stop=(a == NCHUNK - 1))
        w.inc(i, "rv", 1)
        w.wait("acs", 'dve')
        w.wait("rv", 'dve')
        xc = t.xg[:, dc * 513 + 1: dc * 513 + 1 + nq]
        w.tt(t.scrf[:, 512:512 + nq], pv, t.scrf[:, 0:nq], op=ALU.mult)
        w.tt(xc, xc, t.scrf[:, 512:512 + nq], op=ALU.add)
    if out_fuse:
        for tl in range(nq // 128):
            w.wait("dve", 'pe')
            i = None
            for a in range(NCHUNK):
                i = w.trans(t.p_mis[:, a * 128:(a + 1) * 128],
                            t.xg[:, a * 513 + 1 + tl * 128:a * 513 + 1 + (tl + 1) * 128],
                            t.eyef_sb[:, :])
            w.inc(i, "pe", 1)
            w.wait("pe", 'dve')
            ob = tl % 2
            w.wait("oput", 'dve', val=t.ot_tok[ob])
            ot = t.xtok[:, ob * 1024:(ob + 1) * 1024]
            for a in range(NCHUNK):
                w.copy(ot[:, a * 128:(a + 1) * 128], t.p_mis[:, a * 128:(a + 1) * 128])
            w.wait("dve", 'sync')
            w.inc(w.dma('sync', t.out_h[q0 + tl * 128:q0 + (tl + 1) * 128, :], ot), "oput", 16)
            t.ot_tok[ob] = w.val("oput")
    else:
        w.wait("dve", 'sync')
        xgv = t.xg.rearrange("p (a c) -> p a c", c=513)[:, :, 1:1 + nq]
        w.inc(w.dma('sync', t.x_hbm[:, :, q0:q0 + nq].rearrange("a p c -> p a c"), xgv), "dsx", 16)
    w.wait("dve", 'pe')


def _attnb_pre(w, t, qg):
    nq = QGS[qg]
    q0 = QG0[qg]
    _ln_fm(w, t, qg)
    w.wait("dve", 'pe')
    for dc in range(NCHUNK):
        pd = t.p_mis[:, (dc % 2) * 512:(dc % 2) * 512 + nq]
        if dc >= 2:
            w.wait("dve", 'pe', val=w.val("dve") - 3)
        i = None
        for a in range(NCHUNK):
            i = w.mm(pd, t.wbig[:, a * 1024 + dc * 128:a * 1024 + (dc + 1) * 128],
                     t.xn_sb[:, a * 520 + 1:a * 520 + 1 + nq],
                     start=(a == 0), stop=(a == NCHUNK - 1))
        w.inc(i, "qk", 1)
        w.wait("qk", 'dve')
        bq = t.bchunk_sb[:, 8 + dc:9 + dc]
        mn = t.scrf[:, 0:nq]
        mx = t.scrf[:, 512:512 + nq]
        w.ts(mn, pd, bq, 0.0, op0=ALU.add, op1=ALU.min)
        w.ts(mx, pd, bq, 0.0, op0=ALU.add, op1=ALU.max)
        w.wait("dve", 'act')
        w.inc(w.act(t.scrf[:, 1024:1024 + nq], mn, AF.Exp), "acs", 1)
        w.wait("acs", 'dve')
        w.wait("dst", 'dve', val=w.val("dst") - 16)
        w.tt(t.stage[:, (dc % 2) * 1024:(dc % 2) * 1024 + nq],
                   t.scrf[:, 1024:1024 + nq], mx, op=ALU.add)
        w.wait("dve", 'pool')
        w.inc(w.dma('pool', t.qb_hbm[dc, :, q0:q0 + nq],
                    t.stage[:, (dc % 2) * 1024:(dc % 2) * 1024 + nq]), "dst", 16)
    ntt_q = nq // 128
    for tl in range(ntt_q):
        ttg = q0 // 128 + tl
        mcol = t.maskv_sb[:, ttg:ttg + 1]
        for half in range(2):
            pk = t.p_mis[:, 0:512]
            if tl > 0 or half > 0:
                w.wait("dve", 'pe')
            i = None
            for a in range(NCHUNK):
                i = w.mm(pk, t.xn_sb[:, a * 520 + 1 + tl * 128:a * 520 + 1 + (tl + 1) * 128],
                         t.wbig[:, 8192 + a * 1024 + half * 512:8192 + a * 1024 + (half + 1) * 512],
                         start=(a == 0), stop=False)
            w.mm(pk, t.onesb[0:1, 0:128],
                 t.brow_sb[0:1, 3 * E + half * 512:3 * E + (half + 1) * 512],
                 start=False, stop=False)
            i = w.mm(pk, t.onesb[0:1, 0:128],
                     t.brow_lo[0:1, 3 * E + half * 512:3 * E + (half + 1) * 512],
                     start=False, stop=True)
            w.inc(i, "qk", 1)
            w.wait("qk", 'dve')
            mn = t.scrf[:, 0:512]
            mx = t.scrf[:, 512:1024]
            w.ts(mn, pk, 0.0, None, op0=ALU.min)
            w.ts(mx, pk, 0.0, None, op0=ALU.max)
            w.wait("dve", 'act')
            w.inc(w.act(t.scrf[:, 1024:1536], mn, AF.Exp), "acs", 1)
            w.wait("acs", 'dve')
            w.wait("rv", 'dve')
            w.tt(t.scrf[:, 1536:2048], t.scrf[:, 1024:1536], mx, op=ALU.add)
            w.ts(t.scrb[:, half * 512:(half + 1) * 512], t.scrf[:, 1536:2048], mcol)
            w.wait("dve", 'pe')
            pv = t.p_mis[:, 512:1024]
            i = None
            for a in range(NCHUNK):
                i = w.mm(pv, t.xn_sb[:, a * 520 + 1 + tl * 128:a * 520 + 1 + (tl + 1) * 128],
                         t.wbig[:, 16384 + a * 1024 + half * 512:16384 + a * 1024 + (half + 1) * 512],
                         start=(a == 0), stop=False)
            w.mm(pv, t.onesb[0:1, 0:128],
                 t.brow_sb[0:1, 4 * E + half * 512:4 * E + (half + 1) * 512],
                 start=False, stop=False)
            i = w.mm(pv, t.onesb[0:1, 0:128],
                     t.brow_lo[0:1, 4 * E + half * 512:4 * E + (half + 1) * 512],
                     start=False, stop=True)
            w.inc(i, "qk", 1)
            w.wait("qk", 'dve')
            w.copy(t.scrb[:, 1024 + half * 512:1024 + (half + 1) * 512], pv)
        w.wait("dve", 'pe')
        i = None
        for pp in range(NCHUNK):
            i = w.mm(t.p_qk_a[:, pp * 128:(pp + 1) * 128], t.scrb[:, pp * 128:(pp + 1) * 128],
                     t.scrb[:, 1024 + pp * 128:1024 + (pp + 1) * 128],
                     start=True, stop=True)
        w.inc(i, "rv", 1)
        w.wait("rv", 'dve')
        w.tt(t.kvacc_b[:, :], t.kvacc_b[:, :], t.p_qk_a[:, :], op=ALU.add)
    w.wait("dve", 'pe')


def _attnb_reduce(w, t):
    w.wait("dve", 'pool')
    w.inc(w.dma('pool', t.kv_in, t.kvacc_b[:, :]), "dst", 16)
    w.wait("dst", 'pool')
    w.inc(w.cc([t.kv_in], [t.kv_out]), "cc", 1)
    w.wait("cc", 'pool')
    w.inc(w.dma('pool', t.scrf[:, 0:1024], t.kv_out), "dst", 16)
    w.wait("dst", 'dve')
    i = None
    for a in range(NCHUNK):
        w.copy(t.kvblk_b[0:64, a * 128:a * 128 + 64], t.scrf[0:64, a * 128:a * 128 + 64])
        i = w.copy(t.kvblk_b[64:128, a * 128 + 64:a * 128 + 128],
                   t.scrf[64:128, a * 128 + 64:a * 128 + 128])
    w.wait("dve", 'pe')


def _attnb_out(w, t, qg):
    nq = QGS[qg]
    q0 = QG0[qg]
    w.wait("dst", 'sync')
    qk_tok = [None, None]
    for p in range(NCHUNK):
        sl = p % 2
        qp = t.qtile[:, sl * 512:sl * 512 + nq]
        w.wait("qk", 'sync', val=(qk_tok[sl] if qk_tok[sl] is not None else w.val("qk")))
        w.inc(w.dma('sync', qp, t.qb_hbm[p, :, q0:q0 + nq]), "dq", 16)
        w.wait("dq", 'pe')
        w.wait("dve", 'pe', val=w.val("dve") - 1 if p >= 2 else w.val("dve"))
        w.inc(w.mm(t.p_mis[:, sl * 512:sl * 512 + nq],
                   t.kvblk_b[:, p * 128:(p + 1) * 128], qp), "qk", 1)
        qk_tok[sl] = w.val("qk")
        w.wait("qk", 'dve')
        if p == 0:
            w.wait("pe", 'dve')
        w.copy(t.ofb[:, p * 512:p * 512 + nq], t.p_mis[:, sl * 512:sl * 512 + nq])
    w.barrier()
    w.wait("dve", 'pe')
    w.wait("dve", 'sync')
    w.wait("pe", 'sync')
    xgv = t.xg.rearrange("p (a c) -> p a c", c=513)[:, :, 1:1 + nq]
    w.inc(w.dma('sync', xgv, t.x_hbm[:, :, q0:q0 + nq].rearrange("a p c -> p a c")), "dx", 16)
    w.wait("dx", 'dve')
    wc_tok = [None, None]
    for fc in range(NCHUNK):
        wb = fc % 2
        w.wait("pe", 'pool', val=(wc_tok[wb] if wc_tok[wb] is not None else w.val("pe")))
        w.inc(w.dma('pool', t.wcol[:, wb * 1024:(wb + 1) * 1024].rearrange("p (a c) -> p a c", a=8),
                    t.w_bo[:, fc * 128:(fc + 1) * 128].rearrange("(a p) c -> p a c", p=128)),
              "dw", 16)
        w.wait("dw", 'pe')
        w.wait("dve", 'pe')
        i = None
        for a in range(NCHUNK):
            i = w.mm(t.p_kv[:, 0:nq], t.wcol[:, wb * 1024 + a * 128:wb * 1024 + (a + 1) * 128],
                     t.ofb[:, a * 512:a * 512 + nq], start=(a == 0), stop=(a == NCHUNK - 1))
        w.inc(i, "pe", 1)
        wc_tok[wb] = w.val("pe")
        w.wait("pe", 'dve')
        xc = t.xg[:, fc * 513 + 1: fc * 513 + 1 + nq]
        w.ts(t.scrf[:, 0:nq], t.p_kv[:, 0:nq], t.bchunk_sb[:, 16 + fc:17 + fc], op0=ALU.add)
        w.tt(xc, xc, t.scrf[:, 0:nq], op=ALU.add)
        w.barrier()
    sl_ = (qg % 2) * 8
    w.copy(t.stash[:, sl_:sl_ + 8].rearrange("p (a one) -> p a one", one=1),
           t.xg.rearrange("p (a c) -> p a c", c=513)[:, :, nq:nq + 1])
    w.wait("dve", 'pe')


def _output(w, t):
    ot_tok = [0, 0]
    ld_tok = [0, 0]
    for tt_ in range(NTT):
        lb = tt_ % 2
        xf = t.xfst[:, 0:1024] if lb == 0 else t.scrf[:, 0:1024]
        w.wait("pe", 'sync', val=ld_tok[lb])
        w.inc(w.dma('sync', xf.rearrange("p (a c) -> p a c", a=8),
                    t.x_hbm[:, :, tt_ * 128:(tt_ + 1) * 128].rearrange("a p c -> p a c")),
              "dx", 16)
        w.wait("dx", 'pe')
        w.wait("dve", 'pe')
        i = None
        for a in range(NCHUNK):
            i = w.trans(t.p_mis[:, a * 128:(a + 1) * 128], xf[:, a * 128:(a + 1) * 128],
                        t.eyef_sb[:, :])
        w.inc(i, "pe", 1)
        ld_tok[lb] = w.val("pe")
        w.wait("pe", 'dve')
        ob = tt_ % 2
        w.wait("oput", 'dve', val=ot_tok[ob])
        ot = t.xtok[:, ob * 1024:(ob + 1) * 1024]
        i = None
        for a in range(NCHUNK):
            i = w.copy(ot[:, a * 128:(a + 1) * 128], t.p_mis[:, a * 128:(a + 1) * 128])
        w.wait("dve", 'sync')
        w.inc(w.dma('sync', t.out_h[tt_ * 128:(tt_ + 1) * 128, :], ot), "oput", 16)
        ot_tok[ob] = w.val("oput")



def _host_inputs(inputs):
    x = np.ascontiguousarray(np.asarray(inputs["L0_attn_a_x"], dtype=np.float32))
    g = lambda k: np.ascontiguousarray(np.asarray(inputs[k], dtype=np.float32))
    shared = {
        "w_aq": g("L0_attn_a_Wq"), "w_ak": g("L0_attn_a_Wk"),
        "w_av": g("L0_attn_a_Wv"), "w_ao": g("L0_attn_a_Wo"),
        "w_fr": np.ascontiguousarray(g("L0_ffn_W_r").T),
        "w_fk": np.ascontiguousarray(g("L0_ffn_W_k").T),
        "w_fv": np.ascontiguousarray(g("L0_ffn_W_v").T),
        "w_bq": np.ascontiguousarray(g("L0_attn_b_Wq").T),
        "w_bk": np.ascontiguousarray(g("L0_attn_b_Wk").T),
        "w_bv": np.ascontiguousarray(g("L0_attn_b_Wv").T),
        "w_bo": np.ascontiguousarray(g("L0_attn_b_Wo").T),
    }
    b_rows = np.stack([g("L0_attn_a_bq"), g("L0_attn_a_bk"), g("L0_attn_a_bv"),
                       g("L0_attn_b_bk"), g("L0_attn_b_bv")]).astype(np.float32)
    b_chunk = np.zeros((128, 4 * NCHUNK), np.float32)
    b_chunk[:, 0:8] = g("L0_attn_a_bo").reshape(NCHUNK, 128).T
    b_chunk[:, 8:16] = g("L0_attn_b_bq").reshape(NCHUNK, 128).T
    b_chunk[:, 16:24] = g("L0_attn_b_bo").reshape(NCHUNK, 128).T
    mu_chunk = np.zeros((128, 2 * NCHUNK), np.float32)
    mu_chunk[:, 0:8] = g("L0_ffn_mu_r").reshape(NCHUNK, 128).T
    mu_chunk[:, 8:16] = g("L0_ffn_mu_k").reshape(NCHUNK, 128).T
    shared["b_rows"] = np.ascontiguousarray(b_rows)
    shared["b_chunk"] = np.ascontiguousarray(b_chunk)
    shared["mu_chunk"] = np.ascontiguousarray(mu_chunk)
    shared["eyef"] = np.eye(128, dtype=np.float32)
    shared["eyeb"] = np.eye(128, dtype=np.float32).astype(ml_dtypes.bfloat16)
    in_maps = []
    for c in range(8):
        b, h = c // 2, c % 2
        m = dict(shared)
        m["xq"] = np.ascontiguousarray(x[b, 3968 * h: 3968 * h + TOK, :])
        m["xfull"] = np.ascontiguousarray(x[b])
        mk = np.zeros(TOK, np.float32)
        if h == 0:
            mk[0:4096] = 1.0
        else:
            mk[128:TOK] = 1.0
        m["maskv"] = np.ascontiguousarray(mk.reshape(NTT, 128).T)
        in_maps.append(m)
    return in_maps


def run_staged(inputs, stop_stage, cores=None):
    nc = build_program(stop_stage=stop_stage)
    in_maps = _host_inputs(inputs)
    cores = cores if cores is not None else list(range(8))
    res = run_bass_kernel_spmd(nc, [in_maps[c] for c in cores], core_ids=cores)
    return [r["dbg"] for r in res.results]


def kernel(**inputs):
    nc = build_program()
    in_maps = _host_inputs(inputs)
    res = run_bass_kernel_spmd(nc, in_maps, core_ids=list(range(8)))
    y = np.empty((B, S, E), np.float32)
    for c in range(8):
        b, h = c // 2, c % 2
        o = res.results[c]["out"]
        if h == 0:
            y[b, 0:4096] = o[0:4096]
        else:
            y[b, 4096:8192] = o[128:TOK]
    return y


# version 15
# speedup vs baseline: 2.1958x; 1.0084x over previous
import dataclasses
import numpy as np
import ml_dtypes

import concourse.bass as bass
import concourse.mybir as mybir
from concourse.bass_utils import run_bass_kernel_spmd

F32 = mybir.dt.float32
BF16 = mybir.dt.bfloat16
AF = mybir.ActivationFunctionType
ALU = mybir.AluOpType

E = 1024
H = 16
D = 64
S = 8192
B = 4
NCHUNK = 8
TOK = 4224
NTT = 33
NKG = 64
QGS = [512] * 8 + [128]
QG0 = [sum(QGS[:i]) for i in range(len(QGS))]
NQG = len(QGS)
EPS_LN = 1e-5
EPS_NRM = 1e-8
EPS_ATT = 1e-6
NCONST = -3.1 * S + EPS_ATT

SEM_NAMES = ["dx", "dk", "dq", "dw", "dst", "dsx", "qk", "rcp", "rv", "pe", "dve",
             "acs", "cc", "oput"]


def _act_raw(nc, eng, out, in_, func, bias=0.0, scale=1.0):
    bi = eng.activation(out=out, in_=in_, func=AF.Exp, bias=bias, scale=scale)
    i2 = dataclasses.replace(bi.ins, func=func)
    nc.register_instruction(i2, overwrite=True)
    insts = nc.cur_bb.bb.instructions
    assert insts[-1] is bi.ins
    insts[-1] = i2
    return bass.BassInstruction(i2)


class Walk:

    def __init__(self, nc, kind, eng, sems):
        self.nc = nc
        self.kind = kind
        self.eng = eng
        self.sems = sems
        self.c = {s: 0 for s in SEM_NAMES}

    def inc(self, ins, name, n=1):
        self.c[name] += n
        if ins is not None:
            ins.then_inc(self.sems[name], n)

    def wait(self, name, kind, val=None):
        if self.kind == kind:
            v = self.c[name] if val is None else val
            if v > 0:
                self.eng.wait_ge(self.sems[name], v)

    def val(self, name):
        return self.c[name]

    _OWNER = {"qk": 'pe', "rv": 'pe', "pe": 'pe', "rcp": 'act', "acs": 'act',
              "dve": 'dve', "dk": 'sync', "dq": 'sync', "dx": 'sync', "dsx": 'sync',
              "oput": 'sync', "dw": 'pool', "dst": 'pool', "cc": 'pool'}

    def barrier(self):
        for s in SEM_NAMES:
            own = self._OWNER[s]
            for k in ('pe', 'act', 'dve', 'sync', 'pool'):
                if k != own:
                    self.wait(s, k)

    def mm(self, out, lhsT, rhs, start=True, stop=True, tp=None):
        if self.kind != 'pe':
            return None
        return self.eng.matmul(out, lhsT=lhsT, rhs=rhs, start=start, stop=stop,
                               tile_position=tp)

    def trans(self, out, in_, ident):
        if self.kind != 'pe':
            return None
        return self.eng.transpose(out, in_, ident)

    def act(self, out, in_, func, bias=0.0, scale=1.0, accum_out=None):
        if self.kind != 'act':
            return None
        if func in (AF.Reciprocal, AF.Rsqrt):
            return _act_raw(self.nc, self.eng, out, in_, func, bias, scale)
        kw = {} if accum_out is None else {"accum_out": accum_out}
        return self.eng.activation(out=out, in_=in_, func=func, bias=bias,
                                   scale=scale, **kw)

    def _dve(self, f):
        if self.kind != 'dve':
            self.c["dve"] += 1
            return None
        self.eng.wait_ge(self.sems["dve"], self.c["dve"])
        ins = f()
        self.c["dve"] += 1
        ins.then_inc(self.sems["dve"], 1)
        return ins

    def ts(self, out, in0, s1, s2=None, op0=ALU.mult, op1=ALU.bypass):
        return self._dve(lambda: self.eng.tensor_scalar(
            out=out, in0=in0, scalar1=s1, scalar2=s2, op0=op0, op1=op1))

    def tt(self, out, in0, in1, op=ALU.add):
        return self._dve(lambda: self.eng.tensor_tensor(out=out, in0=in0, in1=in1, op=op))

    def stt(self, out, in0, scalar, in1, op0, op1):
        return self._dve(lambda: self.eng.scalar_tensor_tensor(
            out=out, in0=in0, scalar=scalar, in1=in1, op0=op0, op1=op1))

    def copy(self, out, in_):
        return self._dve(lambda: self.eng.tensor_copy(out=out, in_=in_))

    def memset(self, ap, v):
        return self._dve(lambda: self.eng.memset(ap, v))

    def red(self, out, in_, op=ALU.add):
        return self._dve(lambda: self.eng.tensor_reduce(out, in_, mybir.AxisListType.X, op))

    def vrecip_fast(self, out, in_):
        return self._dve(lambda: self.eng.reciprocal_approx_fast(out=out, in_=in_))

    def vrecip(self, out, in_):
        def f():
            if out.dtype == BF16:
                with self.nc.allow_low_precision(reason="bf16 store of fp32 reciprocal"):
                    return self.eng.reciprocal(out=out, in_=in_)
            return self.eng.reciprocal(out=out, in_=in_)
        return self._dve(f)

    def dma(self, queue, out, in_):
        if self.kind != queue:
            return None
        return self.eng.dma_start(out=out, in_=in_)

    def cc(self, ins_, outs_):
        if self.kind != 'pool':
            return None
        return self.eng.collective_compute(
            "AllReduce", ALU.add, replica_groups=[[0, 1], [2, 3], [4, 5], [6, 7]],
            ins=ins_, outs=outs_)


class L:
    pass


def build_program(stop_stage=None):
    nc = bass.Bass("TRN2", target_bir_lowering=False, debug=False)
    t = L()

    t.xq = nc.dram_tensor("xq", [TOK, E], F32, kind="ExternalInput").ap()
    t.xfull = nc.dram_tensor("xfull", [S, E], F32, kind="ExternalInput").ap()
    for nm in ["w_aq", "w_ak", "w_av", "w_ao", "w_fr", "w_fk", "w_fv",
               "w_bq", "w_bk", "w_bv", "w_bo"]:
        setattr(t, nm, nc.dram_tensor(nm, [E, E], F32, kind="ExternalInput").ap())
    t.b_rows = nc.dram_tensor("b_rows", [5, E], F32, kind="ExternalInput").ap()
    t.b_chunk = nc.dram_tensor("b_chunk", [128, 4 * NCHUNK], F32, kind="ExternalInput").ap()
    t.mu_chunk = nc.dram_tensor("mu_chunk", [128, 2 * NCHUNK], F32, kind="ExternalInput").ap()
    t.eyef = nc.dram_tensor("eyef", [128, 128], F32, kind="ExternalInput").ap()
    t.eyeb = nc.dram_tensor("eyeb", [128, 128], BF16, kind="ExternalInput").ap()
    t.maskv = nc.dram_tensor("maskv", [128, NTT], F32, kind="ExternalInput").ap()

    t.knT_hbm = nc.dram_tensor("knT_hbm", [NKG, 128, NCHUNK * 128], BF16, kind="Internal").ap()
    t.vk_hbm = nc.dram_tensor("vk_hbm", [NKG, 128, H * 65], BF16, kind="Internal").ap()
    t.qn_hbm = nc.dram_tensor("qn_hbm", [NCHUNK, 128, TOK], BF16, kind="Internal").ap()
    t.qb_hbm = nc.dram_tensor("qb_hbm", [NCHUNK, 128, TOK], BF16, kind="Internal").ap()
    t.kv_in = nc.dram_tensor("kv_in", [128, 1024], F32, kind="Internal").ap()
    t.kv_out = nc.dram_tensor("kv_out", [128, 1024], F32, kind="Internal").ap()
    t.x_hbm = nc.dram_tensor("x_hbm", [NCHUNK, 128, TOK], F32, kind="Internal").ap()
    t.out_h = nc.dram_tensor("out", [TOK, E], F32, kind="ExternalOutput").ap()
    t.dbg = None
    if stop_stage is not None:
        t.dbg = nc.dram_tensor("dbg", [128, NCHUNK * TOK], F32, kind="ExternalOutput").ap()

    import contextlib
    with contextlib.ExitStack() as st:
        def sb(name, shape, dt):
            return st.enter_context(nc.sbuf_tensor(name, shape, dt))

        def ps(name, shape):
            return st.enter_context(nc.psum_tensor(name, shape, F32))

        t.xg = sb("xg", [128, NCHUNK * 513], F32)
        t.xfst = sb("xfst", [128, 1024], F32)
        t.consts = sb("consts", [128, 16], F32)
        t.onesb = sb("onesb", [128, 576], BF16)
        t.eyef_sb = sb("eyef_sb", [128, 128], F32)
        t.eyeb_sb = sb("eyeb_sb", [128, 128], BF16)
        t.maskv_sb = sb("maskv_sb", [128, NTT], F32)
        t.brow_sb = sb("brow_sb", [1, 5 * E], BF16)
        t.brow_lo = sb("brow_lo", [1, 5 * E], BF16)
        t.bchunk_sb = sb("bchunk_sb", [128, 4 * NCHUNK], F32)
        t.mu_sb = sb("mu_sb", [128, 2 * NCHUNK], F32)
        t.ksum_bf = sb("ksum_bf", [128, NCHUNK], BF16)
        t.vsum_sb = sb("vsum_sb", [64, H], F32)
        t.kvblk_a = sb("kvblk_a", [128, NCHUNK * 128], BF16)
        t.kvblk_b = sb("kvblk_b", [128, NCHUNK * 128], BF16)
        t.kvacc_b = sb("kvacc_b", [128, 1024], F32)
        t.wbig = sb("wbig", [128, 3 * 8192], BF16)
        t.wcol = sb("wcol", [128, 2 * 1024], BF16)
        t.stage = sb("stage", [128, 2048 + H * 65], BF16)
        t.xtok = sb("xtok", [128, 2 * 1024], F32)
        t.scrf = sb("scrf", [128, 3 * 1024], F32)
        t.scrb = sb("scrb", [128, 4 * 1024], BF16)
        t.sml = sb("sml", [128, 96], F32)
        t.stash = sb("stash", [128, 16], F32)
        t.kvacc_a = sb("kvacc_a", [128, 1024], F32)
        t.kstream = sb("kstream", [128, 2 * 1024], BF16)
        t.vstream = sb("vstream", [128, 2 * 1040], BF16)
        t.qtile = sb("qtile", [128, 2 * 512], BF16)
        t.rbuf = sb("rbuf", [128, 2 * 1024], BF16)
        t.ut = sb("ut", [64, 2 * 512], F32)
        t.invn = sb("invn", [65, 1024], BF16)
        t.ofb = sb("ofb", [128, NCHUNK * 512], BF16)
        t.xn_sb = sb("xn_sb", [128, NCHUNK * 520], BF16)
        t.ffb = sb("ffb", [128, 2 * NCHUNK * 512], BF16)
        t.kkb = sb("kkb", [128, NCHUNK * 512], BF16)

        t.p_qk_a = ps("p_qk_a", [128, 1024])
        t.p_qk_b = ps("p_qk_b", [128, 1024])
        t.p_kv = ps("p_kv", [128, 1024])
        t.p_mis = ps("p_mis", [128, 1024])

        for i, v in enumerate([1.1, -1.0, 0.0, 1.0, EPS_LN, EPS_NRM, 4.41, -3.1,
                               1.0 / E, NCONST, 0.5, 2.0]):
            nc.const_aps.register(F32, v, t.consts[:, i:i + 1])

        sems = {s: st.enter_context(nc.semaphore(s)) for s in SEM_NAMES}

        with nc.Block() as block:
            @block.sync
            def _(eng):
                _emit(Walk(nc, 'sync', eng, sems), t, stop_stage)

            @block.tensor
            def _(eng):
                _emit(Walk(nc, 'pe', eng, sems), t, stop_stage)

            @block.scalar
            def _(eng):
                _emit(Walk(nc, 'act', eng, sems), t, stop_stage)

            @block.vector
            def _(eng):
                _emit(Walk(nc, 'dve', eng, sems), t, stop_stage)

            @block.gpsimd
            def _(eng):
                _emit(Walk(nc, 'pool', eng, sems), t, stop_stage)

    return nc



def _emit(w: Walk, t, stop_stage):
    _setup(w, t)
    if stop_stage == 'p0':
        _prep(w, t, stop_p0=True)
        _fin(w, t); return
    _prep(w, t)
    w.barrier()
    if stop_stage == 'prep':
        _dump(w, t); _fin(w, t); return
    if stop_stage == 'a0':
        _attn_a(w, t, 0)
        _dump_a0(w, t); _fin(w, t); return
    t.ot_tok = [0, 0]
    _load3(w, t, [t.w_fr, t.w_fk, t.w_fv])
    for qg in range(NQG):
        _attn_a(w, t, qg)
        w.barrier()
        _ffn(w, t, qg, resident=True)
        w.barrier()
    if stop_stage == 'x2':
        _dump(w, t); _fin(w, t); return
    _load3(w, t, [t.w_bq, t.w_bk, t.w_bv])
    for qg in range(NQG):
        _attnb_pre(w, t, qg)
        w.barrier()
        if stop_stage == 'b0' and qg == 0:
            w.wait("dve", 'pool')
            w.wait("dst", 'pool')
            d = t.dbg
            w.inc(w.dma('pool', d[:, 0:1024], t.kvacc_b[:, :]), "oput", 16)
            w.inc(w.dma('pool', d[:, 1024:2048], t.kvblk_b[:, :]), "oput", 16)
            w.inc(w.dma('pool', d[:, 2048:2560], t.qb_hbm[0, :, 0:512]), "oput", 16)
            w.inc(w.dma('pool', d[:, 2560:3584], t.scrb[:, 0:1024]), "oput", 16)
            w.inc(w.dma('pool', d[:, 3584:4608], t.scrb[:, 1024:2048]), "oput", 16)
            w.inc(w.dma('pool', d[:, 4608:5128], t.xn_sb[:, 0:520]), "oput", 16)
            _fin(w, t); return
    w.barrier()
    _attnb_reduce(w, t)
    w.barrier()
    if stop_stage == 'b1':
        w.wait("dve", 'pool')
        d = t.dbg
        w.inc(w.dma('pool', d[:, 0:1024], t.kvacc_b[:, :]), "oput", 16)
        w.inc(w.dma('pool', d[:, 1024:2048], t.kvblk_b[:, :]), "oput", 16)
        _fin(w, t); return
    _load3(w, t, [t.w_fr, t.w_fk, t.w_fv])
    for qg in range(NQG):
        _attnb_out(w, t, qg)
        w.barrier()
        _ffn(w, t, qg, resident=True, out_fuse=True)
        w.barrier()
    _fin(w, t)


def _dump(w, t):
    w.wait("dve", 'sync')
    w.wait("pe", 'sync')
    w.wait("dst", 'sync')
    w.wait("dsx", 'sync')
    for a in range(NCHUNK):
        w.inc(w.dma('sync', t.dbg[:, a * TOK:(a + 1) * TOK], t.x_hbm[a]), "oput", 16)


def _dump_a0(w, t):
    w.wait("dve", 'pool')
    w.wait("pe", 'pool')
    w.wait("acs", 'pool')
    w.wait("dst", 'pool')
    d = t.dbg
    w.inc(w.dma('pool', d[:, 0:4096], t.ofb[:, :]), "oput", 16)
    w.inc(w.dma('pool', d[0:65, 4096:5120], t.invn[:, :]), "oput", 16)
    w.inc(w.dma('pool', d[0:64, 5120:6144], t.ut[:, :]), "oput", 16)
    w.inc(w.dma('pool', d[64:65, 6144:7168], t.scrf[64:65, 0:1024]), "oput", 16)
    w.inc(w.dma('pool', d[:, 7168:7680], t.qn_hbm[0, :, 0:512]), "oput", 16)
    w.inc(w.dma('pool', d[:, 7680:7808], t.knT_hbm[0, :, 0:128]), "oput", 16)
    w.inc(w.dma('pool', d[:, 7808:7938], t.vk_hbm[0, :, 0:130]), "oput", 16)
    w.inc(w.dma('pool', d[:, 7938:7946], t.ksum_bf[:, :]), "oput", 16)
    w.inc(w.dma('pool', d[0:64, 7946:7962], t.vsum_sb[:, :]), "oput", 16)
    w.inc(w.dma('pool', d[:, 8000:9024], t.kvblk_a[:, :]), "oput", 16)
    w.inc(w.dma('pool', d[:, 9024:9056], t.bchunk_sb[:, :]), "oput", 16)
    w.copy(t.scrf[64:65, 1024:2048], t.p_kv[64:65, 0:1024])
    w.wait("dve", 'pool')
    w.inc(w.dma('pool', d[64:65, 9056:10080], t.scrf[64:65, 1024:2048]), "oput", 16)
    w.inc(w.dma('pool', d[:, 10080:12128], t.rbuf[:, :]), "oput", 16)


def _fin(w, t):
    if w.kind == 'pool':
        w.eng.wait_ge(w.sems["oput"], w.c["oput"])


def _setup(w, t):
    for i, v in enumerate([1.1, -1.0, 0.0, 1.0, EPS_LN, EPS_NRM, 4.41, -3.1,
                           1.0 / E, NCONST, 0.5, 2.0]):
        w.memset(t.consts[:, i:i + 1], v)
    w.memset(t.onesb[:, :], 1.0)
    w.memset(t.kvacc_b[:, :], 0.0)
    w.memset(t.kvacc_a[:, :], 0.0)
    w.memset(t.sml[:, 64:88], 0.0)
    w.memset(t.kvblk_a[:, :], 0.0)
    w.memset(t.kvblk_b[:, :], 0.0)
    w.inc(w.dma('sync', t.eyef_sb[:, :], t.eyef), "dx", 16)
    w.inc(w.dma('sync', t.eyeb_sb[:, :], t.eyeb), "dx", 16)
    w.inc(w.dma('sync', t.maskv_sb[:, :], t.maskv), "dx", 16)
    w.inc(w.dma('sync', t.bchunk_sb[:, :], t.b_chunk), "dx", 16)
    w.inc(w.dma('sync', t.mu_sb[:, :], t.mu_chunk), "dx", 16)
    for r_ in range(5):
        w.wait("dve", 'pool')
        w.inc(w.dma('pool', t.scrf[0:1, 0:E], t.b_rows[r_:r_ + 1, :]), "dw", 16)
        w.wait("dw", 'dve')
        w.ts(t.brow_sb[0:1, r_ * E:(r_ + 1) * E], t.scrf[0:1, 0:E], 1.0)
        w.ts(t.scrf[0:1, E:2 * E], t.brow_sb[0:1, r_ * E:(r_ + 1) * E], -1.0)
        w.tt(t.brow_lo[0:1, r_ * E:(r_ + 1) * E], t.scrf[0:1, E:2 * E],
             t.scrf[0:1, 0:E], op=ALU.add)


def _load3(w, t, srcs):
    w.wait("pe", 'pool')
    for i, src in enumerate(srcs):
        w.inc(w.dma('pool', t.wbig[:, i * 8192:(i + 1) * 8192].rearrange("p (a d) -> p a d", a=8),
                    src.rearrange("(a p) d -> p a d", p=128)), "dw", 16)


def _ln_tok(w, t, xt, xn_out):
    m = t.sml[:, 0:1]
    v = t.sml[:, 1:2]
    r = t.sml[:, 2:3]
    sq = t.scrf[:, E:2 * E]
    w.red(m, xt)
    w.ts(m, m, -1.0 / E)
    w.wait("dve", 'act')
    w.wait("acs", 'act')
    w.inc(w.act(sq, xt, AF.Square, bias=m, accum_out=v), "acs", 1)
    w.wait("acs", 'act')
    w.inc(w.act(r, v, AF.Rsqrt, bias=EPS_LN, scale=1.0 / E), "acs", 1)
    w.wait("acs", 'dve')
    w.ts(xn_out, xt, m, r, op0=ALU.add, op1=ALU.mult)


def _norm_heads(w, t, src_f32, dst_bf):
    sq = t.scrf[:, E:2 * E]
    ss = t.sml[:, 16:32]
    src3 = src_f32.rearrange("p (h d) -> p h d", h=H)
    w.wait("dve", 'act')
    w.wait("acs", 'act')
    w.inc(w.act(sq, src_f32, AF.Square), "acs", 1)
    w.wait("acs", 'dve')
    w.red(ss, sq.rearrange("p (h d) -> p h d", h=H))
    w.wait("dve", 'act')
    w.inc(w.act(t.sml[:, 32:48], ss, AF.Rsqrt), "acs", 1)
    w.wait("acs", 'dve')
    w.tt(t.sml[:, 32:48], t.sml[:, 32:48], ss, op=ALU.mult)
    w.ts(t.sml[:, 32:48], t.sml[:, 32:48], EPS_NRM, op0=ALU.add)
    w.vrecip(ss, t.sml[:, 32:48])
    w.tt(dst_bf.rearrange("p (h d) -> p h d", h=H), src3,
         ss.rearrange("p (h o) -> p h o", o=1).broadcast_to([128, H, D]),
         op=ALU.mult)


def _transpose8(w, t, src_bf, dst_bf, pt, dve_val=None):
    w.wait("dve", 'pe', val=(dve_val if dve_val is not None else w.val("dve")))
    i = None
    for a in range(NCHUNK):
        i = w.trans(pt[:, a * 64:(a + 1) * 64].bitcast(BF16),
                    src_bf[:, a * 128:(a + 1) * 128], t.eyeb_sb[:, :])
    w.inc(i, "pe", 1)
    w.wait("pe", 'dve')
    w.copy(dst_bf[:, 0:1024], pt[:, 0:512].bitcast(BF16))


def _proj_tok(w, t, xnT, wbase, brow_idx, pdst):
    last = None
    for half in range(2):
        po = pdst[:, half * 512:(half + 1) * 512]
        for a in range(NCHUNK):
            w.mm(po, xnT[:, a * 128:(a + 1) * 128],
                 t.wbig[:, wbase + a * 1024 + half * 512: wbase + a * 1024 + (half + 1) * 512],
                 start=(a == 0), stop=False)
        w.mm(po, t.onesb[0:1, 0:128],
             t.brow_sb[0:1, brow_idx * E + half * 512: brow_idx * E + (half + 1) * 512],
             start=False, stop=False)
        last = w.mm(po, t.onesb[0:1, 0:128],
                    t.brow_lo[0:1, brow_idx * E + half * 512: brow_idx * E + (half + 1) * 512],
                    start=False, stop=True)
    w.inc(last, "pe", 1)


def _prep(w, t, stop_p0=False):
    _load3(w, t, [t.w_ak, t.w_av, t.w_aq])
    w.wait("dw", 'pe')
    w.wait("dx", 'dve')
    w.wait("dve", 'pe')
    ONES_COL = t.onesb[:, 0:1]

    xtok_tok = [0, 0]
    xtok_pe = [0, 0]
    tile_pe = [0, 0]
    flush_pe = [None]
    st_tok = {"kn": 0, "vk": 0, "qn": 0}
    NTOT = NKG + NTT
    ln_dve = {}

    def _load_ln(tc2):
        if tc2 >= NTOT:
            return
        full2 = tc2 < NKG
        buf2 = tc2 % 2
        xt2 = t.xtok[:, buf2 * 1024:(buf2 + 1) * 1024]
        src2 = (t.xfull[tc2 * 128:(tc2 + 1) * 128, :] if full2
                else t.xq[(tc2 - NKG) * 128:(tc2 - NKG + 1) * 128, :])
        w.wait("dve", 'sync', val=xtok_tok[buf2])
        w.wait("pe", 'sync', val=xtok_pe[buf2])
        w.inc(w.dma('sync', xt2, src2), "dx", 16)
        w.wait("dx", 'dve')
        nb2 = t.scrb if tc2 % 2 == 0 else t.ffb
        w.wait("pe", 'dve', val=tile_pe[tc2 % 2])
        _ln_tok(w, t, xt2, nb2[:, 0:1024])
        ln_dve[tc2] = w.val("dve")
        xtok_tok[buf2] = w.val("dve")

    def _flush():
        if flush_pe[0] is not None:
            w.wait("pe", 'dve', val=flush_pe[0])
            w.tt(t.kvacc_a[:, :], t.kvacc_a[:, :], t.p_kv[:, :], op=ALU.add)
            w.tt(t.sml[:, 64:72], t.sml[:, 64:72], t.p_qk_b[:, 0:8], op=ALU.add)
            w.tt(t.sml[0:64, 72:88], t.sml[0:64, 72:88], t.p_qk_b[0:64, 16:32], op=ALU.add)
            flush_pe[0] = None

    _load_ln(0)
    for tc in range(NTOT):
        full = tc < NKG
        buf = tc % 2
        xt = t.xtok[:, buf * 1024:(buf + 1) * 1024]
        nbase = t.scrb if tc % 2 == 0 else t.ffb
        xnb = nbase[:, 0:1024]

        if not full:
            tt_ = tc - NKG
            w.wait("dve", 'pe')
            i = None
            for a in range(NCHUNK):
                i = w.trans(t.p_mis[:, a * 128:(a + 1) * 128], xt[:, a * 128:(a + 1) * 128],
                            t.eyef_sb[:, :])
            w.inc(i, "pe", 1)
            xtok_pe[buf] = w.val("pe")
            w.wait("pe", 'dve')
            w.wait("dst", 'dve', val=st_tok.get("xf", 0))
            w.copy(t.xfst[:, 0:1024], t.p_mis[:, 0:1024])
            w.wait("dve", 'pool')
            w.inc(w.dma('pool',
                        t.x_hbm[:, :, tt_ * 128:(tt_ + 1) * 128].rearrange("a p c -> p a c"),
                        t.xfst.rearrange("p (a c) -> p a c", a=8)), "dst", 16)
            st_tok["xf"] = w.val("dst")
            dve_tp = w.val("dve")
        else:
            dve_tp = ln_dve[tc]

        xnT = nbase[:, 1024:2048]
        _transpose8(w, t, xnb, xnT, t.p_qk_a, dve_val=dve_tp)
        w.wait("dve", 'pe')

        if full:
            _proj_tok(w, t, xnT, 0, 1, t.p_mis)
            _load_ln(tc + 1)
            _flush()
            w.wait("pe", 'dve')
            w.wait("pe", 'act')
            kn_tok = t.scrb[:, 2048:3072]
            _norm_heads(w, t, t.p_mis[:, 0:1024], kn_tok)
            if stop_p0:
                w.wait("dve", 'pool')
                w.wait("acs", 'pool')
                d = t.dbg
                w.inc(w.dma('pool', d[:, 0:1024], t.scrb[:, 0:1024]), "oput", 16)
                w.inc(w.dma('pool', d[:, 1024:2048], t.scrb[:, 1024:2048]), "oput", 16)
                w.inc(w.dma('pool', d[:, 2048:3072], t.p_mis[:, 0:1024]), "oput", 16)
                w.inc(w.dma('pool', d[:, 3072:4096], kn_tok), "oput", 16)
                w.inc(w.dma('pool', d[:, 4096:4160], t.sml[:, :]), "oput", 16)
                w.inc(w.dma('pool', d[:, 4160:4288], t.xtok[:, 0:128]), "oput", 16)
                w.inc(w.dma('pool', d[:, 4288:6336], t.wbig[:, 0:2048]), "oput", 16)
                w.inc(w.dma('pool', d[0:1, 6336:11456], t.brow_sb[:, :]), "oput", 16)
                w.inc(w.dma('pool', d[0:1, 11456:16576], t.brow_lo[:, :]), "oput", 16)
                return
            w.wait("dve", 'pe')
            _proj_tok(w, t, xnT, 8192, 2, t.p_mis)
            w.wait("pe", 'dve')
            w.wait("pe", 'act')
            v_tok = t.scrb[:, 3072:4096]
            w.wait("acs", 'act')
            w.inc(w.act(v_tok, t.p_mis[:, 0:1024], AF.Copy), "acs", 1)
            w.wait("acs", 'dve')
            v_lo = nbase[:, 0:1024]
            w.ts(t.scrf[:, 0:1024], v_tok, -1.0)
            w.tt(v_lo, t.scrf[:, 0:1024], t.p_mis[:, 0:1024], op=ALU.add)
            w.wait("dst", 'dve', val=st_tok["vk"])
            vkst = t.stage[:, 2048:2048 + H * 65]
            vk3 = vkst.rearrange("p (h e) -> p h e", h=H)
            if tc == 0:
                w.memset(vk3[:, :, 64:65], 1.0)
            w.copy(vk3[:, :, 0:64], v_tok.rearrange("p (h d) -> p h d", h=H))
            w.wait("dst", 'dve', val=st_tok["kn"])
            knst = t.stage[:, 1024:2048]
            _transpose8(w, t, kn_tok, knst, t.p_qk_a)
            w.wait("dve", 'pe')
            w.wait("acs", 'pe')
            for a in range(NCHUNK):
                w.mm(t.p_kv[:, a * 128:(a + 1) * 128], kn_tok[:, a * 128:(a + 1) * 128],
                     v_tok[:, a * 128:(a + 1) * 128], start=True, stop=True)
            for a in range(NCHUNK):
                w.mm(t.p_qk_b[:, a:a + 1], kn_tok[:, a * 128:(a + 1) * 128], ONES_COL,
                     start=True, stop=True)
            i = None
            for hh in range(H):
                w.mm(t.p_qk_b[0:64, 16 + hh:17 + hh], v_tok[:, hh * 64:(hh + 1) * 64],
                     ONES_COL, start=True, stop=False)
                i = w.mm(t.p_qk_b[0:64, 16 + hh:17 + hh], v_lo[:, hh * 64:(hh + 1) * 64],
                         ONES_COL, start=False, stop=True)
            w.inc(i, "pe", 1)
            flush_pe[0] = w.val("pe")
            w.wait("dve", 'pool')
            w.wait("pe", 'pool', val=0)
            w.inc(w.dma('pool', t.knT_hbm[tc], knst), "dst", 16)
            st_tok["kn"] = w.val("dst")
            w.inc(w.dma('pool', t.vk_hbm[tc], vkst), "dst", 16)
            st_tok["vk"] = w.val("dst")
        else:
            _proj_tok(w, t, xnT, 16384, 0, t.p_mis)
            _load_ln(tc + 1)
            _flush()
            w.wait("pe", 'dve')
            w.wait("pe", 'act')
            qn_tok = t.scrb[:, 2048:3072]
            _norm_heads(w, t, t.p_mis[:, 0:1024], qn_tok)
            w.wait("dst", 'dve', val=st_tok["qn"])
            qnst = t.stage[:, 0:1024]
            _transpose8(w, t, qn_tok, qnst, t.p_qk_a)
            tt_ = tc - NKG
            w.wait("dve", 'pool')
            w.inc(w.dma('pool',
                        t.qn_hbm[:, :, tt_ * 128:(tt_ + 1) * 128].rearrange("a p c -> p a c"),
                        qnst.rearrange("p (a c) -> p a c", a=8)), "dst", 16)
            st_tok["qn"] = w.val("dst")
        tile_pe[tc % 2] = w.val("pe")

    _flush()

    w.wait("pe", 'dve')
    kb3 = t.kvblk_a.rearrange("p (a c) -> p a c", a=NCHUNK)
    ka3 = t.kvacc_a.rearrange("p (a c) -> p a c", a=NCHUNK)
    w.copy(kb3[0:64, :, 0:64], ka3[0:64, :, 0:64])
    w.copy(kb3[64:128, :, 64:128], ka3[64:128, :, 64:128])
    w.copy(t.ksum_bf[:, :], t.sml[:, 64:64 + NCHUNK])
    w.copy(t.vsum_sb[:, :], t.sml[0:64, 72:72 + H])
    w.ts(t.vsum_sb[:, :], t.vsum_sb[:, :], 3.1)


def _rb_slot(t, kg):
    s = kg % 4
    if s < 2:
        return t.rbuf[:, s * 1024:(s + 1) * 1024]
    return t.scrb[:, s * 1024:(s + 1) * 1024]


def _emit_rv(w, t, qg, kg, nq):
    if kg < 0:
        return
    oo = kg // 8
    kk = kg % 8
    ob = oo % 2
    vk_e = t.vstream[:, ob * 1040 + kk * 130: ob * 1040 + kk * 130 + 65]
    vk_o = t.vstream[:, ob * 1040 + kk * 130 + 65: ob * 1040 + kk * 130 + 130]
    rb_ = _rb_slot(t, kg)
    if kg == 0 and t.fin_dve is not None:
        w.wait("dve", 'pe', val=t.fin_dve)
        t.fin_dve = None
    rs_, rv_ = t.rcp_of[kg]
    w.wait(rs_, 'pe', val=rv_)
    w.mm(t.p_kv[0:65, 0:nq], vk_e, rb_[:, 0:nq], start=(kg == 0), stop=(kg == NKG - 1))
    i = w.mm(t.p_kv[0:65, 512:512 + nq], vk_o, rb_[:, 512:512 + nq],
             start=(kg == 0), stop=(kg == NKG - 1))
    w.inc(i, "rv", 1)


def _fin_pair(w, t, qg, p, nq):
    qt = t.qtile[:, (p % 2) * 512:(p % 2) * 512 + nq]
    w.wait("rv", 'dve')
    ue = t.ut[:, 0:nq]
    uo = t.ut[:, 512:512 + nq]
    w.copy(ue, t.p_kv[0:64, 0:nq])
    w.copy(uo, t.p_kv[0:64, 512:512 + nq])
    w.wait("rv", 'pe')
    w.mm(t.p_mis[64:65, 0:nq], t.ksum_bf[0:64, p:p + 1], qt[0:64, :], tp=(0, 64))
    i = w.mm(t.p_mis[64:65, 512:512 + nq], t.ksum_bf[64:128, p:p + 1],
             qt[64:128, :], tp=(64, 64))
    w.inc(i, "pe", 1)
    w.wait("pe", 'dve')
    nr_e = t.scrf[64:65, 0:nq]
    nr_o = t.scrf[64:65, 512:512 + nq]
    w.copy(nr_e, t.p_kv[64:65, 0:nq])
    w.copy(nr_o, t.p_kv[64:65, 512:512 + nq])
    t.fin_dve = w.val("dve")
    w.stt(nr_e, nr_e, 4.41, t.p_mis[64:65, 0:nq], ALU.mult, ALU.subtract)
    w.stt(nr_o, nr_o, 4.41, t.p_mis[64:65, 512:512 + nq], ALU.mult, ALU.subtract)
    w.ts(nr_e, nr_e, NCONST, op0=ALU.add)
    w.ts(nr_o, nr_o, NCONST, op0=ALU.add)
    w.wait("dve", 'act')
    w.wait("pe", 'act')
    w.act(t.invn[64:65, 0:nq], nr_e, AF.Reciprocal)
    w.inc(w.act(t.invn[64:65, 512:512 + nq], nr_o, AF.Reciprocal), "acs", 1)
    w.wait("acs", 'pe')
    for half in range(2):
        co = half * 512
        usrc = ue if half == 0 else uo
        w.mm(t.p_mis[0:64, 0:nq], t.onesb[64:65, 0:64], t.invn[64:65, co:co + nq])
        if half == 0:
            i = w.mm(t.p_mis[0:64, 512:512 + nq], t.kvblk_a[0:64, p * 128:p * 128 + 64],
                     qt[0:64, :], tp=(0, 0))
        else:
            i = w.mm(t.p_mis[0:64, 512:512 + nq],
                     t.kvblk_a[64:128, p * 128 + 64:p * 128 + 128],
                     qt[64:128, :], tp=(64, 0))
        w.inc(i, "pe", 1)
        w.wait("pe", 'dve')
        e1 = t.scrf[0:64, 1024:1024 + nq]
        w.stt(e1, usrc, 4.41, t.p_mis[0:64, 512:512 + nq], ALU.mult, ALU.subtract)
        if half == 1:
            w.wait("dst", 'dve')
        dst = t.ofb[0:64, p * 512:p * 512 + nq] if half == 0 else t.scrb[0:64, 0:nq]
        w.stt(dst, e1, t.vsum_sb[:, 2 * p + half:2 * p + half + 1],
              t.p_mis[0:64, 0:nq], ALU.subtract, ALU.mult)
        if half == 1:
            w.wait("dve", 'pool')
            w.inc(w.dma('pool', t.ofb[64:128, p * 512:p * 512 + nq], t.scrb[0:64, 0:nq]),
                  "dst", 16)
        else:
            w.wait("dve", 'pe')
    t.fin_dve2 = w.val("dve")


def _attn_a(w, t, qg):
    nq = QGS[qg]
    q0 = QG0[qg]
    t.rcp_of = {}
    t.fin_dve = None
    t.fin_dve2 = None
    pend = None
    w.wait("dst", 'sync')
    for p in range(NCHUNK):
        qb = p % 2
        qt = t.qtile[:, qb * 512:qb * 512 + nq]
        w.wait("pe", 'sync', val=w.val("pe"))
        w.wait("qk", 'sync', val=w.val("qk"))
        w.inc(w.dma('sync', qt, t.qn_hbm[p, :, q0:q0 + nq]), "dq", 16)
        w.wait("dve", 'pe')
        dk_tok = {}
        for oct_ in range(10):
            if oct_ >= 2:
                oo = oct_ - 2
                ob = oo % 2
                for kk in range(8):
                    kg = oo * 8 + kk
                    pb = kg % 2
                    if kg < 5:
                        pqk = (t.p_qk_a, t.p_qk_b)[kg % 2]
                    else:
                        pqk = (t.p_qk_a, t.p_qk_b, t.p_mis)[kg % 3]
                    kn_e = t.kstream[0:64, ob * 1024 + kk * 128: ob * 1024 + (kk + 1) * 128]
                    kn_o = t.kstream[64:128, ob * 1024 + kk * 128: ob * 1024 + (kk + 1) * 128]
                    if kk == 0:
                        w.wait("dk", 'pe', val=dk_tok[oo])
                        if oo == 0:
                            w.wait("dq", 'pe')
                    w.wait("rcp", 'pe', val=w.val("rcp") - (0 if kg == 0 else 2))
                    if kg == 5 and t.fin_dve2 is not None:
                        w.wait("dve", 'pe', val=t.fin_dve2)
                        t.fin_dve2 = None
                    w.mm(pqk[:, 0:nq], kn_e, qt[0:64, :], tp=(0, 0))
                    i = w.mm(pqk[:, 512:512 + nq], kn_o, qt[64:128, :], tp=(64, 0))
                    w.inc(i, "qk", 1)
                    _emit_rv(w, t, qg, kg - 2, nq)
                    if kg == 1 and pend is not None:
                        _fin_pair(w, t, qg, pend, nq)
                        pend = None
                    rb_ = _rb_slot(t, kg)
                    w.wait("qk", 'act')
                    w.wait("rv", 'act', val=w.val("rv") - 2)
                    if nq == 512:
                        i = w.act(rb_[:, 0:1024], pqk[:, 0:1024], AF.Reciprocal,
                                  bias=1.1, scale=-1.0)
                    else:
                        w.act(rb_[:, 0:nq], pqk[:, 0:nq], AF.Reciprocal, bias=1.1, scale=-1.0)
                        i = w.act(rb_[:, 512:512 + nq], pqk[:, 512:512 + nq], AF.Reciprocal,
                                  bias=1.1, scale=-1.0)
                    w.inc(i, "rcp", 1)
                    t.rcp_of[kg] = ("rcp", w.val("rcp"))
            if oct_ < 8:
                ob2 = oct_ % 2
                w.wait("qk", 'sync', val=w.val("qk"))
                w.wait("rv", 'sync', val=w.val("rv") + (2 if oct_ >= 2 else 0))
                w.inc(w.dma('sync',
                            t.kstream[:, ob2 * 1024:(ob2 + 1) * 1024]
                            .rearrange("p (k c) -> p k c", k=8),
                            t.knT_hbm[oct_ * 8:(oct_ + 1) * 8, :, p * 128:(p + 1) * 128]
                            .rearrange("k p c -> p k c")),
                      "dk", 16)
                w.inc(w.dma('sync',
                            t.vstream[:, ob2 * 1040:(ob2 + 1) * 1040]
                            .rearrange("p (k c) -> p k c", k=8),
                            t.vk_hbm[oct_ * 8:(oct_ + 1) * 8, :, 2 * p * 65:(2 * p + 2) * 65]
                            .rearrange("k p c -> p k c")),
                      "dk", 16)
                dk_tok[oct_] = w.val("dk")
        _emit_rv(w, t, qg, NKG - 2, nq)
        _emit_rv(w, t, qg, NKG - 1, nq)
        pend = p
    _fin_pair(w, t, qg, pend, nq)
    w.wait("dst", 'pe')
    w.wait("dve", 'sync')
    xgv = t.xg.rearrange("p (a c) -> p a c", c=513)[:, :, 1:1 + nq]
    w.inc(w.dma('sync', xgv, t.x_hbm[:, :, q0:q0 + nq].rearrange("a p c -> p a c")), "dx", 16)
    w.wait("dx", 'dve')
    wc_tok = [None, None]
    for fc in range(NCHUNK):
        wb = fc % 2
        w.wait("pe", 'pool', val=(wc_tok[wb] if wc_tok[wb] is not None else w.val("pe")))
        w.inc(w.dma('pool', t.wcol[:, wb * 1024:(wb + 1) * 1024].rearrange("p (a c) -> p a c", a=8),
                    t.w_ao[:, fc * 128:(fc + 1) * 128].rearrange("(a p) c -> p a c", p=128)),
              "dw", 16)
        w.wait("dw", 'pe')
        w.wait("dve", 'pe')
        i = None
        for a in range(NCHUNK):
            i = w.mm(t.p_mis[:, 0:nq], t.wcol[:, wb * 1024 + a * 128:wb * 1024 + (a + 1) * 128],
                     t.ofb[:, a * 512:a * 512 + nq], start=(a == 0), stop=(a == NCHUNK - 1))
        w.inc(i, "pe", 1)
        wc_tok[wb] = w.val("pe")
        w.wait("pe", 'dve')
        xc = t.xg[:, fc * 513 + 1: fc * 513 + 1 + nq]
        w.ts(t.scrf[:, 0:nq], t.p_mis[:, 0:nq], t.bchunk_sb[:, fc:fc + 1], op0=ALU.add)
        w.tt(xc, xc, t.scrf[:, 0:nq], op=ALU.add)
    sl_ = (qg % 2) * 8
    w.copy(t.stash[:, sl_:sl_ + 8].rearrange("p (a one) -> p a one", one=1),
           t.xg.rearrange("p (a c) -> p a c", c=513)[:, :, nq:nq + 1])
    w.wait("dve", 'pe')


def _ln_fm(w, t, qg, resident=False):
    nq = QGS[qg]
    q0 = QG0[qg]
    nb = nq + 1
    bnd = qg > 0
    ONES_COL = t.onesb[:, 0:1]
    if resident:
        if bnd:
            sl_ = ((qg + 1) % 2) * 8
            w.copy(t.xg.rearrange("p (a c) -> p a c", c=513)[:, :, 0:1],
                   t.stash[:, sl_:sl_ + 8].rearrange("p (a one) -> p a one", one=1))
    else:
        w.wait("dve", 'sync')
        w.wait("pe", 'sync')
        if not bnd:
            xgv = t.xg.rearrange("p (a c) -> p a c", c=513)[:, :, 1:1 + nq]
            w.inc(w.dma('sync', xgv, t.x_hbm[:, :, 0:nq].rearrange("a p c -> p a c")), "dx", 16)
        else:
            xgv = t.xg.rearrange("p (a c) -> p a c", c=513)[:, :, 0:nb]
            w.inc(w.dma('sync', xgv, t.x_hbm[:, :, q0 - 1:q0 + nq].rearrange("a p c -> p a c")),
                  "dx", 16)
        w.wait("dx", 'dve')
    cast_tok = [None, None]
    for a in range(NCHUNK):
        sl = a % 2
        xb = t.scrb[:, sl * 1024:sl * 1024 + nb]
        x2 = t.scrb[:, 2048 + sl * 1024:2048 + sl * 1024 + nb]
        w.wait("pe", 'dve', val=(cast_tok[sl] if cast_tok[sl] is not None else w.val("pe")))
        if bnd:
            xsrc = t.xg[:, a * 513:a * 513 + nb]
            w.ts(xb, xsrc, 1.0)
            w.tt(x2, xsrc, xsrc, op=ALU.mult)
        else:
            xsrc = t.xg[:, a * 513 + 1:a * 513 + 1 + nq]
            w.ts(xb[:, 1:nb], xsrc, 1.0)
            w.tt(x2[:, 1:nb], xsrc, xsrc, op=ALU.mult)
        w.wait("dve", 'pe')
        w.mm(t.p_qk_a[0:1, 0:nq], ONES_COL, xb[:, 1:nb], start=(a == 0), stop=(a == NCHUNK - 1))
        i = w.mm(t.p_qk_b[0:1, 0:nq], ONES_COL, x2[:, 1:nb], start=(a == 0), stop=(a == NCHUNK - 1))
        if bnd:
            w.mm(t.p_qk_a[0:1, 512:513], ONES_COL, xb[:, 0:1],
                 start=(a == 0), stop=(a == NCHUNK - 1))
            i = w.mm(t.p_qk_b[0:1, 512:513], ONES_COL, x2[:, 0:1],
                     start=(a == 0), stop=(a == NCHUNK - 1))
        w.inc(i, "pe", 1)
        cast_tok[sl] = w.val("pe")
    w.wait("pe", 'dve')
    ncols = 513 if bnd else nq
    mrow = t.scrf[0:1, 0:ncols]
    vrow = t.scrf[0:1, 1024:1024 + ncols]
    w.ts(mrow, t.p_qk_a[0:1, 0:ncols], 1.0 / E)
    w.ts(vrow, t.p_qk_b[0:1, 0:ncols], 1.0 / E)
    w.tt(t.scrf[0:1, 2048:2048 + ncols], mrow, mrow, op=ALU.mult)
    w.tt(vrow, vrow, t.scrf[0:1, 2048:2048 + ncols], op=ALU.subtract)
    w.wait("dve", 'act')
    w.inc(w.act(t.scrf[0:1, 2048:2048 + ncols], vrow, AF.Rsqrt, bias=EPS_LN), "acs", 1)
    w.wait("acs", 'dve')
    w.ts(t.scrb[0:1, 0:ncols], mrow, 1.0)
    w.ts(t.scrb[0:1, 1024:1024 + ncols], t.scrf[0:1, 2048:2048 + ncols], 1.0)
    w.wait("dve", 'pe')
    w.mm(t.p_qk_b[:, 0:nq], t.onesb[0:1, 0:128], t.scrb[0:1, 0:nq])
    i = w.mm(t.p_qk_a[:, 0:nq], t.onesb[0:1, 0:128], t.scrb[0:1, 1024:1024 + nq])
    if bnd:
        w.mm(t.p_qk_b[:, 512:513], t.onesb[0:1, 0:128], t.scrb[0:1, 512:513])
        i = w.mm(t.p_qk_a[:, 512:513], t.onesb[0:1, 0:128], t.scrb[0:1, 1024 + 512:1024 + 513])
    w.inc(i, "pe", 1)
    w.wait("pe", 'dve')
    for a in range(NCHUNK):
        w.tt(t.scrf[:, 0:nq], t.xg[:, a * 513 + 1:a * 513 + 1 + nq],
             t.p_qk_b[:, 0:nq], op=ALU.subtract)
        w.tt(t.xn_sb[:, a * 520 + 1:a * 520 + 1 + nq], t.scrf[:, 0:nq],
             t.p_qk_a[:, 0:nq], op=ALU.mult)
        if bnd:
            w.tt(t.scrf[:, 512:513], t.xg[:, a * 513:a * 513 + 1],
                 t.p_qk_b[:, 512:513], op=ALU.subtract)
            w.tt(t.xn_sb[:, a * 520:a * 520 + 1], t.scrf[:, 512:513],
                 t.p_qk_a[:, 512:513], op=ALU.mult)
        else:
            w.memset(t.xn_sb[:, a * 520:a * 520 + 1], 0.0)


def _ffn(w, t, qg, resident=False, out_fuse=False):
    nq = QGS[qg]
    q0 = QG0[qg]
    _ln_fm(w, t, qg, resident=resident)
    i = None
    for a in range(NCHUNK):
        xn_c = t.xn_sb[:, a * 520 + 1:a * 520 + 1 + nq]
        xn_p = t.xn_sb[:, a * 520:a * 520 + nq]
        d = t.scrb[:, 0:nq]
        w.tt(d, xn_c, xn_p, op=ALU.subtract)
        w.stt(t.ffb[:, a * 512:a * 512 + nq], d, t.mu_sb[:, a:a + 1], xn_p, ALU.mult, ALU.add)
        i = w.stt(t.ffb[:, 4096 + a * 512:4096 + a * 512 + nq], d,
                  t.mu_sb[:, 8 + a:9 + a], xn_p, ALU.mult, ALU.add)
    w.wait("dve", 'pe')
    for dc in range(NCHUNK):
        pd = t.p_mis[:, (dc % 2) * 512:(dc % 2) * 512 + nq]
        if dc >= 2:
            w.wait("acs", 'pe', val=w.val("acs") - 1)
        i = None
        for a in range(NCHUNK):
            i = w.mm(pd, t.wbig[:, 8192 + a * 1024 + dc * 128:8192 + a * 1024 + (dc + 1) * 128],
                     t.ffb[:, 4096 + a * 512:4096 + a * 512 + nq],
                     start=(a == 0), stop=(a == NCHUNK - 1))
        w.inc(i, "qk", 1)
        w.wait("qk", 'act')
        w.inc(w.act(t.scrf[:, (dc % 2) * 512:(dc % 2) * 512 + nq], pd, AF.Relu), "acs", 1)
        w.wait("acs", 'dve')
        w.tt(t.kkb[:, dc * 512:dc * 512 + nq],
                   t.scrf[:, (dc % 2) * 512:(dc % 2) * 512 + nq],
                   t.scrf[:, (dc % 2) * 512:(dc % 2) * 512 + nq], op=ALU.mult)
    w.wait("dve", 'pe')
    for dc in range(NCHUNK):
        pr = t.p_mis[:, 0:nq]
        pv = t.p_mis[:, 512:512 + nq]
        if dc >= 1:
            w.wait("acs", 'pe')
        i = None
        for a in range(NCHUNK):
            i = w.mm(pr, t.wbig[:, a * 1024 + dc * 128:a * 1024 + (dc + 1) * 128],
                     t.ffb[:, a * 512:a * 512 + nq], start=(a == 0), stop=(a == NCHUNK - 1))
        w.inc(i, "qk", 1)
        w.wait("qk", 'act')
        if dc >= 1:
            w.wait("dve", 'act')
        w.inc(w.act(t.scrf[:, 0:nq], pr, AF.Sigmoid), "acs", 1)
        if dc >= 1:
            w.wait("dve", 'pe')
        i = None
        for a in range(NCHUNK):
            i = w.mm(pv, t.wbig[:, 16384 + a * 1024 + dc * 128:16384 + a * 1024 + (dc + 1) * 128],
                     t.kkb[:, a * 512:a * 512 + nq], start=(a == 0), stop=(a == NCHUNK - 1))
        w.inc(i, "rv", 1)
        w.wait("acs", 'dve')
        w.wait("rv", 'dve')
        xc = t.xg[:, dc * 513 + 1: dc * 513 + 1 + nq]
        w.tt(t.scrf[:, 512:512 + nq], pv, t.scrf[:, 0:nq], op=ALU.mult)
        w.tt(xc, xc, t.scrf[:, 512:512 + nq], op=ALU.add)
    if out_fuse:
        for tl in range(nq // 128):
            w.wait("dve", 'pe')
            i = None
            for a in range(NCHUNK):
                i = w.trans(t.p_mis[:, a * 128:(a + 1) * 128],
                            t.xg[:, a * 513 + 1 + tl * 128:a * 513 + 1 + (tl + 1) * 128],
                            t.eyef_sb[:, :])
            w.inc(i, "pe", 1)
            w.wait("pe", 'dve')
            ob = tl % 2
            w.wait("oput", 'dve', val=t.ot_tok[ob])
            ot = t.xtok[:, ob * 1024:(ob + 1) * 1024]
            for a in range(NCHUNK):
                w.copy(ot[:, a * 128:(a + 1) * 128], t.p_mis[:, a * 128:(a + 1) * 128])
            w.wait("dve", 'sync')
            w.inc(w.dma('sync', t.out_h[q0 + tl * 128:q0 + (tl + 1) * 128, :], ot), "oput", 16)
            t.ot_tok[ob] = w.val("oput")
    else:
        w.wait("dve", 'sync')
        xgv = t.xg.rearrange("p (a c) -> p a c", c=513)[:, :, 1:1 + nq]
        w.inc(w.dma('sync', t.x_hbm[:, :, q0:q0 + nq].rearrange("a p c -> p a c"), xgv), "dsx", 16)
    w.wait("dve", 'pe')


def _attnb_pre(w, t, qg):
    nq = QGS[qg]
    q0 = QG0[qg]
    _ln_fm(w, t, qg)
    w.wait("dve", 'pe')
    for dc in range(NCHUNK):
        pd = t.p_mis[:, (dc % 2) * 512:(dc % 2) * 512 + nq]
        if dc >= 2:
            w.wait("dve", 'pe', val=w.val("dve") - 3)
        i = None
        for a in range(NCHUNK):
            i = w.mm(pd, t.wbig[:, a * 1024 + dc * 128:a * 1024 + (dc + 1) * 128],
                     t.xn_sb[:, a * 520 + 1:a * 520 + 1 + nq],
                     start=(a == 0), stop=(a == NCHUNK - 1))
        w.inc(i, "qk", 1)
        w.wait("qk", 'dve')
        bq = t.bchunk_sb[:, 8 + dc:9 + dc]
        mn = t.scrf[:, 0:nq]
        mx = t.scrf[:, 512:512 + nq]
        w.ts(mn, pd, bq, 0.0, op0=ALU.add, op1=ALU.min)
        w.ts(mx, pd, bq, 0.0, op0=ALU.add, op1=ALU.max)
        w.wait("dve", 'act')
        w.inc(w.act(t.scrf[:, 1024:1024 + nq], mn, AF.Exp), "acs", 1)
        w.wait("acs", 'dve')
        w.wait("dst", 'dve', val=w.val("dst") - 16)
        w.tt(t.stage[:, (dc % 2) * 1024:(dc % 2) * 1024 + nq],
                   t.scrf[:, 1024:1024 + nq], mx, op=ALU.add)
        w.wait("dve", 'pool')
        w.inc(w.dma('pool', t.qb_hbm[dc, :, q0:q0 + nq],
                    t.stage[:, (dc % 2) * 1024:(dc % 2) * 1024 + nq]), "dst", 16)
    ntt_q = nq // 128
    for tl in range(ntt_q):
        ttg = q0 // 128 + tl
        mcol = t.maskv_sb[:, ttg:ttg + 1]
        for half in range(2):
            pk = t.p_mis[:, 0:512]
            if tl > 0 or half > 0:
                w.wait("dve", 'pe')
            i = None
            for a in range(NCHUNK):
                i = w.mm(pk, t.xn_sb[:, a * 520 + 1 + tl * 128:a * 520 + 1 + (tl + 1) * 128],
                         t.wbig[:, 8192 + a * 1024 + half * 512:8192 + a * 1024 + (half + 1) * 512],
                         start=(a == 0), stop=False)
            w.mm(pk, t.onesb[0:1, 0:128],
                 t.brow_sb[0:1, 3 * E + half * 512:3 * E + (half + 1) * 512],
                 start=False, stop=False)
            i = w.mm(pk, t.onesb[0:1, 0:128],
                     t.brow_lo[0:1, 3 * E + half * 512:3 * E + (half + 1) * 512],
                     start=False, stop=True)
            w.inc(i, "qk", 1)
            w.wait("qk", 'dve')
            mn = t.scrf[:, 0:512]
            mx = t.scrf[:, 512:1024]
            w.ts(mn, pk, 0.0, None, op0=ALU.min)
            w.ts(mx, pk, 0.0, None, op0=ALU.max)
            w.wait("dve", 'act')
            w.inc(w.act(t.scrf[:, 1024:1536], mn, AF.Exp), "acs", 1)
            w.wait("acs", 'dve')
            w.wait("rv", 'dve')
            w.tt(t.scrf[:, 1536:2048], t.scrf[:, 1024:1536], mx, op=ALU.add)
            w.ts(t.scrb[:, half * 512:(half + 1) * 512], t.scrf[:, 1536:2048], mcol)
            w.wait("dve", 'pe')
            pv = t.p_mis[:, 512:1024]
            i = None
            for a in range(NCHUNK):
                i = w.mm(pv, t.xn_sb[:, a * 520 + 1 + tl * 128:a * 520 + 1 + (tl + 1) * 128],
                         t.wbig[:, 16384 + a * 1024 + half * 512:16384 + a * 1024 + (half + 1) * 512],
                         start=(a == 0), stop=False)
            w.mm(pv, t.onesb[0:1, 0:128],
                 t.brow_sb[0:1, 4 * E + half * 512:4 * E + (half + 1) * 512],
                 start=False, stop=False)
            i = w.mm(pv, t.onesb[0:1, 0:128],
                     t.brow_lo[0:1, 4 * E + half * 512:4 * E + (half + 1) * 512],
                     start=False, stop=True)
            w.inc(i, "qk", 1)
            w.wait("qk", 'dve')
            w.copy(t.scrb[:, 1024 + half * 512:1024 + (half + 1) * 512], pv)
        w.wait("dve", 'pe')
        i = None
        for pp in range(NCHUNK):
            i = w.mm(t.p_qk_a[:, pp * 128:(pp + 1) * 128], t.scrb[:, pp * 128:(pp + 1) * 128],
                     t.scrb[:, 1024 + pp * 128:1024 + (pp + 1) * 128],
                     start=True, stop=True)
        w.inc(i, "rv", 1)
        w.wait("rv", 'dve')
        w.tt(t.kvacc_b[:, :], t.kvacc_b[:, :], t.p_qk_a[:, :], op=ALU.add)
    w.wait("dve", 'pe')


def _attnb_reduce(w, t):
    w.wait("dve", 'pool')
    w.inc(w.dma('pool', t.kv_in, t.kvacc_b[:, :]), "dst", 16)
    w.wait("dst", 'pool')
    w.inc(w.cc([t.kv_in], [t.kv_out]), "cc", 1)
    w.wait("cc", 'pool')
    w.inc(w.dma('pool', t.scrf[:, 0:1024], t.kv_out), "dst", 16)
    w.wait("dst", 'dve')
    i = None
    for a in range(NCHUNK):
        w.copy(t.kvblk_b[0:64, a * 128:a * 128 + 64], t.scrf[0:64, a * 128:a * 128 + 64])
        i = w.copy(t.kvblk_b[64:128, a * 128 + 64:a * 128 + 128],
                   t.scrf[64:128, a * 128 + 64:a * 128 + 128])
    w.wait("dve", 'pe')


def _attnb_out(w, t, qg):
    nq = QGS[qg]
    q0 = QG0[qg]
    w.wait("dst", 'sync')
    qk_tok = [None, None]
    for p in range(NCHUNK):
        sl = p % 2
        qp = t.qtile[:, sl * 512:sl * 512 + nq]
        w.wait("qk", 'sync', val=(qk_tok[sl] if qk_tok[sl] is not None else w.val("qk")))
        w.inc(w.dma('sync', qp, t.qb_hbm[p, :, q0:q0 + nq]), "dq", 16)
        w.wait("dq", 'pe')
        w.wait("dve", 'pe', val=w.val("dve") - 1 if p >= 2 else w.val("dve"))
        w.inc(w.mm(t.p_mis[:, sl * 512:sl * 512 + nq],
                   t.kvblk_b[:, p * 128:(p + 1) * 128], qp), "qk", 1)
        qk_tok[sl] = w.val("qk")
        w.wait("qk", 'dve')
        if p == 0:
            w.wait("pe", 'dve')
        w.copy(t.ofb[:, p * 512:p * 512 + nq], t.p_mis[:, sl * 512:sl * 512 + nq])
    w.barrier()
    w.wait("dve", 'pe')
    w.wait("dve", 'sync')
    w.wait("pe", 'sync')
    xgv = t.xg.rearrange("p (a c) -> p a c", c=513)[:, :, 1:1 + nq]
    w.inc(w.dma('sync', xgv, t.x_hbm[:, :, q0:q0 + nq].rearrange("a p c -> p a c")), "dx", 16)
    w.wait("dx", 'dve')
    wc_tok = [None, None]
    for fc in range(NCHUNK):
        wb = fc % 2
        w.wait("pe", 'pool', val=(wc_tok[wb] if wc_tok[wb] is not None else w.val("pe")))
        w.inc(w.dma('pool', t.wcol[:, wb * 1024:(wb + 1) * 1024].rearrange("p (a c) -> p a c", a=8),
                    t.w_bo[:, fc * 128:(fc + 1) * 128].rearrange("(a p) c -> p a c", p=128)),
              "dw", 16)
        w.wait("dw", 'pe')
        w.wait("dve", 'pe')
        i = None
        for a in range(NCHUNK):
            i = w.mm(t.p_kv[:, 0:nq], t.wcol[:, wb * 1024 + a * 128:wb * 1024 + (a + 1) * 128],
                     t.ofb[:, a * 512:a * 512 + nq], start=(a == 0), stop=(a == NCHUNK - 1))
        w.inc(i, "pe", 1)
        wc_tok[wb] = w.val("pe")
        w.wait("pe", 'dve')
        xc = t.xg[:, fc * 513 + 1: fc * 513 + 1 + nq]
        w.ts(t.scrf[:, 0:nq], t.p_kv[:, 0:nq], t.bchunk_sb[:, 16 + fc:17 + fc], op0=ALU.add)
        w.tt(xc, xc, t.scrf[:, 0:nq], op=ALU.add)
        w.barrier()
    sl_ = (qg % 2) * 8
    w.copy(t.stash[:, sl_:sl_ + 8].rearrange("p (a one) -> p a one", one=1),
           t.xg.rearrange("p (a c) -> p a c", c=513)[:, :, nq:nq + 1])
    w.wait("dve", 'pe')


def _output(w, t):
    ot_tok = [0, 0]
    ld_tok = [0, 0]
    for tt_ in range(NTT):
        lb = tt_ % 2
        xf = t.xfst[:, 0:1024] if lb == 0 else t.scrf[:, 0:1024]
        w.wait("pe", 'sync', val=ld_tok[lb])
        w.inc(w.dma('sync', xf.rearrange("p (a c) -> p a c", a=8),
                    t.x_hbm[:, :, tt_ * 128:(tt_ + 1) * 128].rearrange("a p c -> p a c")),
              "dx", 16)
        w.wait("dx", 'pe')
        w.wait("dve", 'pe')
        i = None
        for a in range(NCHUNK):
            i = w.trans(t.p_mis[:, a * 128:(a + 1) * 128], xf[:, a * 128:(a + 1) * 128],
                        t.eyef_sb[:, :])
        w.inc(i, "pe", 1)
        ld_tok[lb] = w.val("pe")
        w.wait("pe", 'dve')
        ob = tt_ % 2
        w.wait("oput", 'dve', val=ot_tok[ob])
        ot = t.xtok[:, ob * 1024:(ob + 1) * 1024]
        i = None
        for a in range(NCHUNK):
            i = w.copy(ot[:, a * 128:(a + 1) * 128], t.p_mis[:, a * 128:(a + 1) * 128])
        w.wait("dve", 'sync')
        w.inc(w.dma('sync', t.out_h[tt_ * 128:(tt_ + 1) * 128, :], ot), "oput", 16)
        ot_tok[ob] = w.val("oput")



def _host_inputs(inputs):
    x = np.ascontiguousarray(np.asarray(inputs["L0_attn_a_x"], dtype=np.float32))
    g = lambda k: np.ascontiguousarray(np.asarray(inputs[k], dtype=np.float32))
    shared = {
        "w_aq": g("L0_attn_a_Wq"), "w_ak": g("L0_attn_a_Wk"),
        "w_av": g("L0_attn_a_Wv"), "w_ao": g("L0_attn_a_Wo"),
        "w_fr": np.ascontiguousarray(g("L0_ffn_W_r").T),
        "w_fk": np.ascontiguousarray(g("L0_ffn_W_k").T),
        "w_fv": np.ascontiguousarray(g("L0_ffn_W_v").T),
        "w_bq": np.ascontiguousarray(g("L0_attn_b_Wq").T),
        "w_bk": np.ascontiguousarray(g("L0_attn_b_Wk").T),
        "w_bv": np.ascontiguousarray(g("L0_attn_b_Wv").T),
        "w_bo": np.ascontiguousarray(g("L0_attn_b_Wo").T),
    }
    b_rows = np.stack([g("L0_attn_a_bq"), g("L0_attn_a_bk"), g("L0_attn_a_bv"),
                       g("L0_attn_b_bk"), g("L0_attn_b_bv")]).astype(np.float32)
    b_chunk = np.zeros((128, 4 * NCHUNK), np.float32)
    b_chunk[:, 0:8] = g("L0_attn_a_bo").reshape(NCHUNK, 128).T
    b_chunk[:, 8:16] = g("L0_attn_b_bq").reshape(NCHUNK, 128).T
    b_chunk[:, 16:24] = g("L0_attn_b_bo").reshape(NCHUNK, 128).T
    mu_chunk = np.zeros((128, 2 * NCHUNK), np.float32)
    mu_chunk[:, 0:8] = g("L0_ffn_mu_r").reshape(NCHUNK, 128).T
    mu_chunk[:, 8:16] = g("L0_ffn_mu_k").reshape(NCHUNK, 128).T
    shared["b_rows"] = np.ascontiguousarray(b_rows)
    shared["b_chunk"] = np.ascontiguousarray(b_chunk)
    shared["mu_chunk"] = np.ascontiguousarray(mu_chunk)
    shared["eyef"] = np.eye(128, dtype=np.float32)
    shared["eyeb"] = np.eye(128, dtype=np.float32).astype(ml_dtypes.bfloat16)
    in_maps = []
    for c in range(8):
        b, h = c // 2, c % 2
        m = dict(shared)
        m["xq"] = np.ascontiguousarray(x[b, 3968 * h: 3968 * h + TOK, :])
        m["xfull"] = np.ascontiguousarray(x[b])
        mk = np.zeros(TOK, np.float32)
        if h == 0:
            mk[0:4096] = 1.0
        else:
            mk[128:TOK] = 1.0
        m["maskv"] = np.ascontiguousarray(mk.reshape(NTT, 128).T)
        in_maps.append(m)
    return in_maps


def run_staged(inputs, stop_stage, cores=None):
    nc = build_program(stop_stage=stop_stage)
    in_maps = _host_inputs(inputs)
    cores = cores if cores is not None else list(range(8))
    res = run_bass_kernel_spmd(nc, [in_maps[c] for c in cores], core_ids=cores)
    return [r["dbg"] for r in res.results]


def kernel(**inputs):
    nc = build_program()
    in_maps = _host_inputs(inputs)
    res = run_bass_kernel_spmd(nc, in_maps, core_ids=list(range(8)))
    y = np.empty((B, S, E), np.float32)
    for c in range(8):
        b, h = c // 2, c % 2
        o = res.results[c]["out"]
        if h == 0:
            y[b, 0:4096] = o[0:4096]
        else:
            y[b, 4096:8192] = o[128:TOK]
    return y


# version 16
# speedup vs baseline: 2.1984x; 1.0012x over previous
import dataclasses
import numpy as np
import ml_dtypes

import concourse.bass as bass
import concourse.mybir as mybir
from concourse.bass_utils import run_bass_kernel_spmd

F32 = mybir.dt.float32
BF16 = mybir.dt.bfloat16
AF = mybir.ActivationFunctionType
ALU = mybir.AluOpType

E = 1024
H = 16
D = 64
S = 8192
B = 4
NCHUNK = 8
TOK = 4224
NTT = 33
NKG = 64
QGS = [512] * 8 + [128]
QG0 = [sum(QGS[:i]) for i in range(len(QGS))]
NQG = len(QGS)
EPS_LN = 1e-5
EPS_NRM = 1e-8
EPS_ATT = 1e-6
NCONST = -3.1 * S + EPS_ATT

SEM_NAMES = ["dx", "dk", "dq", "dw", "dst", "dsx", "qk", "rcp", "rv", "pe", "dve",
             "acs", "cc", "oput"]


def _act_raw(nc, eng, out, in_, func, bias=0.0, scale=1.0):
    bi = eng.activation(out=out, in_=in_, func=AF.Exp, bias=bias, scale=scale)
    i2 = dataclasses.replace(bi.ins, func=func)
    nc.register_instruction(i2, overwrite=True)
    insts = nc.cur_bb.bb.instructions
    assert insts[-1] is bi.ins
    insts[-1] = i2
    return bass.BassInstruction(i2)


class Walk:

    def __init__(self, nc, kind, eng, sems):
        self.nc = nc
        self.kind = kind
        self.eng = eng
        self.sems = sems
        self.c = {s: 0 for s in SEM_NAMES}

    def inc(self, ins, name, n=1):
        self.c[name] += n
        if ins is not None:
            ins.then_inc(self.sems[name], n)

    def wait(self, name, kind, val=None):
        if self.kind == kind:
            v = self.c[name] if val is None else val
            if v > 0:
                self.eng.wait_ge(self.sems[name], v)

    def val(self, name):
        return self.c[name]

    _OWNER = {"qk": 'pe', "rv": 'pe', "pe": 'pe', "rcp": 'act', "acs": 'act',
              "dve": 'dve', "dk": 'sync', "dq": 'sync', "dx": 'sync', "dsx": 'sync',
              "oput": 'sync', "dw": 'pool', "dst": 'pool', "cc": 'pool'}

    def barrier(self):
        for s in SEM_NAMES:
            own = self._OWNER[s]
            for k in ('pe', 'act', 'dve', 'sync', 'pool'):
                if k != own:
                    self.wait(s, k)

    def mm(self, out, lhsT, rhs, start=True, stop=True, tp=None):
        if self.kind != 'pe':
            return None
        return self.eng.matmul(out, lhsT=lhsT, rhs=rhs, start=start, stop=stop,
                               tile_position=tp)

    def trans(self, out, in_, ident):
        if self.kind != 'pe':
            return None
        return self.eng.transpose(out, in_, ident)

    def act(self, out, in_, func, bias=0.0, scale=1.0, accum_out=None):
        if self.kind != 'act':
            return None
        if func in (AF.Reciprocal, AF.Rsqrt):
            return _act_raw(self.nc, self.eng, out, in_, func, bias, scale)
        kw = {} if accum_out is None else {"accum_out": accum_out}
        return self.eng.activation(out=out, in_=in_, func=func, bias=bias,
                                   scale=scale, **kw)

    def _dve(self, f):
        if self.kind != 'dve':
            self.c["dve"] += 1
            return None
        self.eng.wait_ge(self.sems["dve"], self.c["dve"])
        ins = f()
        self.c["dve"] += 1
        ins.then_inc(self.sems["dve"], 1)
        return ins

    def ts(self, out, in0, s1, s2=None, op0=ALU.mult, op1=ALU.bypass):
        return self._dve(lambda: self.eng.tensor_scalar(
            out=out, in0=in0, scalar1=s1, scalar2=s2, op0=op0, op1=op1))

    def tt(self, out, in0, in1, op=ALU.add):
        return self._dve(lambda: self.eng.tensor_tensor(out=out, in0=in0, in1=in1, op=op))

    def stt(self, out, in0, scalar, in1, op0, op1):
        return self._dve(lambda: self.eng.scalar_tensor_tensor(
            out=out, in0=in0, scalar=scalar, in1=in1, op0=op0, op1=op1))

    def copy(self, out, in_):
        return self._dve(lambda: self.eng.tensor_copy(out=out, in_=in_))

    def memset(self, ap, v):
        return self._dve(lambda: self.eng.memset(ap, v))

    def red(self, out, in_, op=ALU.add):
        return self._dve(lambda: self.eng.tensor_reduce(out, in_, mybir.AxisListType.X, op))

    def vrecip_fast(self, out, in_):
        return self._dve(lambda: self.eng.reciprocal_approx_fast(out=out, in_=in_))

    def vrecip(self, out, in_):
        def f():
            if out.dtype == BF16:
                with self.nc.allow_low_precision(reason="bf16 store of fp32 reciprocal"):
                    return self.eng.reciprocal(out=out, in_=in_)
            return self.eng.reciprocal(out=out, in_=in_)
        return self._dve(f)

    def dma(self, queue, out, in_):
        if self.kind != queue:
            return None
        return self.eng.dma_start(out=out, in_=in_)

    def cc(self, ins_, outs_):
        if self.kind != 'pool':
            return None
        return self.eng.collective_compute(
            "AllReduce", ALU.add, replica_groups=[[0, 1], [2, 3], [4, 5], [6, 7]],
            ins=ins_, outs=outs_)


class L:
    pass


def build_program(stop_stage=None):
    nc = bass.Bass("TRN2", target_bir_lowering=False, debug=False)
    t = L()

    t.xq = nc.dram_tensor("xq", [TOK, E], F32, kind="ExternalInput").ap()
    t.xfull = nc.dram_tensor("xfull", [S, E], F32, kind="ExternalInput").ap()
    for nm in ["w_aq", "w_ak", "w_av", "w_ao", "w_fr", "w_fk", "w_fv",
               "w_bq", "w_bk", "w_bv", "w_bo"]:
        setattr(t, nm, nc.dram_tensor(nm, [E, E], F32, kind="ExternalInput").ap())
    t.b_rows = nc.dram_tensor("b_rows", [5, E], F32, kind="ExternalInput").ap()
    t.b_chunk = nc.dram_tensor("b_chunk", [128, 4 * NCHUNK], F32, kind="ExternalInput").ap()
    t.mu_chunk = nc.dram_tensor("mu_chunk", [128, 2 * NCHUNK], F32, kind="ExternalInput").ap()
    t.eyef = nc.dram_tensor("eyef", [128, 128], F32, kind="ExternalInput").ap()
    t.eyeb = nc.dram_tensor("eyeb", [128, 128], BF16, kind="ExternalInput").ap()
    t.maskv = nc.dram_tensor("maskv", [128, NTT], F32, kind="ExternalInput").ap()

    t.knT_hbm = nc.dram_tensor("knT_hbm", [NKG, 128, NCHUNK * 128], BF16, kind="Internal").ap()
    t.vk_hbm = nc.dram_tensor("vk_hbm", [NKG, 128, H * 65], BF16, kind="Internal").ap()
    t.qn_hbm = nc.dram_tensor("qn_hbm", [NCHUNK, 128, TOK], BF16, kind="Internal").ap()
    t.qb_hbm = nc.dram_tensor("qb_hbm", [NCHUNK, 128, TOK], BF16, kind="Internal").ap()
    t.kv_in = nc.dram_tensor("kv_in", [128, 1024], F32, kind="Internal").ap()
    t.kv_out = nc.dram_tensor("kv_out", [128, 1024], F32, kind="Internal").ap()
    t.x_hbm = nc.dram_tensor("x_hbm", [NCHUNK, 128, TOK], F32, kind="Internal").ap()
    t.out_h = nc.dram_tensor("out", [TOK, E], F32, kind="ExternalOutput").ap()
    t.dbg = None
    if stop_stage is not None:
        t.dbg = nc.dram_tensor("dbg", [128, NCHUNK * TOK], F32, kind="ExternalOutput").ap()

    import contextlib
    with contextlib.ExitStack() as st:
        def sb(name, shape, dt):
            return st.enter_context(nc.sbuf_tensor(name, shape, dt))

        def ps(name, shape):
            return st.enter_context(nc.psum_tensor(name, shape, F32))

        t.xg = sb("xg", [128, NCHUNK * 513], F32)
        t.xfst = sb("xfst", [128, 1024], F32)
        t.consts = sb("consts", [128, 16], F32)
        t.onesb = sb("onesb", [128, 576], BF16)
        t.eyef_sb = sb("eyef_sb", [128, 128], F32)
        t.eyeb_sb = sb("eyeb_sb", [128, 128], BF16)
        t.maskv_sb = sb("maskv_sb", [128, NTT], F32)
        t.brow_sb = sb("brow_sb", [1, 5 * E], BF16)
        t.brow_lo = sb("brow_lo", [1, 5 * E], BF16)
        t.bchunk_sb = sb("bchunk_sb", [128, 4 * NCHUNK], F32)
        t.mu_sb = sb("mu_sb", [128, 2 * NCHUNK], F32)
        t.ksum_bf = sb("ksum_bf", [128, NCHUNK], BF16)
        t.vsum_sb = sb("vsum_sb", [64, H], F32)
        t.kvblk_a = sb("kvblk_a", [128, NCHUNK * 128], BF16)
        t.kvblk_b = sb("kvblk_b", [128, NCHUNK * 128], BF16)
        t.kvacc_b = sb("kvacc_b", [128, 1024], F32)
        t.wbig = sb("wbig", [128, 3 * 8192], BF16)
        t.wcol = sb("wcol", [128, 2 * 1024], BF16)
        t.stage = sb("stage", [128, 2048 + H * 65], BF16)
        t.xtok = sb("xtok", [128, 2 * 1024], F32)
        t.scrf = sb("scrf", [128, 3 * 1024], F32)
        t.scrb = sb("scrb", [128, 4 * 1024], BF16)
        t.sml = sb("sml", [128, 96], F32)
        t.stash = sb("stash", [128, 16], F32)
        t.kvacc_a = sb("kvacc_a", [128, 1024], F32)
        t.kstream = sb("kstream", [128, 2 * 1024], BF16)
        t.vstream = sb("vstream", [128, 2 * 1040], BF16)
        t.qtile = sb("qtile", [128, 2 * 512], BF16)
        t.rbuf = sb("rbuf", [128, 2 * 1024], BF16)
        t.ut = sb("ut", [64, 2 * 512], F32)
        t.invn = sb("invn", [65, 1024], BF16)
        t.ofb = sb("ofb", [128, NCHUNK * 512], BF16)
        t.xn_sb = sb("xn_sb", [128, NCHUNK * 520], BF16)
        t.ffb = sb("ffb", [128, 2 * NCHUNK * 512], BF16)
        t.kkb = sb("kkb", [128, NCHUNK * 512], BF16)

        t.p_qk_a = ps("p_qk_a", [128, 1024])
        t.p_qk_b = ps("p_qk_b", [128, 1024])
        t.p_kv = ps("p_kv", [128, 1024])
        t.p_mis = ps("p_mis", [128, 1024])

        for i, v in enumerate([1.1, -1.0, 0.0, 1.0, EPS_LN, EPS_NRM, 4.41, -3.1,
                               1.0 / E, NCONST, 0.5, 2.0]):
            nc.const_aps.register(F32, v, t.consts[:, i:i + 1])

        sems = {s: st.enter_context(nc.semaphore(s)) for s in SEM_NAMES}

        with nc.Block() as block:
            @block.sync
            def _(eng):
                _emit(Walk(nc, 'sync', eng, sems), t, stop_stage)

            @block.tensor
            def _(eng):
                _emit(Walk(nc, 'pe', eng, sems), t, stop_stage)

            @block.scalar
            def _(eng):
                _emit(Walk(nc, 'act', eng, sems), t, stop_stage)

            @block.vector
            def _(eng):
                _emit(Walk(nc, 'dve', eng, sems), t, stop_stage)

            @block.gpsimd
            def _(eng):
                _emit(Walk(nc, 'pool', eng, sems), t, stop_stage)

    return nc



def _emit(w: Walk, t, stop_stage):
    _setup(w, t)
    if stop_stage == 'p0':
        _prep(w, t, stop_p0=True)
        _fin(w, t); return
    _prep(w, t)
    w.barrier()
    if stop_stage == 'prep':
        _dump(w, t); _fin(w, t); return
    if stop_stage == 'a0':
        _attn_a(w, t, 0)
        _dump_a0(w, t); _fin(w, t); return
    t.ot_tok = [0, 0]
    _load3(w, t, [t.w_fr, t.w_fk, t.w_fv])
    for qg in range(NQG):
        _attn_a(w, t, qg)
        w.barrier()
        _ffn(w, t, qg, resident=True)
        w.barrier()
    if stop_stage == 'x2':
        _dump(w, t); _fin(w, t); return
    _load3(w, t, [t.w_bq, t.w_bk, t.w_bv])
    for qg in range(NQG):
        _attnb_pre(w, t, qg)
        w.barrier()
        if stop_stage == 'b0' and qg == 0:
            w.wait("dve", 'pool')
            w.wait("dst", 'pool')
            d = t.dbg
            w.inc(w.dma('pool', d[:, 0:1024], t.kvacc_b[:, :]), "oput", 16)
            w.inc(w.dma('pool', d[:, 1024:2048], t.kvblk_b[:, :]), "oput", 16)
            w.inc(w.dma('pool', d[:, 2048:2560], t.qb_hbm[0, :, 0:512]), "oput", 16)
            w.inc(w.dma('pool', d[:, 2560:3584], t.scrb[:, 0:1024]), "oput", 16)
            w.inc(w.dma('pool', d[:, 3584:4608], t.scrb[:, 1024:2048]), "oput", 16)
            w.inc(w.dma('pool', d[:, 4608:5128], t.xn_sb[:, 0:520]), "oput", 16)
            _fin(w, t); return
    w.barrier()
    _attnb_reduce(w, t)
    w.barrier()
    if stop_stage == 'b1':
        w.wait("dve", 'pool')
        d = t.dbg
        w.inc(w.dma('pool', d[:, 0:1024], t.kvacc_b[:, :]), "oput", 16)
        w.inc(w.dma('pool', d[:, 1024:2048], t.kvblk_b[:, :]), "oput", 16)
        _fin(w, t); return
    _load3(w, t, [t.w_fr, t.w_fk, t.w_fv])
    for qg in range(NQG):
        _attnb_out(w, t, qg)
        w.barrier()
        _ffn(w, t, qg, resident=True, out_fuse=True)
        w.barrier()
    _fin(w, t)


def _dump(w, t):
    w.wait("dve", 'sync')
    w.wait("pe", 'sync')
    w.wait("dst", 'sync')
    w.wait("dsx", 'sync')
    for a in range(NCHUNK):
        w.inc(w.dma('sync', t.dbg[:, a * TOK:(a + 1) * TOK], t.x_hbm[a]), "oput", 16)


def _dump_a0(w, t):
    w.wait("dve", 'pool')
    w.wait("pe", 'pool')
    w.wait("acs", 'pool')
    w.wait("dst", 'pool')
    d = t.dbg
    w.inc(w.dma('pool', d[:, 0:4096], t.ofb[:, :]), "oput", 16)
    w.inc(w.dma('pool', d[0:65, 4096:5120], t.invn[:, :]), "oput", 16)
    w.inc(w.dma('pool', d[0:64, 5120:6144], t.ut[:, :]), "oput", 16)
    w.inc(w.dma('pool', d[64:65, 6144:7168], t.scrf[64:65, 0:1024]), "oput", 16)
    w.inc(w.dma('pool', d[:, 7168:7680], t.qn_hbm[0, :, 0:512]), "oput", 16)
    w.inc(w.dma('pool', d[:, 7680:7808], t.knT_hbm[0, :, 0:128]), "oput", 16)
    w.inc(w.dma('pool', d[:, 7808:7938], t.vk_hbm[0, :, 0:130]), "oput", 16)
    w.inc(w.dma('pool', d[:, 7938:7946], t.ksum_bf[:, :]), "oput", 16)
    w.inc(w.dma('pool', d[0:64, 7946:7962], t.vsum_sb[:, :]), "oput", 16)
    w.inc(w.dma('pool', d[:, 8000:9024], t.kvblk_a[:, :]), "oput", 16)
    w.inc(w.dma('pool', d[:, 9024:9056], t.bchunk_sb[:, :]), "oput", 16)
    w.copy(t.scrf[64:65, 1024:2048], t.p_kv[64:65, 0:1024])
    w.wait("dve", 'pool')
    w.inc(w.dma('pool', d[64:65, 9056:10080], t.scrf[64:65, 1024:2048]), "oput", 16)
    w.inc(w.dma('pool', d[:, 10080:12128], t.rbuf[:, :]), "oput", 16)


def _fin(w, t):
    if w.kind == 'pool':
        w.eng.wait_ge(w.sems["oput"], w.c["oput"])


def _setup(w, t):
    for i, v in enumerate([1.1, -1.0, 0.0, 1.0, EPS_LN, EPS_NRM, 4.41, -3.1,
                           1.0 / E, NCONST, 0.5, 2.0]):
        w.memset(t.consts[:, i:i + 1], v)
    w.memset(t.onesb[:, :], 1.0)
    w.memset(t.kvacc_b[:, :], 0.0)
    w.memset(t.kvacc_a[:, :], 0.0)
    w.memset(t.sml[:, 64:88], 0.0)
    w.memset(t.kvblk_a[:, :], 0.0)
    w.memset(t.kvblk_b[:, :], 0.0)
    w.inc(w.dma('sync', t.eyef_sb[:, :], t.eyef), "dx", 16)
    w.inc(w.dma('sync', t.eyeb_sb[:, :], t.eyeb), "dx", 16)
    w.inc(w.dma('sync', t.maskv_sb[:, :], t.maskv), "dx", 16)
    w.inc(w.dma('sync', t.bchunk_sb[:, :], t.b_chunk), "dx", 16)
    w.inc(w.dma('sync', t.mu_sb[:, :], t.mu_chunk), "dx", 16)
    for r_ in range(5):
        w.wait("dve", 'pool')
        w.inc(w.dma('pool', t.scrf[0:1, 0:E], t.b_rows[r_:r_ + 1, :]), "dw", 16)
        w.wait("dw", 'dve')
        w.ts(t.brow_sb[0:1, r_ * E:(r_ + 1) * E], t.scrf[0:1, 0:E], 1.0)
        w.ts(t.scrf[0:1, E:2 * E], t.brow_sb[0:1, r_ * E:(r_ + 1) * E], -1.0)
        w.tt(t.brow_lo[0:1, r_ * E:(r_ + 1) * E], t.scrf[0:1, E:2 * E],
             t.scrf[0:1, 0:E], op=ALU.add)


def _load3(w, t, srcs):
    w.wait("pe", 'pool')
    for i, src in enumerate(srcs):
        w.inc(w.dma('pool', t.wbig[:, i * 8192:(i + 1) * 8192].rearrange("p (a d) -> p a d", a=8),
                    src.rearrange("(a p) d -> p a d", p=128)), "dw", 16)


def _ln_tok(w, t, xt, xn_out):
    m = t.sml[:, 0:1]
    v = t.sml[:, 1:2]
    r = t.sml[:, 2:3]
    sq = t.scrf[:, E:2 * E]
    w.red(m, xt)
    w.ts(m, m, -1.0 / E)
    w.wait("dve", 'act')
    w.wait("acs", 'act')
    w.inc(w.act(sq, xt, AF.Square, bias=m, accum_out=v), "acs", 1)
    w.wait("acs", 'act')
    w.inc(w.act(r, v, AF.Rsqrt, bias=EPS_LN, scale=1.0 / E), "acs", 1)
    w.wait("acs", 'dve')
    w.ts(xn_out, xt, m, r, op0=ALU.add, op1=ALU.mult)


def _norm_heads(w, t, src_f32, dst_bf):
    sq = t.scrf[:, E:2 * E]
    ss = t.sml[:, 16:32]
    src3 = src_f32.rearrange("p (h d) -> p h d", h=H)
    w.wait("dve", 'act')
    w.wait("acs", 'act')
    w.inc(w.act(sq, src_f32, AF.Square), "acs", 1)
    w.wait("acs", 'dve')
    w.red(ss, sq.rearrange("p (h d) -> p h d", h=H))
    w.wait("dve", 'act')
    w.inc(w.act(t.sml[:, 32:48], ss, AF.Rsqrt), "acs", 1)
    w.wait("acs", 'dve')
    w.tt(t.sml[:, 32:48], t.sml[:, 32:48], ss, op=ALU.mult)
    w.ts(t.sml[:, 32:48], t.sml[:, 32:48], EPS_NRM, op0=ALU.add)
    w.vrecip(ss, t.sml[:, 32:48])
    w.tt(dst_bf.rearrange("p (h d) -> p h d", h=H), src3,
         ss.rearrange("p (h o) -> p h o", o=1).broadcast_to([128, H, D]),
         op=ALU.mult)


def _transpose8(w, t, src_bf, dst_bf, pt, dve_val=None):
    w.wait("dve", 'pe', val=(dve_val if dve_val is not None else w.val("dve")))
    i = None
    for a in range(NCHUNK):
        i = w.trans(pt[:, a * 64:(a + 1) * 64].bitcast(BF16),
                    src_bf[:, a * 128:(a + 1) * 128], t.eyeb_sb[:, :])
    w.inc(i, "pe", 1)
    w.wait("pe", 'dve')
    w.copy(dst_bf[:, 0:1024], pt[:, 0:512].bitcast(BF16))


def _proj_tok(w, t, xnT, wbase, brow_idx, pdst):
    last = None
    for half in range(2):
        po = pdst[:, half * 512:(half + 1) * 512]
        for a in range(NCHUNK):
            w.mm(po, xnT[:, a * 128:(a + 1) * 128],
                 t.wbig[:, wbase + a * 1024 + half * 512: wbase + a * 1024 + (half + 1) * 512],
                 start=(a == 0), stop=False)
        w.mm(po, t.onesb[0:1, 0:128],
             t.brow_sb[0:1, brow_idx * E + half * 512: brow_idx * E + (half + 1) * 512],
             start=False, stop=False)
        last = w.mm(po, t.onesb[0:1, 0:128],
                    t.brow_lo[0:1, brow_idx * E + half * 512: brow_idx * E + (half + 1) * 512],
                    start=False, stop=True)
    w.inc(last, "pe", 1)


def _prep(w, t, stop_p0=False):
    _load3(w, t, [t.w_ak, t.w_av, t.w_aq])
    w.wait("dw", 'pe')
    w.wait("dx", 'dve')
    w.wait("dve", 'pe')
    ONES_COL = t.onesb[:, 0:1]

    xtok_tok = [0, 0]
    xtok_pe = [0, 0]
    tile_pe = [0, 0]
    flush_pe = [None]
    st_tok = {"kn": 0, "vk": 0, "qn": 0}
    NTOT = NKG + NTT
    ln_dve = {}

    def _load_ln(tc2):
        if tc2 >= NTOT:
            return
        full2 = tc2 < NKG
        buf2 = tc2 % 2
        xt2 = t.xtok[:, buf2 * 1024:(buf2 + 1) * 1024]
        src2 = (t.xfull[tc2 * 128:(tc2 + 1) * 128, :] if full2
                else t.xq[(tc2 - NKG) * 128:(tc2 - NKG + 1) * 128, :])
        w.wait("dve", 'sync', val=xtok_tok[buf2])
        w.wait("pe", 'sync', val=xtok_pe[buf2])
        w.inc(w.dma('sync', xt2, src2), "dx", 16)
        w.wait("dx", 'dve')
        nb2 = t.scrb if tc2 % 2 == 0 else t.ffb
        w.wait("pe", 'dve', val=tile_pe[tc2 % 2])
        _ln_tok(w, t, xt2, nb2[:, 0:1024])
        ln_dve[tc2] = w.val("dve")
        xtok_tok[buf2] = w.val("dve")

    def _flush():
        if flush_pe[0] is not None:
            w.wait("pe", 'dve', val=flush_pe[0])
            w.tt(t.kvacc_a[:, :], t.kvacc_a[:, :], t.p_kv[:, :], op=ALU.add)
            w.tt(t.sml[:, 64:72], t.sml[:, 64:72], t.p_qk_b[:, 0:8], op=ALU.add)
            w.tt(t.sml[0:64, 72:88], t.sml[0:64, 72:88], t.p_qk_b[0:64, 16:32], op=ALU.add)
            flush_pe[0] = None

    _load_ln(0)
    for tc in range(NTOT):
        full = tc < NKG
        buf = tc % 2
        xt = t.xtok[:, buf * 1024:(buf + 1) * 1024]
        nbase = t.scrb if tc % 2 == 0 else t.ffb
        xnb = nbase[:, 0:1024]

        if not full:
            tt_ = tc - NKG
            w.wait("dve", 'pe')
            i = None
            for a in range(NCHUNK):
                i = w.trans(t.p_mis[:, a * 128:(a + 1) * 128], xt[:, a * 128:(a + 1) * 128],
                            t.eyef_sb[:, :])
            w.inc(i, "pe", 1)
            xtok_pe[buf] = w.val("pe")
            w.wait("pe", 'dve')
            w.wait("dst", 'dve', val=st_tok.get("xf", 0))
            w.copy(t.xfst[:, 0:1024], t.p_mis[:, 0:1024])
            w.wait("dve", 'pool')
            w.inc(w.dma('pool',
                        t.x_hbm[:, :, tt_ * 128:(tt_ + 1) * 128].rearrange("a p c -> p a c"),
                        t.xfst.rearrange("p (a c) -> p a c", a=8)), "dst", 16)
            st_tok["xf"] = w.val("dst")
            dve_tp = w.val("dve")
        else:
            dve_tp = ln_dve[tc]

        xnT = nbase[:, 1024:2048]
        _transpose8(w, t, xnb, xnT, t.p_qk_a, dve_val=dve_tp)
        w.wait("dve", 'pe')

        if full:
            _proj_tok(w, t, xnT, 0, 1, t.p_mis)
            _load_ln(tc + 1)
            _flush()
            w.wait("pe", 'dve')
            w.wait("pe", 'act')
            kn_tok = t.scrb[:, 2048:3072]
            _norm_heads(w, t, t.p_mis[:, 0:1024], kn_tok)
            if stop_p0:
                w.wait("dve", 'pool')
                w.wait("acs", 'pool')
                d = t.dbg
                w.inc(w.dma('pool', d[:, 0:1024], t.scrb[:, 0:1024]), "oput", 16)
                w.inc(w.dma('pool', d[:, 1024:2048], t.scrb[:, 1024:2048]), "oput", 16)
                w.inc(w.dma('pool', d[:, 2048:3072], t.p_mis[:, 0:1024]), "oput", 16)
                w.inc(w.dma('pool', d[:, 3072:4096], kn_tok), "oput", 16)
                w.inc(w.dma('pool', d[:, 4096:4160], t.sml[:, :]), "oput", 16)
                w.inc(w.dma('pool', d[:, 4160:4288], t.xtok[:, 0:128]), "oput", 16)
                w.inc(w.dma('pool', d[:, 4288:6336], t.wbig[:, 0:2048]), "oput", 16)
                w.inc(w.dma('pool', d[0:1, 6336:11456], t.brow_sb[:, :]), "oput", 16)
                w.inc(w.dma('pool', d[0:1, 11456:16576], t.brow_lo[:, :]), "oput", 16)
                return
            w.wait("dve", 'pe')
            _proj_tok(w, t, xnT, 8192, 2, t.p_mis)
            w.wait("pe", 'dve')
            w.wait("pe", 'act')
            v_tok = t.scrb[:, 3072:4096]
            w.wait("acs", 'act')
            w.inc(w.act(v_tok, t.p_mis[:, 0:1024], AF.Copy), "acs", 1)
            w.wait("acs", 'dve')
            v_lo = nbase[:, 0:1024]
            w.ts(t.scrf[:, 0:1024], v_tok, -1.0)
            w.tt(v_lo, t.scrf[:, 0:1024], t.p_mis[:, 0:1024], op=ALU.add)
            w.wait("dst", 'dve', val=st_tok["vk"])
            vkst = t.stage[:, 2048:2048 + H * 65]
            vk3 = vkst.rearrange("p (h e) -> p h e", h=H)
            if tc == 0:
                w.memset(vk3[:, :, 64:65], 1.0)
            w.copy(vk3[:, :, 0:64], v_tok.rearrange("p (h d) -> p h d", h=H))
            w.wait("dst", 'dve', val=st_tok["kn"])
            knst = t.stage[:, 1024:2048]
            _transpose8(w, t, kn_tok, knst, t.p_qk_a)
            w.wait("dve", 'pe')
            w.wait("acs", 'pe')
            for a in range(NCHUNK):
                w.mm(t.p_kv[:, a * 128:(a + 1) * 128], kn_tok[:, a * 128:(a + 1) * 128],
                     v_tok[:, a * 128:(a + 1) * 128], start=True, stop=True)
            for a in range(NCHUNK):
                w.mm(t.p_qk_b[:, a:a + 1], kn_tok[:, a * 128:(a + 1) * 128], ONES_COL,
                     start=True, stop=True)
            i = None
            for hh in range(H):
                w.mm(t.p_qk_b[0:64, 16 + hh:17 + hh], v_tok[:, hh * 64:(hh + 1) * 64],
                     ONES_COL, start=True, stop=False)
                i = w.mm(t.p_qk_b[0:64, 16 + hh:17 + hh], v_lo[:, hh * 64:(hh + 1) * 64],
                         ONES_COL, start=False, stop=True)
            w.inc(i, "pe", 1)
            flush_pe[0] = w.val("pe")
            w.wait("dve", 'pool')
            w.wait("pe", 'pool', val=0)
            w.inc(w.dma('pool', t.knT_hbm[tc], knst), "dst", 16)
            st_tok["kn"] = w.val("dst")
            w.inc(w.dma('pool', t.vk_hbm[tc], vkst), "dst", 16)
            st_tok["vk"] = w.val("dst")
        else:
            _proj_tok(w, t, xnT, 16384, 0, t.p_mis)
            _load_ln(tc + 1)
            _flush()
            w.wait("pe", 'dve')
            w.wait("pe", 'act')
            qn_tok = t.scrb[:, 2048:3072]
            _norm_heads(w, t, t.p_mis[:, 0:1024], qn_tok)
            w.wait("dst", 'dve', val=st_tok["qn"])
            qnst = t.stage[:, 0:1024]
            _transpose8(w, t, qn_tok, qnst, t.p_qk_a)
            tt_ = tc - NKG
            w.wait("dve", 'pool')
            w.inc(w.dma('pool',
                        t.qn_hbm[:, :, tt_ * 128:(tt_ + 1) * 128].rearrange("a p c -> p a c"),
                        qnst.rearrange("p (a c) -> p a c", a=8)), "dst", 16)
            st_tok["qn"] = w.val("dst")
        tile_pe[tc % 2] = w.val("pe")

    _flush()

    w.wait("pe", 'dve')
    kb3 = t.kvblk_a.rearrange("p (a c) -> p a c", a=NCHUNK)
    ka3 = t.kvacc_a.rearrange("p (a c) -> p a c", a=NCHUNK)
    w.copy(kb3[0:64, :, 0:64], ka3[0:64, :, 0:64])
    w.copy(kb3[64:128, :, 64:128], ka3[64:128, :, 64:128])
    w.copy(t.ksum_bf[:, :], t.sml[:, 64:64 + NCHUNK])
    w.copy(t.vsum_sb[:, :], t.sml[0:64, 72:72 + H])
    w.ts(t.vsum_sb[:, :], t.vsum_sb[:, :], 3.1)


def _rb_slot(t, kg):
    s = kg % 4
    if s < 2:
        return t.rbuf[:, s * 1024:(s + 1) * 1024]
    return t.scrb[:, s * 1024:(s + 1) * 1024]


def _emit_rv(w, t, qg, kg, nq):
    if kg < 0:
        return
    oo = kg // 8
    kk = kg % 8
    ob = oo % 2
    vk_e = t.vstream[:, ob * 1040 + kk * 130: ob * 1040 + kk * 130 + 65]
    vk_o = t.vstream[:, ob * 1040 + kk * 130 + 65: ob * 1040 + kk * 130 + 130]
    rb_ = _rb_slot(t, kg)
    if kg == 0 and t.fin_dve is not None:
        w.wait("dve", 'pe', val=t.fin_dve)
        t.fin_dve = None
    rs_, rv_ = t.rcp_of[kg]
    w.wait(rs_, 'pe', val=rv_)
    w.mm(t.p_kv[0:65, 0:nq], vk_e, rb_[:, 0:nq], start=(kg == 0), stop=(kg == NKG - 1))
    i = w.mm(t.p_kv[0:65, 512:512 + nq], vk_o, rb_[:, 512:512 + nq],
             start=(kg == 0), stop=(kg == NKG - 1))
    w.inc(i, "rv", 1)


def _fin_pair(w, t, qg, p, nq):
    qt = t.qtile[:, (p % 2) * 512:(p % 2) * 512 + nq]
    w.wait("rv", 'dve')
    ue = t.ut[:, 0:nq]
    uo = t.ut[:, 512:512 + nq]
    w.copy(ue, t.p_kv[0:64, 0:nq])
    w.copy(uo, t.p_kv[0:64, 512:512 + nq])
    w.wait("rv", 'pe')
    w.mm(t.p_mis[64:65, 0:nq], t.ksum_bf[0:64, p:p + 1], qt[0:64, :], tp=(0, 64))
    i = w.mm(t.p_mis[64:65, 512:512 + nq], t.ksum_bf[64:128, p:p + 1],
             qt[64:128, :], tp=(64, 64))
    w.inc(i, "pe", 1)
    w.wait("pe", 'dve')
    nr_e = t.scrf[64:65, 0:nq]
    nr_o = t.scrf[64:65, 512:512 + nq]
    w.copy(nr_e, t.p_kv[64:65, 0:nq])
    w.copy(nr_o, t.p_kv[64:65, 512:512 + nq])
    t.fin_dve = w.val("dve")
    w.stt(nr_e, nr_e, 4.41, t.p_mis[64:65, 0:nq], ALU.mult, ALU.subtract)
    w.stt(nr_o, nr_o, 4.41, t.p_mis[64:65, 512:512 + nq], ALU.mult, ALU.subtract)
    w.ts(nr_e, nr_e, NCONST, op0=ALU.add)
    w.ts(nr_o, nr_o, NCONST, op0=ALU.add)
    w.wait("dve", 'act')
    w.wait("pe", 'act')
    w.act(t.invn[64:65, 0:nq], nr_e, AF.Reciprocal)
    w.inc(w.act(t.invn[64:65, 512:512 + nq], nr_o, AF.Reciprocal), "acs", 1)
    w.wait("acs", 'pe')
    for half in range(2):
        co = half * 512
        usrc = ue if half == 0 else uo
        w.mm(t.p_mis[0:64, 0:nq], t.onesb[64:65, 0:64], t.invn[64:65, co:co + nq])
        if half == 0:
            i = w.mm(t.p_mis[0:64, 512:512 + nq], t.kvblk_a[0:64, p * 128:p * 128 + 64],
                     qt[0:64, :], tp=(0, 0))
        else:
            i = w.mm(t.p_mis[0:64, 512:512 + nq],
                     t.kvblk_a[64:128, p * 128 + 64:p * 128 + 128],
                     qt[64:128, :], tp=(64, 0))
        w.inc(i, "pe", 1)
        w.wait("pe", 'dve')
        e1 = t.scrf[0:64, 1024:1024 + nq]
        w.stt(e1, usrc, 4.41, t.p_mis[0:64, 512:512 + nq], ALU.mult, ALU.subtract)
        if half == 1:
            w.wait("dst", 'dve')
        dst = t.ofb[0:64, p * 512:p * 512 + nq] if half == 0 else t.scrb[0:64, 0:nq]
        w.stt(dst, e1, t.vsum_sb[:, 2 * p + half:2 * p + half + 1],
              t.p_mis[0:64, 0:nq], ALU.subtract, ALU.mult)
        if half == 1:
            w.wait("dve", 'pool')
            w.inc(w.dma('pool', t.ofb[64:128, p * 512:p * 512 + nq], t.scrb[0:64, 0:nq]),
                  "dst", 16)
        else:
            w.wait("dve", 'pe')
    t.fin_dve2 = w.val("dve")


def _attn_a(w, t, qg):
    nq = QGS[qg]
    q0 = QG0[qg]
    t.rcp_of = {}
    t.fin_dve = None
    t.fin_dve2 = None
    pend = None
    w.wait("dst", 'sync')
    for p in range(NCHUNK):
        qb = p % 2
        qt = t.qtile[:, qb * 512:qb * 512 + nq]
        w.wait("pe", 'sync', val=w.val("pe"))
        w.wait("qk", 'sync', val=w.val("qk"))
        w.inc(w.dma('sync', qt, t.qn_hbm[p, :, q0:q0 + nq]), "dq", 16)
        w.wait("dve", 'pe')
        dk_tok = {}
        for oct_ in range(10):
            if oct_ >= 2:
                oo = oct_ - 2
                ob = oo % 2
                for kk in range(8):
                    kg = oo * 8 + kk
                    pb = kg % 2
                    if kg < 5:
                        pqk = (t.p_qk_a, t.p_qk_b)[kg % 2]
                    else:
                        pqk = (t.p_qk_a, t.p_qk_b, t.p_mis)[kg % 3]
                    kn_e = t.kstream[0:64, ob * 1024 + kk * 128: ob * 1024 + (kk + 1) * 128]
                    kn_o = t.kstream[64:128, ob * 1024 + kk * 128: ob * 1024 + (kk + 1) * 128]
                    if kk == 0:
                        w.wait("dk", 'pe', val=dk_tok[oo])
                        if oo == 0:
                            w.wait("dq", 'pe')
                    w.wait("rcp", 'pe', val=w.val("rcp") - (0 if kg == 0 else 2))
                    if kg == 5 and t.fin_dve2 is not None:
                        w.wait("dve", 'pe', val=t.fin_dve2)
                        t.fin_dve2 = None
                    w.mm(pqk[:, 0:nq], kn_e, qt[0:64, :], tp=(0, 0))
                    i = w.mm(pqk[:, 512:512 + nq], kn_o, qt[64:128, :], tp=(64, 0))
                    w.inc(i, "qk", 1)
                    _emit_rv(w, t, qg, kg - 2, nq)
                    if kg == 1 and pend is not None:
                        _fin_pair(w, t, qg, pend, nq)
                        pend = None
                    rb_ = _rb_slot(t, kg)
                    w.wait("qk", 'act')
                    w.wait("rv", 'act', val=w.val("rv") - 2)
                    if nq == 512:
                        i = w.act(rb_[:, 0:1024], pqk[:, 0:1024], AF.Reciprocal,
                                  bias=1.1, scale=-1.0)
                    else:
                        w.act(rb_[:, 0:nq], pqk[:, 0:nq], AF.Reciprocal, bias=1.1, scale=-1.0)
                        i = w.act(rb_[:, 512:512 + nq], pqk[:, 512:512 + nq], AF.Reciprocal,
                                  bias=1.1, scale=-1.0)
                    w.inc(i, "rcp", 1)
                    t.rcp_of[kg] = ("rcp", w.val("rcp"))
            if oct_ < 8:
                ob2 = oct_ % 2
                w.wait("qk", 'sync', val=w.val("qk"))
                w.wait("rv", 'sync', val=w.val("rv") + (2 if oct_ >= 2 else 0))
                w.inc(w.dma('sync',
                            t.kstream[:, ob2 * 1024:(ob2 + 1) * 1024]
                            .rearrange("p (k c) -> p k c", k=8),
                            t.knT_hbm[oct_ * 8:(oct_ + 1) * 8, :, p * 128:(p + 1) * 128]
                            .rearrange("k p c -> p k c")),
                      "dk", 16)
                w.inc(w.dma('sync',
                            t.vstream[:, ob2 * 1040:(ob2 + 1) * 1040]
                            .rearrange("p (k c) -> p k c", k=8),
                            t.vk_hbm[oct_ * 8:(oct_ + 1) * 8, :, 2 * p * 65:(2 * p + 2) * 65]
                            .rearrange("k p c -> p k c")),
                      "dk", 16)
                dk_tok[oct_] = w.val("dk")
        _emit_rv(w, t, qg, NKG - 2, nq)
        _emit_rv(w, t, qg, NKG - 1, nq)
        pend = p
    _fin_pair(w, t, qg, pend, nq)
    w.wait("dst", 'pe')
    w.wait("dve", 'sync')
    xgv = t.xg.rearrange("p (a c) -> p a c", c=513)[:, :, 1:1 + nq]
    w.inc(w.dma('sync', xgv, t.x_hbm[:, :, q0:q0 + nq].rearrange("a p c -> p a c")), "dx", 16)
    w.wait("dx", 'dve')
    wc_tok = [None, None]
    for fc in range(NCHUNK):
        wb = fc % 2
        w.wait("pe", 'pool', val=(wc_tok[wb] if wc_tok[wb] is not None else w.val("pe")))
        w.inc(w.dma('pool', t.wcol[:, wb * 1024:(wb + 1) * 1024].rearrange("p (a c) -> p a c", a=8),
                    t.w_ao[:, fc * 128:(fc + 1) * 128].rearrange("(a p) c -> p a c", p=128)),
              "dw", 16)
        w.wait("dw", 'pe')
        w.wait("dve", 'pe')
        i = None
        for a in range(NCHUNK):
            i = w.mm(t.p_mis[:, 0:nq], t.wcol[:, wb * 1024 + a * 128:wb * 1024 + (a + 1) * 128],
                     t.ofb[:, a * 512:a * 512 + nq], start=(a == 0), stop=(a == NCHUNK - 1))
        w.inc(i, "pe", 1)
        wc_tok[wb] = w.val("pe")
        w.wait("pe", 'dve')
        xc = t.xg[:, fc * 513 + 1: fc * 513 + 1 + nq]
        w.ts(t.scrf[:, 0:nq], t.p_mis[:, 0:nq], t.bchunk_sb[:, fc:fc + 1], op0=ALU.add)
        w.tt(xc, xc, t.scrf[:, 0:nq], op=ALU.add)
    sl_ = (qg % 2) * 8
    w.copy(t.stash[:, sl_:sl_ + 8].rearrange("p (a one) -> p a one", one=1),
           t.xg.rearrange("p (a c) -> p a c", c=513)[:, :, nq:nq + 1])
    w.wait("dve", 'pe')


def _ln_fm(w, t, qg, resident=False):
    nq = QGS[qg]
    q0 = QG0[qg]
    nb = nq + 1
    bnd = qg > 0
    ONES_COL = t.onesb[:, 0:1]
    if resident:
        if bnd:
            sl_ = ((qg + 1) % 2) * 8
            w.copy(t.xg.rearrange("p (a c) -> p a c", c=513)[:, :, 0:1],
                   t.stash[:, sl_:sl_ + 8].rearrange("p (a one) -> p a one", one=1))
    else:
        w.wait("dve", 'sync')
        w.wait("pe", 'sync')
        if not bnd:
            xgv = t.xg.rearrange("p (a c) -> p a c", c=513)[:, :, 1:1 + nq]
            w.inc(w.dma('sync', xgv, t.x_hbm[:, :, 0:nq].rearrange("a p c -> p a c")), "dx", 16)
        else:
            xgv = t.xg.rearrange("p (a c) -> p a c", c=513)[:, :, 0:nb]
            w.inc(w.dma('sync', xgv, t.x_hbm[:, :, q0 - 1:q0 + nq].rearrange("a p c -> p a c")),
                  "dx", 16)
        w.wait("dx", 'dve')
    cast_tok = [None, None]
    for a in range(NCHUNK):
        sl = a % 2
        xb = t.scrb[:, sl * 1024:sl * 1024 + nb]
        x2 = t.scrb[:, 2048 + sl * 1024:2048 + sl * 1024 + nb]
        w.wait("pe", 'dve', val=(cast_tok[sl] if cast_tok[sl] is not None else w.val("pe")))
        if bnd:
            xsrc = t.xg[:, a * 513:a * 513 + nb]
            w.ts(xb, xsrc, 1.0)
            w.tt(x2, xsrc, xsrc, op=ALU.mult)
        else:
            xsrc = t.xg[:, a * 513 + 1:a * 513 + 1 + nq]
            w.ts(xb[:, 1:nb], xsrc, 1.0)
            w.tt(x2[:, 1:nb], xsrc, xsrc, op=ALU.mult)
        w.wait("dve", 'pe')
        w.mm(t.p_qk_a[0:1, 0:nq], ONES_COL, xb[:, 1:nb], start=(a == 0), stop=(a == NCHUNK - 1))
        i = w.mm(t.p_qk_b[0:1, 0:nq], ONES_COL, x2[:, 1:nb], start=(a == 0), stop=(a == NCHUNK - 1))
        if bnd:
            w.mm(t.p_qk_a[0:1, 512:513], ONES_COL, xb[:, 0:1],
                 start=(a == 0), stop=(a == NCHUNK - 1))
            i = w.mm(t.p_qk_b[0:1, 512:513], ONES_COL, x2[:, 0:1],
                     start=(a == 0), stop=(a == NCHUNK - 1))
        w.inc(i, "pe", 1)
        cast_tok[sl] = w.val("pe")
    w.wait("pe", 'dve')
    ncols = 513 if bnd else nq
    mrow = t.scrf[0:1, 0:ncols]
    vrow = t.scrf[0:1, 1024:1024 + ncols]
    w.ts(mrow, t.p_qk_a[0:1, 0:ncols], 1.0 / E)
    w.ts(vrow, t.p_qk_b[0:1, 0:ncols], 1.0 / E)
    w.tt(t.scrf[0:1, 2048:2048 + ncols], mrow, mrow, op=ALU.mult)
    w.tt(vrow, vrow, t.scrf[0:1, 2048:2048 + ncols], op=ALU.subtract)
    w.wait("dve", 'act')
    w.inc(w.act(t.scrf[0:1, 2048:2048 + ncols], vrow, AF.Rsqrt, bias=EPS_LN), "acs", 1)
    w.wait("acs", 'dve')
    w.ts(t.scrb[0:1, 0:ncols], mrow, 1.0)
    w.ts(t.scrb[0:1, 1024:1024 + ncols], t.scrf[0:1, 2048:2048 + ncols], 1.0)
    w.wait("dve", 'pe')
    w.mm(t.p_qk_b[:, 0:nq], t.onesb[0:1, 0:128], t.scrb[0:1, 0:nq])
    i = w.mm(t.p_qk_a[:, 0:nq], t.onesb[0:1, 0:128], t.scrb[0:1, 1024:1024 + nq])
    if bnd:
        w.mm(t.p_qk_b[:, 512:513], t.onesb[0:1, 0:128], t.scrb[0:1, 512:513])
        i = w.mm(t.p_qk_a[:, 512:513], t.onesb[0:1, 0:128], t.scrb[0:1, 1024 + 512:1024 + 513])
    w.inc(i, "pe", 1)
    w.wait("pe", 'dve')
    for a in range(NCHUNK):
        w.tt(t.scrf[:, 0:nq], t.xg[:, a * 513 + 1:a * 513 + 1 + nq],
             t.p_qk_b[:, 0:nq], op=ALU.subtract)
        w.tt(t.xn_sb[:, a * 520 + 1:a * 520 + 1 + nq], t.scrf[:, 0:nq],
             t.p_qk_a[:, 0:nq], op=ALU.mult)
        if bnd:
            w.tt(t.scrf[:, 512:513], t.xg[:, a * 513:a * 513 + 1],
                 t.p_qk_b[:, 512:513], op=ALU.subtract)
            w.tt(t.xn_sb[:, a * 520:a * 520 + 1], t.scrf[:, 512:513],
                 t.p_qk_a[:, 512:513], op=ALU.mult)
        else:
            w.memset(t.xn_sb[:, a * 520:a * 520 + 1], 0.0)


def _ffn(w, t, qg, resident=False, out_fuse=False):
    nq = QGS[qg]
    q0 = QG0[qg]
    _ln_fm(w, t, qg, resident=resident)
    i = None
    for a in range(NCHUNK):
        xn_c = t.xn_sb[:, a * 520 + 1:a * 520 + 1 + nq]
        xn_p = t.xn_sb[:, a * 520:a * 520 + nq]
        d = t.scrb[:, 0:nq]
        w.tt(d, xn_c, xn_p, op=ALU.subtract)
        w.stt(t.ffb[:, a * 512:a * 512 + nq], d, t.mu_sb[:, a:a + 1], xn_p, ALU.mult, ALU.add)
        i = w.stt(t.ffb[:, 4096 + a * 512:4096 + a * 512 + nq], d,
                  t.mu_sb[:, 8 + a:9 + a], xn_p, ALU.mult, ALU.add)
    w.wait("dve", 'pe')
    for dc in range(NCHUNK):
        pd = (t.p_mis[:, 0:nq], t.p_mis[:, 512:512 + nq],
              t.p_qk_a[:, 0:nq], t.p_qk_a[:, 512:512 + nq])[dc % 4]
        if dc >= 4:
            w.wait("acs", 'pe', val=w.val("acs") - 3)
        i = None
        for a in range(NCHUNK):
            i = w.mm(pd, t.wbig[:, 8192 + a * 1024 + dc * 128:8192 + a * 1024 + (dc + 1) * 128],
                     t.ffb[:, 4096 + a * 512:4096 + a * 512 + nq],
                     start=(a == 0), stop=(a == NCHUNK - 1))
        w.inc(i, "qk", 1)
        w.wait("qk", 'act')
        w.inc(w.act(t.scrf[:, (dc % 2) * 512:(dc % 2) * 512 + nq], pd, AF.Relu), "acs", 1)
        w.wait("acs", 'dve')
        w.tt(t.kkb[:, dc * 512:dc * 512 + nq],
                   t.scrf[:, (dc % 2) * 512:(dc % 2) * 512 + nq],
                   t.scrf[:, (dc % 2) * 512:(dc % 2) * 512 + nq], op=ALU.mult)
    w.wait("dve", 'pe')
    sig_tok = {}
    gate_tok = {}
    for dc in range(NCHUNK):
        pbase = t.p_mis if dc % 2 == 0 else t.p_qk_b
        pr = pbase[:, 0:nq]
        pv = pbase[:, 512:512 + nq]
        rsl = t.scrf[:, 0:nq] if dc % 2 == 0 else t.scrf[:, 2048:2048 + nq]
        vsl = t.scrf[:, 512:512 + nq] if dc % 2 == 0 else t.scrf[:, 2560:2560 + nq]
        if dc >= 2:
            w.wait("acs", 'pe', val=sig_tok[dc - 2])
        i = None
        for a in range(NCHUNK):
            i = w.mm(pr, t.wbig[:, a * 1024 + dc * 128:a * 1024 + (dc + 1) * 128],
                     t.ffb[:, a * 512:a * 512 + nq], start=(a == 0), stop=(a == NCHUNK - 1))
        w.inc(i, "qk", 1)
        w.wait("qk", 'act')
        if dc >= 2:
            w.wait("dve", 'act', val=gate_tok[dc - 2])
        w.inc(w.act(rsl, pr, AF.Sigmoid), "acs", 1)
        sig_tok[dc] = w.val("acs")
        if dc >= 2:
            w.wait("dve", 'pe', val=gate_tok[dc - 2])
        i = None
        for a in range(NCHUNK):
            i = w.mm(pv, t.wbig[:, 16384 + a * 1024 + dc * 128:16384 + a * 1024 + (dc + 1) * 128],
                     t.kkb[:, a * 512:a * 512 + nq], start=(a == 0), stop=(a == NCHUNK - 1))
        w.inc(i, "rv", 1)
        w.wait("acs", 'dve', val=sig_tok[dc])
        w.wait("rv", 'dve')
        xc = t.xg[:, dc * 513 + 1: dc * 513 + 1 + nq]
        w.tt(vsl, pv, rsl, op=ALU.mult)
        w.tt(xc, xc, vsl, op=ALU.add)
        gate_tok[dc] = w.val("dve")
    if out_fuse:
        for tl in range(nq // 128):
            w.wait("dve", 'pe')
            i = None
            for a in range(NCHUNK):
                i = w.trans(t.p_mis[:, a * 128:(a + 1) * 128],
                            t.xg[:, a * 513 + 1 + tl * 128:a * 513 + 1 + (tl + 1) * 128],
                            t.eyef_sb[:, :])
            w.inc(i, "pe", 1)
            w.wait("pe", 'dve')
            ob = tl % 2
            w.wait("oput", 'dve', val=t.ot_tok[ob])
            ot = t.xtok[:, ob * 1024:(ob + 1) * 1024]
            for a in range(NCHUNK):
                w.copy(ot[:, a * 128:(a + 1) * 128], t.p_mis[:, a * 128:(a + 1) * 128])
            w.wait("dve", 'sync')
            w.inc(w.dma('sync', t.out_h[q0 + tl * 128:q0 + (tl + 1) * 128, :], ot), "oput", 16)
            t.ot_tok[ob] = w.val("oput")
    else:
        w.wait("dve", 'sync')
        xgv = t.xg.rearrange("p (a c) -> p a c", c=513)[:, :, 1:1 + nq]
        w.inc(w.dma('sync', t.x_hbm[:, :, q0:q0 + nq].rearrange("a p c -> p a c"), xgv), "dsx", 16)
    w.wait("dve", 'pe')


def _attnb_pre(w, t, qg):
    nq = QGS[qg]
    q0 = QG0[qg]
    _ln_fm(w, t, qg)
    w.wait("dve", 'pe')
    for dc in range(NCHUNK):
        pd = t.p_mis[:, (dc % 2) * 512:(dc % 2) * 512 + nq]
        if dc >= 2:
            w.wait("dve", 'pe', val=w.val("dve") - 3)
        i = None
        for a in range(NCHUNK):
            i = w.mm(pd, t.wbig[:, a * 1024 + dc * 128:a * 1024 + (dc + 1) * 128],
                     t.xn_sb[:, a * 520 + 1:a * 520 + 1 + nq],
                     start=(a == 0), stop=(a == NCHUNK - 1))
        w.inc(i, "qk", 1)
        w.wait("qk", 'dve')
        bq = t.bchunk_sb[:, 8 + dc:9 + dc]
        mn = t.scrf[:, 0:nq]
        mx = t.scrf[:, 512:512 + nq]
        w.ts(mn, pd, bq, 0.0, op0=ALU.add, op1=ALU.min)
        w.ts(mx, pd, bq, 0.0, op0=ALU.add, op1=ALU.max)
        w.wait("dve", 'act')
        w.inc(w.act(t.scrf[:, 1024:1024 + nq], mn, AF.Exp), "acs", 1)
        w.wait("acs", 'dve')
        w.wait("dst", 'dve', val=w.val("dst") - 16)
        w.tt(t.stage[:, (dc % 2) * 1024:(dc % 2) * 1024 + nq],
                   t.scrf[:, 1024:1024 + nq], mx, op=ALU.add)
        w.wait("dve", 'pool')
        w.inc(w.dma('pool', t.qb_hbm[dc, :, q0:q0 + nq],
                    t.stage[:, (dc % 2) * 1024:(dc % 2) * 1024 + nq]), "dst", 16)
    ntt_q = nq // 128
    for tl in range(ntt_q):
        ttg = q0 // 128 + tl
        mcol = t.maskv_sb[:, ttg:ttg + 1]
        for half in range(2):
            pk = t.p_mis[:, 0:512]
            if tl > 0 or half > 0:
                w.wait("dve", 'pe')
            i = None
            for a in range(NCHUNK):
                i = w.mm(pk, t.xn_sb[:, a * 520 + 1 + tl * 128:a * 520 + 1 + (tl + 1) * 128],
                         t.wbig[:, 8192 + a * 1024 + half * 512:8192 + a * 1024 + (half + 1) * 512],
                         start=(a == 0), stop=False)
            w.mm(pk, t.onesb[0:1, 0:128],
                 t.brow_sb[0:1, 3 * E + half * 512:3 * E + (half + 1) * 512],
                 start=False, stop=False)
            i = w.mm(pk, t.onesb[0:1, 0:128],
                     t.brow_lo[0:1, 3 * E + half * 512:3 * E + (half + 1) * 512],
                     start=False, stop=True)
            w.inc(i, "qk", 1)
            w.wait("qk", 'dve')
            mn = t.scrf[:, 0:512]
            mx = t.scrf[:, 512:1024]
            w.ts(mn, pk, 0.0, None, op0=ALU.min)
            w.ts(mx, pk, 0.0, None, op0=ALU.max)
            w.wait("dve", 'act')
            w.inc(w.act(t.scrf[:, 1024:1536], mn, AF.Exp), "acs", 1)
            w.wait("acs", 'dve')
            w.wait("rv", 'dve')
            w.tt(t.scrf[:, 1536:2048], t.scrf[:, 1024:1536], mx, op=ALU.add)
            w.ts(t.scrb[:, half * 512:(half + 1) * 512], t.scrf[:, 1536:2048], mcol)
            w.wait("dve", 'pe')
            pv = t.p_mis[:, 512:1024]
            i = None
            for a in range(NCHUNK):
                i = w.mm(pv, t.xn_sb[:, a * 520 + 1 + tl * 128:a * 520 + 1 + (tl + 1) * 128],
                         t.wbig[:, 16384 + a * 1024 + half * 512:16384 + a * 1024 + (half + 1) * 512],
                         start=(a == 0), stop=False)
            w.mm(pv, t.onesb[0:1, 0:128],
                 t.brow_sb[0:1, 4 * E + half * 512:4 * E + (half + 1) * 512],
                 start=False, stop=False)
            i = w.mm(pv, t.onesb[0:1, 0:128],
                     t.brow_lo[0:1, 4 * E + half * 512:4 * E + (half + 1) * 512],
                     start=False, stop=True)
            w.inc(i, "qk", 1)
            w.wait("qk", 'dve')
            w.copy(t.scrb[:, 1024 + half * 512:1024 + (half + 1) * 512], pv)
        w.wait("dve", 'pe')
        i = None
        for pp in range(NCHUNK):
            i = w.mm(t.p_qk_a[:, pp * 128:(pp + 1) * 128], t.scrb[:, pp * 128:(pp + 1) * 128],
                     t.scrb[:, 1024 + pp * 128:1024 + (pp + 1) * 128],
                     start=True, stop=True)
        w.inc(i, "rv", 1)
        w.wait("rv", 'dve')
        w.tt(t.kvacc_b[:, :], t.kvacc_b[:, :], t.p_qk_a[:, :], op=ALU.add)
    w.wait("dve", 'pe')


def _attnb_reduce(w, t):
    w.wait("dve", 'pool')
    w.inc(w.dma('pool', t.kv_in, t.kvacc_b[:, :]), "dst", 16)
    w.wait("dst", 'pool')
    w.inc(w.cc([t.kv_in], [t.kv_out]), "cc", 1)
    w.wait("cc", 'pool')
    w.inc(w.dma('pool', t.scrf[:, 0:1024], t.kv_out), "dst", 16)
    w.wait("dst", 'dve')
    i = None
    for a in range(NCHUNK):
        w.copy(t.kvblk_b[0:64, a * 128:a * 128 + 64], t.scrf[0:64, a * 128:a * 128 + 64])
        i = w.copy(t.kvblk_b[64:128, a * 128 + 64:a * 128 + 128],
                   t.scrf[64:128, a * 128 + 64:a * 128 + 128])
    w.wait("dve", 'pe')


def _attnb_out(w, t, qg):
    nq = QGS[qg]
    q0 = QG0[qg]
    w.wait("dst", 'sync')
    qk_tok = [None, None]
    for p in range(NCHUNK):
        sl = p % 2
        qp = t.qtile[:, sl * 512:sl * 512 + nq]
        w.wait("qk", 'sync', val=(qk_tok[sl] if qk_tok[sl] is not None else w.val("qk")))
        w.inc(w.dma('sync', qp, t.qb_hbm[p, :, q0:q0 + nq]), "dq", 16)
        w.wait("dq", 'pe')
        w.wait("dve", 'pe', val=w.val("dve") - 1 if p >= 2 else w.val("dve"))
        w.inc(w.mm(t.p_mis[:, sl * 512:sl * 512 + nq],
                   t.kvblk_b[:, p * 128:(p + 1) * 128], qp), "qk", 1)
        qk_tok[sl] = w.val("qk")
        w.wait("qk", 'dve')
        if p == 0:
            w.wait("pe", 'dve')
        w.copy(t.ofb[:, p * 512:p * 512 + nq], t.p_mis[:, sl * 512:sl * 512 + nq])
    w.barrier()
    w.wait("dve", 'pe')
    w.wait("dve", 'sync')
    w.wait("pe", 'sync')
    xgv = t.xg.rearrange("p (a c) -> p a c", c=513)[:, :, 1:1 + nq]
    w.inc(w.dma('sync', xgv, t.x_hbm[:, :, q0:q0 + nq].rearrange("a p c -> p a c")), "dx", 16)
    w.wait("dx", 'dve')
    wc_tok = [None, None]
    for fc in range(NCHUNK):
        wb = fc % 2
        w.wait("pe", 'pool', val=(wc_tok[wb] if wc_tok[wb] is not None else w.val("pe")))
        w.inc(w.dma('pool', t.wcol[:, wb * 1024:(wb + 1) * 1024].rearrange("p (a c) -> p a c", a=8),
                    t.w_bo[:, fc * 128:(fc + 1) * 128].rearrange("(a p) c -> p a c", p=128)),
              "dw", 16)
        w.wait("dw", 'pe')
        w.wait("dve", 'pe')
        i = None
        for a in range(NCHUNK):
            i = w.mm(t.p_kv[:, 0:nq], t.wcol[:, wb * 1024 + a * 128:wb * 1024 + (a + 1) * 128],
                     t.ofb[:, a * 512:a * 512 + nq], start=(a == 0), stop=(a == NCHUNK - 1))
        w.inc(i, "pe", 1)
        wc_tok[wb] = w.val("pe")
        w.wait("pe", 'dve')
        xc = t.xg[:, fc * 513 + 1: fc * 513 + 1 + nq]
        w.ts(t.scrf[:, 0:nq], t.p_kv[:, 0:nq], t.bchunk_sb[:, 16 + fc:17 + fc], op0=ALU.add)
        w.tt(xc, xc, t.scrf[:, 0:nq], op=ALU.add)
        w.barrier()
    sl_ = (qg % 2) * 8
    w.copy(t.stash[:, sl_:sl_ + 8].rearrange("p (a one) -> p a one", one=1),
           t.xg.rearrange("p (a c) -> p a c", c=513)[:, :, nq:nq + 1])
    w.wait("dve", 'pe')


def _output(w, t):
    ot_tok = [0, 0]
    ld_tok = [0, 0]
    for tt_ in range(NTT):
        lb = tt_ % 2
        xf = t.xfst[:, 0:1024] if lb == 0 else t.scrf[:, 0:1024]
        w.wait("pe", 'sync', val=ld_tok[lb])
        w.inc(w.dma('sync', xf.rearrange("p (a c) -> p a c", a=8),
                    t.x_hbm[:, :, tt_ * 128:(tt_ + 1) * 128].rearrange("a p c -> p a c")),
              "dx", 16)
        w.wait("dx", 'pe')
        w.wait("dve", 'pe')
        i = None
        for a in range(NCHUNK):
            i = w.trans(t.p_mis[:, a * 128:(a + 1) * 128], xf[:, a * 128:(a + 1) * 128],
                        t.eyef_sb[:, :])
        w.inc(i, "pe", 1)
        ld_tok[lb] = w.val("pe")
        w.wait("pe", 'dve')
        ob = tt_ % 2
        w.wait("oput", 'dve', val=ot_tok[ob])
        ot = t.xtok[:, ob * 1024:(ob + 1) * 1024]
        i = None
        for a in range(NCHUNK):
            i = w.copy(ot[:, a * 128:(a + 1) * 128], t.p_mis[:, a * 128:(a + 1) * 128])
        w.wait("dve", 'sync')
        w.inc(w.dma('sync', t.out_h[tt_ * 128:(tt_ + 1) * 128, :], ot), "oput", 16)
        ot_tok[ob] = w.val("oput")



def _host_inputs(inputs):
    x = np.ascontiguousarray(np.asarray(inputs["L0_attn_a_x"], dtype=np.float32))
    g = lambda k: np.ascontiguousarray(np.asarray(inputs[k], dtype=np.float32))
    shared = {
        "w_aq": g("L0_attn_a_Wq"), "w_ak": g("L0_attn_a_Wk"),
        "w_av": g("L0_attn_a_Wv"), "w_ao": g("L0_attn_a_Wo"),
        "w_fr": np.ascontiguousarray(g("L0_ffn_W_r").T),
        "w_fk": np.ascontiguousarray(g("L0_ffn_W_k").T),
        "w_fv": np.ascontiguousarray(g("L0_ffn_W_v").T),
        "w_bq": np.ascontiguousarray(g("L0_attn_b_Wq").T),
        "w_bk": np.ascontiguousarray(g("L0_attn_b_Wk").T),
        "w_bv": np.ascontiguousarray(g("L0_attn_b_Wv").T),
        "w_bo": np.ascontiguousarray(g("L0_attn_b_Wo").T),
    }
    b_rows = np.stack([g("L0_attn_a_bq"), g("L0_attn_a_bk"), g("L0_attn_a_bv"),
                       g("L0_attn_b_bk"), g("L0_attn_b_bv")]).astype(np.float32)
    b_chunk = np.zeros((128, 4 * NCHUNK), np.float32)
    b_chunk[:, 0:8] = g("L0_attn_a_bo").reshape(NCHUNK, 128).T
    b_chunk[:, 8:16] = g("L0_attn_b_bq").reshape(NCHUNK, 128).T
    b_chunk[:, 16:24] = g("L0_attn_b_bo").reshape(NCHUNK, 128).T
    mu_chunk = np.zeros((128, 2 * NCHUNK), np.float32)
    mu_chunk[:, 0:8] = g("L0_ffn_mu_r").reshape(NCHUNK, 128).T
    mu_chunk[:, 8:16] = g("L0_ffn_mu_k").reshape(NCHUNK, 128).T
    shared["b_rows"] = np.ascontiguousarray(b_rows)
    shared["b_chunk"] = np.ascontiguousarray(b_chunk)
    shared["mu_chunk"] = np.ascontiguousarray(mu_chunk)
    shared["eyef"] = np.eye(128, dtype=np.float32)
    shared["eyeb"] = np.eye(128, dtype=np.float32).astype(ml_dtypes.bfloat16)
    in_maps = []
    for c in range(8):
        b, h = c // 2, c % 2
        m = dict(shared)
        m["xq"] = np.ascontiguousarray(x[b, 3968 * h: 3968 * h + TOK, :])
        m["xfull"] = np.ascontiguousarray(x[b])
        mk = np.zeros(TOK, np.float32)
        if h == 0:
            mk[0:4096] = 1.0
        else:
            mk[128:TOK] = 1.0
        m["maskv"] = np.ascontiguousarray(mk.reshape(NTT, 128).T)
        in_maps.append(m)
    return in_maps


def run_staged(inputs, stop_stage, cores=None):
    nc = build_program(stop_stage=stop_stage)
    in_maps = _host_inputs(inputs)
    cores = cores if cores is not None else list(range(8))
    res = run_bass_kernel_spmd(nc, [in_maps[c] for c in cores], core_ids=cores)
    return [r["dbg"] for r in res.results]


def kernel(**inputs):
    nc = build_program()
    in_maps = _host_inputs(inputs)
    res = run_bass_kernel_spmd(nc, in_maps, core_ids=list(range(8)))
    y = np.empty((B, S, E), np.float32)
    for c in range(8):
        b, h = c // 2, c % 2
        o = res.results[c]["out"]
        if h == 0:
            y[b, 0:4096] = o[0:4096]
        else:
            y[b, 4096:8192] = o[128:TOK]
    return y


# version 17 (final)
# speedup vs baseline: 2.2558x; 1.0261x over previous
import dataclasses
import numpy as np
import ml_dtypes

import concourse.bass as bass
import concourse.mybir as mybir
from concourse.bass_utils import run_bass_kernel_spmd

F32 = mybir.dt.float32
BF16 = mybir.dt.bfloat16
AF = mybir.ActivationFunctionType
ALU = mybir.AluOpType

E = 1024
H = 16
D = 64
S = 8192
B = 4
NCHUNK = 8
TOK = 4224
NTT = 33
NKG = 64
QGS = [512] * 8 + [128]
QG0 = [sum(QGS[:i]) for i in range(len(QGS))]
NQG = len(QGS)
EPS_LN = 1e-5
EPS_NRM = 1e-8
EPS_ATT = 1e-6
NCONST = -3.1 * S + EPS_ATT

SEM_NAMES = ["dx", "dk", "dq", "dw", "dst", "dsx", "qk", "rcp", "rv", "pe", "dve",
             "acs", "cc", "oput"]


def _act_raw(nc, eng, out, in_, func, bias=0.0, scale=1.0):
    bi = eng.activation(out=out, in_=in_, func=AF.Exp, bias=bias, scale=scale)
    i2 = dataclasses.replace(bi.ins, func=func)
    nc.register_instruction(i2, overwrite=True)
    insts = nc.cur_bb.bb.instructions
    assert insts[-1] is bi.ins
    insts[-1] = i2
    return bass.BassInstruction(i2)


class Walk:

    def __init__(self, nc, kind, eng, sems):
        self.nc = nc
        self.kind = kind
        self.eng = eng
        self.sems = sems
        self.c = {s: 0 for s in SEM_NAMES}

    def inc(self, ins, name, n=1):
        self.c[name] += n
        if ins is not None:
            ins.then_inc(self.sems[name], n)

    def wait(self, name, kind, val=None):
        if self.kind == kind:
            v = self.c[name] if val is None else val
            if v > 0:
                self.eng.wait_ge(self.sems[name], v)

    def val(self, name):
        return self.c[name]

    _OWNER = {"qk": 'pe', "rv": 'pe', "pe": 'pe', "rcp": 'act', "acs": 'act',
              "dve": 'dve', "dk": 'sync', "dq": 'sync', "dx": 'sync', "dsx": 'sync',
              "oput": 'sync', "dw": 'pool', "dst": 'pool', "cc": 'pool'}

    def barrier(self):
        for s in SEM_NAMES:
            own = self._OWNER[s]
            for k in ('pe', 'act', 'dve', 'sync', 'pool'):
                if k != own:
                    self.wait(s, k)

    def mm(self, out, lhsT, rhs, start=True, stop=True, tp=None):
        if self.kind != 'pe':
            return None
        return self.eng.matmul(out, lhsT=lhsT, rhs=rhs, start=start, stop=stop,
                               tile_position=tp)

    def trans(self, out, in_, ident):
        if self.kind != 'pe':
            return None
        return self.eng.transpose(out, in_, ident)

    def act(self, out, in_, func, bias=0.0, scale=1.0, accum_out=None):
        if self.kind != 'act':
            return None
        if func in (AF.Reciprocal, AF.Rsqrt):
            return _act_raw(self.nc, self.eng, out, in_, func, bias, scale)
        kw = {} if accum_out is None else {"accum_out": accum_out}
        return self.eng.activation(out=out, in_=in_, func=func, bias=bias,
                                   scale=scale, **kw)

    def _dve(self, f):
        if self.kind != 'dve':
            self.c["dve"] += 1
            return None
        self.eng.wait_ge(self.sems["dve"], self.c["dve"])
        ins = f()
        self.c["dve"] += 1
        ins.then_inc(self.sems["dve"], 1)
        return ins

    def ts(self, out, in0, s1, s2=None, op0=ALU.mult, op1=ALU.bypass):
        return self._dve(lambda: self.eng.tensor_scalar(
            out=out, in0=in0, scalar1=s1, scalar2=s2, op0=op0, op1=op1))

    def tt(self, out, in0, in1, op=ALU.add):
        return self._dve(lambda: self.eng.tensor_tensor(out=out, in0=in0, in1=in1, op=op))

    def stt(self, out, in0, scalar, in1, op0, op1):
        return self._dve(lambda: self.eng.scalar_tensor_tensor(
            out=out, in0=in0, scalar=scalar, in1=in1, op0=op0, op1=op1))

    def copy(self, out, in_):
        return self._dve(lambda: self.eng.tensor_copy(out=out, in_=in_))

    def memset(self, ap, v):
        return self._dve(lambda: self.eng.memset(ap, v))

    def red(self, out, in_, op=ALU.add):
        return self._dve(lambda: self.eng.tensor_reduce(out, in_, mybir.AxisListType.X, op))

    def vrecip_fast(self, out, in_):
        return self._dve(lambda: self.eng.reciprocal_approx_fast(out=out, in_=in_))

    def vrecip(self, out, in_):
        def f():
            if out.dtype == BF16:
                with self.nc.allow_low_precision(reason="bf16 store of fp32 reciprocal"):
                    return self.eng.reciprocal(out=out, in_=in_)
            return self.eng.reciprocal(out=out, in_=in_)
        return self._dve(f)

    def dma(self, queue, out, in_):
        if self.kind != queue:
            return None
        return self.eng.dma_start(out=out, in_=in_)

    def cc(self, ins_, outs_):
        if self.kind != 'pool':
            return None
        return self.eng.collective_compute(
            "AllReduce", ALU.add, replica_groups=[[0, 1], [2, 3], [4, 5], [6, 7]],
            ins=ins_, outs=outs_)


class L:
    pass


def build_program(stop_stage=None):
    nc = bass.Bass("TRN2", target_bir_lowering=False, debug=False)
    t = L()

    t.xq = nc.dram_tensor("xq", [TOK, E], F32, kind="ExternalInput").ap()
    t.xfull = nc.dram_tensor("xfull", [S, E], F32, kind="ExternalInput").ap()
    for nm in ["w_aq", "w_ak", "w_av", "w_ao", "w_fr", "w_fk", "w_fv",
               "w_bq", "w_bk", "w_bv", "w_bo"]:
        setattr(t, nm, nc.dram_tensor(nm, [E, E], F32, kind="ExternalInput").ap())
    t.b_rows = nc.dram_tensor("b_rows", [5, E], F32, kind="ExternalInput").ap()
    t.b_chunk = nc.dram_tensor("b_chunk", [128, 4 * NCHUNK], F32, kind="ExternalInput").ap()
    t.mu_chunk = nc.dram_tensor("mu_chunk", [128, 2 * NCHUNK], F32, kind="ExternalInput").ap()
    t.eyef = nc.dram_tensor("eyef", [128, 128], F32, kind="ExternalInput").ap()
    t.eyeb = nc.dram_tensor("eyeb", [128, 128], BF16, kind="ExternalInput").ap()
    t.maskv = nc.dram_tensor("maskv", [128, NTT], F32, kind="ExternalInput").ap()

    t.knT_hbm = nc.dram_tensor("knT_hbm", [NKG, 128, NCHUNK * 128], BF16, kind="Internal").ap()
    t.vk_hbm = nc.dram_tensor("vk_hbm", [NKG, 128, H * 65], BF16, kind="Internal").ap()
    t.qn_hbm = nc.dram_tensor("qn_hbm", [NCHUNK, 128, TOK], BF16, kind="Internal").ap()
    t.qb_hbm = nc.dram_tensor("qb_hbm", [NCHUNK, 128, TOK], BF16, kind="Internal").ap()
    t.kv_in = nc.dram_tensor("kv_in", [128, 1024], F32, kind="Internal").ap()
    t.kv_out = nc.dram_tensor("kv_out", [128, 1024], F32, kind="Internal").ap()
    t.x_hbm = nc.dram_tensor("x_hbm", [NCHUNK, 128, TOK], F32, kind="Internal").ap()
    t.out_h = nc.dram_tensor("out", [TOK, E], F32, kind="ExternalOutput").ap()
    t.dbg = None
    if stop_stage is not None:
        t.dbg = nc.dram_tensor("dbg", [128, NCHUNK * TOK], F32, kind="ExternalOutput").ap()

    import contextlib
    with contextlib.ExitStack() as st:
        def sb(name, shape, dt):
            return st.enter_context(nc.sbuf_tensor(name, shape, dt))

        def ps(name, shape):
            return st.enter_context(nc.psum_tensor(name, shape, F32))

        t.xg = sb("xg", [128, NCHUNK * 513], F32)
        t.xfst = sb("xfst", [128, 1024], F32)
        t.consts = sb("consts", [128, 16], F32)
        t.onesb = sb("onesb", [128, 576], BF16)
        t.eyef_sb = sb("eyef_sb", [128, 128], F32)
        t.eyeb_sb = sb("eyeb_sb", [128, 128], BF16)
        t.maskv_sb = sb("maskv_sb", [128, NTT], F32)
        t.brow_sb = sb("brow_sb", [1, 5 * E], BF16)
        t.brow_lo = sb("brow_lo", [1, 5 * E], BF16)
        t.bchunk_sb = sb("bchunk_sb", [128, 4 * NCHUNK], F32)
        t.mu_sb = sb("mu_sb", [128, 2 * NCHUNK], F32)
        t.ksum_bf = sb("ksum_bf", [128, NCHUNK], BF16)
        t.vsum_sb = sb("vsum_sb", [64, H], F32)
        t.kvblk_a = sb("kvblk_a", [128, NCHUNK * 128], BF16)
        t.kvblk_b = sb("kvblk_b", [128, NCHUNK * 128], BF16)
        t.kvacc_b = sb("kvacc_b", [128, 1024], F32)
        t.wbig = sb("wbig", [128, 3 * 8192], BF16)
        t.wcol = sb("wcol", [128, 2 * 1024], BF16)
        t.stage = sb("stage", [128, 2048 + H * 65], BF16)
        t.xtok = sb("xtok", [128, 2 * 1024], F32)
        t.scrf = sb("scrf", [128, 3 * 1024], F32)
        t.scrb = sb("scrb", [128, 4 * 1024], BF16)
        t.sml = sb("sml", [128, 96], F32)
        t.stash = sb("stash", [128, 16], F32)
        t.kvacc_a = sb("kvacc_a", [128, 1024], F32)
        t.kstream = sb("kstream", [128, 2 * 1024], BF16)
        t.vstream = sb("vstream", [128, 2 * 1040], BF16)
        t.qtile = sb("qtile", [128, 2 * 512], BF16)
        t.rbuf = sb("rbuf", [128, 2 * 1024], BF16)
        t.ut = sb("ut", [64, 2 * 512], F32)
        t.invn = sb("invn", [65, 1024], BF16)
        t.ofb = sb("ofb", [128, NCHUNK * 512], BF16)
        t.xn_sb = sb("xn_sb", [128, NCHUNK * 520], BF16)
        t.ffb = sb("ffb", [128, 2 * NCHUNK * 512], BF16)
        t.kkb = sb("kkb", [128, NCHUNK * 512], BF16)

        t.p_qk_a = ps("p_qk_a", [128, 1024])
        t.p_qk_b = ps("p_qk_b", [128, 1024])
        t.p_kv = ps("p_kv", [128, 1024])
        t.p_mis = ps("p_mis", [128, 1024])

        for i, v in enumerate([1.1, -1.0, 0.0, 1.0, EPS_LN, EPS_NRM, 4.41, -3.1,
                               1.0 / E, NCONST, 0.5, 2.0]):
            nc.const_aps.register(F32, v, t.consts[:, i:i + 1])

        sems = {s: st.enter_context(nc.semaphore(s)) for s in SEM_NAMES}

        with nc.Block() as block:
            @block.sync
            def _(eng):
                _emit(Walk(nc, 'sync', eng, sems), t, stop_stage)

            @block.tensor
            def _(eng):
                _emit(Walk(nc, 'pe', eng, sems), t, stop_stage)

            @block.scalar
            def _(eng):
                _emit(Walk(nc, 'act', eng, sems), t, stop_stage)

            @block.vector
            def _(eng):
                _emit(Walk(nc, 'dve', eng, sems), t, stop_stage)

            @block.gpsimd
            def _(eng):
                _emit(Walk(nc, 'pool', eng, sems), t, stop_stage)

    return nc



def _emit(w: Walk, t, stop_stage):
    _setup(w, t)
    if stop_stage == 'p0':
        _prep(w, t, stop_p0=True)
        _fin(w, t); return
    _prep(w, t)
    w.barrier()
    if stop_stage == 'prep':
        _dump(w, t); _fin(w, t); return
    if stop_stage == 'a0':
        _attn_a(w, t, 0)
        _dump_a0(w, t); _fin(w, t); return
    t.ot_tok = [0, 0]
    _load3(w, t, [t.w_fr, t.w_fk, t.w_fv])
    for qg in range(NQG):
        _attn_a(w, t, qg)
        w.barrier()
        _ffn(w, t, qg, resident=True)
        w.barrier()
    if stop_stage == 'x2':
        _dump(w, t); _fin(w, t); return
    _load3(w, t, [t.w_bq, t.w_bk, t.w_bv])
    for qg in range(NQG):
        _attnb_pre(w, t, qg)
        w.barrier()
        if stop_stage == 'b0' and qg == 0:
            w.wait("dve", 'pool')
            w.wait("dst", 'pool')
            d = t.dbg
            w.inc(w.dma('pool', d[:, 0:1024], t.kvacc_b[:, :]), "oput", 16)
            w.inc(w.dma('pool', d[:, 1024:2048], t.kvblk_b[:, :]), "oput", 16)
            w.inc(w.dma('pool', d[:, 2048:2560], t.qb_hbm[0, :, 0:512]), "oput", 16)
            w.inc(w.dma('pool', d[:, 2560:3584], t.scrb[:, 0:1024]), "oput", 16)
            w.inc(w.dma('pool', d[:, 3584:4608], t.scrb[:, 1024:2048]), "oput", 16)
            w.inc(w.dma('pool', d[:, 4608:5128], t.xn_sb[:, 0:520]), "oput", 16)
            _fin(w, t); return
    w.barrier()
    _attnb_reduce(w, t)
    w.barrier()
    if stop_stage == 'b1':
        w.wait("dve", 'pool')
        d = t.dbg
        w.inc(w.dma('pool', d[:, 0:1024], t.kvacc_b[:, :]), "oput", 16)
        w.inc(w.dma('pool', d[:, 1024:2048], t.kvblk_b[:, :]), "oput", 16)
        _fin(w, t); return
    _load3(w, t, [t.w_fr, t.w_fk, t.w_fv])
    for qg in range(NQG):
        _attnb_out(w, t, qg)
        w.barrier()
        _ffn(w, t, qg, resident=True, out_fuse=True)
        w.barrier()
    _fin(w, t)


def _dump(w, t):
    w.wait("dve", 'sync')
    w.wait("pe", 'sync')
    w.wait("dst", 'sync')
    w.wait("dsx", 'sync')
    for a in range(NCHUNK):
        w.inc(w.dma('sync', t.dbg[:, a * TOK:(a + 1) * TOK], t.x_hbm[a]), "oput", 16)


def _dump_a0(w, t):
    w.wait("dve", 'pool')
    w.wait("pe", 'pool')
    w.wait("acs", 'pool')
    w.wait("dst", 'pool')
    d = t.dbg
    w.inc(w.dma('pool', d[:, 0:4096], t.ofb[:, :]), "oput", 16)
    w.inc(w.dma('pool', d[0:65, 4096:5120], t.invn[:, :]), "oput", 16)
    w.inc(w.dma('pool', d[0:64, 5120:6144], t.ut[:, :]), "oput", 16)
    w.inc(w.dma('pool', d[64:65, 6144:7168], t.scrf[64:65, 0:1024]), "oput", 16)
    w.inc(w.dma('pool', d[:, 7168:7680], t.qn_hbm[0, :, 0:512]), "oput", 16)
    w.inc(w.dma('pool', d[:, 7680:7808], t.knT_hbm[0, :, 0:128]), "oput", 16)
    w.inc(w.dma('pool', d[:, 7808:7938], t.vk_hbm[0, :, 0:130]), "oput", 16)
    w.inc(w.dma('pool', d[:, 7938:7946], t.ksum_bf[:, :]), "oput", 16)
    w.inc(w.dma('pool', d[0:64, 7946:7962], t.vsum_sb[:, :]), "oput", 16)
    w.inc(w.dma('pool', d[:, 8000:9024], t.kvblk_a[:, :]), "oput", 16)
    w.inc(w.dma('pool', d[:, 9024:9056], t.bchunk_sb[:, :]), "oput", 16)
    w.copy(t.scrf[64:65, 1024:2048], t.p_kv[64:65, 0:1024])
    w.wait("dve", 'pool')
    w.inc(w.dma('pool', d[64:65, 9056:10080], t.scrf[64:65, 1024:2048]), "oput", 16)
    w.inc(w.dma('pool', d[:, 10080:12128], t.rbuf[:, :]), "oput", 16)


def _fin(w, t):
    if w.kind == 'pool':
        w.eng.wait_ge(w.sems["oput"], w.c["oput"])


def _setup(w, t):
    for i, v in enumerate([1.1, -1.0, 0.0, 1.0, EPS_LN, EPS_NRM, 4.41, -3.1,
                           1.0 / E, NCONST, 0.5, 2.0]):
        w.memset(t.consts[:, i:i + 1], v)
    w.memset(t.onesb[:, :], 1.0)
    w.memset(t.kvacc_b[:, :], 0.0)
    w.memset(t.kvacc_a[:, :], 0.0)
    w.memset(t.sml[:, 64:88], 0.0)
    w.memset(t.kvblk_a[:, :], 0.0)
    w.memset(t.kvblk_b[:, :], 0.0)
    w.inc(w.dma('sync', t.eyef_sb[:, :], t.eyef), "dx", 16)
    w.inc(w.dma('sync', t.eyeb_sb[:, :], t.eyeb), "dx", 16)
    w.inc(w.dma('sync', t.maskv_sb[:, :], t.maskv), "dx", 16)
    w.inc(w.dma('sync', t.bchunk_sb[:, :], t.b_chunk), "dx", 16)
    w.inc(w.dma('sync', t.mu_sb[:, :], t.mu_chunk), "dx", 16)
    for r_ in range(5):
        w.wait("dve", 'pool')
        w.inc(w.dma('pool', t.scrf[0:1, 0:E], t.b_rows[r_:r_ + 1, :]), "dw", 16)
        w.wait("dw", 'dve')
        w.ts(t.brow_sb[0:1, r_ * E:(r_ + 1) * E], t.scrf[0:1, 0:E], 1.0)
        w.ts(t.scrf[0:1, E:2 * E], t.brow_sb[0:1, r_ * E:(r_ + 1) * E], -1.0)
        w.tt(t.brow_lo[0:1, r_ * E:(r_ + 1) * E], t.scrf[0:1, E:2 * E],
             t.scrf[0:1, 0:E], op=ALU.add)


def _load3(w, t, srcs):
    w.wait("pe", 'pool')
    for i, src in enumerate(srcs):
        w.inc(w.dma('pool', t.wbig[:, i * 8192:(i + 1) * 8192].rearrange("p (a d) -> p a d", a=8),
                    src.rearrange("(a p) d -> p a d", p=128)), "dw", 16)


def _ln_tok(w, t, xt, xn_out):
    m = t.sml[:, 0:1]
    v = t.sml[:, 1:2]
    r = t.sml[:, 2:3]
    sq = t.scrf[:, E:2 * E]
    w.red(m, xt)
    w.ts(m, m, -1.0 / E)
    w.wait("dve", 'act')
    w.wait("acs", 'act')
    w.inc(w.act(sq, xt, AF.Square, bias=m, accum_out=v), "acs", 1)
    w.wait("acs", 'act')
    w.inc(w.act(r, v, AF.Rsqrt, bias=EPS_LN, scale=1.0 / E), "acs", 1)
    w.wait("acs", 'dve')
    w.ts(xn_out, xt, m, r, op0=ALU.add, op1=ALU.mult)


def _norm_heads(w, t, src_f32, dst_bf):
    sq = t.scrf[:, E:2 * E]
    ss = t.sml[:, 16:32]
    src3 = src_f32.rearrange("p (h d) -> p h d", h=H)
    w.wait("dve", 'act')
    w.wait("acs", 'act')
    w.inc(w.act(sq, src_f32, AF.Square), "acs", 1)
    w.wait("acs", 'dve')
    w.red(ss, sq.rearrange("p (h d) -> p h d", h=H))
    w.wait("dve", 'act')
    w.inc(w.act(t.sml[:, 32:48], ss, AF.Rsqrt), "acs", 1)
    w.wait("acs", 'dve')
    w.tt(t.sml[:, 32:48], t.sml[:, 32:48], ss, op=ALU.mult)
    w.ts(t.sml[:, 32:48], t.sml[:, 32:48], EPS_NRM, op0=ALU.add)
    w.vrecip(ss, t.sml[:, 32:48])
    w.tt(dst_bf.rearrange("p (h d) -> p h d", h=H), src3,
         ss.rearrange("p (h o) -> p h o", o=1).broadcast_to([128, H, D]),
         op=ALU.mult)


def _transpose8(w, t, src_bf, dst_bf, pt, dve_val=None):
    w.wait("dve", 'pe', val=(dve_val if dve_val is not None else w.val("dve")))
    i = None
    for a in range(NCHUNK):
        i = w.trans(pt[:, a * 64:(a + 1) * 64].bitcast(BF16),
                    src_bf[:, a * 128:(a + 1) * 128], t.eyeb_sb[:, :])
    w.inc(i, "pe", 1)
    w.wait("pe", 'dve')
    w.copy(dst_bf[:, 0:1024], pt[:, 0:512].bitcast(BF16))


def _proj_tok(w, t, xnT, wbase, brow_idx, pdst):
    last = None
    for half in range(2):
        po = pdst[:, half * 512:(half + 1) * 512]
        for a in range(NCHUNK):
            w.mm(po, xnT[:, a * 128:(a + 1) * 128],
                 t.wbig[:, wbase + a * 1024 + half * 512: wbase + a * 1024 + (half + 1) * 512],
                 start=(a == 0), stop=False)
        w.mm(po, t.onesb[0:1, 0:128],
             t.brow_sb[0:1, brow_idx * E + half * 512: brow_idx * E + (half + 1) * 512],
             start=False, stop=False)
        last = w.mm(po, t.onesb[0:1, 0:128],
                    t.brow_lo[0:1, brow_idx * E + half * 512: brow_idx * E + (half + 1) * 512],
                    start=False, stop=True)
    w.inc(last, "pe", 1)


def _prep(w, t, stop_p0=False):
    _load3(w, t, [t.w_ak, t.w_av, t.w_aq])
    w.wait("dw", 'pe')
    w.wait("dx", 'dve')
    w.wait("dve", 'pe')
    ONES_COL = t.onesb[:, 0:1]

    xtok_tok = [0, 0]
    xtok_pe = [0, 0]
    tile_pe = [0, 0]
    flush_pe = [None]
    st_tok = {"kn": 0, "vk": 0, "qn": 0}
    NTOT = NKG + NTT
    ln_dve = {}

    def _load_ln(tc2):
        if tc2 >= NTOT:
            return
        full2 = tc2 < NKG
        buf2 = tc2 % 2
        xt2 = t.xtok[:, buf2 * 1024:(buf2 + 1) * 1024]
        src2 = (t.xfull[tc2 * 128:(tc2 + 1) * 128, :] if full2
                else t.xq[(tc2 - NKG) * 128:(tc2 - NKG + 1) * 128, :])
        w.wait("dve", 'sync', val=xtok_tok[buf2])
        w.wait("pe", 'sync', val=xtok_pe[buf2])
        w.inc(w.dma('sync', xt2, src2), "dx", 16)
        w.wait("dx", 'dve')
        nb2 = t.scrb if tc2 % 2 == 0 else t.ffb
        w.wait("pe", 'dve', val=tile_pe[tc2 % 2])
        _ln_tok(w, t, xt2, nb2[:, 0:1024])
        ln_dve[tc2] = w.val("dve")
        xtok_tok[buf2] = w.val("dve")

    def _flush():
        if flush_pe[0] is not None:
            w.wait("pe", 'dve', val=flush_pe[0])
            w.tt(t.kvacc_a[:, :], t.kvacc_a[:, :], t.p_kv[:, :], op=ALU.add)
            w.tt(t.sml[:, 64:72], t.sml[:, 64:72], t.p_qk_b[:, 0:8], op=ALU.add)
            w.tt(t.sml[0:64, 72:88], t.sml[0:64, 72:88], t.p_qk_b[0:64, 16:32], op=ALU.add)
            flush_pe[0] = None

    _load_ln(0)
    for tc in range(NTOT):
        full = tc < NKG
        buf = tc % 2
        xt = t.xtok[:, buf * 1024:(buf + 1) * 1024]
        nbase = t.scrb if tc % 2 == 0 else t.ffb
        xnb = nbase[:, 0:1024]

        if not full:
            tt_ = tc - NKG
            w.wait("dve", 'pe')
            i = None
            for a in range(NCHUNK):
                i = w.trans(t.p_mis[:, a * 128:(a + 1) * 128], xt[:, a * 128:(a + 1) * 128],
                            t.eyef_sb[:, :])
            w.inc(i, "pe", 1)
            xtok_pe[buf] = w.val("pe")
            w.wait("pe", 'dve')
            w.wait("dst", 'dve', val=st_tok.get("xf", 0))
            w.copy(t.xfst[:, 0:1024], t.p_mis[:, 0:1024])
            w.wait("dve", 'pool')
            w.inc(w.dma('pool',
                        t.x_hbm[:, :, tt_ * 128:(tt_ + 1) * 128].rearrange("a p c -> p a c"),
                        t.xfst.rearrange("p (a c) -> p a c", a=8)), "dst", 16)
            st_tok["xf"] = w.val("dst")
            dve_tp = w.val("dve")
        else:
            dve_tp = ln_dve[tc]

        xnT = nbase[:, 1024:2048]
        _transpose8(w, t, xnb, xnT, t.p_qk_a, dve_val=dve_tp)
        w.wait("dve", 'pe')

        if full:
            _proj_tok(w, t, xnT, 0, 1, t.p_mis)
            _load_ln(tc + 1)
            _flush()
            w.wait("pe", 'dve')
            w.wait("pe", 'act')
            kn_tok = t.scrb[:, 2048:3072]
            _norm_heads(w, t, t.p_mis[:, 0:1024], kn_tok)
            if stop_p0:
                w.wait("dve", 'pool')
                w.wait("acs", 'pool')
                d = t.dbg
                w.inc(w.dma('pool', d[:, 0:1024], t.scrb[:, 0:1024]), "oput", 16)
                w.inc(w.dma('pool', d[:, 1024:2048], t.scrb[:, 1024:2048]), "oput", 16)
                w.inc(w.dma('pool', d[:, 2048:3072], t.p_mis[:, 0:1024]), "oput", 16)
                w.inc(w.dma('pool', d[:, 3072:4096], kn_tok), "oput", 16)
                w.inc(w.dma('pool', d[:, 4096:4160], t.sml[:, :]), "oput", 16)
                w.inc(w.dma('pool', d[:, 4160:4288], t.xtok[:, 0:128]), "oput", 16)
                w.inc(w.dma('pool', d[:, 4288:6336], t.wbig[:, 0:2048]), "oput", 16)
                w.inc(w.dma('pool', d[0:1, 6336:11456], t.brow_sb[:, :]), "oput", 16)
                w.inc(w.dma('pool', d[0:1, 11456:16576], t.brow_lo[:, :]), "oput", 16)
                return
            w.wait("dve", 'pe')
            _proj_tok(w, t, xnT, 8192, 2, t.p_mis)
            w.wait("pe", 'dve')
            w.wait("pe", 'act')
            v_tok = t.scrb[:, 3072:4096]
            w.wait("acs", 'act')
            w.inc(w.act(v_tok, t.p_mis[:, 0:1024], AF.Copy), "acs", 1)
            w.wait("acs", 'dve')
            v_lo = nbase[:, 0:1024]
            w.ts(t.scrf[:, 0:1024], v_tok, -1.0)
            w.tt(v_lo, t.scrf[:, 0:1024], t.p_mis[:, 0:1024], op=ALU.add)
            w.wait("dst", 'dve', val=st_tok["vk"])
            vkst = t.stage[:, 2048:2048 + H * 65]
            vk3 = vkst.rearrange("p (h e) -> p h e", h=H)
            if tc == 0:
                w.memset(vk3[:, :, 64:65], 1.0)
            w.copy(vk3[:, :, 0:64], v_tok.rearrange("p (h d) -> p h d", h=H))
            w.wait("dst", 'dve', val=st_tok["kn"])
            knst = t.stage[:, 1024:2048]
            _transpose8(w, t, kn_tok, knst, t.p_qk_a)
            w.wait("dve", 'pe')
            w.wait("acs", 'pe')
            for a in range(NCHUNK):
                w.mm(t.p_kv[:, a * 128:(a + 1) * 128], kn_tok[:, a * 128:(a + 1) * 128],
                     v_tok[:, a * 128:(a + 1) * 128], start=True, stop=True)
            for a in range(NCHUNK):
                w.mm(t.p_qk_b[:, a:a + 1], kn_tok[:, a * 128:(a + 1) * 128], ONES_COL,
                     start=True, stop=True)
            i = None
            for hh in range(H):
                w.mm(t.p_qk_b[0:64, 16 + hh:17 + hh], v_tok[:, hh * 64:(hh + 1) * 64],
                     ONES_COL, start=True, stop=False)
                i = w.mm(t.p_qk_b[0:64, 16 + hh:17 + hh], v_lo[:, hh * 64:(hh + 1) * 64],
                         ONES_COL, start=False, stop=True)
            w.inc(i, "pe", 1)
            flush_pe[0] = w.val("pe")
            w.wait("dve", 'pool')
            w.wait("pe", 'pool', val=0)
            w.inc(w.dma('pool', t.knT_hbm[tc], knst), "dst", 16)
            st_tok["kn"] = w.val("dst")
            w.inc(w.dma('pool', t.vk_hbm[tc], vkst), "dst", 16)
            st_tok["vk"] = w.val("dst")
        else:
            _proj_tok(w, t, xnT, 16384, 0, t.p_mis)
            _load_ln(tc + 1)
            _flush()
            w.wait("pe", 'dve')
            w.wait("pe", 'act')
            qn_tok = t.scrb[:, 2048:3072]
            _norm_heads(w, t, t.p_mis[:, 0:1024], qn_tok)
            w.wait("dst", 'dve', val=st_tok["qn"])
            qnst = t.stage[:, 0:1024]
            _transpose8(w, t, qn_tok, qnst, t.p_qk_a)
            tt_ = tc - NKG
            w.wait("dve", 'pool')
            w.inc(w.dma('pool',
                        t.qn_hbm[:, :, tt_ * 128:(tt_ + 1) * 128].rearrange("a p c -> p a c"),
                        qnst.rearrange("p (a c) -> p a c", a=8)), "dst", 16)
            st_tok["qn"] = w.val("dst")
        tile_pe[tc % 2] = w.val("pe")

    _flush()

    w.wait("pe", 'dve')
    kb3 = t.kvblk_a.rearrange("p (a c) -> p a c", a=NCHUNK)
    ka3 = t.kvacc_a.rearrange("p (a c) -> p a c", a=NCHUNK)
    w.copy(kb3[0:64, :, 0:64], ka3[0:64, :, 0:64])
    w.copy(kb3[64:128, :, 64:128], ka3[64:128, :, 64:128])
    w.copy(t.ksum_bf[:, :], t.sml[:, 64:64 + NCHUNK])
    w.copy(t.vsum_sb[:, :], t.sml[0:64, 72:72 + H])
    w.ts(t.vsum_sb[:, :], t.vsum_sb[:, :], 3.1)


def _rb_slot(t, kg):
    s = kg % 4
    if s < 2:
        return t.rbuf[:, s * 1024:(s + 1) * 1024]
    return t.scrb[:, s * 1024:(s + 1) * 1024]


def _emit_rv(w, t, qg, kg, nq):
    if kg < 0:
        return
    oo = kg // 8
    kk = kg % 8
    ob = oo % 2
    vk_e = t.vstream[:, ob * 1040 + kk * 130: ob * 1040 + kk * 130 + 65]
    vk_o = t.vstream[:, ob * 1040 + kk * 130 + 65: ob * 1040 + kk * 130 + 130]
    rb_ = _rb_slot(t, kg)
    if kg == 0 and t.fin_dve is not None:
        w.wait("dve", 'pe', val=t.fin_dve)
        t.fin_dve = None
    rs_, rv_ = t.rcp_of[kg]
    w.wait(rs_, 'pe', val=rv_)
    w.mm(t.p_kv[0:65, 0:nq], vk_e, rb_[:, 0:nq], start=(kg == 0), stop=(kg == NKG - 1))
    i = w.mm(t.p_kv[0:65, 512:512 + nq], vk_o, rb_[:, 512:512 + nq],
             start=(kg == 0), stop=(kg == NKG - 1))
    w.inc(i, "rv", 1)


def _fin_pair(w, t, qg, p, nq):
    qt = t.qtile[:, (p % 2) * 512:(p % 2) * 512 + nq]
    w.wait("rv", 'dve')
    ue = t.ut[:, 0:nq]
    uo = t.ut[:, 512:512 + nq]
    w.copy(ue, t.p_kv[0:64, 0:nq])
    w.copy(uo, t.p_kv[0:64, 512:512 + nq])
    w.wait("rv", 'pe')
    w.mm(t.p_mis[64:65, 0:nq], t.ksum_bf[0:64, p:p + 1], qt[0:64, :], tp=(0, 64))
    i = w.mm(t.p_mis[64:65, 512:512 + nq], t.ksum_bf[64:128, p:p + 1],
             qt[64:128, :], tp=(64, 64))
    w.inc(i, "pe", 1)
    w.wait("pe", 'dve')
    nr_e = t.scrf[64:65, 0:nq]
    nr_o = t.scrf[64:65, 512:512 + nq]
    w.copy(nr_e, t.p_kv[64:65, 0:nq])
    w.copy(nr_o, t.p_kv[64:65, 512:512 + nq])
    t.fin_dve = w.val("dve")
    w.stt(nr_e, nr_e, 4.41, t.p_mis[64:65, 0:nq], ALU.mult, ALU.subtract)
    w.stt(nr_o, nr_o, 4.41, t.p_mis[64:65, 512:512 + nq], ALU.mult, ALU.subtract)
    w.ts(nr_e, nr_e, NCONST, op0=ALU.add)
    w.ts(nr_o, nr_o, NCONST, op0=ALU.add)
    w.wait("dve", 'act')
    w.wait("pe", 'act')
    w.act(t.invn[64:65, 0:nq], nr_e, AF.Reciprocal)
    w.inc(w.act(t.invn[64:65, 512:512 + nq], nr_o, AF.Reciprocal), "acs", 1)
    w.wait("acs", 'pe')
    for half in range(2):
        co = half * 512
        usrc = ue if half == 0 else uo
        w.mm(t.p_mis[0:64, 0:nq], t.onesb[64:65, 0:64], t.invn[64:65, co:co + nq])
        if half == 0:
            i = w.mm(t.p_mis[0:64, 512:512 + nq], t.kvblk_a[0:64, p * 128:p * 128 + 64],
                     qt[0:64, :], tp=(0, 0))
        else:
            i = w.mm(t.p_mis[0:64, 512:512 + nq],
                     t.kvblk_a[64:128, p * 128 + 64:p * 128 + 128],
                     qt[64:128, :], tp=(64, 0))
        w.inc(i, "pe", 1)
        w.wait("pe", 'dve')
        e1 = t.scrf[0:64, 1024:1024 + nq]
        w.stt(e1, usrc, 4.41, t.p_mis[0:64, 512:512 + nq], ALU.mult, ALU.subtract)
        if half == 1:
            w.wait("dst", 'dve')
        dst = t.ofb[0:64, p * 512:p * 512 + nq] if half == 0 else t.scrb[0:64, 0:nq]
        w.stt(dst, e1, t.vsum_sb[:, 2 * p + half:2 * p + half + 1],
              t.p_mis[0:64, 0:nq], ALU.subtract, ALU.mult)
        if half == 1:
            w.wait("dve", 'pool')
            w.inc(w.dma('pool', t.ofb[64:128, p * 512:p * 512 + nq], t.scrb[0:64, 0:nq]),
                  "dst", 16)
        else:
            w.wait("dve", 'pe')
    t.fin_dve2 = w.val("dve")


def _attn_a(w, t, qg):
    nq = QGS[qg]
    q0 = QG0[qg]
    t.rcp_of = {}
    t.fin_dve = None
    t.fin_dve2 = None
    pend = None
    w.wait("dst", 'sync')
    for p in range(NCHUNK):
        qb = p % 2
        qt = t.qtile[:, qb * 512:qb * 512 + nq]
        w.wait("pe", 'sync', val=w.val("pe"))
        w.wait("qk", 'sync', val=w.val("qk"))
        w.inc(w.dma('sync', qt, t.qn_hbm[p, :, q0:q0 + nq]), "dq", 16)
        w.wait("dve", 'pe')
        dk_tok = {}
        for oct_ in range(10):
            if oct_ >= 2:
                oo = oct_ - 2
                ob = oo % 2
                for kk in range(8):
                    kg = oo * 8 + kk
                    pb = kg % 2
                    if kg < 5:
                        pqk = (t.p_qk_a, t.p_qk_b)[kg % 2]
                    else:
                        pqk = (t.p_qk_a, t.p_qk_b, t.p_mis)[kg % 3]
                    kn_e = t.kstream[0:64, ob * 1024 + kk * 128: ob * 1024 + (kk + 1) * 128]
                    kn_o = t.kstream[64:128, ob * 1024 + kk * 128: ob * 1024 + (kk + 1) * 128]
                    if kk == 0:
                        w.wait("dk", 'pe', val=dk_tok[oo])
                        if oo == 0:
                            w.wait("dq", 'pe')
                    w.wait("rcp", 'pe', val=w.val("rcp") - (0 if kg == 0 else 2))
                    if kg == 5 and t.fin_dve2 is not None:
                        w.wait("dve", 'pe', val=t.fin_dve2)
                        t.fin_dve2 = None
                    w.mm(pqk[:, 0:nq], kn_e, qt[0:64, :], tp=(0, 0))
                    i = w.mm(pqk[:, 512:512 + nq], kn_o, qt[64:128, :], tp=(64, 0))
                    w.inc(i, "qk", 1)
                    _emit_rv(w, t, qg, kg - 2, nq)
                    if kg == 1 and pend is not None:
                        _fin_pair(w, t, qg, pend, nq)
                        pend = None
                    rb_ = _rb_slot(t, kg)
                    w.wait("qk", 'act')
                    w.wait("rv", 'act', val=w.val("rv") - 2)
                    if nq == 512:
                        i = w.act(rb_[:, 0:1024], pqk[:, 0:1024], AF.Reciprocal,
                                  bias=1.1, scale=-1.0)
                    else:
                        w.act(rb_[:, 0:nq], pqk[:, 0:nq], AF.Reciprocal, bias=1.1, scale=-1.0)
                        i = w.act(rb_[:, 512:512 + nq], pqk[:, 512:512 + nq], AF.Reciprocal,
                                  bias=1.1, scale=-1.0)
                    w.inc(i, "rcp", 1)
                    t.rcp_of[kg] = ("rcp", w.val("rcp"))
            if oct_ < 8:
                ob2 = oct_ % 2
                w.wait("qk", 'sync', val=w.val("qk"))
                w.wait("rv", 'sync', val=w.val("rv") + (2 if oct_ >= 2 else 0))
                w.inc(w.dma('sync',
                            t.kstream[:, ob2 * 1024:(ob2 + 1) * 1024]
                            .rearrange("p (k c) -> p k c", k=8),
                            t.knT_hbm[oct_ * 8:(oct_ + 1) * 8, :, p * 128:(p + 1) * 128]
                            .rearrange("k p c -> p k c")),
                      "dk", 16)
                w.inc(w.dma('sync',
                            t.vstream[:, ob2 * 1040:(ob2 + 1) * 1040]
                            .rearrange("p (k c) -> p k c", k=8),
                            t.vk_hbm[oct_ * 8:(oct_ + 1) * 8, :, 2 * p * 65:(2 * p + 2) * 65]
                            .rearrange("k p c -> p k c")),
                      "dk", 16)
                dk_tok[oct_] = w.val("dk")
        _emit_rv(w, t, qg, NKG - 2, nq)
        _emit_rv(w, t, qg, NKG - 1, nq)
        pend = p
    _fin_pair(w, t, qg, pend, nq)
    w.wait("dst", 'pe')
    w.wait("dve", 'sync')
    xgv = t.xg.rearrange("p (a c) -> p a c", c=513)[:, :, 1:1 + nq]
    w.inc(w.dma('sync', xgv, t.x_hbm[:, :, q0:q0 + nq].rearrange("a p c -> p a c")), "dx", 16)
    w.wait("dx", 'dve')
    wc_tok = [None, None]
    for fc in range(NCHUNK):
        wb = fc % 2
        w.wait("pe", 'pool', val=(wc_tok[wb] if wc_tok[wb] is not None else w.val("pe")))
        w.inc(w.dma('pool', t.wcol[:, wb * 1024:(wb + 1) * 1024].rearrange("p (a c) -> p a c", a=8),
                    t.w_ao[:, fc * 128:(fc + 1) * 128].rearrange("(a p) c -> p a c", p=128)),
              "dw", 16)
        w.wait("dw", 'pe')
        w.wait("dve", 'pe')
        i = None
        for a in range(NCHUNK):
            i = w.mm(t.p_mis[:, 0:nq], t.wcol[:, wb * 1024 + a * 128:wb * 1024 + (a + 1) * 128],
                     t.ofb[:, a * 512:a * 512 + nq], start=(a == 0), stop=(a == NCHUNK - 1))
        w.inc(i, "pe", 1)
        wc_tok[wb] = w.val("pe")
        w.wait("pe", 'dve')
        xc = t.xg[:, fc * 513 + 1: fc * 513 + 1 + nq]
        w.ts(t.scrf[:, 0:nq], t.p_mis[:, 0:nq], t.bchunk_sb[:, fc:fc + 1], op0=ALU.add)
        w.tt(xc, xc, t.scrf[:, 0:nq], op=ALU.add)
    sl_ = (qg % 2) * 8
    w.copy(t.stash[:, sl_:sl_ + 8].rearrange("p (a one) -> p a one", one=1),
           t.xg.rearrange("p (a c) -> p a c", c=513)[:, :, nq:nq + 1])
    w.wait("dve", 'pe')


def _ln_fm(w, t, qg, resident=False):
    nq = QGS[qg]
    q0 = QG0[qg]
    nb = nq + 1
    bnd = qg > 0
    ONES_COL = t.onesb[:, 0:1]
    if resident:
        if bnd:
            sl_ = ((qg + 1) % 2) * 8
            w.copy(t.xg.rearrange("p (a c) -> p a c", c=513)[:, :, 0:1],
                   t.stash[:, sl_:sl_ + 8].rearrange("p (a one) -> p a one", one=1))
    else:
        w.wait("dve", 'sync')
        w.wait("pe", 'sync')
        if not bnd:
            xgv = t.xg.rearrange("p (a c) -> p a c", c=513)[:, :, 1:1 + nq]
            w.inc(w.dma('sync', xgv, t.x_hbm[:, :, 0:nq].rearrange("a p c -> p a c")), "dx", 16)
        else:
            xgv = t.xg.rearrange("p (a c) -> p a c", c=513)[:, :, 0:nb]
            w.inc(w.dma('sync', xgv, t.x_hbm[:, :, q0 - 1:q0 + nq].rearrange("a p c -> p a c")),
                  "dx", 16)
        w.wait("dx", 'dve')
    cast_tok = [None, None]
    for a in range(NCHUNK):
        sl = a % 2
        xb = t.scrb[:, sl * 1024:sl * 1024 + nb]
        x2 = t.scrb[:, 2048 + sl * 1024:2048 + sl * 1024 + nb]
        w.wait("pe", 'dve', val=(cast_tok[sl] if cast_tok[sl] is not None else w.val("pe")))
        if bnd:
            xsrc = t.xg[:, a * 513:a * 513 + nb]
            w.ts(xb, xsrc, 1.0)
            w.tt(x2, xsrc, xsrc, op=ALU.mult)
        else:
            xsrc = t.xg[:, a * 513 + 1:a * 513 + 1 + nq]
            w.ts(xb[:, 1:nb], xsrc, 1.0)
            w.tt(x2[:, 1:nb], xsrc, xsrc, op=ALU.mult)
        w.wait("dve", 'pe')
        w.mm(t.p_qk_a[0:1, 0:nq], ONES_COL, xb[:, 1:nb], start=(a == 0), stop=(a == NCHUNK - 1))
        i = w.mm(t.p_qk_b[0:1, 0:nq], ONES_COL, x2[:, 1:nb], start=(a == 0), stop=(a == NCHUNK - 1))
        if bnd:
            w.mm(t.p_qk_a[0:1, 512:513], ONES_COL, xb[:, 0:1],
                 start=(a == 0), stop=(a == NCHUNK - 1))
            i = w.mm(t.p_qk_b[0:1, 512:513], ONES_COL, x2[:, 0:1],
                     start=(a == 0), stop=(a == NCHUNK - 1))
        w.inc(i, "pe", 1)
        cast_tok[sl] = w.val("pe")
    w.wait("pe", 'dve')
    ncols = 513 if bnd else nq
    mrow = t.scrf[0:1, 0:ncols]
    vrow = t.scrf[0:1, 1024:1024 + ncols]
    w.ts(mrow, t.p_qk_a[0:1, 0:ncols], 1.0 / E)
    w.ts(vrow, t.p_qk_b[0:1, 0:ncols], 1.0 / E)
    w.tt(t.scrf[0:1, 2048:2048 + ncols], mrow, mrow, op=ALU.mult)
    w.tt(vrow, vrow, t.scrf[0:1, 2048:2048 + ncols], op=ALU.subtract)
    w.wait("dve", 'act')
    w.inc(w.act(t.scrf[0:1, 2048:2048 + ncols], vrow, AF.Rsqrt, bias=EPS_LN), "acs", 1)
    w.wait("acs", 'dve')
    w.ts(t.scrb[0:1, 0:ncols], mrow, 1.0)
    w.ts(t.scrb[0:1, 1024:1024 + ncols], t.scrf[0:1, 2048:2048 + ncols], 1.0)
    w.wait("dve", 'pe')
    w.mm(t.p_qk_b[:, 0:nq], t.onesb[0:1, 0:128], t.scrb[0:1, 0:nq])
    i = w.mm(t.p_qk_a[:, 0:nq], t.onesb[0:1, 0:128], t.scrb[0:1, 1024:1024 + nq])
    if bnd:
        w.mm(t.p_qk_b[:, 512:513], t.onesb[0:1, 0:128], t.scrb[0:1, 512:513])
        i = w.mm(t.p_qk_a[:, 512:513], t.onesb[0:1, 0:128], t.scrb[0:1, 1024 + 512:1024 + 513])
    w.inc(i, "pe", 1)
    w.wait("pe", 'dve')
    for a in range(NCHUNK):
        w.tt(t.scrf[:, 0:nq], t.xg[:, a * 513 + 1:a * 513 + 1 + nq],
             t.p_qk_b[:, 0:nq], op=ALU.subtract)
        w.tt(t.xn_sb[:, a * 520 + 1:a * 520 + 1 + nq], t.scrf[:, 0:nq],
             t.p_qk_a[:, 0:nq], op=ALU.mult)
        if bnd:
            w.tt(t.scrf[:, 512:513], t.xg[:, a * 513:a * 513 + 1],
                 t.p_qk_b[:, 512:513], op=ALU.subtract)
            w.tt(t.xn_sb[:, a * 520:a * 520 + 1], t.scrf[:, 512:513],
                 t.p_qk_a[:, 512:513], op=ALU.mult)
        else:
            w.memset(t.xn_sb[:, a * 520:a * 520 + 1], 0.0)


def _ffn(w, t, qg, resident=False, out_fuse=False):
    nq = QGS[qg]
    q0 = QG0[qg]
    _ln_fm(w, t, qg, resident=resident)
    i = None
    for a in range(NCHUNK):
        xn_c = t.xn_sb[:, a * 520 + 1:a * 520 + 1 + nq]
        xn_p = t.xn_sb[:, a * 520:a * 520 + nq]
        d = t.scrb[:, 0:nq]
        w.tt(d, xn_c, xn_p, op=ALU.subtract)
        w.stt(t.ffb[:, a * 512:a * 512 + nq], d, t.mu_sb[:, a:a + 1], xn_p, ALU.mult, ALU.add)
        i = w.stt(t.ffb[:, 4096 + a * 512:4096 + a * 512 + nq], d,
                  t.mu_sb[:, 8 + a:9 + a], xn_p, ALU.mult, ALU.add)
    w.wait("dve", 'pe')
    for dc in range(NCHUNK):
        pd = (t.p_mis[:, 0:nq], t.p_mis[:, 512:512 + nq],
              t.p_qk_a[:, 0:nq], t.p_qk_a[:, 512:512 + nq])[dc % 4]
        if dc >= 4:
            w.wait("acs", 'pe', val=w.val("acs") - 3)
        i = None
        for a in range(NCHUNK):
            i = w.mm(pd, t.wbig[:, 8192 + a * 1024 + dc * 128:8192 + a * 1024 + (dc + 1) * 128],
                     t.ffb[:, 4096 + a * 512:4096 + a * 512 + nq],
                     start=(a == 0), stop=(a == NCHUNK - 1))
        w.inc(i, "qk", 1)
        w.wait("qk", 'act')
        w.inc(w.act(t.scrf[:, (dc % 2) * 512:(dc % 2) * 512 + nq], pd, AF.Relu), "acs", 1)
        w.wait("acs", 'dve')
        w.tt(t.kkb[:, dc * 512:dc * 512 + nq],
                   t.scrf[:, (dc % 2) * 512:(dc % 2) * 512 + nq],
                   t.scrf[:, (dc % 2) * 512:(dc % 2) * 512 + nq], op=ALU.mult)
    w.wait("dve", 'pe')
    sig_tok = {}
    gate_tok = {}
    for dc in range(NCHUNK):
        pbase = t.p_mis if dc % 2 == 0 else t.p_qk_b
        pr = pbase[:, 0:nq]
        pv = pbase[:, 512:512 + nq]
        rsl = t.scrf[:, 0:nq] if dc % 2 == 0 else t.scrf[:, 2048:2048 + nq]
        vsl = t.scrf[:, 512:512 + nq] if dc % 2 == 0 else t.scrf[:, 2560:2560 + nq]
        if dc >= 2:
            w.wait("acs", 'pe', val=sig_tok[dc - 2])
        i = None
        for a in range(NCHUNK):
            i = w.mm(pr, t.wbig[:, a * 1024 + dc * 128:a * 1024 + (dc + 1) * 128],
                     t.ffb[:, a * 512:a * 512 + nq], start=(a == 0), stop=(a == NCHUNK - 1))
        w.inc(i, "qk", 1)
        w.wait("qk", 'act')
        if dc >= 2:
            w.wait("dve", 'act', val=gate_tok[dc - 2])
        w.inc(w.act(rsl, pr, AF.Sigmoid), "acs", 1)
        sig_tok[dc] = w.val("acs")
        if dc >= 2:
            w.wait("dve", 'pe', val=gate_tok[dc - 2])
        i = None
        for a in range(NCHUNK):
            i = w.mm(pv, t.wbig[:, 16384 + a * 1024 + dc * 128:16384 + a * 1024 + (dc + 1) * 128],
                     t.kkb[:, a * 512:a * 512 + nq], start=(a == 0), stop=(a == NCHUNK - 1))
        w.inc(i, "rv", 1)
        w.wait("acs", 'dve', val=sig_tok[dc])
        w.wait("rv", 'dve')
        xc = t.xg[:, dc * 513 + 1: dc * 513 + 1 + nq]
        w.tt(vsl, pv, rsl, op=ALU.mult)
        w.tt(xc, xc, vsl, op=ALU.add)
        gate_tok[dc] = w.val("dve")
    if out_fuse:
        for tl in range(nq // 128):
            w.wait("dve", 'pe')
            i = None
            for a in range(NCHUNK):
                i = w.trans(t.p_mis[:, a * 128:(a + 1) * 128],
                            t.xg[:, a * 513 + 1 + tl * 128:a * 513 + 1 + (tl + 1) * 128],
                            t.eyef_sb[:, :])
            w.inc(i, "pe", 1)
            w.wait("pe", 'dve')
            ob = tl % 2
            w.wait("oput", 'dve', val=t.ot_tok[ob])
            ot = t.xtok[:, ob * 1024:(ob + 1) * 1024]
            for a in range(NCHUNK):
                w.copy(ot[:, a * 128:(a + 1) * 128], t.p_mis[:, a * 128:(a + 1) * 128])
            w.wait("dve", 'sync')
            w.inc(w.dma('sync', t.out_h[q0 + tl * 128:q0 + (tl + 1) * 128, :], ot), "oput", 16)
            t.ot_tok[ob] = w.val("oput")
    else:
        w.wait("dve", 'sync')
        xgv = t.xg.rearrange("p (a c) -> p a c", c=513)[:, :, 1:1 + nq]
        w.inc(w.dma('sync', t.x_hbm[:, :, q0:q0 + nq].rearrange("a p c -> p a c"), xgv), "dsx", 16)
    w.wait("dve", 'pe')


def _attnb_pre(w, t, qg):
    nq = QGS[qg]
    q0 = QG0[qg]
    _ln_fm(w, t, qg)
    w.wait("dve", 'pe')
    for dc in range(NCHUNK):
        pd = t.p_mis[:, (dc % 2) * 512:(dc % 2) * 512 + nq]
        if dc >= 2:
            w.wait("dve", 'pe', val=w.val("dve") - 3)
        i = None
        for a in range(NCHUNK):
            i = w.mm(pd, t.wbig[:, a * 1024 + dc * 128:a * 1024 + (dc + 1) * 128],
                     t.xn_sb[:, a * 520 + 1:a * 520 + 1 + nq],
                     start=(a == 0), stop=(a == NCHUNK - 1))
        w.inc(i, "qk", 1)
        w.wait("qk", 'dve')
        bq = t.bchunk_sb[:, 8 + dc:9 + dc]
        mn = t.scrf[:, 0:nq]
        mx = t.scrf[:, 512:512 + nq]
        w.ts(mn, pd, bq, 0.0, op0=ALU.add, op1=ALU.min)
        w.ts(mx, pd, bq, 0.0, op0=ALU.add, op1=ALU.max)
        w.wait("dve", 'act')
        w.inc(w.act(t.scrf[:, 1024:1024 + nq], mn, AF.Exp), "acs", 1)
        w.wait("acs", 'dve')
        w.wait("dst", 'dve', val=w.val("dst") - 16)
        w.tt(t.stage[:, (dc % 2) * 1024:(dc % 2) * 1024 + nq],
                   t.scrf[:, 1024:1024 + nq], mx, op=ALU.add)
        w.wait("dve", 'pool')
        w.inc(w.dma('pool', t.qb_hbm[dc, :, q0:q0 + nq],
                    t.stage[:, (dc % 2) * 1024:(dc % 2) * 1024 + nq]), "dst", 16)
    ntt_q = nq // 128
    for tl in range(ntt_q):
        ttg = q0 // 128 + tl
        mcol = t.maskv_sb[:, ttg:ttg + 1]
        for half in range(2):
            pk = t.p_mis[:, 0:512]
            if tl > 0 or half > 0:
                w.wait("dve", 'pe')
            i = None
            for a in range(NCHUNK):
                i = w.mm(pk, t.xn_sb[:, a * 520 + 1 + tl * 128:a * 520 + 1 + (tl + 1) * 128],
                         t.wbig[:, 8192 + a * 1024 + half * 512:8192 + a * 1024 + (half + 1) * 512],
                         start=(a == 0), stop=False)
            w.mm(pk, t.onesb[0:1, 0:128],
                 t.brow_sb[0:1, 3 * E + half * 512:3 * E + (half + 1) * 512],
                 start=False, stop=False)
            i = w.mm(pk, t.onesb[0:1, 0:128],
                     t.brow_lo[0:1, 3 * E + half * 512:3 * E + (half + 1) * 512],
                     start=False, stop=True)
            w.inc(i, "qk", 1)
            w.wait("qk", 'dve')
            mn = t.scrf[:, 0:512]
            mx = t.scrf[:, 512:1024]
            w.ts(mn, pk, 0.0, None, op0=ALU.min)
            w.ts(mx, pk, 0.0, None, op0=ALU.max)
            w.wait("dve", 'act')
            w.inc(w.act(t.scrf[:, 1024:1536], mn, AF.Exp), "acs", 1)
            w.wait("acs", 'dve')
            w.wait("rv", 'dve')
            w.tt(t.scrf[:, 1536:2048], t.scrf[:, 1024:1536], mx, op=ALU.add)
            w.ts(t.scrb[:, half * 512:(half + 1) * 512], t.scrf[:, 1536:2048], mcol)
            w.wait("dve", 'pe')
            pv = t.p_mis[:, 512:1024]
            i = None
            for a in range(NCHUNK):
                i = w.mm(pv, t.xn_sb[:, a * 520 + 1 + tl * 128:a * 520 + 1 + (tl + 1) * 128],
                         t.wbig[:, 16384 + a * 1024 + half * 512:16384 + a * 1024 + (half + 1) * 512],
                         start=(a == 0), stop=False)
            w.mm(pv, t.onesb[0:1, 0:128],
                 t.brow_sb[0:1, 4 * E + half * 512:4 * E + (half + 1) * 512],
                 start=False, stop=False)
            i = w.mm(pv, t.onesb[0:1, 0:128],
                     t.brow_lo[0:1, 4 * E + half * 512:4 * E + (half + 1) * 512],
                     start=False, stop=True)
            w.inc(i, "qk", 1)
            w.wait("qk", 'dve')
            w.copy(t.scrb[:, 1024 + half * 512:1024 + (half + 1) * 512], pv)
        w.wait("dve", 'pe')
        i = None
        for pp in range(NCHUNK):
            i = w.mm(t.p_qk_a[:, pp * 128:(pp + 1) * 128], t.scrb[:, pp * 128:(pp + 1) * 128],
                     t.scrb[:, 1024 + pp * 128:1024 + (pp + 1) * 128],
                     start=True, stop=True)
        w.inc(i, "rv", 1)
        w.wait("rv", 'dve')
        w.tt(t.kvacc_b[:, :], t.kvacc_b[:, :], t.p_qk_a[:, :], op=ALU.add)
    w.wait("dve", 'pe')


def _attnb_reduce(w, t):
    w.wait("dve", 'pool')
    w.inc(w.dma('pool', t.kv_in, t.kvacc_b[:, :]), "dst", 16)
    w.wait("dst", 'pool')
    w.inc(w.cc([t.kv_in], [t.kv_out]), "cc", 1)
    w.wait("cc", 'pool')
    w.inc(w.dma('pool', t.scrf[:, 0:1024], t.kv_out), "dst", 16)
    w.wait("dst", 'dve')
    i = None
    for a in range(NCHUNK):
        w.copy(t.kvblk_b[0:64, a * 128:a * 128 + 64], t.scrf[0:64, a * 128:a * 128 + 64])
        i = w.copy(t.kvblk_b[64:128, a * 128 + 64:a * 128 + 128],
                   t.scrf[64:128, a * 128 + 64:a * 128 + 128])
    w.wait("dve", 'pe')


def _attnb_out(w, t, qg):
    nq = QGS[qg]
    q0 = QG0[qg]
    w.wait("dst", 'sync')
    qk_tok = [None, None]
    for p in range(NCHUNK):
        sl = p % 2
        qp = t.qtile[:, sl * 512:sl * 512 + nq]
        w.wait("qk", 'sync', val=(qk_tok[sl] if qk_tok[sl] is not None else w.val("qk")))
        w.inc(w.dma('sync', qp, t.qb_hbm[p, :, q0:q0 + nq]), "dq", 16)
        w.wait("dq", 'pe')
        w.wait("dve", 'pe', val=w.val("dve") - 1 if p >= 2 else w.val("dve"))
        w.inc(w.mm(t.p_mis[:, sl * 512:sl * 512 + nq],
                   t.kvblk_b[:, p * 128:(p + 1) * 128], qp), "qk", 1)
        qk_tok[sl] = w.val("qk")
        w.wait("qk", 'dve')
        if p == 0:
            w.wait("pe", 'dve')
        w.copy(t.ofb[:, p * 512:p * 512 + nq], t.p_mis[:, sl * 512:sl * 512 + nq])
    w.barrier()
    w.wait("dve", 'pe')
    w.wait("dve", 'sync')
    w.wait("pe", 'sync')
    xgv = t.xg.rearrange("p (a c) -> p a c", c=513)[:, :, 1:1 + nq]
    w.inc(w.dma('sync', xgv, t.x_hbm[:, :, q0:q0 + nq].rearrange("a p c -> p a c")), "dx", 16)
    w.wait("dx", 'dve')
    wc_tok = [None, None]
    for fc in range(NCHUNK):
        wb = fc % 2
        w.wait("pe", 'pool', val=(wc_tok[wb] if wc_tok[wb] is not None else w.val("pe")))
        w.inc(w.dma('pool', t.wcol[:, wb * 1024:(wb + 1) * 1024].rearrange("p (a c) -> p a c", a=8),
                    t.w_bo[:, fc * 128:(fc + 1) * 128].rearrange("(a p) c -> p a c", p=128)),
              "dw", 16)
        w.wait("dw", 'pe')
        w.wait("dve", 'pe')
        i = None
        for a in range(NCHUNK):
            i = w.mm(t.p_kv[:, 0:nq], t.wcol[:, wb * 1024 + a * 128:wb * 1024 + (a + 1) * 128],
                     t.ofb[:, a * 512:a * 512 + nq], start=(a == 0), stop=(a == NCHUNK - 1))
        w.inc(i, "pe", 1)
        wc_tok[wb] = w.val("pe")
        w.wait("pe", 'dve')
        xc = t.xg[:, fc * 513 + 1: fc * 513 + 1 + nq]
        w.ts(t.scrf[:, 0:nq], t.p_kv[:, 0:nq], t.bchunk_sb[:, 16 + fc:17 + fc], op0=ALU.add)
        w.tt(xc, xc, t.scrf[:, 0:nq], op=ALU.add)
    sl_ = (qg % 2) * 8
    w.copy(t.stash[:, sl_:sl_ + 8].rearrange("p (a one) -> p a one", one=1),
           t.xg.rearrange("p (a c) -> p a c", c=513)[:, :, nq:nq + 1])
    w.wait("dve", 'pe')


def _output(w, t):
    ot_tok = [0, 0]
    ld_tok = [0, 0]
    for tt_ in range(NTT):
        lb = tt_ % 2
        xf = t.xfst[:, 0:1024] if lb == 0 else t.scrf[:, 0:1024]
        w.wait("pe", 'sync', val=ld_tok[lb])
        w.inc(w.dma('sync', xf.rearrange("p (a c) -> p a c", a=8),
                    t.x_hbm[:, :, tt_ * 128:(tt_ + 1) * 128].rearrange("a p c -> p a c")),
              "dx", 16)
        w.wait("dx", 'pe')
        w.wait("dve", 'pe')
        i = None
        for a in range(NCHUNK):
            i = w.trans(t.p_mis[:, a * 128:(a + 1) * 128], xf[:, a * 128:(a + 1) * 128],
                        t.eyef_sb[:, :])
        w.inc(i, "pe", 1)
        ld_tok[lb] = w.val("pe")
        w.wait("pe", 'dve')
        ob = tt_ % 2
        w.wait("oput", 'dve', val=ot_tok[ob])
        ot = t.xtok[:, ob * 1024:(ob + 1) * 1024]
        i = None
        for a in range(NCHUNK):
            i = w.copy(ot[:, a * 128:(a + 1) * 128], t.p_mis[:, a * 128:(a + 1) * 128])
        w.wait("dve", 'sync')
        w.inc(w.dma('sync', t.out_h[tt_ * 128:(tt_ + 1) * 128, :], ot), "oput", 16)
        ot_tok[ob] = w.val("oput")



def _host_inputs(inputs):
    x = np.ascontiguousarray(np.asarray(inputs["L0_attn_a_x"], dtype=np.float32))
    g = lambda k: np.ascontiguousarray(np.asarray(inputs[k], dtype=np.float32))
    shared = {
        "w_aq": g("L0_attn_a_Wq"), "w_ak": g("L0_attn_a_Wk"),
        "w_av": g("L0_attn_a_Wv"), "w_ao": g("L0_attn_a_Wo"),
        "w_fr": np.ascontiguousarray(g("L0_ffn_W_r").T),
        "w_fk": np.ascontiguousarray(g("L0_ffn_W_k").T),
        "w_fv": np.ascontiguousarray(g("L0_ffn_W_v").T),
        "w_bq": np.ascontiguousarray(g("L0_attn_b_Wq").T),
        "w_bk": np.ascontiguousarray(g("L0_attn_b_Wk").T),
        "w_bv": np.ascontiguousarray(g("L0_attn_b_Wv").T),
        "w_bo": np.ascontiguousarray(g("L0_attn_b_Wo").T),
    }
    b_rows = np.stack([g("L0_attn_a_bq"), g("L0_attn_a_bk"), g("L0_attn_a_bv"),
                       g("L0_attn_b_bk"), g("L0_attn_b_bv")]).astype(np.float32)
    b_chunk = np.zeros((128, 4 * NCHUNK), np.float32)
    b_chunk[:, 0:8] = g("L0_attn_a_bo").reshape(NCHUNK, 128).T
    b_chunk[:, 8:16] = g("L0_attn_b_bq").reshape(NCHUNK, 128).T
    b_chunk[:, 16:24] = g("L0_attn_b_bo").reshape(NCHUNK, 128).T
    mu_chunk = np.zeros((128, 2 * NCHUNK), np.float32)
    mu_chunk[:, 0:8] = g("L0_ffn_mu_r").reshape(NCHUNK, 128).T
    mu_chunk[:, 8:16] = g("L0_ffn_mu_k").reshape(NCHUNK, 128).T
    shared["b_rows"] = np.ascontiguousarray(b_rows)
    shared["b_chunk"] = np.ascontiguousarray(b_chunk)
    shared["mu_chunk"] = np.ascontiguousarray(mu_chunk)
    shared["eyef"] = np.eye(128, dtype=np.float32)
    shared["eyeb"] = np.eye(128, dtype=np.float32).astype(ml_dtypes.bfloat16)
    in_maps = []
    for c in range(8):
        b, h = c // 2, c % 2
        m = dict(shared)
        m["xq"] = np.ascontiguousarray(x[b, 3968 * h: 3968 * h + TOK, :])
        m["xfull"] = np.ascontiguousarray(x[b])
        mk = np.zeros(TOK, np.float32)
        if h == 0:
            mk[0:4096] = 1.0
        else:
            mk[128:TOK] = 1.0
        m["maskv"] = np.ascontiguousarray(mk.reshape(NTT, 128).T)
        in_maps.append(m)
    return in_maps


def run_staged(inputs, stop_stage, cores=None):
    nc = build_program(stop_stage=stop_stage)
    in_maps = _host_inputs(inputs)
    cores = cores if cores is not None else list(range(8))
    res = run_bass_kernel_spmd(nc, [in_maps[c] for c in cores], core_ids=cores)
    return [r["dbg"] for r in res.results]


def kernel(**inputs):
    nc = build_program()
    in_maps = _host_inputs(inputs)
    res = run_bass_kernel_spmd(nc, in_maps, core_ids=list(range(8)))
    y = np.empty((B, S, E), np.float32)
    for c in range(8):
        b, h = c // 2, c % 2
        o = res.results[c]["out"]
        if h == 0:
            y[b, 0:4096] = o[0:4096]
        else:
            y[b, 4096:8192] = o[128:TOK]
    return y
